# version 01
# baseline (speedup 1.0000x reference)
import contextlib
import numpy as np
import concourse.bass as bass
import concourse.mybir as mybir
from concourse.bass_utils import run_bass_kernel_spmd

F32 = mybir.dt.float32
BF16 = mybir.dt.bfloat16
I32 = mybir.dt.int32
U8 = mybir.dt.uint8
AF = mybir.ActivationFunctionType
ALU = mybir.AluOpType
AX = mybir.AxisListType.X

D = 2048
HID = 5632
L = 8192
NCORES = 8
NT = 1024
NB = 16
BS = 512
KCH = 16
HCH = 44
OCH = 16
CSC = float(1.0 / np.sqrt(128.0))
NW = float(D * HID)
LN_EPS = 1e-5
Q_EPS = 1e-8
_2PI = np.float64(2 * np.pi)
TWO_PI_HI = float(np.float32(_2PI))
TWO_PI_LO = float(np.float32(_2PI - np.float64(np.float32(_2PI))))
INV_2PI = float(np.float32(1.0 / _2PI))
HALF_PI = float(np.float32(np.pi / 2))
LOG1E4 = float(np.log(10000.0))
GSL = D * HID // NCORES // 128
DBG = False

ENGS = ("sync", "scalar", "vector", "tensor", "gpsimd")
DMA_SEMS = {"s_ld", "s_sc", "s_gp", "s_out"}
OWN = {"tensor": "s_mm", "vector": "s_ve", "scalar": "s_se", "gpsimd": "s_gq"}

XLN = 0
HAR = 32768
STR = 122880
TAIL = 188416
PER = 206848
ARENA_B = 210944

DTS = {F32: 4, BF16: 2, I32: 4}


def _rank_slot(b):
    return (b, 0) if b <= 7 else (15 - b, 1)


class P:
    def __init__(self, nc):
        self.nc = nc
        self.q = {e: [] for e in ENGS}
        self.cnt = {}

    def tok(self, sem):
        return (sem, self.cnt.get(sem, 0))

    def op(self, eng, fn, waits=(), inc=None, incn=None, hard=()):
        incs = []
        if inc is not None:
            n = incn if incn is not None else (16 if inc in DMA_SEMS else 1)
            self.cnt[inc] = self.cnt.get(inc, 0) + n
            incs.append((inc, n))
        ws = {}
        hw = {}
        for w in waits:
            if w is None:
                continue
            s, v = w
            if v > 0 and v > ws.get(s, 0):
                ws[s] = v
        for w in hard:
            if w is None:
                continue
            s, v = w
            if v > 0 and v > hw.get(s, 0):
                hw[s] = v
        self.q[eng].append((sorted(ws.items()), sorted(hw.items()), fn, incs))
        return self.tok(inc) if inc is not None else None

    def seq(self, eng, fns, waits=(), hard=()):
        own = OWN[eng]
        t = None
        for i, fn in enumerate(fns):
            self.op(eng, fn, waits=waits if i == 0 else (),
                    hard=(list(hard) + ([t] if t else [])) if i == 0
                    else [t], inc=own)
            t = self.tok(own)
        return t

    def emit(self, sems):
        guar = {e: {} for e in ENGS}
        with self.nc.Block() as blk:
            for eng in ENGS:
                def body(h, eng=eng):
                    g = guar[eng]
                    own = OWN.get(eng)
                    for ws, hws, fn, incs in self.q[eng]:
                        for s, v in ws:
                            if s == own:
                                continue
                            if v > g.get(s, 0):
                                h.wait_ge(sems[s], v)
                                g[s] = v
                        for s, v in hws:
                            if v > g.get(s, 0):
                                h.wait_ge(sems[s], v)
                                g[s] = v
                        inst = fn(h)
                        for s, n in incs:
                            inst = inst.then_inc(sems[s], n)
                getattr(blk, eng)(body)


def build_program(stop="full"):
    nc = bass.Bass("TRN2", target_bir_lowering=False)

    t_xT = nc.dram_tensor("xT", [D, NT], F32, kind="ExternalInput")
    t_pos = nc.dram_tensor("pos", [1, NT], F32, kind="ExternalInput")
    t_g = [nc.dram_tensor("g%d" % (j + 1), [128, GSL], F32,
                          kind="ExternalInput") for j in range(3)]
    t_w1p = nc.dram_tensor("w1p", [HCH, D, 128], F32, kind="ExternalInput")
    t_w2p = nc.dram_tensor("w2p", [HCH, D, 128], F32, kind="ExternalInput")
    t_w3p = nc.dram_tensor("w3p", [OCH, HID, 128], F32, kind="ExternalInput")
    t_wqp = nc.dram_tensor("wqp", [16, D, 128], F32, kind="ExternalInput")
    t_wkp = nc.dram_tensor("wkp", [2, D, 128], F32, kind="ExternalInput")
    t_wvt = nc.dram_tensor("wvt", [D, 256], F32, kind="ExternalInput")
    t_wop = nc.dram_tensor("wop", [16, D, 128], F32, kind="ExternalInput")
    t_bq = nc.dram_tensor("bqp", [128, 16], F32, kind="ExternalInput")
    t_bk = nc.dram_tensor("bkp", [128, 2], F32, kind="ExternalInput")
    t_bv = nc.dram_tensor("bvr", [1, 256], F32, kind="ExternalInput")
    t_sel = nc.dram_tensor("selv", [1, 8], I32, kind="ExternalInput")
    t_out = nc.dram_tensor("yT", [OCH, 128, NT], F32, kind="ExternalOutput")

    t_w1q = nc.dram_tensor("w1q_d", [HCH, 128, D], BF16)
    t_w2q = nc.dram_tensor("w2q_d", [HCH, 128, D], BF16)
    t_w3q = nc.dram_tensor("w3q_d", [OCH, 128, HID], BF16)
    t_xmid = nc.dram_tensor("xmid_d", [KCH, 128, NT], F32)
    t_arin = nc.dram_tensor("ar_in", [1, 8], F32)
    t_arout = nc.dram_tensor("ar_out", [1, 8], F32)
    t_taud = nc.dram_tensor("tau_d", [1, 8], F32)
    t_bshd = nc.dram_tensor("bsh_d", [1, 2], F32)
    t_kvin = nc.dram_tensor("kvag_in", [1, 524288], BF16)
    t_kvout = nc.dram_tensor("kvag_out", [8, 524288], BF16)
    t_qin = nc.dram_tensor("qag_in", [D, NT], BF16)
    t_qout = nc.dram_tensor("qag_out", [8 * D, NT], BF16)
    t_oin = nc.dram_tensor("oag_in", [2, 128, L], BF16)
    t_oout = nc.dram_tensor("oag_out", [16, 128, L], BF16)

    RG = [list(range(NCORES))]

    es = contextlib.ExitStack()
    with es:
        arena = es.enter_context(nc.sbuf_tensor("arena", [128, ARENA_B], U8))
        psA = es.enter_context(nc.psum_tensor("psA", [128, 4096], F32))
        sems = {}
        for s in ("s_ld", "s_sc", "s_gp", "s_out", "s_cc",
                  "s_mm", "s_ve", "s_se", "s_gq"):
            sems[s] = es.enter_context(nc.semaphore(s))

        def V(off, n, dt):
            return arena[:, off:off + n * DTS[dt]].bitcast(dt)

        ones_f = V(PER + 0, 1, F32)
        ones_b = V(PER + 64, 1, BF16)
        onesrow_f = V(PER + 128, 128, F32)[0:1, :]
        taus = V(PER + 704, 8, F32)
        bq_s = V(PER + 768, 16, F32)
        bk_s = V(PER + 832, 2, F32)
        bshift = V(PER + 896, 2, F32)
        invf = V(PER + 960, 1, F32)
        pv_t = V(PER + 1024, 1, F32)
        gtmp = V(PER + 1088, 1, F32)
        bvbc = V(PER + 1152, 256, BF16)
        sbsel = V(PER + 1600, 8, I32)
        prow_full = V(PER + 1664, 256, F32)

        def prow(a, b):
            return prow_full[0:1, a:b]

        xln = V(XLN, 16384, BF16)
        lnscr = V(XLN, 8192, F32)
        stream = V(STR, 16384, F32)
        hbuf = V(HAR, 45056, BF16)

        p = P(nc)
        tk = {}

        x0 = V(HAR, 16384, F32)
        gbuf = [V(STR + k * 11264, 2816, F32) for k in (0, 1)]
        xsq = [V(TAIL + k * 4096, 1024, F32) for k in (0, 1)]
        racc = V(TAIL + 8192, 16, F32)

        p.op("gpsimd", lambda h: h.memset(ones_f[:, :], 1.0), inc="s_gq")
        p.op("gpsimd", lambda h: h.memset(ones_b[:, :], 1.0), inc="s_gq")
        p.op("gpsimd", lambda h: h.memset(onesrow_f, 1.0), inc="s_gq")
        tk["consts"] = p.tok("s_gq")

        p.op("sync", lambda h: h.dma_start(
            out=x0[:, :].rearrange("p (c t) -> p c t", c=KCH),
            in_=t_xT.ap().rearrange("(c p) t -> p c t", p=128)), inc="s_ld")
        tk["x0"] = p.tok("s_ld")
        p.op("sync", lambda h: h.dma_start(out=bq_s[:, :], in_=t_bq.ap()),
             inc="s_ld")
        p.op("sync", lambda h: h.dma_start(out=sbsel[0:1, :], in_=t_sel.ap()),
             inc="s_ld")
        tk["sel"] = p.tok("s_ld")
        p.op("sync", lambda h: h.dma_start(out=bk_s[:, :], in_=t_bk.ap()),
             inc="s_ld")
        tk["bias"] = p.tok("s_ld")
        p.op("gpsimd", lambda h: h.dma_start(
            out=bvbc[:, :], in_=bass.AP(t_bv, 0, [[0, 128], [1, 256]])),
            inc="s_gp")
        tk["bvbc"] = p.tok("s_gp")

        gred = [None] * 12
        for j in range(3):
            for c in range(4):
                i = 4 * j + c
                par = i % 2
                w = [gred[i - 2]] if i >= 2 else []
                p.op("scalar", lambda h, j=j, c=c, par=par: h.dma_start(
                    out=gbuf[par][:, :],
                    in_=t_g[j].ap()[:, c * 2816:(c + 1) * 2816]),
                    waits=w, inc="s_sc")
                ldt = p.tok("s_sc")
                p.op("vector", lambda h, i=i, par=par: h.tensor_reduce(
                    racc[:, i:i + 1], gbuf[par][:, :], axis=AX, op=ALU.add,
                    apply_absolute_value=True), waits=[ldt], inc="s_ve")
                gred[i] = p.tok("s_ve")
        p.op("tensor", lambda h: h.matmul(
            psA[0:1, 0:12], ones_f[:, :], racc[:, 0:12], start=True,
            stop=True), waits=[gred[11], tk["consts"]], inc="s_mm")
        tgmm = p.tok("s_mm")

        tk["gpart"] = p.seq("vector", [
            lambda h: h.tensor_copy(prow(0, 12), psA[0:1, 0:12]),
            lambda h: h.reduce_sum(prow(12, 13), prow(0, 4), axis=AX),
            lambda h: h.reduce_sum(prow(13, 14), prow(4, 8), axis=AX),
            lambda h: h.reduce_sum(prow(14, 15), prow(8, 12), axis=AX),
        ], waits=[tgmm])
        p.op("gpsimd", lambda h: h.dma_start(out=t_arin.ap(),
                                             in_=prow(12, 20)),
             waits=[tk["gpart"]], inc="s_gp")
        t_ = p.tok("s_gp")
        p.op("gpsimd", lambda h: h.collective_compute(
            "AllReduce", ALU.add, replica_groups=RG,
            ins=[t_arin.ap().opt()], outs=[t_arout.ap().opt()]),
            waits=[t_], inc="s_cc", incn=1)
        tar = p.tok("s_cc")
        p.op("gpsimd", lambda h: h.dma_start(out=prow(24, 32),
                                             in_=t_arout.ap()),
             waits=[tar], inc="s_gp")
        tab = p.tok("s_gp")

        ttr = p.seq("vector", [
            lambda h: h.tensor_scalar(prow(32, 35), prow(24, 27), 1.0 / NW,
                                      None, ALU.mult),
            lambda h: h.tensor_scalar(prow(35, 38), prow(32, 35), 0.5,
                                      0.5 * Q_EPS, ALU.mult, ALU.add),
            lambda h: h.tensor_scalar(prow(38, 41), prow(35, 38), -1.0,
                                      None, ALU.mult),
        ], waits=[tab])
        p.op("gpsimd", lambda h: h.dma_start(out=t_taud.ap(),
                                             in_=prow(35, 43)),
             waits=[ttr], inc="s_gp")
        t_ = p.tok("s_gp")
        p.op("gpsimd", lambda h: h.dma_start(
            out=taus[:, :], in_=bass.AP(t_taud, 0, [[0, 128], [1, 8]])),
            waits=[t_], inc="s_gp")
        tk["taus"] = p.tok("s_gp")

        def emit_ln(src, src_toks, extra_mm_waits=()):
            mm_done = [None] * KCH
            for kc in range(KCH):
                par = kc % 2
                w = [src_toks(kc)] if src_toks else []
                if kc >= 2 and mm_done[kc - 2] is not None:
                    w = w + [mm_done[kc - 2]]
                p.op("scalar", lambda h, kc=kc, par=par: h.activation(
                    xsq[par][:, :], src(kc), AF.Square), waits=w, inc="s_se")
                sqt = p.tok("s_se")
                mw = [sqt, tk["consts"]] + list(extra_mm_waits) + \
                    ([src_toks(kc)] if src_toks else [])
                for seg in range(4):
                    def mk(h, kc=kc, seg=seg, par=par):
                        if seg < 2:
                            rh = src(kc)[:, seg * 512:(seg + 1) * 512]
                        else:
                            rh = xsq[par][:, (seg - 2) * 512:(seg - 1) * 512]
                        return h.matmul(psA[0:1, seg * 512:(seg + 1) * 512],
                                        ones_f[:, :], rh,
                                        start=(kc == 0), stop=(kc == KCH - 1))
                    p.op("tensor", mk, waits=mw if seg == 0 else [sqt],
                         inc="s_mm" if seg == 3 else None)
                mm_done[kc] = p.tok("s_mm")
            tsum = mm_done[KCH - 1]

            M, VE, T, R0, RS = 0, 1024, 2048, 3072, 4096

            t1 = p.seq("vector", [
                lambda h: h.tensor_scalar(lnscr[0:1, M:M + 1024],
                                          psA[0:1, 0:1024], 1.0 / D, None,
                                          ALU.mult),
                lambda h: h.tensor_scalar(lnscr[0:1, VE:VE + 1024],
                                          psA[0:1, 1024:2048], 1.0 / D,
                                          LN_EPS, ALU.mult, ALU.add),
                lambda h: h.tensor_tensor(lnscr[0:1, T:T + 1024],
                                          lnscr[0:1, M:M + 1024],
                                          lnscr[0:1, M:M + 1024], ALU.mult),
                lambda h: h.tensor_tensor(lnscr[0:1, VE:VE + 1024],
                                          lnscr[0:1, VE:VE + 1024],
                                          lnscr[0:1, T:T + 1024],
                                          ALU.subtract),
                lambda h: h.reciprocal(lnscr[0:1, R0:R0 + 1024],
                                       lnscr[0:1, VE:VE + 1024]),
            ], waits=[tsum])
            p.op("scalar", lambda h: h.activation(
                lnscr[0:1, RS:RS + 1024], lnscr[0:1, R0:R0 + 1024], AF.Sqrt),
                waits=[t1], inc="s_se")
            t2 = p.tok("s_se")

            t3 = p.seq("vector", [
                lambda h: h.tensor_tensor(lnscr[0:1, T:T + 1024],
                                          lnscr[0:1, RS:RS + 1024],
                                          lnscr[0:1, RS:RS + 1024],
                                          ALU.mult),
                lambda h: h.tensor_tensor(lnscr[0:1, T:T + 1024],
                                          lnscr[0:1, T:T + 1024],
                                          lnscr[0:1, VE:VE + 1024],
                                          ALU.mult),
                lambda h: h.tensor_scalar(lnscr[0:1, T:T + 1024],
                                          lnscr[0:1, T:T + 1024], -0.5, 1.5,
                                          ALU.mult, ALU.add),
                lambda h: h.tensor_tensor(lnscr[0:1, RS:RS + 1024],
                                          lnscr[0:1, RS:RS + 1024],
                                          lnscr[0:1, T:T + 1024], ALU.mult),
                lambda h: h.tensor_tensor(lnscr[0:1, M:M + 1024],
                                          lnscr[0:1, M:M + 1024],
                                          lnscr[0:1, RS:RS + 1024],
                                          ALU.mult),
            ], waits=[t2])
            for i in range(4):
                coff = RS if i < 2 else M
                boff = 2048 + i * 512
                p.op("tensor", lambda h, coff=coff, boff=boff, i=i: h.matmul(
                    psA[:, boff:boff + 512], onesrow_f,
                    lnscr[0:1, coff + (i % 2) * 512:
                          coff + (i % 2) * 512 + 512],
                    start=True, stop=True),
                    waits=[t3, tk["consts"]], inc="s_mm" if i == 3 else None)
            tbc = p.tok("s_mm")
            last = None
            for kc in range(KCH):
                par = kc % 2

                def _x(h, kc=kc, par=par):
                    h.tensor_tensor(xsq[par][:, :], src(kc),
                                    psA[:, 2048:3072], ALU.mult)
                    return h.tensor_tensor(
                        xln[:, kc * 1024:(kc + 1) * 1024], xsq[par][:, :],
                        psA[:, 3072:4096], ALU.subtract)
                p.op("vector", _x, waits=[tbc] +
                     ([src_toks(kc)] if src_toks else []), inc="s_ve")
                last = p.tok("s_ve")
            return last

        tk["xln"] = emit_ln(lambda kc: x0[:, kc * 1024:(kc + 1) * 1024],
                            lambda kc: tk["x0"],
                            extra_mm_waits=[tk["gpart"]])

        def emit_ffn_B(pre, entry_waits):
            if pre:
                wraw = [[V(STR + k * 8192, 2048, F32) for k in (0, 1)],
                        [V(STR + 16384 + k * 8192, 2048, F32)
                         for k in (0, 1)]]
                wqv = [[V(STR + 32768 + k * 4096, 2048, BF16)
                        for k in (0, 1)],
                       [V(STR + 40960 + k * 4096, 2048, BF16)
                        for k in (0, 1)]]
                ltmp = V(STR + 49152, 2048, BF16)
                sigb = V(STR + 53248, 1024, BF16)
            else:
                wqv = [[V(TAIL + k * 4096, 2048, BF16) for k in (0, 1)],
                       [V(TAIL + 8192 + k * 4096, 2048, BF16)
                        for k in (0, 1)]]
                sigb = V(TAIL + 16384, 1024, BF16)
            xb = [[psA[:, 0:1024], psA[:, 1024:2048]],
                  [psA[:, 2048:3072], psA[:, 3072:4096]]]
            wsrc = (t_w1p, t_w2p)
            wdst = (t_w1q, t_w2q)
            tcols = ((0, 3), (1, 4))
            qtok = [[None] * HCH, [None] * HCH]
            stok = [[None] * HCH, [None] * HCH]
            mmtok = [[None] * HCH, [None] * HCH]
            sigtok = [None] * HCH
            htok = [None] * HCH
            for hc in range(HCH):
                par = hc % 2
                for wi in range(2):
                    fw = [mmtok[wi][hc - 2]] if hc >= 2 else list(entry_waits)
                    if pre:
                        if hc >= 2:
                            fw.append(qtok[wi][hc - 2])
                        p.op("sync", lambda h, wi=wi, hc=hc, par=par:
                             h.dma_start(
                                 out=wraw[wi][par][:, :].rearrange(
                                     "p (kc m) -> p kc m", kc=KCH),
                                 in_=bass.AP(wsrc[wi], hc * D * 128,
                                             [[128, 128], [16384, KCH],
                                              [1, 128]])),
                             waits=fw, inc="s_ld")
                        ldt = p.tok("s_ld")
                        qw = [ldt, tk["taus"]]
                        if hc >= 2:
                            qw += [mmtok[wi][hc - 2], stok[wi][hc - 2]]
                        if hc == 0 and wi == 0:
                            qw += list(entry_waits)

                        def _q(h, wi=wi, par=par):
                            tc, nc_ = tcols[wi]
                            h.tensor_single_scalar(
                                ltmp[:, :], wraw[wi][par][:, :],
                                taus[:, nc_:nc_ + 1], ALU.is_lt)
                            return h.scalar_tensor_tensor(
                                wqv[wi][par][:, :], wraw[wi][par][:, :],
                                taus[:, tc:tc + 1], ltmp[:, :],
                                ALU.is_gt, ALU.subtract)
                        p.op("vector", _q, waits=qw, inc="s_ve")
                        qtok[wi][hc] = p.tok("s_ve")
                        p.op("gpsimd", lambda h, wi=wi, hc=hc, par=par:
                             h.dma_start(out=wdst[wi].ap()[hc, :, :],
                                         in_=wqv[wi][par][:, :]),
                             waits=[qtok[wi][hc]], inc="s_gp")
                        stok[wi][hc] = p.tok("s_gp")
                    else:
                        p.op("sync", lambda h, wi=wi, hc=hc, par=par:
                             h.dma_start(out=wqv[wi][par][:, :],
                                         in_=wdst[wi].ap()[hc, :, :]),
                             waits=fw, inc="s_ld")
                        qtok[wi][hc] = p.tok("s_ld")
                for wi in range(2):
                    mw = [qtok[wi][hc], tk["xln"]]
                    if hc >= 2:
                        mw.append(sigtok[hc - 2] if wi == 0
                                  else htok[hc - 2])
                    if hc < 2:
                        mw += list(entry_waits)
                    for kc in range(KCH):
                        for half in (0, 1):
                            lst = (kc == KCH - 1 and half == 1)
                            p.op("tensor", lambda h, wi=wi, par=par, kc=kc,
                                 half=half: h.matmul(
                                     xb[wi][par][:, half * 512:
                                                 (half + 1) * 512],
                                     wqv[wi][par][:, kc * 128:
                                                  (kc + 1) * 128],
                                     xln[:, kc * 1024 + half * 512:
                                         kc * 1024 + half * 512 + 512],
                                     start=(kc == 0), stop=(kc == KCH - 1)),
                                 waits=mw if (kc == 0 and half == 0) else [],
                                 inc="s_mm" if lst else None)
                    mmtok[wi][hc] = p.tok("s_mm")
                sw = [mmtok[0][hc]]
                if hc >= 1:
                    sw.append(htok[hc - 1])
                p.op("scalar", lambda h, par=par: h.activation(
                    sigb[:, :], xb[0][par][:, :], AF.Sigmoid),
                    waits=sw, inc="s_se")
                sigtok[hc] = p.tok("s_se")
                p.op("vector", lambda h, hc=hc, par=par: h.tensor_tensor(
                    hbuf[:, hc * 1024:(hc + 1) * 1024], sigb[:, :],
                    xb[1][par][:, :], ALU.mult),
                    waits=[sigtok[hc], mmtok[1][hc]], inc="s_ve")
                htok[hc] = p.tok("s_ve")
            return htok, sigtok, stok

        def emit_ffn_C(pre, htok, sigtok, resid, centry=()):
            if pre:
                w3raw = [V(STR + k * 5632, 1408, F32) for k in (0, 1)]
                w3qs = [V(STR + 11264 + k * 2816, 1408, BF16) for k in (0, 1)]
                ltmp2 = V(STR + 16896, 1408, BF16)
                xres = [V(STR + 19776 + k * 4096, 1024, F32) for k in (0, 1)]
                outb = [V(STR + 27968 + k * 4096, 1024, F32) for k in (0, 1)]
            else:
                w3qs = [V(TAIL + k * 2816, 1408, BF16) for k in (0, 1)]
                outb = [V(TAIL + 5632 + k * 4096, 1024, F32) for k in (0, 1)]
            yb = [psA[:, 0:1024], psA[:, 1024:2048]]
            NS = 4 * OCH
            ld_tok = [None] * NS
            qv_tok = [None] * NS
            sc_tok = [None] * NS
            mm_sub = [None] * NS
            eptok = [None] * OCH
            dmtok = [None] * OCH
            entry = [sigtok[HCH - 1], htok[HCH - 1]] + list(centry)

            def prep(si):
                oc, j = si // 4, si % 4
                jp = si % 2
                fw = [mm_sub[si - 2]] if si >= 2 else list(entry)
                if pre:
                    lw = list(fw)
                    if si >= 2:
                        lw.append(qv_tok[si - 2])
                    else:
                        lw += list(entry)
                    p.op("sync", lambda h, oc=oc, j=j, jp=jp:
                         h.dma_start(
                             out=w3raw[jp][:, :].rearrange(
                                 "p (kc m) -> p kc m", kc=11),
                             in_=bass.AP(t_w3p,
                                         oc * HID * 128 + j * 1408 * 128,
                                         [[128, 128], [16384, 11],
                                          [1, 128]])),
                         waits=lw, inc="s_ld")
                    ldt = p.tok("s_ld")
                    qw = [ldt, tk["taus"]] + fw
                    if si >= 2:
                        qw.append(sc_tok[si - 2])

                    def _q3(h, jp=jp):
                        h.tensor_single_scalar(
                            ltmp2[:, :], w3raw[jp][:, :], taus[:, 5:6],
                            ALU.is_lt)
                        return h.scalar_tensor_tensor(
                            w3qs[jp][:, :], w3raw[jp][:, :],
                            taus[:, 2:3], ltmp2[:, :],
                            ALU.is_gt, ALU.subtract)
                    p.op("vector", _q3, waits=qw, inc="s_ve")
                    qv_tok[si] = p.tok("s_ve")
                    ld_tok[si] = qv_tok[si]
                    p.op("gpsimd", lambda h, oc=oc, j=j, jp=jp:
                         h.dma_start(
                             out=t_w3q.ap()[oc, :,
                                            j * 1408:(j + 1) * 1408],
                             in_=w3qs[jp][:, :]),
                         waits=[qv_tok[si]], inc="s_gp")
                    sc_tok[si] = p.tok("s_gp")
                else:
                    p.op("sync", lambda h, oc=oc, j=j, jp=jp:
                         h.dma_start(
                             out=w3qs[jp][:, :],
                             in_=t_w3q.ap()[oc, :,
                                            j * 1408:(j + 1) * 1408]),
                         waits=fw, inc="s_ld")
                    ld_tok[si] = p.tok("s_ld")

            prep(0)
            prep(1)
            for oc in range(OCH):
                par = oc % 2
                for kc in range(HCH):
                    si = 4 * oc + kc // 11
                    jp = si % 2
                    kw = [ld_tok[si], htok[kc]]
                    if kc == 0:
                        kw.append(htok[0])
                        if oc >= 2:
                            kw.append(eptok[oc - 2])
                        else:
                            kw += list(entry)
                    for half in (0, 1):
                        sub_last = (kc % 11 == 10 and half == 1)
                        p.op("tensor", lambda h, par=par, kc=kc, jp=jp,
                             half=half: h.matmul(
                                 yb[par][:, half * 512:(half + 1) * 512],
                                 w3qs[jp][:, (kc % 11) * 128:
                                          (kc % 11) * 128 + 128],
                                 hbuf[:, kc * 1024 + half * 512:
                                      kc * 1024 + half * 512 + 512],
                                 start=(kc == 0), stop=(kc == HCH - 1)),
                             waits=kw if half == 0 else [],
                             inc="s_mm" if sub_last else None)
                        if sub_last:
                            mm_sub[si] = p.tok("s_mm")
                    if kc % 11 == 10 and si + 2 < NS:
                        prep(si + 2)
                ymm = mm_sub[4 * oc + 3]
                if resid == "xT":
                    rw = [eptok[oc - 2]] if oc >= 2 else \
                        [htok[HCH - 1]] + list(centry)
                    p.op("scalar", lambda h, oc=oc, par=par: h.dma_start(
                        out=xres[par][:, :],
                        in_=t_xT.ap()[oc * 128:(oc + 1) * 128, :]),
                        waits=rw, inc="s_sc")
                    rt = [p.tok("s_sc")]
                    rsl = lambda par=par: xres[par][:, :]
                else:
                    rt = []
                    rsl = lambda oc=oc: stream[:, oc * 1024:(oc + 1) * 1024]
                ew = [ymm] + rt
                if oc >= 2:
                    ew.append(dmtok[oc - 2])
                else:
                    ew += [htok[HCH - 1]] + list(centry)
                p.op("vector", lambda h, par=par, rsl=rsl:
                     h.scalar_tensor_tensor(
                         outb[par][:, :], yb[par][:, :], 0.5, rsl(),
                         ALU.mult, ALU.add), waits=ew, inc="s_ve")
                eptok[oc] = p.tok("s_ve")
                if pre:
                    p.op("gpsimd", lambda h, oc=oc, par=par: h.dma_start(
                        out=t_xmid.ap()[oc, :, :], in_=outb[par][:, :]),
                        waits=[eptok[oc]], inc="s_gp")
                    dmtok[oc] = p.tok("s_gp")
                    if stop == "ffnpre":
                        p.op("gpsimd", lambda h, oc=oc, par=par: h.dma_start(
                            out=t_out.ap()[oc, :, :], in_=outb[par][:, :]),
                            waits=[eptok[oc]], inc="s_out")
                        dmtok[oc] = p.tok("s_out")
                else:
                    p.op("gpsimd", lambda h, oc=oc, par=par: h.dma_start(
                        out=t_out.ap()[oc, :, :], in_=outb[par][:, :]),
                        waits=[eptok[oc]], inc="s_out")
                    dmtok[oc] = p.tok("s_out")
            return eptok, dmtok

        htokB, sigtokB, stokB = emit_ffn_B(True, entry_waits=[gred[11]])
        eptokC, dmtokC = emit_ffn_C(
            True, htokB, sigtokB, resid="xT",
            centry=[stokB[0][HCH - 1], stokB[1][HCH - 1]])

        if stop == "ffnpre":
            if DBG:
                end = [p.tok("s_ve"), p.tok("s_mm"), p.tok("s_se"),
                       p.tok("s_gp"), p.tok("s_ld"), p.tok("s_sc")]
                dump = nc.dram_tensor("dbg", [8, 128, 1024], F32,
                                      kind="ExternalOutput")
                p.op("gpsimd", lambda h: h.dma_start(
                    out=dump.ap()[0, 0:1, 0:1024],
                    in_=V(PER, 1024, F32)[0:1, :]), waits=end, inc="s_out")
                p.op("gpsimd", lambda h: h.dma_start(
                    out=dump.ap()[1, :, :],
                    in_=xln[:, 0:1024]), waits=end, inc="s_out")
                p.op("gpsimd", lambda h: h.dma_start(
                    out=dump.ap()[2, :, :],
                    in_=hbuf[:, 0:1024]), waits=end, inc="s_out")
                p.op("gpsimd", lambda h: h.dma_start(
                    out=dump.ap()[3, :, :],
                    in_=t_w1q.ap()[0, :, 0:1024]), waits=end, inc="s_out")
                p.op("gpsimd", lambda h: h.dma_start(
                    out=dump.ap()[4, :, :],
                    in_=x0[:, 0:1024]), waits=end, inc="s_out")
                p.op("gpsimd", lambda h: h.dma_start(
                    out=dump.ap()[5, 0:1, :],
                    in_=lnscr[0:1, 4096:5120]), waits=end, inc="s_out")
                dmph = nc.dram_tensor("dbg_h", [44, 128, 1024], F32,
                                      kind="ExternalOutput")
                for hc in range(HCH):
                    p.op("gpsimd", lambda h, hc=hc: h.dma_start(
                        out=dmph.ap()[hc, :, :],
                        in_=hbuf[:, hc * 1024:(hc + 1) * 1024]),
                        waits=end, inc="s_out")
                dmpw = nc.dram_tensor("dbg_w3", [OCH, 128, HID], F32,
                                      kind="ExternalOutput")
                for oc in range(OCH):
                    p.op("gpsimd", lambda h, oc=oc: h.dma_start(
                        out=dmpw.ap()[oc, :, :],
                        in_=t_w3q.ap()[oc, :, :]),
                        waits=end, inc="s_out")
                dmpx = nc.dram_tensor("dbg_xln", [KCH, 128, 1024], F32,
                                      kind="ExternalOutput")
                for kc in range(KCH):
                    p.op("gpsimd", lambda h, kc=kc: h.dma_start(
                        out=dmpx.ap()[kc, :, :],
                        in_=xln[:, kc * 1024:(kc + 1) * 1024]),
                        waits=end, inc="s_out")
            p.op("gpsimd",
                 lambda h: h.wait_ge(sems["s_out"],
                                     16 * (OCH + (6 + HCH + OCH + KCH
                                                  if DBG else 0))))
            p.emit(sems)
            return nc

        p.op("sync", lambda h: h.dma_start(
            out=stream[:, :].rearrange("p (c t) -> p c t", c=KCH),
            in_=t_xmid.ap().rearrange("c p t -> p c t")),
            waits=[dmtokC[OCH - 1], eptokC[OCH - 1]], inc="s_ld")
        tk["stream"] = p.tok("s_ld")
        tk["xln"] = emit_ln(
            lambda kc: stream[:, kc * 1024:(kc + 1) * 1024],
            lambda kc: tk["stream"],
            extra_mm_waits=[eptokC[OCH - 1]])
        tk["xln2"] = tk["xln"]

        masks = V(HAR, 2048, BF16)
        kctr = V(HAR + 4096, 2048, BF16)
        vctr = V(HAR + 8192, 2048, BF16)
        cosf = V(HAR + 12288, 1024, F32)
        ssin = V(HAR + 16384, 1024, F32)
        posb = V(HAR + 20480, 1024, F32)
        tt1 = V(HAR + 24576, 1024, F32)
        kint = V(HAR + 28672, 1024, I32)
        ktf = V(HAR + 32768, 1024, F32)
        rtmp = V(HAR + 36864, 1024, F32)
        rtmp2 = V(HAR + 40960, 1024, F32)
        pm = V(HAR + 45056, 128, F32)
        pm2 = V(HAR + 45696, 128, F32)
        qrb = [V(HAR + 46336 + k * 2048, 1024, BF16) for k in (0, 1)]
        wkb = [V(HAR + 50432 + k * 4096, 2048, BF16) for k in (0, 1)]
        wvb = V(HAR + 58624, 4096, BF16)
        wqb = [V(HAR + 66816 + k * 4096, 2048, BF16) for k in (0, 1)]
        itile = V(HAR + 75008, 512, F32)
        jmi = V(HAR + 77056, 128, F32)
        Qmy = V(HAR + 4096, 16384, BF16)
        Kh = V(HAR + 36864, 8192, BF16)
        Vh = V(HAR + 53248, 8192, BF16)
        pt = [V(HAR + 69632 + k * 1024, 512, BF16) for k in (0, 1)]
        rbuf = V(HAR + 71680, 512, F32)
        qsqs = [V(HAR + 73728 + k * 1024, 512, BF16) for k in (0, 1)]
        otile = [V(HAR + 75776 + k * 1024, 512, BF16) for k in (0, 1)]
        wob = wqb
        qraw = [V(TAIL + k * 4096, 1024, F32) for k in (0, 1)]
        rp = V(TAIL + 8192, 512, F32)

        hfree = [p.tok("s_mm")]
        p.op("gpsimd", lambda h: h.iota(
            itile[:, :], [[-1, 512]], channel_multiplier=1,
            allow_small_or_imprecise_dtypes=True), waits=hfree, inc="s_gq")
        tio = p.tok("s_gq")
        mtk = None
        for m in range(4):
            p.op("vector", lambda h, m=m: h.tensor_single_scalar(
                masks[:, m * 512:(m + 1) * 512], itile[:, :],
                float(-128 * m), ALU.is_le),
                waits=[tio, tk["xln2"]], inc="s_ve")
            mtk = p.tok("s_ve")
        tk["masks"] = mtk
        p.op("gpsimd", lambda h: h.iota(
            jmi[:, :], [[1, 128]], channel_multiplier=-1,
            allow_small_or_imprecise_dtypes=True), waits=hfree, inc="s_gq")
        tjm = p.tok("s_gq")

        tk["pm"] = p.seq("vector", [
            lambda h: h.tensor_single_scalar(pm[:, :], jmi[:, :], 64.0,
                                             ALU.is_equal),
            lambda h: h.tensor_single_scalar(pm2[:, :], jmi[:, :], -64.0,
                                             ALU.is_equal),
            lambda h: h.tensor_tensor(pm[:, :], pm[:, :], pm2[:, :],
                                      ALU.add),
        ], waits=[tjm])

        p.op("gpsimd", lambda h: h.iota(
            pv_t[:, :], [[1, 1]], channel_multiplier=1,
            allow_small_or_imprecise_dtypes=True), inc="s_gq")
        tpv = p.tok("s_gq")

        t_ = p.seq("vector", [
            lambda h: h.tensor_single_scalar(gtmp[:, :], pv_t[:, :], 64.0,
                                             ALU.is_ge),
            lambda h: h.scalar_tensor_tensor(gtmp[:, :], gtmp[:, :], -64.0,
                                             pv_t[:, :], ALU.mult, ALU.add),
            lambda h: h.tensor_single_scalar(gtmp[:, :], gtmp[:, :],
                                             -LOG1E4 / 64.0, ALU.mult),
        ], waits=[tpv])
        p.op("scalar", lambda h: h.activation(invf[:, :], gtmp[:, :],
                                              AF.Exp), waits=[t_],
             inc="s_se")
        tk["invf"] = p.tok("s_se")
        p.op("gpsimd", lambda h: h.dma_start(
            out=posb[:, :], in_=bass.AP(t_pos, 0, [[0, 128], [1, NT]])),
            waits=hfree, inc="s_gp")
        tk["posb"] = p.tok("s_gp")
        p.op("vector", lambda h: h.tensor_single_scalar(
            posb[:, :], posb[:, :], invf[:, 0:1], ALU.mult),
            waits=[tk["invf"], tk["posb"]], inc="s_ve")
        tang = p.tok("s_ve")

        def _red(dst, shift, tprev):
            fns = []
            if shift:
                fns.append(lambda h: h.tensor_scalar(
                    tt1[:, :], posb[:, :], INV_2PI, 0.25, ALU.mult, ALU.add))
            else:
                fns.append(lambda h: h.tensor_single_scalar(
                    tt1[:, :], posb[:, :], INV_2PI, ALU.mult))
            fns.append(lambda h: h.tensor_copy(kint[:, :], tt1[:, :]))
            fns.append(lambda h: h.tensor_copy(ktf[:, :], kint[:, :]))
            fns.append(lambda h: h.scalar_tensor_tensor(
                rtmp[:, :], ktf[:, :], -TWO_PI_HI, posb[:, :], ALU.mult,
                ALU.add))
            if shift:
                fns.append(lambda h: h.tensor_single_scalar(
                    rtmp[:, :], rtmp[:, :], HALF_PI, ALU.add))
            fns.append(lambda h: h.scalar_tensor_tensor(
                rtmp[:, :], ktf[:, :], -TWO_PI_LO, rtmp[:, :], ALU.mult,
                ALU.add))
            t = p.seq("vector", fns, hard=[tprev])
            p.op("scalar", lambda h, dst=dst: h.activation(
                dst[:, :], rtmp[:, :], AF.Sin), waits=[t], inc="s_se")
            return p.tok("s_se")

        tsin = _red(ssin, False, tang)
        p.op("vector", lambda h: h.tensor_single_scalar(
            ssin[0:64, :], ssin[0:64, :], -1.0, ALU.mult),
            waits=[tsin], inc="s_ve")
        tflip = p.tok("s_ve")
        tk["rope_tab"] = _red(cosf, True, tflip)

        for c in range(2):
            p.op("gpsimd", lambda h, c=c: h.dma_start(
                out=wkb[c][:, :].rearrange("p (kc m) -> p kc m", kc=KCH),
                in_=bass.AP(t_wkp, c * D * 128,
                            [[128, 128], [16384, KCH], [1, 128]])),
                waits=hfree, inc="s_gp")
        tk["wkb"] = p.tok("s_gp")
        p.op("gpsimd", lambda h: h.dma_start(
            out=wvb[:, :].rearrange("p (kc m) -> p kc m", kc=KCH),
            in_=t_wvt.ap().rearrange("(kc p) m -> p kc m", p=128)),
            waits=hfree, inc="s_gp")
        tk["wvb"] = p.tok("s_gp")

        kparts = []
        for c in range(2):
            kb = psA[:, c * 1024:(c + 1) * 1024]
            swb = psA[:, 2048 + c * 1024:3072 + c * 1024]
            mw = [tk["wkb"], tk["xln2"]]
            for kc in range(KCH):
                for half in (0, 1):
                    lst = (kc == KCH - 1 and half == 1)
                    p.op("tensor", lambda h, c=c, kc=kc, half=half, kb=kb:
                         h.matmul(kb[:, half * 512:(half + 1) * 512],
                                  wkb[c][:, kc * 128:(kc + 1) * 128],
                                  xln[:, kc * 1024 + half * 512:
                                      kc * 1024 + half * 512 + 512],
                                  start=(kc == 0), stop=(kc == KCH - 1)),
                         waits=mw if (kc == 0 and half == 0) else [],
                         inc="s_mm" if lst else None)
            tmm = p.tok("s_mm")
            p.op("scalar", lambda h, c=c, kb=kb: h.activation(
                qraw[c][:, :], kb[:, :], AF.Identity, bias=bk_s[:, c:c + 1]),
                waits=[tmm], inc="s_se")
            tev = p.tok("s_se")
            p.op("tensor", lambda h, swb=swb, c=c: h.matmul(
                swb[:, 0:512], pm[:, :], qraw[c][:, 0:512], start=True,
                stop=True), waits=[tev, tk["pm"]])
            p.op("tensor", lambda h, swb=swb, c=c: h.matmul(
                swb[:, 512:1024], pm[:, :], qraw[c][:, 512:1024],
                start=True, stop=True), inc="s_mm")
            tsw = p.tok("s_mm")

            def _rope(h, c=c, swb=swb):
                h.tensor_tensor(rtmp[:, :], qraw[c][:, :], cosf[:, :],
                                ALU.mult)
                h.tensor_tensor(rtmp2[:, :], swb[:, :], ssin[:, :],
                                ALU.mult)
                return h.tensor_tensor(kctr[:, c * 1024:(c + 1) * 1024],
                                       rtmp[:, :], rtmp2[:, :], ALU.add)
            p.op("vector", _rope, waits=[tsw, tk["rope_tab"]], inc="s_ve")
            kparts.append(p.tok("s_ve"))
        tk["kctr"] = kparts[-1]

        vev = [None] * 8
        for tt in range(8):
            par = tt % 2
            vb = psA[:, par * 512:par * 512 + 256]
            w = [tk["wvb"], tk["xln2"], tk["kctr"]]
            if tt >= 2:
                w.append(vev[tt - 2])
            for kc in range(KCH):
                p.op("tensor", lambda h, tt=tt, kc=kc, vb=vb: h.matmul(
                    vb[:, :],
                    xln[:, kc * 1024 + tt * 128:kc * 1024 + tt * 128 + 128],
                    wvb[:, kc * 256:(kc + 1) * 256],
                    start=(kc == 0), stop=(kc == KCH - 1)),
                    waits=w if kc == 0 else [],
                    inc="s_mm" if kc == KCH - 1 else None)
            tv = p.tok("s_mm")
            p.op("vector", lambda h, tt=tt, vb=vb: h.tensor_tensor(
                vctr[:, tt * 256:(tt + 1) * 256], vb[:, :], bvbc[:, :],
                ALU.add), waits=[tv, tk["bvbc"]], inc="s_ve")
            vev[tt] = p.tok("s_ve")
        tk["vctr"] = vev[-1]

        p.op("gpsimd", lambda h: h.dma_start(
            out=bass.AP(t_kvin, 0, [[1024, 128], [131072, 2], [1, 1024]]),
            in_=kctr[:, :].rearrange("p (c t) -> p c t", c=2)),
            waits=[tk["kctr"]], inc="s_gp")
        p.op("gpsimd", lambda h: h.dma_start(
            out=bass.AP(t_kvin, 262144, [[256, 128], [32768, 8], [1, 256]]),
            in_=vctr[:, :].rearrange("p (tt m) -> p tt m", tt=8)),
            waits=[tk["vctr"]], inc="s_gp")
        t_ = p.tok("s_gp")
        p.op("gpsimd", lambda h: h.collective_compute(
            "AllGather", ALU.bypass, replica_groups=RG,
            ins=[t_kvin.ap().opt()], outs=[t_kvout.ap().opt()]),
            waits=[t_], inc="s_cc", incn=1)
        tk["cc_kv"] = p.tok("s_cc")

        qdma = [None] * 16
        qropet = [None] * 16
        for c in range(16):
            par = c % 2
            qb = psA[:, par * 1024:(par + 1) * 1024]
            swb = psA[:, 2048 + par * 1024:3072 + par * 1024]
            lw = [tk["xln2"], tk["vctr"], hfree[0]]
            if c >= 2:
                lw.append(qropet[c - 2])
            p.op("gpsimd", lambda h, c=c, par=par: h.dma_start(
                out=wqb[par][:, :].rearrange("p (kc m) -> p kc m", kc=KCH),
                in_=bass.AP(t_wqp, c * D * 128,
                            [[128, 128], [16384, KCH], [1, 128]])),
                waits=lw, inc="s_gp")
            tw = p.tok("s_gp")
            mw = [tw, tk["vctr"]]
            if c >= 2:
                mw.append(qropet[c - 2])
            for kc in range(KCH):
                for half in (0, 1):
                    lst = (kc == KCH - 1 and half == 1)
                    p.op("tensor", lambda h, par=par, kc=kc, half=half,
                         qb=qb: h.matmul(
                             qb[:, half * 512:(half + 1) * 512],
                             wqb[par][:, kc * 128:(kc + 1) * 128],
                             xln[:, kc * 1024 + half * 512:
                                 kc * 1024 + half * 512 + 512],
                             start=(kc == 0), stop=(kc == KCH - 1)),
                         waits=mw if (kc == 0 and half == 0) else [],
                         inc="s_mm" if lst else None)
            tmm = p.tok("s_mm")
            sw2 = [tmm]
            if c >= 2:
                sw2.append(qropet[c - 2])
            p.op("scalar", lambda h, c=c, par=par, qb=qb: h.activation(
                qraw[par][:, :], qb[:, :], AF.Identity, bias=bq_s[:, c:c + 1]),
                waits=sw2, inc="s_se")
            tev = p.tok("s_se")
            p.op("tensor", lambda h, swb=swb, par=par: h.matmul(
                swb[:, 0:512], pm[:, :], qraw[par][:, 0:512], start=True,
                stop=True), waits=[tev, tk["pm"]])
            p.op("tensor", lambda h, swb=swb, par=par: h.matmul(
                swb[:, 512:1024], pm[:, :], qraw[par][:, 512:1024],
                start=True, stop=True), inc="s_mm")
            tsw = p.tok("s_mm")
            qw = [tsw, tk["rope_tab"]]
            if c >= 2:
                qw.append(qdma[c - 2])

            def _ropeq(h, par=par, swb=swb):
                h.tensor_tensor(rtmp[:, :], qraw[par][:, :], cosf[:, :],
                                ALU.mult)
                h.tensor_tensor(rtmp2[:, :], swb[:, :], ssin[:, :],
                                ALU.mult)
                return h.tensor_tensor(qrb[par][:, :], rtmp[:, :],
                                       rtmp2[:, :], ALU.add)
            p.op("vector", _ropeq, waits=qw, inc="s_ve")
            qropet[c] = p.tok("s_ve")
            p.op("gpsimd", lambda h, c=c, par=par: h.dma_start(
                out=t_qin.ap()[c * 128:(c + 1) * 128, :],
                in_=qrb[par][:, :]), waits=[qropet[c]], inc="s_gp")
            qdma[c] = p.tok("s_gp")
        p.op("gpsimd", lambda h: h.collective_compute(
            "AllGather", ALU.bypass, replica_groups=RG,
            ins=[t_qin.ap().opt()], outs=[t_qout.ap().opt()]),
            waits=[qdma[15]], inc="s_cc", incn=1)
        tk["cc_q"] = p.tok("s_cc")

        safe = [tk["cc_kv"], qdma[15], qropet[15], p.tok("s_mm"),
                p.tok("s_se"), tk["sel"]]
        regs = {}

        def _store_regs(h):
            regs["kvk"] = h.value_load(sbsel[0:1, 0:1])
            regs["kvv"] = h.value_load(sbsel[0:1, 1:2])
            regs["qof"] = h.value_load(sbsel[0:1, 2:3])
            regs["colA"] = h.value_load(sbsel[0:1, 3:4])
            regs["colB"] = h.value_load(sbsel[0:1, 4:5])
            return h.nop()
        p.op("sync", _store_regs, waits=safe)
        p.op("sync", lambda h: h.dma_start(
            out=Kh[:, :].rearrange("p (r t) -> p r t", r=8),
            in_=bass.AP(t_kvout, regs["kvk"],
                        [[1024, 128], [524288, 8], [1, 1024]])),
            waits=safe, inc="s_ld")
        for r in range(8):
            p.op("sync", lambda h, r=r: h.dma_start(
                out=Vh[:, r * 1024:(r + 1) * 1024].rearrange(
                    "p (j m) -> p j m", j=8),
                in_=bass.AP(t_kvout, regs["kvv"] + (262144 + r * 524288),
                            [[256, 128], [32768, 8], [1, 128]])),
                waits=safe, inc="s_ld")
        tk["kvh"] = p.tok("s_ld")
        for hh in range(2):
            p.op("sync", lambda h, hh=hh: h.dma_start(
                out=Qmy[:, hh * 8192:(hh + 1) * 8192].rearrange(
                    "p (r t) -> p r t", r=8),
                in_=bass.AP(t_qout, regs["qof"] + hh * 131072,
                            [[1024, 128], [2097152, 8], [1, 1024]])),
                waits=[tk["cc_q"]] + safe, inc="s_ld")
        tk["qmy"] = p.tok("s_ld")

        def keypos(kt):
            b = kt // 4
            r, s = _rank_slot(b)
            return r * 1024 + s * 512 + (kt % 4) * 128

        def qpos(h_i, b):
            r, s = _rank_slot(b)
            return h_i * 8192 + r * 1024 + s * 512

        red_t = [None] * 48
        idx = 0
        for src, nsl, base in ((Qmy, 32, 64), (Kh, 16, 96)):
            for j in range(nsl):
                par = idx % 2
                w = [tk["qmy"], tk["kvh"], tk["masks"]]
                if idx >= 2:
                    w.append(red_t[idx - 2])
                p.op("scalar", lambda h, src=src, j=j, par=par:
                     h.activation(qsqs[par][:, :],
                                  src[:, j * 512:(j + 1) * 512], AF.Square),
                     waits=w, inc="s_se")
                ts_ = p.tok("s_se")
                rw = [ts_, tk["consts"]]
                if idx >= 2:
                    rw.append(red_t[idx - 2])
                p.op("tensor", lambda h, par=par: h.matmul(
                    psA[0:1, par * 512:(par + 1) * 512], ones_b[:, :],
                    qsqs[par][:, :], start=True, stop=True),
                    waits=rw, inc="s_mm")
                tm = p.tok("s_mm")
                p.op("vector", lambda h, base=base, j=j, par=par:
                     h.reduce_max(prow(base + j, base + j + 1),
                                  psA[0:1, par * 512:(par + 1) * 512],
                                  axis=AX), waits=[tm], inc="s_ve")
                red_t[idx] = p.tok("s_ve")
                idx += 1

        t_ = p.seq("vector", [
            lambda h: h.reduce_max(prow(112, 113), prow(64, 80), axis=AX),
            lambda h: h.reduce_max(prow(113, 114), prow(80, 96), axis=AX),
            lambda h: h.reduce_max(prow(114, 115), prow(96, 112), axis=AX),
            lambda h: h.tensor_tensor(prow(115, 116), prow(112, 113),
                                      prow(114, 115), ALU.mult),
            lambda h: h.tensor_tensor(prow(116, 117), prow(113, 114),
                                      prow(114, 115), ALU.mult),
        ], hard=[red_t[47]])
        p.op("scalar", lambda h: h.activation(prow(117, 119),
                                              prow(115, 117), AF.Sqrt),
             waits=[t_], inc="s_se")
        t_ = p.tok("s_se")
        p.op("vector", lambda h: h.tensor_single_scalar(
            prow(119, 121), prow(117, 119), -CSC, ALU.mult),
            waits=[t_], inc="s_ve")
        t_ = p.tok("s_ve")
        p.op("gpsimd", lambda h: h.dma_start(out=t_bshd.ap(),
                                             in_=prow(119, 121)),
             waits=[t_], inc="s_gp")
        t_ = p.tok("s_gp")
        p.op("gpsimd", lambda h: h.dma_start(
            out=bshift[:, :], in_=bass.AP(t_bshd, 0, [[0, 128], [1, 2]])),
            waits=[t_], inc="s_gp")
        tk["bshift"] = p.tok("s_gp")

        ps_s = [psA[:, 0:512], psA[:, 512:1024]]
        ps_o = [psA[:, 1024:1536], psA[:, 1536:2048]]
        ps_p = [psA[0:1, 2048:2560], psA[0:1, 2560:3072]]
        ps_rb = psA[:, 3072:3584]
        exp_done = []
        pt_ready = []
        av_done = []
        ot_done = []
        odma_done = []
        rp_done = []
        rb_ev = None
        g = 0
        for h_i in range(2):
            for b in range(NB):
                ti = h_i * NB + b
                ob = ti % 2
                nkt = 4 * (b + 1)
                qoff = qpos(h_i, b)
                for kt in range(nkt):
                    sp = g % 2
                    w = [tk["qmy"], tk["kvh"], tk["bshift"], tk["masks"]]
                    if g >= 2:
                        w.append(exp_done[g - 2])
                    kp = keypos(kt)
                    p.op("tensor", lambda h, kp=kp, sp=sp, qoff=qoff:
                         h.matmul(ps_s[sp][:, :],
                                  Kh[:, kp:kp + 128],
                                  Qmy[:, qoff:qoff + 512],
                                  start=True, stop=True),
                         waits=w, inc="s_mm")
                    tsc = p.tok("s_mm")
                    ew = [tsc]
                    if g >= 2:
                        ew.append(av_done[g - 2])
                    p.op("scalar", lambda h, sp=sp, h_i=h_i: h.activation(
                        pt[sp][:, :], ps_s[sp][:, :], AF.Exp,
                        bias=bshift[:, h_i:h_i + 1], scale=CSC),
                        waits=ew, inc="s_se")
                    exp_done.append(p.tok("s_se"))
                    ptok = exp_done[g]
                    if kt >= 4 * b:
                        m = kt - 4 * b
                        p.op("vector", lambda h, sp=sp, m=m: h.tensor_tensor(
                            pt[sp][:, :], pt[sp][:, :],
                            masks[:, m * 512:(m + 1) * 512], ALU.mult),
                            waits=[ptok], inc="s_ve")
                        ptok = p.tok("s_ve")
                    pt_ready.append(ptok)
                    avw = [ptok]
                    if kt == 0 and ti >= 2:
                        avw.append(ot_done[ti - 2])
                        avw.append(rp_done[ti - 2])
                    p.op("tensor", lambda h, kp=kp, sp=sp, ob=ob, kt=kt,
                         nkt=nkt:
                         h.matmul(ps_o[ob][:, :],
                                  Vh[:, kp:kp + 128],
                                  pt[sp][:, :], start=(kt == 0),
                                  stop=(kt == nkt - 1)), waits=avw)
                    p.op("tensor", lambda h, kt=kt, sp=sp, ob=ob, nkt=nkt:
                         h.matmul(ps_p[ob][:, :], ones_b[:, :],
                                  pt[sp][:, :], start=(kt == 0),
                                  stop=(kt == nkt - 1)), inc="s_mm")
                    av_done.append(p.tok("s_mm"))
                    g += 1
                tav = av_done[g - 1]
                p.op("vector", lambda h, ob=ob: h.reciprocal(
                    rp[0:1, :], ps_p[ob][:, :]), waits=[tav], inc="s_ve")
                rp_done.append(p.tok("s_ve"))
                p.op("tensor", lambda h: h.matmul(
                    ps_rb[:, :], onesrow_f, rp[0:1, :], start=True,
                    stop=True),
                    waits=[rp_done[ti], tk["consts"]] +
                    ([rb_ev] if rb_ev else []), inc="s_mm")
                trb = p.tok("s_mm")
                p.op("scalar", lambda h: h.copy(rbuf[:, :], ps_rb[:, :]),
                     waits=[trb] + ([ot_done[ti - 1]] if ti >= 1 else []),
                     inc="s_se")
                rb_ev = p.tok("s_se")
                ow = [rb_ev, tav]
                if ti >= 2:
                    ow.append(odma_done[ti - 2])
                p.op("vector", lambda h, ob=ob: h.tensor_tensor(
                    otile[ob][:, :], ps_o[ob][:, :], rbuf[:, :], ALU.mult),
                    waits=ow, inc="s_ve")
                ot_done.append(p.tok("s_ve"))
                p.op("gpsimd", lambda h, b=b, h_i=h_i, ob=ob:
                     h.dma_start(
                         out=bass.AP(t_oin, h_i * 1048576 + b * 512,
                                     [[8192, 128], [1, 512]]),
                         in_=otile[ob][:, :]), waits=[ot_done[ti]],
                     inc="s_gp")
                odma_done.append(p.tok("s_gp"))
        p.op("gpsimd", lambda h: h.collective_compute(
            "AllGather", ALU.bypass, replica_groups=RG,
            ins=[t_oin.ap().opt()], outs=[t_oout.ap().opt()]),
            waits=[odma_done[-1]], inc="s_cc", incn=1)
        tk["cc_o"] = p.tok("s_cc")

        otv = xln
        ow_ = [tk["cc_o"], p.tok("s_mm"), p.tok("s_ve")]
        p.op("sync", lambda h: h.dma_start(
            out=otv[:, :].rearrange("p (c t) -> p c t", c=KCH)[:, :, 0:512],
            in_=bass.AP(t_oout, regs["colA"],
                        [[8192, 128], [1048576, KCH], [1, 512]])),
            waits=ow_, inc="s_ld")
        p.op("sync", lambda h: h.dma_start(
            out=otv[:, :].rearrange("p (c t) -> p c t", c=KCH)[:, :, 512:1024],
            in_=bass.AP(t_oout, regs["colB"],
                        [[8192, 128], [1048576, KCH], [1, 512]])),
            waits=ow_, inc="s_ld")
        tk["otv"] = p.tok("s_ld")
        res2 = [None] * OCH
        for oc in range(OCH):
            par = oc % 2
            w = [tk["cc_o"]]
            if oc >= 2:
                w.append(res2[oc - 2])
            p.op("gpsimd", lambda h, oc=oc, par=par: h.dma_start(
                out=wob[par][:, :].rearrange("p (kc m) -> p kc m", kc=KCH),
                in_=bass.AP(t_wop, oc * D * 128,
                            [[128, 128], [16384, KCH], [1, 128]])),
                waits=w, inc="s_gp")
            tw = p.tok("s_gp")
            yb = psA[:, par * 1024:(par + 1) * 1024]
            mw = [tw, tk["otv"]]
            if oc >= 2:
                mw.append(res2[oc - 2])
            for kc in range(KCH):
                for half in (0, 1):
                    lst = (kc == KCH - 1 and half == 1)
                    p.op("tensor", lambda h, par=par, kc=kc, half=half,
                         yb=yb: h.matmul(
                             yb[:, half * 512:(half + 1) * 512],
                             wob[par][:, kc * 128:(kc + 1) * 128],
                             otv[:, kc * 1024 + half * 512:
                                 kc * 1024 + half * 512 + 512],
                             start=(kc == 0), stop=(kc == KCH - 1)),
                         waits=mw if (kc == 0 and half == 0) else [],
                         inc="s_mm" if lst else None)
            tmm = p.tok("s_mm")
            p.op("vector", lambda h, oc=oc, yb=yb: h.tensor_tensor(
                stream[:, oc * 1024:(oc + 1) * 1024],
                stream[:, oc * 1024:(oc + 1) * 1024], yb[:, :], ALU.add),
                waits=[tmm, tk["stream"]], inc="s_ve")
            res2[oc] = p.tok("s_ve")

        if stop == "attn":
            for oc in range(OCH):
                p.op("gpsimd", lambda h, oc=oc: h.dma_start(
                    out=t_out.ap()[oc, :, :],
                    in_=stream[:, oc * 1024:(oc + 1) * 1024]),
                    waits=[res2[oc]], inc="s_out")
            p.op("gpsimd",
                 lambda h: h.wait_ge(sems["s_out"], 16 * OCH))
            p.emit(sems)
            return nc

        tk["xln"] = emit_ln(
            lambda kc: stream[:, kc * 1024:(kc + 1) * 1024],
            lambda kc: res2[kc], extra_mm_waits=[res2[OCH - 1]])
        pbar = [p.tok("s_mm"), p.tok("s_gp"), p.tok("s_ld"),
                p.tok("s_se"), tk["xln"]]
        htok4, sigtok4, _ = emit_ffn_B(False, entry_waits=pbar)
        emit_ffn_C(False, htok4, sigtok4, resid="stream")

        p.op("gpsimd", lambda h: h.wait_ge(sems["s_out"], 16 * OCH))
        p.emit(sems)
    return nc



def _prep_inputs(L0_attn_x, L0_attn_wq, L0_attn_wk, L0_attn_wv, L0_attn_wo,
                 L0_attn_bq, L0_attn_bk, L0_attn_bv, L0_ffn_w1, L0_ffn_w2,
                 L0_ffn_w3):
    f = np.float32
    x = np.asarray(L0_attn_x, f)[0]
    xT = np.ascontiguousarray(x.T)
    w1 = np.asarray(L0_ffn_w1, f)
    w2 = np.asarray(L0_ffn_w2, f)
    w3 = np.asarray(L0_ffn_w3, f)

    def panels(w, n):
        wT = np.ascontiguousarray(np.asarray(w, f).T)
        return np.ascontiguousarray(
            wT.reshape(wT.shape[0], n, 128).transpose(1, 0, 2))

    shared = {
        "w1p": panels(w1, HCH),
        "w2p": panels(w2, HCH),
        "w3p": panels(w3, OCH),
        "wqp": panels(L0_attn_wq, 16),
        "wkp": panels(L0_attn_wk, 2),
        "wvt": np.ascontiguousarray(np.asarray(L0_attn_wv, f).T),
        "wop": panels(L0_attn_wo, 16),
        "bqp": np.ascontiguousarray(
            np.asarray(L0_attn_bq, f).reshape(16, 128).T),
        "bkp": np.ascontiguousarray(
            np.asarray(L0_attn_bk, f).reshape(2, 128).T),
        "bvr": np.ascontiguousarray(np.asarray(L0_attn_bv, f).reshape(1, 256)),
    }
    gsz = D * HID // NCORES
    in_maps = []
    for i in range(NCORES):
        bA, bB = i, 15 - i
        cols = np.r_[bA * BS:(bA + 1) * BS, bB * BS:(bB + 1) * BS]
        m = dict(shared)
        m["xT"] = np.ascontiguousarray(xT[:, cols])
        m["pos"] = np.ascontiguousarray(cols.astype(f).reshape(1, NT))
        kv = i // 4
        m["selv"] = np.array([[kv * 131072, kv * 128, i * 262144,
                               bA * BS, bB * BS, 0, 0, 0]], np.int32)
        for name, w in (("g1", w1), ("g2", w2), ("g3", w3)):
            m[name] = np.ascontiguousarray(
                w.reshape(-1)[i * gsz:(i + 1) * gsz].reshape(128, GSL))
        in_maps.append(m)
    return in_maps


def _assemble(results):
    fullT = np.empty((D, L), np.float32)
    for i, r in enumerate(results):
        yT = np.asarray(r["yT"]).reshape(D, NT)
        bA, bB = i, 15 - i
        fullT[:, bA * BS:(bA + 1) * BS] = yT[:, 0:BS]
        fullT[:, bB * BS:(bB + 1) * BS] = yT[:, BS:NT]
    return np.ascontiguousarray(fullT.T).reshape(1, L, D)


def kernel(**inputs):
    in_maps = _prep_inputs(**inputs)
    nc = build_program("full")
    res = run_bass_kernel_spmd(nc, in_maps, core_ids=list(range(NCORES)))
    return _assemble(res.results)


# version 02
# speedup vs baseline: 1.0074x; 1.0074x over previous
import contextlib
import numpy as np
import concourse.bass as bass
import concourse.mybir as mybir
from concourse.bass_utils import run_bass_kernel_spmd

F32 = mybir.dt.float32
BF16 = mybir.dt.bfloat16
I32 = mybir.dt.int32
U8 = mybir.dt.uint8
AF = mybir.ActivationFunctionType
ALU = mybir.AluOpType
AX = mybir.AxisListType.X

D = 2048
HID = 5632
L = 8192
NCORES = 8
NT = 1024
NB = 16
BS = 512
KCH = 16
HCH = 44
OCH = 16
CSC = float(1.0 / np.sqrt(128.0))
NW = float(D * HID)
LN_EPS = 1e-5
Q_EPS = 1e-8
_2PI = np.float64(2 * np.pi)
TWO_PI_HI = float(np.float32(_2PI))
TWO_PI_LO = float(np.float32(_2PI - np.float64(np.float32(_2PI))))
INV_2PI = float(np.float32(1.0 / _2PI))
HALF_PI = float(np.float32(np.pi / 2))
LOG1E4 = float(np.log(10000.0))
GSL = D * HID // NCORES // 128
DBG = False

ENGS = ("sync", "scalar", "vector", "tensor", "gpsimd")
DMA_SEMS = {"s_ld", "s_sc", "s_gp", "s_out"}
OWN = {"tensor": "s_mm", "vector": "s_ve", "scalar": "s_se", "gpsimd": "s_gq"}

XLN = 0
HAR = 32768
STR = 122880
TAIL = 188416
PER = 206848
ARENA_B = 210944

DTS = {F32: 4, BF16: 2, I32: 4}


def _rank_slot(b):
    return (b, 0) if b <= 7 else (15 - b, 1)


class P:
    def __init__(self, nc):
        self.nc = nc
        self.q = {e: [] for e in ENGS}
        self.cnt = {}

    def tok(self, sem):
        return (sem, self.cnt.get(sem, 0))

    def op(self, eng, fn, waits=(), inc=None, incn=None, hard=()):
        incs = []
        if inc is not None:
            n = incn if incn is not None else (16 if inc in DMA_SEMS else 1)
            self.cnt[inc] = self.cnt.get(inc, 0) + n
            incs.append((inc, n))
        ws = {}
        hw = {}
        for w in waits:
            if w is None:
                continue
            s, v = w
            if v > 0 and v > ws.get(s, 0):
                ws[s] = v
        for w in hard:
            if w is None:
                continue
            s, v = w
            if v > 0 and v > hw.get(s, 0):
                hw[s] = v
        self.q[eng].append((sorted(ws.items()), sorted(hw.items()), fn, incs))
        return self.tok(inc) if inc is not None else None

    def seq(self, eng, fns, waits=(), hard=()):
        own = OWN[eng]
        t = None
        for i, fn in enumerate(fns):
            self.op(eng, fn, waits=waits if i == 0 else (),
                    hard=(list(hard) + ([t] if t else [])) if i == 0
                    else [t], inc=own)
            t = self.tok(own)
        return t

    def emit(self, sems):
        guar = {e: {} for e in ENGS}
        with self.nc.Block() as blk:
            for eng in ENGS:
                def body(h, eng=eng):
                    g = guar[eng]
                    own = OWN.get(eng)
                    for ws, hws, fn, incs in self.q[eng]:
                        for s, v in ws:
                            if s == own:
                                continue
                            if v > g.get(s, 0):
                                h.wait_ge(sems[s], v)
                                g[s] = v
                        for s, v in hws:
                            if v > g.get(s, 0):
                                h.wait_ge(sems[s], v)
                                g[s] = v
                        inst = fn(h)
                        for s, n in incs:
                            inst = inst.then_inc(sems[s], n)
                getattr(blk, eng)(body)


def build_program(stop="full"):
    nc = bass.Bass("TRN2", target_bir_lowering=False)

    t_xT = nc.dram_tensor("xT", [D, NT], F32, kind="ExternalInput")
    t_pos = nc.dram_tensor("pos", [1, NT], F32, kind="ExternalInput")
    t_g = [nc.dram_tensor("g%d" % (j + 1), [128, GSL], F32,
                          kind="ExternalInput") for j in range(3)]
    t_w1p = nc.dram_tensor("w1p", [HCH, D, 128], F32, kind="ExternalInput")
    t_w2p = nc.dram_tensor("w2p", [HCH, D, 128], F32, kind="ExternalInput")
    t_w3p = nc.dram_tensor("w3p", [OCH, HID, 128], F32, kind="ExternalInput")
    t_wqp = nc.dram_tensor("wqp", [16, D, 128], F32, kind="ExternalInput")
    t_wkp = nc.dram_tensor("wkp", [2, D, 128], F32, kind="ExternalInput")
    t_wvt = nc.dram_tensor("wvt", [D, 256], F32, kind="ExternalInput")
    t_wop = nc.dram_tensor("wop", [16, D, 128], F32, kind="ExternalInput")
    t_bq = nc.dram_tensor("bqp", [128, 16], F32, kind="ExternalInput")
    t_bk = nc.dram_tensor("bkp", [128, 2], F32, kind="ExternalInput")
    t_bv = nc.dram_tensor("bvr", [1, 256], F32, kind="ExternalInput")
    t_sel = nc.dram_tensor("selv", [1, 8], I32, kind="ExternalInput")
    t_out = nc.dram_tensor("yT", [OCH, 128, NT], F32, kind="ExternalOutput")

    t_w1q = nc.dram_tensor("w1q_d", [HCH, 128, D], BF16)
    t_w2q = nc.dram_tensor("w2q_d", [HCH, 128, D], BF16)
    t_w3q = nc.dram_tensor("w3q_d", [OCH, 128, HID], BF16)
    t_xmid = nc.dram_tensor("xmid_d", [KCH, 128, NT], F32)
    t_arin = nc.dram_tensor("ar_in", [1, 8], F32)
    t_arout = nc.dram_tensor("ar_out", [1, 8], F32)
    t_taud = nc.dram_tensor("tau_d", [1, 8], F32)
    t_bshd = nc.dram_tensor("bsh_d", [1, 2], F32)
    t_kvin = nc.dram_tensor("kvag_in", [1, 524288], BF16)
    t_kvout = nc.dram_tensor("kvag_out", [8, 524288], BF16)
    t_qin = nc.dram_tensor("qag_in", [D, NT], BF16)
    t_qout = nc.dram_tensor("qag_out", [8 * D, NT], BF16)
    t_oin = nc.dram_tensor("oag_in", [2, 128, L], BF16)
    t_dbg2 = nc.dram_tensor("dbg2", [2, 128, NT], F32, kind="ExternalOutput")
    t_dbg3 = nc.dram_tensor("dbg3", [1, 128, NT], F32, kind="ExternalOutput")
    t_oout = nc.dram_tensor("oag_out", [16, 128, L], BF16)

    RG = [list(range(NCORES))]

    es = contextlib.ExitStack()
    with es:
        arena = es.enter_context(nc.sbuf_tensor("arena", [128, ARENA_B], U8))
        psA = es.enter_context(nc.psum_tensor("psA", [128, 4096], F32))
        sems = {}
        for s in ("s_ld", "s_sc", "s_gp", "s_out", "s_cc",
                  "s_mm", "s_ve", "s_se", "s_gq"):
            sems[s] = es.enter_context(nc.semaphore(s))

        def V(off, n, dt):
            return arena[:, off:off + n * DTS[dt]].bitcast(dt)

        ones_f = V(PER + 0, 1, F32)
        ones_b = V(PER + 64, 1, BF16)
        onesrow_f = V(PER + 128, 128, F32)[0:1, :]
        taus = V(PER + 704, 8, F32)
        bq_s = V(PER + 768, 16, F32)
        bk_s = V(PER + 832, 2, F32)
        bshift = V(PER + 896, 2, F32)
        invf = V(PER + 960, 1, F32)
        pv_t = V(PER + 1024, 1, F32)
        gtmp = V(PER + 1088, 1, F32)
        bvbc = V(PER + 1152, 256, BF16)
        sbsel = V(PER + 1600, 8, I32)
        prow_full = V(PER + 1664, 256, F32)

        def prow(a, b):
            return prow_full[0:1, a:b]

        xln = V(XLN, 16384, BF16)
        lnscr = V(XLN, 8192, F32)
        stream = V(STR, 16384, F32)
        hbuf = V(HAR, 45056, BF16)

        p = P(nc)
        tk = {}

        x0 = V(HAR, 16384, F32)
        gbuf = [V(STR + k * 11264, 2816, F32) for k in (0, 1)]
        xsq = [V(TAIL + k * 4096, 1024, F32) for k in (0, 1)]
        racc = V(TAIL + 8192, 16, F32)

        p.op("gpsimd", lambda h: h.memset(ones_f[:, :], 1.0), inc="s_gq")
        p.op("gpsimd", lambda h: h.memset(ones_b[:, :], 1.0), inc="s_gq")
        p.op("gpsimd", lambda h: h.memset(onesrow_f, 1.0), inc="s_gq")
        tk["consts"] = p.tok("s_gq")

        p.op("sync", lambda h: h.dma_start(
            out=x0[:, :].rearrange("p (c t) -> p c t", c=KCH),
            in_=t_xT.ap().rearrange("(c p) t -> p c t", p=128)), inc="s_ld")
        tk["x0"] = p.tok("s_ld")
        p.op("sync", lambda h: h.dma_start(out=bq_s[:, :], in_=t_bq.ap()),
             inc="s_ld")
        p.op("sync", lambda h: h.dma_start(out=sbsel[0:1, :], in_=t_sel.ap()),
             inc="s_ld")
        tk["sel"] = p.tok("s_ld")
        p.op("sync", lambda h: h.dma_start(out=bk_s[:, :], in_=t_bk.ap()),
             inc="s_ld")
        tk["bias"] = p.tok("s_ld")
        p.op("gpsimd", lambda h: h.dma_start(
            out=bvbc[:, :], in_=bass.AP(t_bv, 0, [[0, 128], [1, 256]])),
            inc="s_gp")
        tk["bvbc"] = p.tok("s_gp")

        gred = [None] * 12
        for j in range(3):
            for c in range(4):
                i = 4 * j + c
                par = i % 2
                w = [gred[i - 2]] if i >= 2 else []
                p.op("scalar", lambda h, j=j, c=c, par=par: h.dma_start(
                    out=gbuf[par][:, :],
                    in_=t_g[j].ap()[:, c * 2816:(c + 1) * 2816]),
                    waits=w, inc="s_sc")
                ldt = p.tok("s_sc")
                p.op("vector", lambda h, i=i, par=par: h.tensor_reduce(
                    racc[:, i:i + 1], gbuf[par][:, :], axis=AX, op=ALU.add,
                    apply_absolute_value=True), waits=[ldt], inc="s_ve")
                gred[i] = p.tok("s_ve")
        p.op("tensor", lambda h: h.matmul(
            psA[0:1, 0:12], ones_f[:, :], racc[:, 0:12], start=True,
            stop=True), waits=[gred[11], tk["consts"]], inc="s_mm")
        tgmm = p.tok("s_mm")

        tk["gpart"] = p.seq("vector", [
            lambda h: h.tensor_copy(prow(0, 12), psA[0:1, 0:12]),
            lambda h: h.reduce_sum(prow(12, 13), prow(0, 4), axis=AX),
            lambda h: h.reduce_sum(prow(13, 14), prow(4, 8), axis=AX),
            lambda h: h.reduce_sum(prow(14, 15), prow(8, 12), axis=AX),
        ], waits=[tgmm])
        p.op("gpsimd", lambda h: h.dma_start(out=t_arin.ap(),
                                             in_=prow(12, 20)),
             waits=[tk["gpart"]], inc="s_gp")
        t_ = p.tok("s_gp")
        p.op("gpsimd", lambda h: h.collective_compute(
            "AllReduce", ALU.add, replica_groups=RG,
            ins=[t_arin.ap().opt()], outs=[t_arout.ap().opt()]),
            waits=[t_], inc="s_cc", incn=1)
        tar = p.tok("s_cc")
        p.op("gpsimd", lambda h: h.dma_start(out=prow(24, 32),
                                             in_=t_arout.ap()),
             waits=[tar], inc="s_gp")
        tab = p.tok("s_gp")

        ttr = p.seq("vector", [
            lambda h: h.tensor_scalar(prow(32, 35), prow(24, 27), 1.0 / NW,
                                      None, ALU.mult),
            lambda h: h.tensor_scalar(prow(35, 38), prow(32, 35), 0.5,
                                      0.5 * Q_EPS, ALU.mult, ALU.add),
            lambda h: h.tensor_scalar(prow(38, 41), prow(35, 38), -1.0,
                                      None, ALU.mult),
        ], waits=[tab])
        p.op("gpsimd", lambda h: h.dma_start(out=t_taud.ap(),
                                             in_=prow(35, 43)),
             waits=[ttr], inc="s_gp")
        t_ = p.tok("s_gp")
        p.op("gpsimd", lambda h: h.dma_start(
            out=taus[:, :], in_=bass.AP(t_taud, 0, [[0, 128], [1, 8]])),
            waits=[t_], inc="s_gp")
        tk["taus"] = p.tok("s_gp")

        def emit_ln(src, src_toks, extra_mm_waits=()):
            mm_done = [None] * KCH
            for kc in range(KCH):
                par = kc % 2
                w = [src_toks(kc)] if src_toks else []
                if kc >= 2 and mm_done[kc - 2] is not None:
                    w = w + [mm_done[kc - 2]]
                p.op("scalar", lambda h, kc=kc, par=par: h.activation(
                    xsq[par][:, :], src(kc), AF.Square), waits=w, inc="s_se")
                sqt = p.tok("s_se")
                mw = [sqt, tk["consts"]] + list(extra_mm_waits) + \
                    ([src_toks(kc)] if src_toks else [])
                for seg in range(4):
                    def mk(h, kc=kc, seg=seg, par=par):
                        if seg < 2:
                            rh = src(kc)[:, seg * 512:(seg + 1) * 512]
                        else:
                            rh = xsq[par][:, (seg - 2) * 512:(seg - 1) * 512]
                        return h.matmul(psA[0:1, seg * 512:(seg + 1) * 512],
                                        ones_f[:, :], rh,
                                        start=(kc == 0), stop=(kc == KCH - 1))
                    p.op("tensor", mk, waits=mw if seg == 0 else [sqt],
                         inc="s_mm" if seg == 3 else None)
                mm_done[kc] = p.tok("s_mm")
            tsum = mm_done[KCH - 1]

            M, VE, T, R0, RS = 0, 1024, 2048, 3072, 4096

            t1 = p.seq("vector", [
                lambda h: h.tensor_scalar(lnscr[0:1, M:M + 1024],
                                          psA[0:1, 0:1024], 1.0 / D, None,
                                          ALU.mult),
                lambda h: h.tensor_scalar(lnscr[0:1, VE:VE + 1024],
                                          psA[0:1, 1024:2048], 1.0 / D,
                                          LN_EPS, ALU.mult, ALU.add),
                lambda h: h.tensor_tensor(lnscr[0:1, T:T + 1024],
                                          lnscr[0:1, M:M + 1024],
                                          lnscr[0:1, M:M + 1024], ALU.mult),
                lambda h: h.tensor_tensor(lnscr[0:1, VE:VE + 1024],
                                          lnscr[0:1, VE:VE + 1024],
                                          lnscr[0:1, T:T + 1024],
                                          ALU.subtract),
                lambda h: h.reciprocal(lnscr[0:1, R0:R0 + 1024],
                                       lnscr[0:1, VE:VE + 1024]),
            ], waits=[tsum])
            p.op("scalar", lambda h: h.activation(
                lnscr[0:1, RS:RS + 1024], lnscr[0:1, R0:R0 + 1024], AF.Sqrt),
                waits=[t1], inc="s_se")
            t2 = p.tok("s_se")

            t3 = p.seq("vector", [
                lambda h: h.tensor_tensor(lnscr[0:1, T:T + 1024],
                                          lnscr[0:1, RS:RS + 1024],
                                          lnscr[0:1, RS:RS + 1024],
                                          ALU.mult),
                lambda h: h.tensor_tensor(lnscr[0:1, T:T + 1024],
                                          lnscr[0:1, T:T + 1024],
                                          lnscr[0:1, VE:VE + 1024],
                                          ALU.mult),
                lambda h: h.tensor_scalar(lnscr[0:1, T:T + 1024],
                                          lnscr[0:1, T:T + 1024], -0.5, 1.5,
                                          ALU.mult, ALU.add),
                lambda h: h.tensor_tensor(lnscr[0:1, RS:RS + 1024],
                                          lnscr[0:1, RS:RS + 1024],
                                          lnscr[0:1, T:T + 1024], ALU.mult),
                lambda h: h.tensor_tensor(lnscr[0:1, M:M + 1024],
                                          lnscr[0:1, M:M + 1024],
                                          lnscr[0:1, RS:RS + 1024],
                                          ALU.mult),
            ], waits=[t2])
            for i in range(4):
                coff = RS if i < 2 else M
                boff = 2048 + i * 512
                p.op("tensor", lambda h, coff=coff, boff=boff, i=i: h.matmul(
                    psA[:, boff:boff + 512], onesrow_f,
                    lnscr[0:1, coff + (i % 2) * 512:
                          coff + (i % 2) * 512 + 512],
                    start=True, stop=True),
                    waits=[t3, tk["consts"]], inc="s_mm" if i == 3 else None)
            tbc = p.tok("s_mm")
            last = None
            for kc in range(KCH):
                par = kc % 2

                def _x(h, kc=kc, par=par):
                    h.tensor_tensor(xsq[par][:, :], src(kc),
                                    psA[:, 2048:3072], ALU.mult)
                    return h.tensor_tensor(
                        xln[:, kc * 1024:(kc + 1) * 1024], xsq[par][:, :],
                        psA[:, 3072:4096], ALU.subtract)
                p.op("vector", _x, waits=[tbc] +
                     ([src_toks(kc)] if src_toks else []), inc="s_ve")
                last = p.tok("s_ve")
            return last

        tk["xln"] = emit_ln(lambda kc: x0[:, kc * 1024:(kc + 1) * 1024],
                            lambda kc: tk["x0"],
                            extra_mm_waits=[tk["gpart"]])

        def emit_ffn_B(pre, entry_waits):
            if pre:
                wraw = [[V(STR + k * 8192, 2048, F32) for k in (0, 1)],
                        [V(STR + 16384 + k * 8192, 2048, F32)
                         for k in (0, 1)]]
                wqv = [[V(STR + 32768 + k * 4096, 2048, BF16)
                        for k in (0, 1)],
                       [V(STR + 40960 + k * 4096, 2048, BF16)
                        for k in (0, 1)]]
                ltmp = V(STR + 49152, 2048, BF16)
                sigb = V(STR + 53248, 1024, BF16)
            else:
                wqv = [[V(TAIL + k * 4096, 2048, BF16) for k in (0, 1)],
                       [V(TAIL + 8192 + k * 4096, 2048, BF16)
                        for k in (0, 1)]]
                sigb = V(TAIL + 16384, 1024, BF16)
            xb = [[psA[:, 0:1024], psA[:, 1024:2048]],
                  [psA[:, 2048:3072], psA[:, 3072:4096]]]
            wsrc = (t_w1p, t_w2p)
            wdst = (t_w1q, t_w2q)
            tcols = ((0, 3), (1, 4))
            qtok = [[None] * HCH, [None] * HCH]
            stok = [[None] * HCH, [None] * HCH]
            mmtok = [[None] * HCH, [None] * HCH]
            sigtok = [None] * HCH
            htok = [None] * HCH
            for hc in range(HCH):
                par = hc % 2
                for wi in range(2):
                    fw = [mmtok[wi][hc - 2]] if hc >= 2 else list(entry_waits)
                    if pre:
                        if hc >= 2:
                            fw.append(qtok[wi][hc - 2])
                        p.op("sync", lambda h, wi=wi, hc=hc, par=par:
                             h.dma_start(
                                 out=wraw[wi][par][:, :].rearrange(
                                     "p (kc m) -> p kc m", kc=KCH),
                                 in_=bass.AP(wsrc[wi], hc * D * 128,
                                             [[128, 128], [16384, KCH],
                                              [1, 128]])),
                             waits=fw, inc="s_ld")
                        ldt = p.tok("s_ld")
                        qw = [ldt, tk["taus"]]
                        if hc >= 2:
                            qw += [mmtok[wi][hc - 2], stok[wi][hc - 2]]
                        if hc == 0 and wi == 0:
                            qw += list(entry_waits)

                        def _q(h, wi=wi, par=par):
                            tc, nc_ = tcols[wi]
                            h.tensor_single_scalar(
                                ltmp[:, :], wraw[wi][par][:, :],
                                taus[:, nc_:nc_ + 1], ALU.is_lt)
                            return h.scalar_tensor_tensor(
                                wqv[wi][par][:, :], wraw[wi][par][:, :],
                                taus[:, tc:tc + 1], ltmp[:, :],
                                ALU.is_gt, ALU.subtract)
                        p.op("vector", _q, waits=qw, inc="s_ve")
                        qtok[wi][hc] = p.tok("s_ve")
                        p.op("gpsimd", lambda h, wi=wi, hc=hc, par=par:
                             h.dma_start(out=wdst[wi].ap()[hc, :, :],
                                         in_=wqv[wi][par][:, :]),
                             waits=[qtok[wi][hc]], inc="s_gp")
                        stok[wi][hc] = p.tok("s_gp")
                    else:
                        p.op("sync", lambda h, wi=wi, hc=hc, par=par:
                             h.dma_start(out=wqv[wi][par][:, :],
                                         in_=wdst[wi].ap()[hc, :, :]),
                             waits=fw, inc="s_ld")
                        qtok[wi][hc] = p.tok("s_ld")
                for wi in range(2):
                    mw = [qtok[wi][hc], tk["xln"]]
                    if hc >= 2:
                        mw.append(sigtok[hc - 2] if wi == 0
                                  else htok[hc - 2])
                    if hc < 2:
                        mw += list(entry_waits)
                    for kc in range(KCH):
                        for half in (0, 1):
                            lst = (kc == KCH - 1 and half == 1)
                            p.op("tensor", lambda h, wi=wi, par=par, kc=kc,
                                 half=half: h.matmul(
                                     xb[wi][par][:, half * 512:
                                                 (half + 1) * 512],
                                     wqv[wi][par][:, kc * 128:
                                                  (kc + 1) * 128],
                                     xln[:, kc * 1024 + half * 512:
                                         kc * 1024 + half * 512 + 512],
                                     start=(kc == 0), stop=(kc == KCH - 1)),
                                 waits=mw if (kc == 0 and half == 0) else [],
                                 inc="s_mm" if lst else None)
                    mmtok[wi][hc] = p.tok("s_mm")
                sw = [mmtok[0][hc]]
                if hc >= 1:
                    sw.append(htok[hc - 1])
                p.op("scalar", lambda h, par=par: h.activation(
                    sigb[:, :], xb[0][par][:, :], AF.Sigmoid),
                    waits=sw, inc="s_se")
                sigtok[hc] = p.tok("s_se")
                p.op("vector", lambda h, hc=hc, par=par: h.tensor_tensor(
                    hbuf[:, hc * 1024:(hc + 1) * 1024], sigb[:, :],
                    xb[1][par][:, :], ALU.mult),
                    waits=[sigtok[hc], mmtok[1][hc]], inc="s_ve")
                htok[hc] = p.tok("s_ve")
            return htok, sigtok, stok

        def emit_ffn_C(pre, htok, sigtok, resid, centry=()):
            if pre:
                w3raw = [V(STR + k * 5632, 1408, F32) for k in (0, 1)]
                w3qs = [V(STR + 11264 + k * 2816, 1408, BF16) for k in (0, 1)]
                ltmp2 = V(STR + 16896, 1408, BF16)
                xres = [V(STR + 19776 + k * 4096, 1024, F32) for k in (0, 1)]
                outb = [V(STR + 27968 + k * 4096, 1024, F32) for k in (0, 1)]
            else:
                w3qs = [V(XLN + k * 2816, 1408, BF16) for k in (0, 1)]
                outb = [V(XLN + 5632 + k * 4096, 1024, F32) for k in (0, 1)]
            yb = [psA[:, 0:1024], psA[:, 1024:2048]]
            NS = 4 * OCH
            ld_tok = [None] * NS
            qv_tok = [None] * NS
            sc_tok = [None] * NS
            mm_sub = [None] * NS
            eptok = [None] * OCH
            dmtok = [None] * OCH
            entry = [sigtok[HCH - 1], htok[HCH - 1]] + list(centry)
            entry = entry + [p.tok("s_mm")]
            p.op("gpsimd", lambda h: h.nop(),
                 waits=[sigtok[HCH - 1], htok[HCH - 1]], inc="s_gq")
            entry = entry + [p.tok("s_gq")]

            def prep(si):
                oc, j = si // 4, si % 4
                jp = si % 2
                fw = [mm_sub[si - 2]] if si >= 2 else list(entry)
                if pre:
                    lw = list(fw)
                    if si >= 2:
                        lw.append(qv_tok[si - 2])
                    else:
                        lw += list(entry)
                    p.op("sync", lambda h, oc=oc, j=j, jp=jp:
                         h.dma_start(
                             out=w3raw[jp][:, :].rearrange(
                                 "p (kc m) -> p kc m", kc=11),
                             in_=bass.AP(t_w3p,
                                         oc * HID * 128 + j * 1408 * 128,
                                         [[128, 128], [16384, 11],
                                          [1, 128]])),
                         waits=lw, inc="s_ld")
                    ldt = p.tok("s_ld")
                    qw = [ldt, tk["taus"]] + fw
                    if si >= 2:
                        qw.append(sc_tok[si - 2])

                    def _q3(h, jp=jp):
                        h.tensor_single_scalar(
                            ltmp2[:, :], w3raw[jp][:, :], taus[:, 5:6],
                            ALU.is_lt)
                        return h.scalar_tensor_tensor(
                            w3qs[jp][:, :], w3raw[jp][:, :],
                            taus[:, 2:3], ltmp2[:, :],
                            ALU.is_gt, ALU.subtract)
                    p.op("vector", _q3, waits=qw, inc="s_ve")
                    qv_tok[si] = p.tok("s_ve")
                    ld_tok[si] = qv_tok[si]
                    p.op("gpsimd", lambda h, oc=oc, j=j, jp=jp:
                         h.dma_start(
                             out=t_w3q.ap()[oc, :,
                                            j * 1408:(j + 1) * 1408],
                             in_=w3qs[jp][:, :]),
                         waits=[qv_tok[si]], inc="s_gp")
                    sc_tok[si] = p.tok("s_gp")
                else:
                    p.op("sync", lambda h, oc=oc, j=j, jp=jp:
                         h.dma_start(
                             out=w3qs[jp][:, :],
                             in_=t_w3q.ap()[oc, :,
                                            j * 1408:(j + 1) * 1408]),
                         waits=fw, inc="s_ld")
                    ld_tok[si] = p.tok("s_ld")

            prep(0)
            prep(1)
            for oc in range(OCH):
                par = oc % 2
                for kc in range(HCH):
                    si = 4 * oc + kc // 11
                    jp = si % 2
                    kw = [ld_tok[si], htok[kc]]
                    if kc == 0:
                        kw.append(htok[0])
                        if oc >= 2:
                            kw.append(eptok[oc - 2])
                        else:
                            kw += list(entry)
                    for half in (0, 1):
                        sub_last = (kc % 11 == 10 and half == 1)
                        p.op("tensor", lambda h, par=par, kc=kc, jp=jp,
                             half=half: h.matmul(
                                 yb[par][:, half * 512:(half + 1) * 512],
                                 w3qs[jp][:, (kc % 11) * 128:
                                          (kc % 11) * 128 + 128],
                                 hbuf[:, kc * 1024 + half * 512:
                                      kc * 1024 + half * 512 + 512],
                                 start=(kc == 0), stop=(kc == HCH - 1)),
                             waits=kw if half == 0 else [],
                             inc="s_mm" if sub_last else None)
                        if sub_last:
                            mm_sub[si] = p.tok("s_mm")
                    if kc % 11 == 10 and si + 2 < NS:
                        prep(si + 2)
                ymm = mm_sub[4 * oc + 3]
                if resid == "xT":
                    rw = [eptok[oc - 2]] if oc >= 2 else \
                        [htok[HCH - 1]] + list(centry)
                    p.op("scalar", lambda h, oc=oc, par=par: h.dma_start(
                        out=xres[par][:, :],
                        in_=t_xT.ap()[oc * 128:(oc + 1) * 128, :]),
                        waits=rw, inc="s_sc")
                    rt = [p.tok("s_sc")]
                    rsl = lambda par=par: xres[par][:, :]
                else:
                    rt = []
                    rsl = lambda oc=oc: stream[:, oc * 1024:(oc + 1) * 1024]
                ew = [ymm] + rt
                if oc >= 2:
                    ew.append(dmtok[oc - 2])
                else:
                    ew += [htok[HCH - 1]] + list(centry)
                if DBG and not pre and oc == 0:
                    dbg3buf = V(TAIL, 1024, F32)
                    p.op("scalar", lambda h, par=par: h.copy(
                        dbg3buf[:, :], yb[par][:, :]), waits=[ymm],
                        inc="s_se")
                    t_d3 = p.tok("s_se")
                    p.op("gpsimd", lambda h: h.dma_start(
                        out=t_dbg3.ap()[0, :, :], in_=dbg3buf[:, :]),
                        waits=[t_d3], inc="s_gp")
                    ew.append(t_d3)
                p.op("vector", lambda h, par=par, rsl=rsl:
                     h.scalar_tensor_tensor(
                         outb[par][:, :], yb[par][:, :], 0.5, rsl(),
                         ALU.mult, ALU.add), waits=ew, inc="s_ve")
                eptok[oc] = p.tok("s_ve")
                if pre:
                    p.op("gpsimd", lambda h, oc=oc, par=par: h.dma_start(
                        out=t_xmid.ap()[oc, :, :], in_=outb[par][:, :]),
                        waits=[eptok[oc]], inc="s_gp")
                    dmtok[oc] = p.tok("s_gp")
                    if stop == "ffnpre":
                        p.op("gpsimd", lambda h, oc=oc, par=par: h.dma_start(
                            out=t_out.ap()[oc, :, :], in_=outb[par][:, :]),
                            waits=[eptok[oc]], inc="s_out")
                        dmtok[oc] = p.tok("s_out")
                else:
                    p.op("gpsimd", lambda h, oc=oc, par=par: h.dma_start(
                        out=t_out.ap()[oc, :, :], in_=outb[par][:, :]),
                        waits=[eptok[oc]], inc="s_out")
                    dmtok[oc] = p.tok("s_out")
                    if DBG and oc < 2:
                        p.op("gpsimd", lambda h, oc=oc, par=par: h.dma_start(
                            out=t_dbg2.ap()[oc, :, :], in_=outb[par][:, :]),
                            waits=[eptok[oc]], inc="s_gp")
            return eptok, dmtok

        htokB, sigtokB, stokB = emit_ffn_B(True, entry_waits=[gred[11]])
        eptokC, dmtokC = emit_ffn_C(
            True, htokB, sigtokB, resid="xT",
            centry=[stokB[0][HCH - 1], stokB[1][HCH - 1]])

        if stop == "ffnpre":
            if DBG:
                end = [p.tok("s_ve"), p.tok("s_mm"), p.tok("s_se"),
                       p.tok("s_gp"), p.tok("s_ld"), p.tok("s_sc")]
                dump = nc.dram_tensor("dbg", [8, 128, 1024], F32,
                                      kind="ExternalOutput")
                p.op("gpsimd", lambda h: h.dma_start(
                    out=dump.ap()[0, 0:1, 0:1024],
                    in_=V(PER, 1024, F32)[0:1, :]), waits=end, inc="s_out")
                p.op("gpsimd", lambda h: h.dma_start(
                    out=dump.ap()[1, :, :],
                    in_=xln[:, 0:1024]), waits=end, inc="s_out")
                p.op("gpsimd", lambda h: h.dma_start(
                    out=dump.ap()[2, :, :],
                    in_=hbuf[:, 0:1024]), waits=end, inc="s_out")
                p.op("gpsimd", lambda h: h.dma_start(
                    out=dump.ap()[3, :, :],
                    in_=t_w1q.ap()[0, :, 0:1024]), waits=end, inc="s_out")
                p.op("gpsimd", lambda h: h.dma_start(
                    out=dump.ap()[4, :, :],
                    in_=x0[:, 0:1024]), waits=end, inc="s_out")
                p.op("gpsimd", lambda h: h.dma_start(
                    out=dump.ap()[5, 0:1, :],
                    in_=lnscr[0:1, 4096:5120]), waits=end, inc="s_out")
                dmph = nc.dram_tensor("dbg_h", [44, 128, 1024], F32,
                                      kind="ExternalOutput")
                for hc in range(HCH):
                    p.op("gpsimd", lambda h, hc=hc: h.dma_start(
                        out=dmph.ap()[hc, :, :],
                        in_=hbuf[:, hc * 1024:(hc + 1) * 1024]),
                        waits=end, inc="s_out")
                dmpw = nc.dram_tensor("dbg_w3", [OCH, 128, HID], F32,
                                      kind="ExternalOutput")
                for oc in range(OCH):
                    p.op("gpsimd", lambda h, oc=oc: h.dma_start(
                        out=dmpw.ap()[oc, :, :],
                        in_=t_w3q.ap()[oc, :, :]),
                        waits=end, inc="s_out")
                dmpx = nc.dram_tensor("dbg_xln", [KCH, 128, 1024], F32,
                                      kind="ExternalOutput")
                for kc in range(KCH):
                    p.op("gpsimd", lambda h, kc=kc: h.dma_start(
                        out=dmpx.ap()[kc, :, :],
                        in_=xln[:, kc * 1024:(kc + 1) * 1024]),
                        waits=end, inc="s_out")
            p.op("gpsimd",
                 lambda h: h.wait_ge(sems["s_out"],
                                     16 * (OCH + (6 + HCH + OCH + KCH
                                                  if DBG else 0))))
            p.emit(sems)
            return nc

        p.op("sync", lambda h: h.dma_start(
            out=stream[:, :].rearrange("p (c t) -> p c t", c=KCH),
            in_=t_xmid.ap().rearrange("c p t -> p c t")),
            waits=[dmtokC[OCH - 1], eptokC[OCH - 1]], inc="s_ld")
        tk["stream"] = p.tok("s_ld")
        tk["xln"] = emit_ln(
            lambda kc: stream[:, kc * 1024:(kc + 1) * 1024],
            lambda kc: tk["stream"],
            extra_mm_waits=[eptokC[OCH - 1]])
        tk["xln2"] = tk["xln"]

        masks = V(HAR, 2048, BF16)
        kctr = V(HAR + 4096, 2048, BF16)
        vctr = V(HAR + 8192, 2048, BF16)
        cosf = V(HAR + 12288, 1024, F32)
        ssin = V(HAR + 16384, 1024, F32)
        posb = V(HAR + 20480, 1024, F32)
        tt1 = V(HAR + 24576, 1024, F32)
        kint = V(HAR + 28672, 1024, I32)
        ktf = V(HAR + 32768, 1024, F32)
        rtmp = V(HAR + 36864, 1024, F32)
        rtmp2 = V(HAR + 40960, 1024, F32)
        pm = V(HAR + 45056, 128, F32)
        pm2 = V(HAR + 45696, 128, F32)
        qrb = [V(HAR + 46336 + k * 2048, 1024, BF16) for k in (0, 1)]
        wkb = [V(HAR + 50432 + k * 4096, 2048, BF16) for k in (0, 1)]
        wvb = V(HAR + 58624, 4096, BF16)
        wqb = [V(HAR + 66816 + k * 4096, 2048, BF16) for k in (0, 1)]
        itile = V(HAR + 75008, 512, F32)
        jmi = V(HAR + 77056, 128, F32)
        Qmy = V(HAR + 4096, 16384, BF16)
        Kh = V(HAR + 36864, 8192, BF16)
        Vh = V(HAR + 53248, 8192, BF16)
        pt = [V(HAR + 69632 + k * 1024, 512, BF16) for k in (0, 1)]
        rbuf = V(HAR + 71680, 512, F32)
        qsqs = [V(HAR + 73728 + k * 1024, 512, BF16) for k in (0, 1)]
        otile = [V(HAR + 75776 + k * 1024, 512, BF16) for k in (0, 1)]
        wob = wqb
        qraw = [V(TAIL + k * 4096, 1024, F32) for k in (0, 1)]
        rp = V(TAIL + 8192, 512, F32)

        hfree = [p.tok("s_mm")]
        p.op("gpsimd", lambda h: h.iota(
            itile[:, :], [[-1, 512]], channel_multiplier=1,
            allow_small_or_imprecise_dtypes=True), waits=hfree, inc="s_gq")
        tio = p.tok("s_gq")
        mtk = None
        for m in range(4):
            p.op("vector", lambda h, m=m: h.tensor_single_scalar(
                masks[:, m * 512:(m + 1) * 512], itile[:, :],
                float(-128 * m), ALU.is_le),
                waits=[tio, tk["xln2"]], inc="s_ve")
            mtk = p.tok("s_ve")
        tk["masks"] = mtk
        p.op("gpsimd", lambda h: h.iota(
            jmi[:, :], [[1, 128]], channel_multiplier=-1,
            allow_small_or_imprecise_dtypes=True), waits=hfree, inc="s_gq")
        tjm = p.tok("s_gq")

        tk["pm"] = p.seq("vector", [
            lambda h: h.tensor_single_scalar(pm[:, :], jmi[:, :], 64.0,
                                             ALU.is_equal),
            lambda h: h.tensor_single_scalar(pm2[:, :], jmi[:, :], -64.0,
                                             ALU.is_equal),
            lambda h: h.tensor_tensor(pm[:, :], pm[:, :], pm2[:, :],
                                      ALU.add),
        ], waits=[tjm])

        p.op("gpsimd", lambda h: h.iota(
            pv_t[:, :], [[1, 1]], channel_multiplier=1,
            allow_small_or_imprecise_dtypes=True), inc="s_gq")
        tpv = p.tok("s_gq")

        t_ = p.seq("vector", [
            lambda h: h.tensor_single_scalar(gtmp[:, :], pv_t[:, :], 64.0,
                                             ALU.is_ge),
            lambda h: h.scalar_tensor_tensor(gtmp[:, :], gtmp[:, :], -64.0,
                                             pv_t[:, :], ALU.mult, ALU.add),
            lambda h: h.tensor_single_scalar(gtmp[:, :], gtmp[:, :],
                                             -LOG1E4 / 64.0, ALU.mult),
        ], waits=[tpv])
        p.op("scalar", lambda h: h.activation(invf[:, :], gtmp[:, :],
                                              AF.Exp), waits=[t_],
             inc="s_se")
        tk["invf"] = p.tok("s_se")
        p.op("gpsimd", lambda h: h.dma_start(
            out=posb[:, :], in_=bass.AP(t_pos, 0, [[0, 128], [1, NT]])),
            waits=hfree, inc="s_gp")
        tk["posb"] = p.tok("s_gp")
        p.op("vector", lambda h: h.tensor_single_scalar(
            posb[:, :], posb[:, :], invf[:, 0:1], ALU.mult),
            waits=[tk["invf"], tk["posb"]], inc="s_ve")
        tang = p.tok("s_ve")

        def _red(dst, shift, tprev):
            fns = []
            if shift:
                fns.append(lambda h: h.tensor_scalar(
                    tt1[:, :], posb[:, :], INV_2PI, 0.25, ALU.mult, ALU.add))
            else:
                fns.append(lambda h: h.tensor_single_scalar(
                    tt1[:, :], posb[:, :], INV_2PI, ALU.mult))
            fns.append(lambda h: h.tensor_copy(kint[:, :], tt1[:, :]))
            fns.append(lambda h: h.tensor_copy(ktf[:, :], kint[:, :]))
            fns.append(lambda h: h.scalar_tensor_tensor(
                rtmp[:, :], ktf[:, :], -TWO_PI_HI, posb[:, :], ALU.mult,
                ALU.add))
            if shift:
                fns.append(lambda h: h.tensor_single_scalar(
                    rtmp[:, :], rtmp[:, :], HALF_PI, ALU.add))
            fns.append(lambda h: h.scalar_tensor_tensor(
                rtmp[:, :], ktf[:, :], -TWO_PI_LO, rtmp[:, :], ALU.mult,
                ALU.add))
            t = p.seq("vector", fns, hard=[tprev])
            p.op("scalar", lambda h, dst=dst: h.activation(
                dst[:, :], rtmp[:, :], AF.Sin), waits=[t], inc="s_se")
            return p.tok("s_se")

        tsin = _red(ssin, False, tang)
        p.op("vector", lambda h: h.tensor_single_scalar(
            ssin[0:64, :], ssin[0:64, :], -1.0, ALU.mult),
            waits=[tsin], inc="s_ve")
        tflip = p.tok("s_ve")
        tk["rope_tab"] = _red(cosf, True, tflip)

        for c in range(2):
            p.op("gpsimd", lambda h, c=c: h.dma_start(
                out=wkb[c][:, :].rearrange("p (kc m) -> p kc m", kc=KCH),
                in_=bass.AP(t_wkp, c * D * 128,
                            [[128, 128], [16384, KCH], [1, 128]])),
                waits=hfree, inc="s_gp")
        tk["wkb"] = p.tok("s_gp")
        p.op("gpsimd", lambda h: h.dma_start(
            out=wvb[:, :].rearrange("p (kc m) -> p kc m", kc=KCH),
            in_=t_wvt.ap().rearrange("(kc p) m -> p kc m", p=128)),
            waits=hfree, inc="s_gp")
        tk["wvb"] = p.tok("s_gp")

        kparts = []
        for c in range(2):
            kb = psA[:, c * 1024:(c + 1) * 1024]
            swb = psA[:, 2048 + c * 1024:3072 + c * 1024]
            mw = [tk["wkb"], tk["xln2"]]
            for kc in range(KCH):
                for half in (0, 1):
                    lst = (kc == KCH - 1 and half == 1)
                    p.op("tensor", lambda h, c=c, kc=kc, half=half, kb=kb:
                         h.matmul(kb[:, half * 512:(half + 1) * 512],
                                  wkb[c][:, kc * 128:(kc + 1) * 128],
                                  xln[:, kc * 1024 + half * 512:
                                      kc * 1024 + half * 512 + 512],
                                  start=(kc == 0), stop=(kc == KCH - 1)),
                         waits=mw if (kc == 0 and half == 0) else [],
                         inc="s_mm" if lst else None)
            tmm = p.tok("s_mm")
            p.op("scalar", lambda h, c=c, kb=kb: h.activation(
                qraw[c][:, :], kb[:, :], AF.Identity, bias=bk_s[:, c:c + 1]),
                waits=[tmm], inc="s_se")
            tev = p.tok("s_se")
            p.op("tensor", lambda h, swb=swb, c=c: h.matmul(
                swb[:, 0:512], pm[:, :], qraw[c][:, 0:512], start=True,
                stop=True), waits=[tev, tk["pm"]])
            p.op("tensor", lambda h, swb=swb, c=c: h.matmul(
                swb[:, 512:1024], pm[:, :], qraw[c][:, 512:1024],
                start=True, stop=True), inc="s_mm")
            tsw = p.tok("s_mm")

            def _rope(h, c=c, swb=swb):
                h.tensor_tensor(rtmp[:, :], qraw[c][:, :], cosf[:, :],
                                ALU.mult)
                h.tensor_tensor(rtmp2[:, :], swb[:, :], ssin[:, :],
                                ALU.mult)
                return h.tensor_tensor(kctr[:, c * 1024:(c + 1) * 1024],
                                       rtmp[:, :], rtmp2[:, :], ALU.add)
            p.op("vector", _rope, waits=[tsw, tk["rope_tab"]], inc="s_ve")
            kparts.append(p.tok("s_ve"))
        tk["kctr"] = kparts[-1]

        vev = [None] * 8
        for tt in range(8):
            par = tt % 2
            vb = psA[:, par * 512:par * 512 + 256]
            w = [tk["wvb"], tk["xln2"], tk["kctr"]]
            if tt >= 2:
                w.append(vev[tt - 2])
            for kc in range(KCH):
                p.op("tensor", lambda h, tt=tt, kc=kc, vb=vb: h.matmul(
                    vb[:, :],
                    xln[:, kc * 1024 + tt * 128:kc * 1024 + tt * 128 + 128],
                    wvb[:, kc * 256:(kc + 1) * 256],
                    start=(kc == 0), stop=(kc == KCH - 1)),
                    waits=w if kc == 0 else [],
                    inc="s_mm" if kc == KCH - 1 else None)
            tv = p.tok("s_mm")
            p.op("vector", lambda h, tt=tt, vb=vb: h.tensor_tensor(
                vctr[:, tt * 256:(tt + 1) * 256], vb[:, :], bvbc[:, :],
                ALU.add), waits=[tv, tk["bvbc"]], inc="s_ve")
            vev[tt] = p.tok("s_ve")
        tk["vctr"] = vev[-1]

        p.op("gpsimd", lambda h: h.dma_start(
            out=bass.AP(t_kvin, 0, [[1024, 128], [131072, 2], [1, 1024]]),
            in_=kctr[:, :].rearrange("p (c t) -> p c t", c=2)),
            waits=[tk["kctr"]], inc="s_gp")
        p.op("gpsimd", lambda h: h.dma_start(
            out=bass.AP(t_kvin, 262144, [[256, 128], [32768, 8], [1, 256]]),
            in_=vctr[:, :].rearrange("p (tt m) -> p tt m", tt=8)),
            waits=[tk["vctr"]], inc="s_gp")
        t_ = p.tok("s_gp")
        p.op("gpsimd", lambda h: h.collective_compute(
            "AllGather", ALU.bypass, replica_groups=RG,
            ins=[t_kvin.ap().opt()], outs=[t_kvout.ap().opt()]),
            waits=[t_], inc="s_cc", incn=1)
        tk["cc_kv"] = p.tok("s_cc")

        qdma = [None] * 16
        qropet = [None] * 16
        for c in range(16):
            par = c % 2
            qb = psA[:, par * 1024:(par + 1) * 1024]
            swb = psA[:, 2048 + par * 1024:3072 + par * 1024]
            lw = [tk["xln2"], tk["vctr"], hfree[0]]
            if c >= 2:
                lw.append(qropet[c - 2])
            p.op("gpsimd", lambda h, c=c, par=par: h.dma_start(
                out=wqb[par][:, :].rearrange("p (kc m) -> p kc m", kc=KCH),
                in_=bass.AP(t_wqp, c * D * 128,
                            [[128, 128], [16384, KCH], [1, 128]])),
                waits=lw, inc="s_gp")
            tw = p.tok("s_gp")
            mw = [tw, tk["vctr"]]
            if c >= 2:
                mw.append(qropet[c - 2])
            for kc in range(KCH):
                for half in (0, 1):
                    lst = (kc == KCH - 1 and half == 1)
                    p.op("tensor", lambda h, par=par, kc=kc, half=half,
                         qb=qb: h.matmul(
                             qb[:, half * 512:(half + 1) * 512],
                             wqb[par][:, kc * 128:(kc + 1) * 128],
                             xln[:, kc * 1024 + half * 512:
                                 kc * 1024 + half * 512 + 512],
                             start=(kc == 0), stop=(kc == KCH - 1)),
                         waits=mw if (kc == 0 and half == 0) else [],
                         inc="s_mm" if lst else None)
            tmm = p.tok("s_mm")
            sw2 = [tmm]
            if c >= 2:
                sw2.append(qropet[c - 2])
            p.op("scalar", lambda h, c=c, par=par, qb=qb: h.activation(
                qraw[par][:, :], qb[:, :], AF.Identity, bias=bq_s[:, c:c + 1]),
                waits=sw2, inc="s_se")
            tev = p.tok("s_se")
            p.op("tensor", lambda h, swb=swb, par=par: h.matmul(
                swb[:, 0:512], pm[:, :], qraw[par][:, 0:512], start=True,
                stop=True), waits=[tev, tk["pm"]])
            p.op("tensor", lambda h, swb=swb, par=par: h.matmul(
                swb[:, 512:1024], pm[:, :], qraw[par][:, 512:1024],
                start=True, stop=True), inc="s_mm")
            tsw = p.tok("s_mm")
            qw = [tsw, tk["rope_tab"]]
            if c >= 2:
                qw.append(qdma[c - 2])

            def _ropeq(h, par=par, swb=swb):
                h.tensor_tensor(rtmp[:, :], qraw[par][:, :], cosf[:, :],
                                ALU.mult)
                h.tensor_tensor(rtmp2[:, :], swb[:, :], ssin[:, :],
                                ALU.mult)
                return h.tensor_tensor(qrb[par][:, :], rtmp[:, :],
                                       rtmp2[:, :], ALU.add)
            p.op("vector", _ropeq, waits=qw, inc="s_ve")
            qropet[c] = p.tok("s_ve")
            p.op("gpsimd", lambda h, c=c, par=par: h.dma_start(
                out=t_qin.ap()[c * 128:(c + 1) * 128, :],
                in_=qrb[par][:, :]), waits=[qropet[c]], inc="s_gp")
            qdma[c] = p.tok("s_gp")
        p.op("gpsimd", lambda h: h.collective_compute(
            "AllGather", ALU.bypass, replica_groups=RG,
            ins=[t_qin.ap().opt()], outs=[t_qout.ap().opt()]),
            waits=[qdma[15]], inc="s_cc", incn=1)
        tk["cc_q"] = p.tok("s_cc")

        safe = [tk["cc_kv"], qdma[15], qropet[15], p.tok("s_mm"),
                p.tok("s_se"), tk["sel"]]
        regs = {}

        def _store_regs(h):
            regs["kvk"] = h.value_load(sbsel[0:1, 0:1])
            regs["kvv"] = h.value_load(sbsel[0:1, 1:2])
            regs["qof"] = h.value_load(sbsel[0:1, 2:3])
            regs["colA"] = h.value_load(sbsel[0:1, 3:4])
            regs["colB"] = h.value_load(sbsel[0:1, 4:5])
            return h.nop()
        p.op("sync", _store_regs, waits=safe)
        p.op("sync", lambda h: h.dma_start(
            out=Kh[:, :].rearrange("p (r t) -> p r t", r=8),
            in_=bass.AP(t_kvout, regs["kvk"],
                        [[1024, 128], [524288, 8], [1, 1024]])),
            waits=safe, inc="s_ld")
        for r in range(8):
            p.op("sync", lambda h, r=r: h.dma_start(
                out=Vh[:, r * 1024:(r + 1) * 1024].rearrange(
                    "p (j m) -> p j m", j=8),
                in_=bass.AP(t_kvout, regs["kvv"] + (262144 + r * 524288),
                            [[256, 128], [32768, 8], [1, 128]])),
                waits=safe, inc="s_ld")
        tk["kvh"] = p.tok("s_ld")
        for hh in range(2):
            p.op("sync", lambda h, hh=hh: h.dma_start(
                out=Qmy[:, hh * 8192:(hh + 1) * 8192].rearrange(
                    "p (r t) -> p r t", r=8),
                in_=bass.AP(t_qout, regs["qof"] + hh * 131072,
                            [[1024, 128], [2097152, 8], [1, 1024]])),
                waits=[tk["cc_q"]] + safe, inc="s_ld")
        tk["qmy"] = p.tok("s_ld")

        def keypos(kt):
            b = kt // 4
            r, s = _rank_slot(b)
            return r * 1024 + s * 512 + (kt % 4) * 128

        def qpos(h_i, b):
            r, s = _rank_slot(b)
            return h_i * 8192 + r * 1024 + s * 512

        red_t = [None] * 48
        idx = 0
        for src, nsl, base in ((Qmy, 32, 64), (Kh, 16, 96)):
            for j in range(nsl):
                par = idx % 2
                w = [tk["qmy"], tk["kvh"], tk["masks"]]
                if idx >= 2:
                    w.append(red_t[idx - 2])
                p.op("scalar", lambda h, src=src, j=j, par=par:
                     h.activation(qsqs[par][:, :],
                                  src[:, j * 512:(j + 1) * 512], AF.Square),
                     waits=w, inc="s_se")
                ts_ = p.tok("s_se")
                rw = [ts_, tk["consts"]]
                if idx >= 2:
                    rw.append(red_t[idx - 2])
                p.op("tensor", lambda h, par=par: h.matmul(
                    psA[0:1, par * 512:(par + 1) * 512], ones_b[:, :],
                    qsqs[par][:, :], start=True, stop=True),
                    waits=rw, inc="s_mm")
                tm = p.tok("s_mm")
                p.op("vector", lambda h, base=base, j=j, par=par:
                     h.reduce_max(prow(base + j, base + j + 1),
                                  psA[0:1, par * 512:(par + 1) * 512],
                                  axis=AX), waits=[tm], inc="s_ve")
                red_t[idx] = p.tok("s_ve")
                idx += 1

        t_ = p.seq("vector", [
            lambda h: h.reduce_max(prow(112, 113), prow(64, 80), axis=AX),
            lambda h: h.reduce_max(prow(113, 114), prow(80, 96), axis=AX),
            lambda h: h.reduce_max(prow(114, 115), prow(96, 112), axis=AX),
            lambda h: h.tensor_tensor(prow(115, 116), prow(112, 113),
                                      prow(114, 115), ALU.mult),
            lambda h: h.tensor_tensor(prow(116, 117), prow(113, 114),
                                      prow(114, 115), ALU.mult),
        ], hard=[red_t[47]])
        p.op("scalar", lambda h: h.activation(prow(117, 119),
                                              prow(115, 117), AF.Sqrt),
             waits=[t_], inc="s_se")
        t_ = p.tok("s_se")
        p.op("vector", lambda h: h.tensor_single_scalar(
            prow(119, 121), prow(117, 119), -CSC, ALU.mult),
            waits=[t_], inc="s_ve")
        t_ = p.tok("s_ve")
        p.op("gpsimd", lambda h: h.dma_start(out=t_bshd.ap(),
                                             in_=prow(119, 121)),
             waits=[t_], inc="s_gp")
        t_ = p.tok("s_gp")
        p.op("gpsimd", lambda h: h.dma_start(
            out=bshift[:, :], in_=bass.AP(t_bshd, 0, [[0, 128], [1, 2]])),
            waits=[t_], inc="s_gp")
        tk["bshift"] = p.tok("s_gp")

        ps_s = [psA[:, 0:512], psA[:, 512:1024]]
        ps_o = [psA[:, 1024:1536], psA[:, 1536:2048]]
        ps_p = [psA[0:1, 2048:2560], psA[0:1, 2560:3072]]
        ps_rb = psA[:, 3072:3584]
        exp_done = []
        pt_ready = []
        av_done = []
        ot_done = []
        odma_done = []
        rp_done = []
        rb_ev = None
        g = 0
        for h_i in range(2):
            for b in range(NB):
                ti = h_i * NB + b
                ob = ti % 2
                nkt = 4 * (b + 1)
                qoff = qpos(h_i, b)
                for kt in range(nkt):
                    sp = g % 2
                    w = [tk["qmy"], tk["kvh"], tk["bshift"], tk["masks"]]
                    if g >= 2:
                        w.append(exp_done[g - 2])
                    kp = keypos(kt)
                    p.op("tensor", lambda h, kp=kp, sp=sp, qoff=qoff:
                         h.matmul(ps_s[sp][:, :],
                                  Kh[:, kp:kp + 128],
                                  Qmy[:, qoff:qoff + 512],
                                  start=True, stop=True),
                         waits=w, inc="s_mm")
                    tsc = p.tok("s_mm")
                    ew = [tsc]
                    if g >= 2:
                        ew.append(av_done[g - 2])
                    p.op("scalar", lambda h, sp=sp, h_i=h_i: h.activation(
                        pt[sp][:, :], ps_s[sp][:, :], AF.Exp,
                        bias=bshift[:, h_i:h_i + 1], scale=CSC),
                        waits=ew, inc="s_se")
                    exp_done.append(p.tok("s_se"))
                    ptok = exp_done[g]
                    if kt >= 4 * b:
                        m = kt - 4 * b
                        p.op("vector", lambda h, sp=sp, m=m: h.tensor_tensor(
                            pt[sp][:, :], pt[sp][:, :],
                            masks[:, m * 512:(m + 1) * 512], ALU.mult),
                            waits=[ptok], inc="s_ve")
                        ptok = p.tok("s_ve")
                    pt_ready.append(ptok)
                    avw = [ptok]
                    if kt == 0 and ti >= 2:
                        avw.append(ot_done[ti - 2])
                        avw.append(rp_done[ti - 2])
                    p.op("tensor", lambda h, kp=kp, sp=sp, ob=ob, kt=kt,
                         nkt=nkt:
                         h.matmul(ps_o[ob][:, :],
                                  Vh[:, kp:kp + 128],
                                  pt[sp][:, :], start=(kt == 0),
                                  stop=(kt == nkt - 1)), waits=avw)
                    p.op("tensor", lambda h, kt=kt, sp=sp, ob=ob, nkt=nkt:
                         h.matmul(ps_p[ob][:, :], ones_b[:, :],
                                  pt[sp][:, :], start=(kt == 0),
                                  stop=(kt == nkt - 1)), inc="s_mm")
                    av_done.append(p.tok("s_mm"))
                    g += 1
                tav = av_done[g - 1]
                p.op("vector", lambda h, ob=ob: h.reciprocal(
                    rp[0:1, :], ps_p[ob][:, :]), waits=[tav], inc="s_ve")
                rp_done.append(p.tok("s_ve"))
                p.op("tensor", lambda h: h.matmul(
                    ps_rb[:, :], onesrow_f, rp[0:1, :], start=True,
                    stop=True),
                    waits=[rp_done[ti], tk["consts"]] +
                    ([rb_ev] if rb_ev else []), inc="s_mm")
                trb = p.tok("s_mm")
                p.op("scalar", lambda h: h.copy(rbuf[:, :], ps_rb[:, :]),
                     waits=[trb] + ([ot_done[ti - 1]] if ti >= 1 else []),
                     inc="s_se")
                rb_ev = p.tok("s_se")
                ow = [rb_ev, tav]
                if ti >= 2:
                    ow.append(odma_done[ti - 2])
                p.op("vector", lambda h, ob=ob: h.tensor_tensor(
                    otile[ob][:, :], ps_o[ob][:, :], rbuf[:, :], ALU.mult),
                    waits=ow, inc="s_ve")
                ot_done.append(p.tok("s_ve"))
                p.op("gpsimd", lambda h, b=b, h_i=h_i, ob=ob:
                     h.dma_start(
                         out=bass.AP(t_oin, h_i * 1048576 + b * 512,
                                     [[8192, 128], [1, 512]]),
                         in_=otile[ob][:, :]), waits=[ot_done[ti]],
                     inc="s_gp")
                odma_done.append(p.tok("s_gp"))
        p.op("gpsimd", lambda h: h.collective_compute(
            "AllGather", ALU.bypass, replica_groups=RG,
            ins=[t_oin.ap().opt()], outs=[t_oout.ap().opt()]),
            waits=[odma_done[-1]], inc="s_cc", incn=1)
        tk["cc_o"] = p.tok("s_cc")

        otv = xln
        ow_ = [tk["cc_o"], p.tok("s_mm"), p.tok("s_ve")]
        p.op("sync", lambda h: h.dma_start(
            out=otv[:, :].rearrange("p (c t) -> p c t", c=KCH)[:, :, 0:512],
            in_=bass.AP(t_oout, regs["colA"],
                        [[8192, 128], [1048576, KCH], [1, 512]])),
            waits=ow_, inc="s_ld")
        p.op("sync", lambda h: h.dma_start(
            out=otv[:, :].rearrange("p (c t) -> p c t", c=KCH)[:, :, 512:1024],
            in_=bass.AP(t_oout, regs["colB"],
                        [[8192, 128], [1048576, KCH], [1, 512]])),
            waits=ow_, inc="s_ld")
        tk["otv"] = p.tok("s_ld")
        res2 = [None] * OCH
        for oc in range(OCH):
            par = oc % 2
            w = [tk["cc_o"]]
            if oc >= 2:
                w.append(res2[oc - 2])
            p.op("gpsimd", lambda h, oc=oc, par=par: h.dma_start(
                out=wob[par][:, :].rearrange("p (kc m) -> p kc m", kc=KCH),
                in_=bass.AP(t_wop, oc * D * 128,
                            [[128, 128], [16384, KCH], [1, 128]])),
                waits=w, inc="s_gp")
            tw = p.tok("s_gp")
            yb = psA[:, par * 1024:(par + 1) * 1024]
            mw = [tw, tk["otv"]]
            if oc >= 2:
                mw.append(res2[oc - 2])
            for kc in range(KCH):
                for half in (0, 1):
                    lst = (kc == KCH - 1 and half == 1)
                    p.op("tensor", lambda h, par=par, kc=kc, half=half,
                         yb=yb: h.matmul(
                             yb[:, half * 512:(half + 1) * 512],
                             wob[par][:, kc * 128:(kc + 1) * 128],
                             otv[:, kc * 1024 + half * 512:
                                 kc * 1024 + half * 512 + 512],
                             start=(kc == 0), stop=(kc == KCH - 1)),
                         waits=mw if (kc == 0 and half == 0) else [],
                         inc="s_mm" if lst else None)
            tmm = p.tok("s_mm")
            p.op("vector", lambda h, oc=oc, yb=yb: h.tensor_tensor(
                stream[:, oc * 1024:(oc + 1) * 1024],
                stream[:, oc * 1024:(oc + 1) * 1024], yb[:, :], ALU.add),
                waits=[tmm, tk["stream"]], inc="s_ve")
            res2[oc] = p.tok("s_ve")

        if stop == "attn":
            for oc in range(OCH):
                p.op("gpsimd", lambda h, oc=oc: h.dma_start(
                    out=t_out.ap()[oc, :, :],
                    in_=stream[:, oc * 1024:(oc + 1) * 1024]),
                    waits=[res2[oc]], inc="s_out")
            p.op("gpsimd",
                 lambda h: h.wait_ge(sems["s_out"], 16 * OCH))
            p.emit(sems)
            return nc

        tk["xln"] = emit_ln(
            lambda kc: stream[:, kc * 1024:(kc + 1) * 1024],
            lambda kc: res2[kc], extra_mm_waits=[res2[OCH - 1]])
        pbar = [p.tok("s_mm"), p.tok("s_gp"), p.tok("s_ld"),
                p.tok("s_se"), tk["xln"]]
        htok4, sigtok4, _ = emit_ffn_B(False, entry_waits=pbar)
        emit_ffn_C(False, htok4, sigtok4, resid="stream")

        if DBG:
            end = [p.tok("s_ve"), p.tok("s_mm"), p.tok("s_se"),
                   p.tok("s_gp"), p.tok("s_ld"), p.tok("s_sc"),
                   p.tok("s_out")]
            dmph = nc.dram_tensor("dbg_h", [44, 128, 1024], F32,
                                  kind="ExternalOutput")
            for hc in range(HCH):
                p.op("gpsimd", lambda h, hc=hc: h.dma_start(
                    out=dmph.ap()[hc, :, :],
                    in_=hbuf[:, hc * 1024:(hc + 1) * 1024]),
                    waits=end, inc="s_out")
            dmpw = nc.dram_tensor("dbg_w3", [1, 128, HID], F32,
                                  kind="ExternalOutput")
            p.op("gpsimd", lambda h: h.dma_start(
                out=dmpw.ap()[0, :, :], in_=t_w3q.ap()[0, :, :]),
                waits=end, inc="s_out")
            dmps = nc.dram_tensor("dbg_stream", [KCH, 128, 1024], F32,
                                  kind="ExternalOutput")
            for kc in range(KCH):
                p.op("gpsimd", lambda h, kc=kc: h.dma_start(
                    out=dmps.ap()[kc, :, :],
                    in_=stream[:, kc * 1024:(kc + 1) * 1024]),
                    waits=end, inc="s_out")
            dmpx = nc.dram_tensor("dbg_xln", [KCH, 128, 1024], F32,
                                  kind="ExternalOutput")
            for kc in range(KCH):
                p.op("gpsimd", lambda h, kc=kc: h.dma_start(
                    out=dmpx.ap()[kc, :, :],
                    in_=xln[:, kc * 1024:(kc + 1) * 1024]),
                    waits=end, inc="s_out")
            p.op("gpsimd", lambda h: h.wait_ge(
                sems["s_out"], 16 * (OCH + HCH + 1 + KCH + KCH)))
        else:
            p.op("gpsimd", lambda h: h.wait_ge(sems["s_out"], 16 * OCH))
        p.emit(sems)
    return nc



def _prep_inputs(L0_attn_x, L0_attn_wq, L0_attn_wk, L0_attn_wv, L0_attn_wo,
                 L0_attn_bq, L0_attn_bk, L0_attn_bv, L0_ffn_w1, L0_ffn_w2,
                 L0_ffn_w3):
    f = np.float32
    x = np.asarray(L0_attn_x, f)[0]
    xT = np.ascontiguousarray(x.T)
    w1 = np.asarray(L0_ffn_w1, f)
    w2 = np.asarray(L0_ffn_w2, f)
    w3 = np.asarray(L0_ffn_w3, f)

    def panels(w, n):
        wT = np.ascontiguousarray(np.asarray(w, f).T)
        return np.ascontiguousarray(
            wT.reshape(wT.shape[0], n, 128).transpose(1, 0, 2))

    shared = {
        "w1p": panels(w1, HCH),
        "w2p": panels(w2, HCH),
        "w3p": panels(w3, OCH),
        "wqp": panels(L0_attn_wq, 16),
        "wkp": panels(L0_attn_wk, 2),
        "wvt": np.ascontiguousarray(np.asarray(L0_attn_wv, f).T),
        "wop": panels(L0_attn_wo, 16),
        "bqp": np.ascontiguousarray(
            np.asarray(L0_attn_bq, f).reshape(16, 128).T),
        "bkp": np.ascontiguousarray(
            np.asarray(L0_attn_bk, f).reshape(2, 128).T),
        "bvr": np.ascontiguousarray(np.asarray(L0_attn_bv, f).reshape(1, 256)),
    }
    gsz = D * HID // NCORES
    in_maps = []
    for i in range(NCORES):
        bA, bB = i, 15 - i
        cols = np.r_[bA * BS:(bA + 1) * BS, bB * BS:(bB + 1) * BS]
        m = dict(shared)
        m["xT"] = np.ascontiguousarray(xT[:, cols])
        m["pos"] = np.ascontiguousarray(cols.astype(f).reshape(1, NT))
        kv = i // 4
        m["selv"] = np.array([[kv * 131072, kv * 128, i * 262144,
                               bA * BS, bB * BS, 0, 0, 0]], np.int32)
        for name, w in (("g1", w1), ("g2", w2), ("g3", w3)):
            m[name] = np.ascontiguousarray(
                w.reshape(-1)[i * gsz:(i + 1) * gsz].reshape(128, GSL))
        in_maps.append(m)
    return in_maps


def _assemble(results):
    fullT = np.empty((D, L), np.float32)
    for i, r in enumerate(results):
        yT = np.asarray(r["yT"]).reshape(D, NT)
        bA, bB = i, 15 - i
        fullT[:, bA * BS:(bA + 1) * BS] = yT[:, 0:BS]
        fullT[:, bB * BS:(bB + 1) * BS] = yT[:, BS:NT]
    return np.ascontiguousarray(fullT.T).reshape(1, L, D)


def kernel(**inputs):
    in_maps = _prep_inputs(**inputs)
    nc = build_program("full")
    res = run_bass_kernel_spmd(nc, in_maps, core_ids=list(range(NCORES)))
    return _assemble(res.results)


# version 03
# speedup vs baseline: 1.4742x; 1.4633x over previous
import contextlib
import numpy as np
import concourse.bass as bass
import concourse.mybir as mybir
from concourse.bass_utils import run_bass_kernel_spmd

F32 = mybir.dt.float32
BF16 = mybir.dt.bfloat16
I32 = mybir.dt.int32
U8 = mybir.dt.uint8
AF = mybir.ActivationFunctionType
ALU = mybir.AluOpType
AX = mybir.AxisListType.X

D = 2048
HID = 5632
L = 8192
NCORES = 8
NT = 1024
NB = 16
BS = 512
KCH = 16
HCH = 44
OCH = 16
CSC = float(1.0 / np.sqrt(128.0))
NW = float(D * HID)
LN_EPS = 1e-5
Q_EPS = 1e-8
_2PI = np.float64(2 * np.pi)
TWO_PI_HI = float(np.float32(_2PI))
TWO_PI_LO = float(np.float32(_2PI - np.float64(np.float32(_2PI))))
INV_2PI = float(np.float32(1.0 / _2PI))
HALF_PI = float(np.float32(np.pi / 2))
LOG1E4 = float(np.log(10000.0))
GSL = D * HID // NCORES // 128
DBG = False

ENGS = ("sync", "scalar", "vector", "tensor", "gpsimd")
DMA_SEMS = {"s_ld", "s_sc", "s_gp", "s_out"}
OWN = {"tensor": "s_mm", "vector": "s_ve", "scalar": "s_se", "gpsimd": "s_gq"}

XLN = 0
HAR = 32768
STR = 122880
TAIL = 188416
PER = 206848
ARENA_B = 210944

DTS = {F32: 4, BF16: 2, I32: 4}


def _rank_slot(b):
    return (b, 0) if b <= 7 else (15 - b, 1)


class P:
    def __init__(self, nc):
        self.nc = nc
        self.q = {e: [] for e in ENGS}
        self.cnt = {}

    def tok(self, sem):
        return (sem, self.cnt.get(sem, 0))

    def op(self, eng, fn, waits=(), inc=None, incn=None, hard=()):
        incs = []
        if inc is not None:
            n = incn if incn is not None else (16 if inc in DMA_SEMS else 1)
            self.cnt[inc] = self.cnt.get(inc, 0) + n
            incs.append((inc, n))
        ws = {}
        hw = {}
        for w in waits:
            if w is None:
                continue
            s, v = w
            if v > 0 and v > ws.get(s, 0):
                ws[s] = v
        for w in hard:
            if w is None:
                continue
            s, v = w
            if v > 0 and v > hw.get(s, 0):
                hw[s] = v
        self.q[eng].append((sorted(ws.items()), sorted(hw.items()), fn, incs))
        return self.tok(inc) if inc is not None else None

    def seq(self, eng, fns, waits=(), hard=()):
        own = OWN[eng]
        t = None
        for i, fn in enumerate(fns):
            self.op(eng, fn, waits=waits if i == 0 else (),
                    hard=(list(hard) + ([t] if t else [])) if i == 0
                    else [t], inc=own)
            t = self.tok(own)
        return t

    def emit(self, sems):
        guar = {e: {} for e in ENGS}
        with self.nc.Block() as blk:
            for eng in ENGS:
                def body(h, eng=eng):
                    g = guar[eng]
                    own = OWN.get(eng)
                    for ws, hws, fn, incs in self.q[eng]:
                        for s, v in ws:
                            if s == own:
                                continue
                            if v > g.get(s, 0):
                                h.wait_ge(sems[s], v)
                                g[s] = v
                        for s, v in hws:
                            if v > g.get(s, 0):
                                h.wait_ge(sems[s], v)
                                g[s] = v
                        inst = fn(h)
                        for s, n in incs:
                            inst = inst.then_inc(sems[s], n)
                getattr(blk, eng)(body)


def build_program(stop="full"):
    nc = bass.Bass("TRN2", target_bir_lowering=False)

    t_xT = nc.dram_tensor("xT", [D, NT], F32, kind="ExternalInput")
    t_pos = nc.dram_tensor("pos", [1, NT], F32, kind="ExternalInput")
    t_g = [nc.dram_tensor("g%d" % (j + 1), [128, GSL], F32,
                          kind="ExternalInput") for j in range(3)]
    t_w1p = nc.dram_tensor("w1p", [HCH, D, 128], F32, kind="ExternalInput")
    t_w2p = nc.dram_tensor("w2p", [HCH, D, 128], F32, kind="ExternalInput")
    t_w3p = nc.dram_tensor("w3p", [OCH, HID, 128], F32, kind="ExternalInput")
    t_wqp = nc.dram_tensor("wqp", [16, D, 128], F32, kind="ExternalInput")
    t_wkp = nc.dram_tensor("wkp", [2, D, 128], F32, kind="ExternalInput")
    t_wvt = nc.dram_tensor("wvt", [D, 256], F32, kind="ExternalInput")
    t_wop = nc.dram_tensor("wop", [16, D, 128], F32, kind="ExternalInput")
    t_bq = nc.dram_tensor("bqp", [128, 16], F32, kind="ExternalInput")
    t_bk = nc.dram_tensor("bkp", [128, 2], F32, kind="ExternalInput")
    t_bv = nc.dram_tensor("bvr", [1, 256], F32, kind="ExternalInput")
    t_sel = nc.dram_tensor("selv", [1, 8], I32, kind="ExternalInput")
    t_out = nc.dram_tensor("yT", [OCH, 128, NT], F32, kind="ExternalOutput")

    t_w1q = nc.dram_tensor("w1q_d", [HCH, 128, D], BF16)
    t_w2q = nc.dram_tensor("w2q_d", [HCH, 128, D], BF16)
    t_w3q = nc.dram_tensor("w3q_d", [OCH, 128, HID], BF16)
    t_xmid = nc.dram_tensor("xmid_d", [KCH, 128, NT], F32)
    t_arin = nc.dram_tensor("ar_in", [1, 8], F32)
    t_arout = nc.dram_tensor("ar_out", [1, 8], F32)
    t_taud = nc.dram_tensor("tau_d", [1, 8], F32)
    t_bshd = nc.dram_tensor("bsh_d", [1, 2], F32)
    t_rpd = nc.dram_tensor("rp_d", [2, 512], F32)
    t_kvin = nc.dram_tensor("kvag_in", [1, 524288], BF16)
    t_kvout = nc.dram_tensor("kvag_out", [8, 524288], BF16, addr_space="Shared")
    t_qin = nc.dram_tensor("qag_in", [D, NT], BF16)
    t_qout = nc.dram_tensor("qag_out", [8 * D, NT], BF16, addr_space="Shared")
    t_oin = nc.dram_tensor("oag_in", [2, 128, L], BF16)
    t_dbg2 = nc.dram_tensor("dbg2", [2, 128, NT], F32, kind="ExternalOutput")
    t_dbg3 = nc.dram_tensor("dbg3", [1, 128, NT], F32, kind="ExternalOutput")
    t_oout = nc.dram_tensor("oag_out", [16, 128, L], BF16, addr_space="Shared")

    RG = [list(range(NCORES))]

    es = contextlib.ExitStack()
    with es:
        arena = es.enter_context(nc.sbuf_tensor("arena", [128, ARENA_B], U8))
        psA = es.enter_context(nc.psum_tensor("psA", [128, 4096], F32))
        sems = {}
        for s in ("s_ld", "s_sc", "s_gp", "s_out", "s_cc",
                  "s_mm", "s_ve", "s_se", "s_gq"):
            sems[s] = es.enter_context(nc.semaphore(s))

        def V(off, n, dt):
            return arena[:, off:off + n * DTS[dt]].bitcast(dt)

        ones_f = V(PER + 0, 1, F32)
        ones_b = V(PER + 64, 1, BF16)
        onesrow_f = V(PER + 128, 128, F32)[0:1, :]
        taus = V(PER + 704, 8, F32)
        bq_s = V(PER + 768, 16, F32)
        bk_s = V(PER + 832, 2, F32)
        bshift = V(PER + 896, 2, F32)
        invf = V(PER + 960, 1, F32)
        pv_t = V(PER + 1024, 1, F32)
        gtmp = V(PER + 1088, 1, F32)
        bvbc = V(PER + 1152, 256, BF16)
        sbsel = V(PER + 1600, 8, I32)
        prow_full = V(PER + 1664, 256, F32)

        def prow(a, b):
            return prow_full[0:1, a:b]

        xln = V(XLN, 16384, BF16)
        lnscr = V(XLN, 8192, F32)
        stream = V(STR, 16384, F32)
        hbuf = V(HAR, 45056, BF16)

        p = P(nc)
        tk = {}

        x0 = V(HAR, 16384, F32)
        gbuf = [V(STR + k * 11264, 2816, F32) for k in (0, 1)]
        xsq = [V(TAIL + k * 4096, 1024, F32) for k in (0, 1)]
        racc = V(TAIL + 8192, 16, F32)

        p.op("gpsimd", lambda h: h.memset(ones_f[:, :], 1.0), inc="s_gq")
        p.op("gpsimd", lambda h: h.memset(ones_b[:, :], 1.0), inc="s_gq")
        p.op("gpsimd", lambda h: h.memset(onesrow_f, 1.0), inc="s_gq")
        tk["consts"] = p.tok("s_gq")

        p.op("sync", lambda h: h.dma_start(
            out=x0[:, :].rearrange("p (c t) -> p c t", c=KCH),
            in_=t_xT.ap().rearrange("(c p) t -> p c t", p=128)), inc="s_ld")
        tk["x0"] = p.tok("s_ld")
        p.op("sync", lambda h: h.dma_start(out=bq_s[:, :], in_=t_bq.ap()),
             inc="s_ld")
        p.op("sync", lambda h: h.dma_start(out=sbsel[0:1, :], in_=t_sel.ap()),
             inc="s_ld")
        tk["sel"] = p.tok("s_ld")
        p.op("sync", lambda h: h.dma_start(out=bk_s[:, :], in_=t_bk.ap()),
             inc="s_ld")
        tk["bias"] = p.tok("s_ld")
        p.op("gpsimd", lambda h: h.dma_start(
            out=bvbc[:, :], in_=bass.AP(t_bv, 0, [[0, 128], [1, 256]])),
            inc="s_gp")
        tk["bvbc"] = p.tok("s_gp")

        gred = [None] * 12
        for j in range(3):
            for c in range(4):
                i = 4 * j + c
                par = i % 2
                w = [gred[i - 2]] if i >= 2 else []
                p.op("scalar", lambda h, j=j, c=c, par=par: h.dma_start(
                    out=gbuf[par][:, :],
                    in_=t_g[j].ap()[:, c * 2816:(c + 1) * 2816]),
                    waits=w, inc="s_sc")
                ldt = p.tok("s_sc")
                p.op("vector", lambda h, i=i, par=par: h.tensor_reduce(
                    racc[:, i:i + 1], gbuf[par][:, :], axis=AX, op=ALU.add,
                    apply_absolute_value=True), waits=[ldt], inc="s_ve")
                gred[i] = p.tok("s_ve")
        p.op("tensor", lambda h: h.matmul(
            psA[0:1, 0:12], ones_f[:, :], racc[:, 0:12], start=True,
            stop=True), waits=[gred[11], tk["consts"]], inc="s_mm")
        tgmm = p.tok("s_mm")

        tk["gpart"] = p.seq("vector", [
            lambda h: h.tensor_copy(prow(0, 12), psA[0:1, 0:12]),
            lambda h: h.reduce_sum(prow(12, 13), prow(0, 4), axis=AX),
            lambda h: h.reduce_sum(prow(13, 14), prow(4, 8), axis=AX),
            lambda h: h.reduce_sum(prow(14, 15), prow(8, 12), axis=AX),
        ], waits=[tgmm])
        p.op("gpsimd", lambda h: h.dma_start(out=t_arin.ap(),
                                             in_=prow(12, 20)),
             waits=[tk["gpart"]], inc="s_gp")
        t_ = p.tok("s_gp")
        p.op("gpsimd", lambda h: h.collective_compute(
            "AllReduce", ALU.add, replica_groups=RG,
            ins=[t_arin.ap().opt()], outs=[t_arout.ap().opt()]),
            waits=[t_], inc="s_cc", incn=1)
        tar = p.tok("s_cc")
        p.op("gpsimd", lambda h: h.dma_start(out=prow(24, 32),
                                             in_=t_arout.ap()),
             waits=[tar], inc="s_gp")
        tab = p.tok("s_gp")

        ttr = p.seq("vector", [
            lambda h: h.tensor_scalar(prow(32, 35), prow(24, 27), 1.0 / NW,
                                      None, ALU.mult),
            lambda h: h.tensor_scalar(prow(35, 38), prow(32, 35), 0.5,
                                      0.5 * Q_EPS, ALU.mult, ALU.add),
            lambda h: h.tensor_scalar(prow(38, 41), prow(35, 38), -1.0,
                                      None, ALU.mult),
        ], waits=[tab])
        p.op("gpsimd", lambda h: h.dma_start(out=t_taud.ap(),
                                             in_=prow(35, 43)),
             waits=[ttr], inc="s_gp")
        t_ = p.tok("s_gp")
        p.op("gpsimd", lambda h: h.dma_start(
            out=taus[:, :], in_=bass.AP(t_taud, 0, [[0, 128], [1, 8]])),
            waits=[t_], inc="s_gp")
        tk["taus"] = p.tok("s_gp")

        def emit_ln(src, src_toks, extra_mm_waits=()):
            mm_done = [None] * KCH
            for kc in range(KCH):
                par = kc % 2
                w = [src_toks(kc)] if src_toks else []
                if kc >= 2 and mm_done[kc - 2] is not None:
                    w = w + [mm_done[kc - 2]]
                p.op("scalar", lambda h, kc=kc, par=par: h.activation(
                    xsq[par][:, :], src(kc), AF.Square), waits=w, inc="s_se")
                sqt = p.tok("s_se")
                mw = [sqt, tk["consts"]] + list(extra_mm_waits) + \
                    ([src_toks(kc)] if src_toks else [])
                for seg in range(4):
                    def mk(h, kc=kc, seg=seg, par=par):
                        if seg < 2:
                            rh = src(kc)[:, seg * 512:(seg + 1) * 512]
                        else:
                            rh = xsq[par][:, (seg - 2) * 512:(seg - 1) * 512]
                        return h.matmul(psA[0:1, seg * 512:(seg + 1) * 512],
                                        ones_f[:, :], rh,
                                        start=(kc == 0), stop=(kc == KCH - 1))
                    p.op("tensor", mk, waits=mw if seg == 0 else [sqt],
                         inc="s_mm" if seg == 3 else None)
                mm_done[kc] = p.tok("s_mm")
            tsum = mm_done[KCH - 1]

            M, VE, T, R0, RS = 0, 1024, 2048, 3072, 4096

            t1 = p.seq("vector", [
                lambda h: h.tensor_scalar(lnscr[0:1, M:M + 1024],
                                          psA[0:1, 0:1024], 1.0 / D, None,
                                          ALU.mult),
                lambda h: h.tensor_scalar(lnscr[0:1, VE:VE + 1024],
                                          psA[0:1, 1024:2048], 1.0 / D,
                                          LN_EPS, ALU.mult, ALU.add),
                lambda h: h.tensor_tensor(lnscr[0:1, T:T + 1024],
                                          lnscr[0:1, M:M + 1024],
                                          lnscr[0:1, M:M + 1024], ALU.mult),
                lambda h: h.tensor_tensor(lnscr[0:1, VE:VE + 1024],
                                          lnscr[0:1, VE:VE + 1024],
                                          lnscr[0:1, T:T + 1024],
                                          ALU.subtract),
                lambda h: h.reciprocal(lnscr[0:1, R0:R0 + 1024],
                                       lnscr[0:1, VE:VE + 1024]),
            ], waits=[tsum])
            p.op("scalar", lambda h: h.activation(
                lnscr[0:1, RS:RS + 1024], lnscr[0:1, R0:R0 + 1024], AF.Sqrt),
                waits=[t1], inc="s_se")
            t2 = p.tok("s_se")

            t3 = p.seq("vector", [
                lambda h: h.tensor_tensor(lnscr[0:1, T:T + 1024],
                                          lnscr[0:1, RS:RS + 1024],
                                          lnscr[0:1, RS:RS + 1024],
                                          ALU.mult),
                lambda h: h.tensor_tensor(lnscr[0:1, T:T + 1024],
                                          lnscr[0:1, T:T + 1024],
                                          lnscr[0:1, VE:VE + 1024],
                                          ALU.mult),
                lambda h: h.tensor_scalar(lnscr[0:1, T:T + 1024],
                                          lnscr[0:1, T:T + 1024], -0.5, 1.5,
                                          ALU.mult, ALU.add),
                lambda h: h.tensor_tensor(lnscr[0:1, RS:RS + 1024],
                                          lnscr[0:1, RS:RS + 1024],
                                          lnscr[0:1, T:T + 1024], ALU.mult),
                lambda h: h.tensor_tensor(lnscr[0:1, M:M + 1024],
                                          lnscr[0:1, M:M + 1024],
                                          lnscr[0:1, RS:RS + 1024],
                                          ALU.mult),
            ], waits=[t2])
            for i in range(4):
                coff = RS if i < 2 else M
                boff = 2048 + i * 512
                p.op("tensor", lambda h, coff=coff, boff=boff, i=i: h.matmul(
                    psA[:, boff:boff + 512], onesrow_f,
                    lnscr[0:1, coff + (i % 2) * 512:
                          coff + (i % 2) * 512 + 512],
                    start=True, stop=True),
                    waits=[t3, tk["consts"]], inc="s_mm" if i == 3 else None)
            tbc = p.tok("s_mm")
            last = None
            for kc in range(KCH):
                par = kc % 2

                def _x(h, kc=kc, par=par):
                    h.tensor_tensor(xsq[par][:, :], src(kc),
                                    psA[:, 2048:3072], ALU.mult)
                    return h.tensor_tensor(
                        xln[:, kc * 1024:(kc + 1) * 1024], xsq[par][:, :],
                        psA[:, 3072:4096], ALU.subtract)
                p.op("vector", _x, waits=[tbc] +
                     ([src_toks(kc)] if src_toks else []), inc="s_ve")
                last = p.tok("s_ve")
            return last

        tk["xln"] = emit_ln(lambda kc: x0[:, kc * 1024:(kc + 1) * 1024],
                            lambda kc: tk["x0"],
                            extra_mm_waits=[tk["gpart"]])

        def emit_ffn_B(pre, entry_waits):
            if pre:
                wraw = [[V(STR + k * 8192, 2048, F32) for k in (0, 1)],
                        [V(STR + 16384 + k * 8192, 2048, F32)
                         for k in (0, 1)]]
                wqv = [[V(STR + 32768 + k * 4096, 2048, BF16)
                        for k in (0, 1)],
                       [V(STR + 40960 + k * 4096, 2048, BF16)
                        for k in (0, 1)]]
                ltmp = V(STR + 49152, 2048, BF16)
                sigb = V(STR + 53248, 1024, BF16)
            else:
                wqv = [[V(TAIL + k * 4096, 2048, BF16) for k in (0, 1)],
                       [V(TAIL + 8192 + k * 4096, 2048, BF16)
                        for k in (0, 1)]]
                sigb = V(TAIL + 16384, 1024, BF16)
            xb = [[psA[:, 0:1024], psA[:, 1024:2048]],
                  [psA[:, 2048:3072], psA[:, 3072:4096]]]
            wsrc = (t_w1p, t_w2p)
            wdst = (t_w1q, t_w2q)
            tcols = ((0, 3), (1, 4))
            qtok = [[None] * HCH, [None] * HCH]
            stok = [[None] * HCH, [None] * HCH]
            mmtok = [[None] * HCH, [None] * HCH]
            sigtok = [None] * HCH
            htok = [None] * HCH
            for hc in range(HCH):
                par = hc % 2
                for wi in range(2):
                    fw = [mmtok[wi][hc - 2]] if hc >= 2 else list(entry_waits)
                    if pre:
                        if hc >= 2:
                            fw.append(qtok[wi][hc - 2])
                        p.op("sync", lambda h, wi=wi, hc=hc, par=par:
                             h.dma_start(
                                 out=wraw[wi][par][:, :].rearrange(
                                     "p (kc m) -> p kc m", kc=KCH),
                                 in_=bass.AP(wsrc[wi], hc * D * 128,
                                             [[128, 128], [16384, KCH],
                                              [1, 128]])),
                             waits=fw, inc="s_ld")
                        ldt = p.tok("s_ld")
                        qw = [ldt, tk["taus"]]
                        if hc >= 2:
                            qw += [mmtok[wi][hc - 2], stok[wi][hc - 2]]
                        if hc == 0 and wi == 0:
                            qw += list(entry_waits)

                        def _q(h, wi=wi, par=par):
                            tc, nc_ = tcols[wi]
                            h.tensor_single_scalar(
                                ltmp[:, :], wraw[wi][par][:, :],
                                taus[:, nc_:nc_ + 1], ALU.is_lt)
                            return h.scalar_tensor_tensor(
                                wqv[wi][par][:, :], wraw[wi][par][:, :],
                                taus[:, tc:tc + 1], ltmp[:, :],
                                ALU.is_gt, ALU.subtract)
                        p.op("vector", _q, waits=qw, inc="s_ve")
                        qtok[wi][hc] = p.tok("s_ve")
                        p.op("gpsimd", lambda h, wi=wi, hc=hc, par=par:
                             h.dma_start(out=wdst[wi].ap()[hc, :, :],
                                         in_=wqv[wi][par][:, :]),
                             waits=[qtok[wi][hc]], inc="s_gp")
                        stok[wi][hc] = p.tok("s_gp")
                    else:
                        p.op("sync", lambda h, wi=wi, hc=hc, par=par:
                             h.dma_start(out=wqv[wi][par][:, :],
                                         in_=wdst[wi].ap()[hc, :, :]),
                             waits=fw, inc="s_ld")
                        qtok[wi][hc] = p.tok("s_ld")
                for wi in range(2):
                    mw = [qtok[wi][hc], tk["xln"]]
                    if hc >= 2:
                        mw.append(sigtok[hc - 2] if wi == 0
                                  else htok[hc - 2])
                    if hc < 2:
                        mw += list(entry_waits)
                    for kc in range(KCH):
                        for half in (0, 1):
                            lst = (kc == KCH - 1 and half == 1)
                            p.op("tensor", lambda h, wi=wi, par=par, kc=kc,
                                 half=half: h.matmul(
                                     xb[wi][par][:, half * 512:
                                                 (half + 1) * 512],
                                     wqv[wi][par][:, kc * 128:
                                                  (kc + 1) * 128],
                                     xln[:, kc * 1024 + half * 512:
                                         kc * 1024 + half * 512 + 512],
                                     start=(kc == 0), stop=(kc == KCH - 1)),
                                 waits=mw if (kc == 0 and half == 0) else [],
                                 inc="s_mm" if lst else None)
                    mmtok[wi][hc] = p.tok("s_mm")
                sw = [mmtok[0][hc]]
                if hc >= 1:
                    sw.append(htok[hc - 1])
                p.op("scalar", lambda h, par=par: h.activation(
                    sigb[:, :], xb[0][par][:, :], AF.Sigmoid),
                    waits=sw, inc="s_se")
                sigtok[hc] = p.tok("s_se")
                p.op("vector", lambda h, hc=hc, par=par: h.tensor_tensor(
                    hbuf[:, hc * 1024:(hc + 1) * 1024], sigb[:, :],
                    xb[1][par][:, :], ALU.mult),
                    waits=[sigtok[hc], mmtok[1][hc]], inc="s_ve")
                htok[hc] = p.tok("s_ve")
            return htok, sigtok, stok

        def emit_ffn_C(pre, htok, sigtok, resid, centry=()):
            if pre:
                w3raw = [V(STR + k * 5632, 1408, F32) for k in (0, 1, 2)]
                w3qs = [V(STR + 16896 + k * 2816, 1408, BF16)
                        for k in (0, 1, 2)]
                ltmp2 = V(STR + 25344, 1408, BF16)
                xres = [V(STR + 28288 + k * 4096, 1024, F32) for k in (0, 1)]
                outb = [V(STR + 36480 + k * 4096, 1024, F32) for k in (0, 1)]
            else:
                w3qs = [V(XLN + k * 2816, 1408, BF16) for k in (0, 1, 2)]
                outb = [V(XLN + 8448 + k * 4096, 1024, F32) for k in (0, 1)]
            yb = [psA[:, 0:1024], psA[:, 1024:2048]]
            NS = 4 * OCH
            ld_tok = [None] * NS
            qv_tok = [None] * NS
            sc_tok = [None] * NS
            mm_sub = [None] * NS
            eptok = [None] * OCH
            dmtok = [None] * OCH
            entry = [sigtok[HCH - 1], htok[HCH - 1]] + list(centry)
            entry = entry + [p.tok("s_mm")]
            p.op("gpsimd", lambda h: h.nop(),
                 waits=[sigtok[HCH - 1], htok[HCH - 1]], inc="s_gq")
            entry = entry + [p.tok("s_gq")]

            def prep(si):
                oc, j = si // 4, si % 4
                jp = si % 3
                fw = [mm_sub[si - 3]] if si >= 3 else list(entry)
                if pre:
                    lw = list(fw)
                    if si >= 3:
                        lw.append(qv_tok[si - 3])
                    else:
                        lw += list(entry)
                    p.op("sync", lambda h, oc=oc, j=j, jp=jp:
                         h.dma_start(
                             out=w3raw[jp][:, :].rearrange(
                                 "p (kc m) -> p kc m", kc=11),
                             in_=bass.AP(t_w3p,
                                         oc * HID * 128 + j * 1408 * 128,
                                         [[128, 128], [16384, 11],
                                          [1, 128]])),
                         waits=lw, inc="s_ld")
                    ldt = p.tok("s_ld")
                    qw = [ldt, tk["taus"]] + fw
                    if si >= 3:
                        qw.append(sc_tok[si - 3])

                    def _q3(h, jp=jp):
                        h.tensor_single_scalar(
                            ltmp2[:, :], w3raw[jp][:, :], taus[:, 5:6],
                            ALU.is_lt)
                        return h.scalar_tensor_tensor(
                            w3qs[jp][:, :], w3raw[jp][:, :],
                            taus[:, 2:3], ltmp2[:, :],
                            ALU.is_gt, ALU.subtract)
                    p.op("vector", _q3, waits=qw, inc="s_ve")
                    qv_tok[si] = p.tok("s_ve")
                    ld_tok[si] = qv_tok[si]
                    p.op("gpsimd", lambda h, oc=oc, j=j, jp=jp:
                         h.dma_start(
                             out=t_w3q.ap()[oc, :,
                                            j * 1408:(j + 1) * 1408],
                             in_=w3qs[jp][:, :]),
                         waits=[qv_tok[si]], inc="s_gp")
                    sc_tok[si] = p.tok("s_gp")
                else:
                    p.op("sync", lambda h, oc=oc, j=j, jp=jp:
                         h.dma_start(
                             out=w3qs[jp][:, :],
                             in_=t_w3q.ap()[oc, :,
                                            j * 1408:(j + 1) * 1408]),
                         waits=fw, inc="s_ld")
                    ld_tok[si] = p.tok("s_ld")

            prep(0)
            prep(1)
            prep(2)
            for oc in range(OCH):
                par = oc % 2
                for kc in range(HCH):
                    si = 4 * oc + kc // 11
                    jp = si % 3
                    kw = [ld_tok[si], htok[kc]]
                    if kc == 0:
                        kw.append(htok[0])
                        if oc >= 2:
                            kw.append(eptok[oc - 2])
                        else:
                            kw += list(entry)
                    for half in (0, 1):
                        sub_last = (kc % 11 == 10 and half == 1)
                        p.op("tensor", lambda h, par=par, kc=kc, jp=jp,
                             half=half: h.matmul(
                                 yb[par][:, half * 512:(half + 1) * 512],
                                 w3qs[jp][:, (kc % 11) * 128:
                                          (kc % 11) * 128 + 128],
                                 hbuf[:, kc * 1024 + half * 512:
                                      kc * 1024 + half * 512 + 512],
                                 start=(kc == 0), stop=(kc == HCH - 1)),
                             waits=kw if half == 0 else [],
                             inc="s_mm" if sub_last else None)
                        if sub_last:
                            mm_sub[si] = p.tok("s_mm")
                    if kc % 11 == 10 and si + 3 < NS:
                        prep(si + 3)
                ymm = mm_sub[4 * oc + 3]
                if resid == "xT":
                    rw = [eptok[oc - 2]] if oc >= 2 else \
                        [htok[HCH - 1]] + list(centry)
                    p.op("scalar", lambda h, oc=oc, par=par: h.dma_start(
                        out=xres[par][:, :],
                        in_=t_xT.ap()[oc * 128:(oc + 1) * 128, :]),
                        waits=rw, inc="s_sc")
                    rt = [p.tok("s_sc")]
                    rsl = lambda par=par: xres[par][:, :]
                else:
                    rt = []
                    rsl = lambda oc=oc: stream[:, oc * 1024:(oc + 1) * 1024]
                ew = [ymm] + rt
                if oc >= 2:
                    ew.append(dmtok[oc - 2])
                else:
                    ew += [htok[HCH - 1]] + list(centry)
                if DBG and not pre and oc == 0:
                    dbg3buf = V(TAIL, 1024, F32)
                    p.op("scalar", lambda h, par=par: h.copy(
                        dbg3buf[:, :], yb[par][:, :]), waits=[ymm],
                        inc="s_se")
                    t_d3 = p.tok("s_se")
                    p.op("gpsimd", lambda h: h.dma_start(
                        out=t_dbg3.ap()[0, :, :], in_=dbg3buf[:, :]),
                        waits=[t_d3], inc="s_gp")
                    ew.append(t_d3)
                p.op("vector", lambda h, par=par, rsl=rsl:
                     h.scalar_tensor_tensor(
                         outb[par][:, :], yb[par][:, :], 0.5, rsl(),
                         ALU.mult, ALU.add), waits=ew, inc="s_ve")
                eptok[oc] = p.tok("s_ve")
                if pre:
                    p.op("gpsimd", lambda h, oc=oc, par=par: h.dma_start(
                        out=t_xmid.ap()[oc, :, :], in_=outb[par][:, :]),
                        waits=[eptok[oc]], inc="s_gp")
                    dmtok[oc] = p.tok("s_gp")
                    if stop == "ffnpre":
                        p.op("gpsimd", lambda h, oc=oc, par=par: h.dma_start(
                            out=t_out.ap()[oc, :, :], in_=outb[par][:, :]),
                            waits=[eptok[oc]], inc="s_out")
                        dmtok[oc] = p.tok("s_out")
                else:
                    p.op("gpsimd", lambda h, oc=oc, par=par: h.dma_start(
                        out=t_out.ap()[oc, :, :], in_=outb[par][:, :]),
                        waits=[eptok[oc]], inc="s_out")
                    dmtok[oc] = p.tok("s_out")
                    if DBG and oc < 2:
                        p.op("gpsimd", lambda h, oc=oc, par=par: h.dma_start(
                            out=t_dbg2.ap()[oc, :, :], in_=outb[par][:, :]),
                            waits=[eptok[oc]], inc="s_gp")
            return eptok, dmtok

        htokB, sigtokB, stokB = emit_ffn_B(True, entry_waits=[gred[11]])
        eptokC, dmtokC = emit_ffn_C(
            True, htokB, sigtokB, resid="xT",
            centry=[stokB[0][HCH - 1], stokB[1][HCH - 1]])

        if stop == "ffnpre":
            if DBG:
                end = [p.tok("s_ve"), p.tok("s_mm"), p.tok("s_se"),
                       p.tok("s_gp"), p.tok("s_ld"), p.tok("s_sc")]
                dump = nc.dram_tensor("dbg", [8, 128, 1024], F32,
                                      kind="ExternalOutput")
                p.op("gpsimd", lambda h: h.dma_start(
                    out=dump.ap()[0, 0:1, 0:1024],
                    in_=V(PER, 1024, F32)[0:1, :]), waits=end, inc="s_out")
                p.op("gpsimd", lambda h: h.dma_start(
                    out=dump.ap()[1, :, :],
                    in_=xln[:, 0:1024]), waits=end, inc="s_out")
                p.op("gpsimd", lambda h: h.dma_start(
                    out=dump.ap()[2, :, :],
                    in_=hbuf[:, 0:1024]), waits=end, inc="s_out")
                p.op("gpsimd", lambda h: h.dma_start(
                    out=dump.ap()[3, :, :],
                    in_=t_w1q.ap()[0, :, 0:1024]), waits=end, inc="s_out")
                p.op("gpsimd", lambda h: h.dma_start(
                    out=dump.ap()[4, :, :],
                    in_=x0[:, 0:1024]), waits=end, inc="s_out")
                p.op("gpsimd", lambda h: h.dma_start(
                    out=dump.ap()[5, 0:1, :],
                    in_=lnscr[0:1, 4096:5120]), waits=end, inc="s_out")
                dmph = nc.dram_tensor("dbg_h", [44, 128, 1024], F32,
                                      kind="ExternalOutput")
                for hc in range(HCH):
                    p.op("gpsimd", lambda h, hc=hc: h.dma_start(
                        out=dmph.ap()[hc, :, :],
                        in_=hbuf[:, hc * 1024:(hc + 1) * 1024]),
                        waits=end, inc="s_out")
                dmpw = nc.dram_tensor("dbg_w3", [OCH, 128, HID], F32,
                                      kind="ExternalOutput")
                for oc in range(OCH):
                    p.op("gpsimd", lambda h, oc=oc: h.dma_start(
                        out=dmpw.ap()[oc, :, :],
                        in_=t_w3q.ap()[oc, :, :]),
                        waits=end, inc="s_out")
                dmpx = nc.dram_tensor("dbg_xln", [KCH, 128, 1024], F32,
                                      kind="ExternalOutput")
                for kc in range(KCH):
                    p.op("gpsimd", lambda h, kc=kc: h.dma_start(
                        out=dmpx.ap()[kc, :, :],
                        in_=xln[:, kc * 1024:(kc + 1) * 1024]),
                        waits=end, inc="s_out")
            p.op("gpsimd",
                 lambda h: h.wait_ge(sems["s_out"],
                                     16 * (OCH + (6 + HCH + OCH + KCH
                                                  if DBG else 0))))
            p.emit(sems)
            return nc

        p.op("sync", lambda h: h.dma_start(
            out=stream[:, :].rearrange("p (c t) -> p c t", c=KCH),
            in_=t_xmid.ap().rearrange("c p t -> p c t")),
            waits=[dmtokC[OCH - 1], eptokC[OCH - 1]], inc="s_ld")
        tk["stream"] = p.tok("s_ld")
        tk["xln"] = emit_ln(
            lambda kc: stream[:, kc * 1024:(kc + 1) * 1024],
            lambda kc: tk["stream"],
            extra_mm_waits=[eptokC[OCH - 1]])
        tk["xln2"] = tk["xln"]

        masks = V(HAR, 2048, BF16)
        kctr = V(HAR + 4096, 2048, BF16)
        vctr = V(HAR + 8192, 2048, BF16)
        cosf = V(HAR + 12288, 1024, F32)
        ssin = V(HAR + 16384, 1024, F32)
        posb = V(HAR + 20480, 1024, F32)
        tt1 = V(HAR + 24576, 1024, F32)
        kint = V(HAR + 28672, 1024, I32)
        ktf = V(HAR + 32768, 1024, F32)
        rtmp = V(HAR + 36864, 1024, F32)
        rtmp2 = V(HAR + 40960, 1024, F32)
        pm = V(HAR + 45056, 128, F32)
        pm2 = V(HAR + 45696, 128, F32)
        qrb = [V(HAR + 46336 + k * 2048, 1024, BF16) for k in (0, 1)]
        wkb = [V(HAR + 50432 + k * 4096, 2048, BF16) for k in (0, 1)]
        wvb = V(HAR + 58624, 4096, BF16)
        wqb = [V(HAR + 66816 + k * 4096, 2048, BF16) for k in (0, 1)]
        itile = V(HAR + 75008, 512, F32)
        jmi = V(HAR + 77056, 128, F32)
        Qmy = V(HAR + 4096, 16384, BF16)
        Kh = V(HAR + 36864, 8192, BF16)
        Vh = V(HAR + 53248, 8192, BF16)
        pt = [V(HAR + 69632 + k * 1024, 512, BF16) for k in (0, 1, 2)]
        rbuf = [V(HAR + 72704 + k * 2048, 512, F32) for k in (0, 1)]
        qsqs = [V(HAR + 76800 + k * 1024, 512, BF16) for k in (0, 1)]
        otile = [V(HAR + 78848 + k * 1024, 512, BF16) for k in (0, 1)]
        wob = wqb
        qraw = [V(TAIL + k * 4096, 1024, F32) for k in (0, 1)]
        rp = [V(TAIL + 8192 + k * 2048, 512, F32) for k in (0, 1)]

        hfree = [p.tok("s_mm")]
        p.op("gpsimd", lambda h: h.iota(
            itile[:, :], [[-1, 512]], channel_multiplier=1,
            allow_small_or_imprecise_dtypes=True), waits=hfree, inc="s_gq")
        tio = p.tok("s_gq")
        mtk = None
        for m in range(4):
            p.op("vector", lambda h, m=m: h.tensor_single_scalar(
                masks[:, m * 512:(m + 1) * 512], itile[:, :],
                float(-128 * m), ALU.is_le),
                waits=[tio, tk["xln2"]], inc="s_ve")
            mtk = p.tok("s_ve")
        tk["masks"] = mtk
        p.op("gpsimd", lambda h: h.iota(
            jmi[:, :], [[1, 128]], channel_multiplier=-1,
            allow_small_or_imprecise_dtypes=True), waits=hfree, inc="s_gq")
        tjm = p.tok("s_gq")

        tk["pm"] = p.seq("vector", [
            lambda h: h.tensor_single_scalar(pm[:, :], jmi[:, :], 64.0,
                                             ALU.is_equal),
            lambda h: h.tensor_single_scalar(pm2[:, :], jmi[:, :], -64.0,
                                             ALU.is_equal),
            lambda h: h.tensor_tensor(pm[:, :], pm[:, :], pm2[:, :],
                                      ALU.add),
        ], waits=[tjm])

        p.op("gpsimd", lambda h: h.iota(
            pv_t[:, :], [[1, 1]], channel_multiplier=1,
            allow_small_or_imprecise_dtypes=True), inc="s_gq")
        tpv = p.tok("s_gq")

        t_ = p.seq("vector", [
            lambda h: h.tensor_single_scalar(gtmp[:, :], pv_t[:, :], 64.0,
                                             ALU.is_ge),
            lambda h: h.scalar_tensor_tensor(gtmp[:, :], gtmp[:, :], -64.0,
                                             pv_t[:, :], ALU.mult, ALU.add),
            lambda h: h.tensor_single_scalar(gtmp[:, :], gtmp[:, :],
                                             -LOG1E4 / 64.0, ALU.mult),
        ], waits=[tpv])
        p.op("scalar", lambda h: h.activation(invf[:, :], gtmp[:, :],
                                              AF.Exp), waits=[t_],
             inc="s_se")
        tk["invf"] = p.tok("s_se")
        p.op("gpsimd", lambda h: h.dma_start(
            out=posb[:, :], in_=bass.AP(t_pos, 0, [[0, 128], [1, NT]])),
            waits=hfree, inc="s_gp")
        tk["posb"] = p.tok("s_gp")
        p.op("vector", lambda h: h.tensor_single_scalar(
            posb[:, :], posb[:, :], invf[:, 0:1], ALU.mult),
            waits=[tk["invf"], tk["posb"]], inc="s_ve")
        tang = p.tok("s_ve")

        def _red(dst, shift, tprev):
            fns = []
            if shift:
                fns.append(lambda h: h.tensor_scalar(
                    tt1[:, :], posb[:, :], INV_2PI, 0.25, ALU.mult, ALU.add))
            else:
                fns.append(lambda h: h.tensor_single_scalar(
                    tt1[:, :], posb[:, :], INV_2PI, ALU.mult))
            fns.append(lambda h: h.tensor_copy(kint[:, :], tt1[:, :]))
            fns.append(lambda h: h.tensor_copy(ktf[:, :], kint[:, :]))
            fns.append(lambda h: h.scalar_tensor_tensor(
                rtmp[:, :], ktf[:, :], -TWO_PI_HI, posb[:, :], ALU.mult,
                ALU.add))
            if shift:
                fns.append(lambda h: h.tensor_single_scalar(
                    rtmp[:, :], rtmp[:, :], HALF_PI, ALU.add))
            fns.append(lambda h: h.scalar_tensor_tensor(
                rtmp[:, :], ktf[:, :], -TWO_PI_LO, rtmp[:, :], ALU.mult,
                ALU.add))
            t = p.seq("vector", fns, hard=[tprev])
            p.op("scalar", lambda h, dst=dst: h.activation(
                dst[:, :], rtmp[:, :], AF.Sin), waits=[t], inc="s_se")
            return p.tok("s_se")

        tsin = _red(ssin, False, tang)
        p.op("vector", lambda h: h.tensor_single_scalar(
            ssin[0:64, :], ssin[0:64, :], -1.0, ALU.mult),
            waits=[tsin], inc="s_ve")
        tflip = p.tok("s_ve")
        tk["rope_tab"] = _red(cosf, True, tflip)

        for c in range(2):
            p.op("gpsimd", lambda h, c=c: h.dma_start(
                out=wkb[c][:, :].rearrange("p (kc m) -> p kc m", kc=KCH),
                in_=bass.AP(t_wkp, c * D * 128,
                            [[128, 128], [16384, KCH], [1, 128]])),
                waits=hfree, inc="s_gp")
        tk["wkb"] = p.tok("s_gp")
        p.op("gpsimd", lambda h: h.dma_start(
            out=wvb[:, :].rearrange("p (kc m) -> p kc m", kc=KCH),
            in_=t_wvt.ap().rearrange("(kc p) m -> p kc m", p=128)),
            waits=hfree, inc="s_gp")
        tk["wvb"] = p.tok("s_gp")

        kparts = []
        for c in range(2):
            kb = psA[:, c * 1024:(c + 1) * 1024]
            swb = psA[:, 2048 + c * 1024:3072 + c * 1024]
            mw = [tk["wkb"], tk["xln2"]]
            for kc in range(KCH):
                for half in (0, 1):
                    lst = (kc == KCH - 1 and half == 1)
                    p.op("tensor", lambda h, c=c, kc=kc, half=half, kb=kb:
                         h.matmul(kb[:, half * 512:(half + 1) * 512],
                                  wkb[c][:, kc * 128:(kc + 1) * 128],
                                  xln[:, kc * 1024 + half * 512:
                                      kc * 1024 + half * 512 + 512],
                                  start=(kc == 0), stop=(kc == KCH - 1)),
                         waits=mw if (kc == 0 and half == 0) else [],
                         inc="s_mm" if lst else None)
            tmm = p.tok("s_mm")
            p.op("scalar", lambda h, c=c, kb=kb: h.activation(
                qraw[c][:, :], kb[:, :], AF.Identity, bias=bk_s[:, c:c + 1]),
                waits=[tmm], inc="s_se")
            tev = p.tok("s_se")
            p.op("tensor", lambda h, swb=swb, c=c: h.matmul(
                swb[:, 0:512], pm[:, :], qraw[c][:, 0:512], start=True,
                stop=True), waits=[tev, tk["pm"]])
            p.op("tensor", lambda h, swb=swb, c=c: h.matmul(
                swb[:, 512:1024], pm[:, :], qraw[c][:, 512:1024],
                start=True, stop=True), inc="s_mm")
            tsw = p.tok("s_mm")

            def _rope(h, c=c, swb=swb):
                h.tensor_tensor(rtmp[:, :], qraw[c][:, :], cosf[:, :],
                                ALU.mult)
                h.tensor_tensor(rtmp2[:, :], swb[:, :], ssin[:, :],
                                ALU.mult)
                return h.tensor_tensor(kctr[:, c * 1024:(c + 1) * 1024],
                                       rtmp[:, :], rtmp2[:, :], ALU.add)
            p.op("vector", _rope, waits=[tsw, tk["rope_tab"]], inc="s_ve")
            kparts.append(p.tok("s_ve"))
        tk["kctr"] = kparts[-1]

        vev = [None] * 8
        for tt in range(8):
            par = tt % 2
            vb = psA[:, par * 512:par * 512 + 256]
            w = [tk["wvb"], tk["xln2"], tk["kctr"]]
            if tt >= 2:
                w.append(vev[tt - 2])
            for kc in range(KCH):
                p.op("tensor", lambda h, tt=tt, kc=kc, vb=vb: h.matmul(
                    vb[:, :],
                    xln[:, kc * 1024 + tt * 128:kc * 1024 + tt * 128 + 128],
                    wvb[:, kc * 256:(kc + 1) * 256],
                    start=(kc == 0), stop=(kc == KCH - 1)),
                    waits=w if kc == 0 else [],
                    inc="s_mm" if kc == KCH - 1 else None)
            tv = p.tok("s_mm")
            p.op("vector", lambda h, tt=tt, vb=vb: h.tensor_tensor(
                vctr[:, tt * 256:(tt + 1) * 256], vb[:, :], bvbc[:, :],
                ALU.add), waits=[tv, tk["bvbc"]], inc="s_ve")
            vev[tt] = p.tok("s_ve")
        tk["vctr"] = vev[-1]

        p.op("gpsimd", lambda h: h.dma_start(
            out=bass.AP(t_kvin, 0, [[1024, 128], [131072, 2], [1, 1024]]),
            in_=kctr[:, :].rearrange("p (c t) -> p c t", c=2)),
            waits=[tk["kctr"]], inc="s_gp")
        p.op("gpsimd", lambda h: h.dma_start(
            out=bass.AP(t_kvin, 262144, [[256, 128], [32768, 8], [1, 256]]),
            in_=vctr[:, :].rearrange("p (tt m) -> p tt m", tt=8)),
            waits=[tk["vctr"]], inc="s_gp")
        t_ = p.tok("s_gp")
        p.op("gpsimd", lambda h: h.collective_compute(
            "AllGather", ALU.bypass, replica_groups=RG,
            ins=[t_kvin.ap().opt()], outs=[t_kvout.ap().opt()]),
            waits=[t_], inc="s_cc", incn=1)
        tk["cc_kv"] = p.tok("s_cc")

        qdma = [None] * 16
        qropet = [None] * 16
        for c in range(16):
            par = c % 2
            qb = psA[:, par * 1024:(par + 1) * 1024]
            swb = psA[:, 2048 + par * 1024:3072 + par * 1024]
            lw = [tk["xln2"], tk["vctr"], hfree[0]]
            if c >= 2:
                lw.append(qropet[c - 2])
            p.op("gpsimd", lambda h, c=c, par=par: h.dma_start(
                out=wqb[par][:, :].rearrange("p (kc m) -> p kc m", kc=KCH),
                in_=bass.AP(t_wqp, c * D * 128,
                            [[128, 128], [16384, KCH], [1, 128]])),
                waits=lw, inc="s_gp")
            tw = p.tok("s_gp")
            mw = [tw, tk["vctr"]]
            if c >= 2:
                mw.append(qropet[c - 2])
            for kc in range(KCH):
                for half in (0, 1):
                    lst = (kc == KCH - 1 and half == 1)
                    p.op("tensor", lambda h, par=par, kc=kc, half=half,
                         qb=qb: h.matmul(
                             qb[:, half * 512:(half + 1) * 512],
                             wqb[par][:, kc * 128:(kc + 1) * 128],
                             xln[:, kc * 1024 + half * 512:
                                 kc * 1024 + half * 512 + 512],
                             start=(kc == 0), stop=(kc == KCH - 1)),
                         waits=mw if (kc == 0 and half == 0) else [],
                         inc="s_mm" if lst else None)
            tmm = p.tok("s_mm")
            sw2 = [tmm]
            if c >= 2:
                sw2.append(qropet[c - 2])
            p.op("scalar", lambda h, c=c, par=par, qb=qb: h.activation(
                qraw[par][:, :], qb[:, :], AF.Identity, bias=bq_s[:, c:c + 1]),
                waits=sw2, inc="s_se")
            tev = p.tok("s_se")
            p.op("tensor", lambda h, swb=swb, par=par: h.matmul(
                swb[:, 0:512], pm[:, :], qraw[par][:, 0:512], start=True,
                stop=True), waits=[tev, tk["pm"]])
            p.op("tensor", lambda h, swb=swb, par=par: h.matmul(
                swb[:, 512:1024], pm[:, :], qraw[par][:, 512:1024],
                start=True, stop=True), inc="s_mm")
            tsw = p.tok("s_mm")
            qw = [tsw, tk["rope_tab"]]
            if c >= 2:
                qw.append(qdma[c - 2])

            def _ropeq(h, par=par, swb=swb):
                h.tensor_tensor(rtmp[:, :], qraw[par][:, :], cosf[:, :],
                                ALU.mult)
                h.tensor_tensor(rtmp2[:, :], swb[:, :], ssin[:, :],
                                ALU.mult)
                return h.tensor_tensor(qrb[par][:, :], rtmp[:, :],
                                       rtmp2[:, :], ALU.add)
            p.op("vector", _ropeq, waits=qw, inc="s_ve")
            qropet[c] = p.tok("s_ve")
            p.op("gpsimd", lambda h, c=c, par=par: h.dma_start(
                out=t_qin.ap()[c * 128:(c + 1) * 128, :],
                in_=qrb[par][:, :]), waits=[qropet[c]], inc="s_gp")
            qdma[c] = p.tok("s_gp")
        p.op("gpsimd", lambda h: h.collective_compute(
            "AllGather", ALU.bypass, replica_groups=RG,
            ins=[t_qin.ap().opt()], outs=[t_qout.ap().opt()]),
            waits=[qdma[15]], inc="s_cc", incn=1)
        tk["cc_q"] = p.tok("s_cc")

        safe = [tk["cc_kv"], qdma[15], qropet[15], p.tok("s_mm"),
                p.tok("s_se"), tk["sel"]]
        regs = {}

        def _store_regs(h):
            regs["kvk"] = h.value_load(sbsel[0:1, 0:1])
            regs["kvv"] = h.value_load(sbsel[0:1, 1:2])
            regs["qof"] = h.value_load(sbsel[0:1, 2:3])
            regs["colA"] = h.value_load(sbsel[0:1, 3:4])
            regs["colB"] = h.value_load(sbsel[0:1, 4:5])
            return h.nop()
        p.op("sync", _store_regs, waits=safe)
        p.op("sync", lambda h: h.dma_start(
            out=Kh[:, :].rearrange("p (r t) -> p r t", r=8),
            in_=bass.AP(t_kvout, regs["kvk"],
                        [[1024, 128], [524288, 8], [1, 1024]])),
            waits=safe, inc="s_ld")
        for r in range(8):
            p.op("sync", lambda h, r=r: h.dma_start(
                out=Vh[:, r * 1024:(r + 1) * 1024].rearrange(
                    "p (j m) -> p j m", j=8),
                in_=bass.AP(t_kvout, regs["kvv"] + (262144 + r * 524288),
                            [[256, 128], [32768, 8], [1, 128]])),
                waits=safe, inc="s_ld")
        tk["kvh"] = p.tok("s_ld")
        for hh in range(2):
            p.op("sync", lambda h, hh=hh: h.dma_start(
                out=Qmy[:, hh * 8192:(hh + 1) * 8192].rearrange(
                    "p (r t) -> p r t", r=8),
                in_=bass.AP(t_qout, regs["qof"] + hh * 131072,
                            [[1024, 128], [2097152, 8], [1, 1024]])),
                waits=[tk["cc_q"]] + safe, inc="s_ld")
        tk["qmy"] = p.tok("s_ld")

        def keypos(kt):
            b = kt // 4
            r, s = _rank_slot(b)
            return r * 1024 + s * 512 + (kt % 4) * 128

        def qpos(h_i, b):
            r, s = _rank_slot(b)
            return h_i * 8192 + r * 1024 + s * 512

        red_t = [None] * 48
        idx = 0
        for src, nsl, base in ((Qmy, 32, 64), (Kh, 16, 96)):
            for j in range(nsl):
                par = idx % 2
                w = [tk["qmy"], tk["kvh"], tk["masks"]]
                if idx >= 2:
                    w.append(red_t[idx - 2])
                p.op("scalar", lambda h, src=src, j=j, par=par:
                     h.activation(qsqs[par][:, :],
                                  src[:, j * 512:(j + 1) * 512], AF.Square),
                     waits=w, inc="s_se")
                ts_ = p.tok("s_se")
                rw = [ts_, tk["consts"]]
                if idx >= 2:
                    rw.append(red_t[idx - 2])
                p.op("tensor", lambda h, par=par: h.matmul(
                    psA[0:1, par * 512:(par + 1) * 512], ones_b[:, :],
                    qsqs[par][:, :], start=True, stop=True),
                    waits=rw, inc="s_mm")
                tm = p.tok("s_mm")
                p.op("vector", lambda h, base=base, j=j, par=par:
                     h.reduce_max(prow(base + j, base + j + 1),
                                  psA[0:1, par * 512:(par + 1) * 512],
                                  axis=AX), waits=[tm], inc="s_ve")
                red_t[idx] = p.tok("s_ve")
                idx += 1

        t_ = p.seq("vector", [
            lambda h: h.reduce_max(prow(112, 113), prow(64, 80), axis=AX),
            lambda h: h.reduce_max(prow(113, 114), prow(80, 96), axis=AX),
            lambda h: h.reduce_max(prow(114, 115), prow(96, 112), axis=AX),
            lambda h: h.tensor_tensor(prow(115, 116), prow(112, 113),
                                      prow(114, 115), ALU.mult),
            lambda h: h.tensor_tensor(prow(116, 117), prow(113, 114),
                                      prow(114, 115), ALU.mult),
        ], hard=[red_t[47]])
        p.op("scalar", lambda h: h.activation(prow(117, 119),
                                              prow(115, 117), AF.Sqrt),
             waits=[t_], inc="s_se")
        t_ = p.tok("s_se")
        p.op("vector", lambda h: h.tensor_single_scalar(
            prow(119, 121), prow(117, 119), -CSC, ALU.mult),
            waits=[t_], inc="s_ve")
        t_ = p.tok("s_ve")
        p.op("gpsimd", lambda h: h.dma_start(out=t_bshd.ap(),
                                             in_=prow(119, 121)),
             waits=[t_], inc="s_gp")
        t_ = p.tok("s_gp")
        p.op("gpsimd", lambda h: h.dma_start(
            out=bshift[:, :], in_=bass.AP(t_bshd, 0, [[0, 128], [1, 2]])),
            waits=[t_], inc="s_gp")
        tk["bshift"] = p.tok("s_gp")

        ps_s = [psA[:, 0:512], psA[:, 512:1024], psA[:, 1024:1536]]
        ps_o = [psA[:, 1536:2048], psA[:, 2048:2560]]
        ps_p = [psA[0:1, 2560:3072], psA[0:1, 3072:3584]]
        flat = []
        tmeta = []
        for h_i in range(2):
            for b in range(NB):
                ti = h_i * NB + b
                nkt = 4 * (b + 1)
                tmeta.append((h_i, b, nkt))
                for kt in range(nkt):
                    flat.append((ti, h_i, b, kt, nkt))
        NG = len(flat)
        scr_tok = [None] * NG
        exp_done = [None] * NG
        pt_ready = [None] * NG
        av_done = [None] * NG
        rp_done = [None] * 32
        ot_done = [None] * 32
        odma_done = [None] * 32
        d1_done = [None] * 32
        d2_done = [None] * 32

        def emit_S(g):
            ti, h_i, b, kt, nkt = flat[g]
            sp = g % 3
            kp = keypos(kt)
            qoff = qpos(h_i, b)
            w = [tk["qmy"], tk["kvh"], tk["bshift"], tk["masks"]]
            if g >= 3:
                w.append(exp_done[g - 3])
            p.op("tensor", lambda h, kp=kp, sp=sp, qoff=qoff:
                 h.matmul(ps_s[sp][:, :], Kh[:, kp:kp + 128],
                          Qmy[:, qoff:qoff + 512], start=True, stop=True),
                 waits=w, inc="s_mm")
            scr_tok[g] = p.tok("s_mm")

        emit_S(0)
        if NG > 1:
            emit_S(1)
        if NG > 2:
            emit_S(2)
        for g in range(NG):
            ti, h_i, b, kt, nkt = flat[g]
            sp = g % 3
            ob = ti % 2
            ew = [scr_tok[g]]
            if g >= 3:
                ew.append(av_done[g - 3])
            p.op("scalar", lambda h, sp=sp, h_i=h_i: h.activation(
                pt[sp][:, :], ps_s[sp][:, :], AF.Exp,
                bias=bshift[:, h_i:h_i + 1], scale=CSC),
                waits=ew, inc="s_se")
            exp_done[g] = p.tok("s_se")
            ptok = exp_done[g]
            if kt >= 4 * b:
                m = kt - 4 * b
                p.op("vector", lambda h, sp=sp, m=m: h.tensor_tensor(
                    pt[sp][:, :], pt[sp][:, :],
                    masks[:, m * 512:(m + 1) * 512], ALU.mult),
                    waits=[ptok], inc="s_ve")
                ptok = p.tok("s_ve")
            pt_ready[g] = ptok
            if g + 3 < NG:
                emit_S(g + 3)
            avw = [ptok]
            if kt == 0 and ti >= 2:
                avw.append(ot_done[ti - 2])
                avw.append(rp_done[ti - 2])
            kp = keypos(kt)
            p.op("tensor", lambda h, kp=kp, sp=sp, ob=ob, kt=kt, nkt=nkt:
                 h.matmul(ps_o[ob][:, :], Vh[:, kp:kp + 128],
                          pt[sp][:, :], start=(kt == 0),
                          stop=(kt == nkt - 1)), waits=avw)
            p.op("tensor", lambda h, sp=sp, ob=ob, kt=kt, nkt=nkt:
                 h.matmul(ps_p[ob][:, :], ones_b[:, :], pt[sp][:, :],
                          start=(kt == 0), stop=(kt == nkt - 1)),
                 inc="s_mm")
            av_done[g] = p.tok("s_mm")
            if kt == nkt - 1:
                tp_ = ti % 2
                rw = [av_done[g]]
                if ti >= 2:
                    rw.append(d1_done[ti - 2])
                p.op("vector", lambda h, ob=ob, tp_=tp_: h.reciprocal(
                    rp[tp_][0:1, :], ps_p[ob][:, :]), waits=rw, inc="s_ve")
                rp_done[ti] = p.tok("s_ve")
                p.op("gpsimd", lambda h, tp_=tp_: h.dma_start(
                    out=t_rpd.ap()[tp_:tp_ + 1, :], in_=rp[tp_][0:1, :]),
                    waits=[rp_done[ti]], inc="s_gp")
                d1_done[ti] = p.tok("s_gp")
                d2w = [d1_done[ti]]
                if ti >= 2:
                    d2w.append(ot_done[ti - 2])
                p.op("gpsimd", lambda h, tp_=tp_: h.dma_start(
                    out=rbuf[tp_][:, :],
                    in_=bass.AP(t_rpd, tp_ * 512, [[0, 128], [1, 512]])),
                    waits=d2w, inc="s_gp")
                d2_done[ti] = p.tok("s_gp")
                ow = [d2_done[ti], av_done[g]]
                if ti >= 2:
                    ow.append(odma_done[ti - 2])
                p.op("vector", lambda h, ob=ob, tp_=tp_: h.tensor_tensor(
                    otile[tp_][:, :], ps_o[ob][:, :], rbuf[tp_][:, :],
                    ALU.mult), waits=ow, inc="s_ve")
                ot_done[ti] = p.tok("s_ve")
                p.op("gpsimd", lambda h, b=b, h_i=h_i, tp_=tp_:
                     h.dma_start(
                         out=bass.AP(t_oin, h_i * 1048576 + b * 512,
                                     [[8192, 128], [1, 512]]),
                         in_=otile[tp_][:, :]), waits=[ot_done[ti]],
                     inc="s_gp")
                odma_done[ti] = p.tok("s_gp")
        p.op("gpsimd", lambda h: h.collective_compute(
            "AllGather", ALU.bypass, replica_groups=RG,
            ins=[t_oin.ap().opt()], outs=[t_oout.ap().opt()]),
            waits=[odma_done[-1]], inc="s_cc", incn=1)
        tk["cc_o"] = p.tok("s_cc")

        otv = xln
        ow_ = [tk["cc_o"], p.tok("s_mm"), p.tok("s_ve")]
        p.op("sync", lambda h: h.dma_start(
            out=otv[:, :].rearrange("p (c t) -> p c t", c=KCH)[:, :, 0:512],
            in_=bass.AP(t_oout, regs["colA"],
                        [[8192, 128], [1048576, KCH], [1, 512]])),
            waits=ow_, inc="s_ld")
        p.op("sync", lambda h: h.dma_start(
            out=otv[:, :].rearrange("p (c t) -> p c t", c=KCH)[:, :, 512:1024],
            in_=bass.AP(t_oout, regs["colB"],
                        [[8192, 128], [1048576, KCH], [1, 512]])),
            waits=ow_, inc="s_ld")
        tk["otv"] = p.tok("s_ld")
        res2 = [None] * OCH
        for oc in range(OCH):
            par = oc % 2
            w = [tk["cc_o"]]
            if oc >= 2:
                w.append(res2[oc - 2])
            p.op("gpsimd", lambda h, oc=oc, par=par: h.dma_start(
                out=wob[par][:, :].rearrange("p (kc m) -> p kc m", kc=KCH),
                in_=bass.AP(t_wop, oc * D * 128,
                            [[128, 128], [16384, KCH], [1, 128]])),
                waits=w, inc="s_gp")
            tw = p.tok("s_gp")
            yb = psA[:, par * 1024:(par + 1) * 1024]
            mw = [tw, tk["otv"]]
            if oc >= 2:
                mw.append(res2[oc - 2])
            for kc in range(KCH):
                for half in (0, 1):
                    lst = (kc == KCH - 1 and half == 1)
                    p.op("tensor", lambda h, par=par, kc=kc, half=half,
                         yb=yb: h.matmul(
                             yb[:, half * 512:(half + 1) * 512],
                             wob[par][:, kc * 128:(kc + 1) * 128],
                             otv[:, kc * 1024 + half * 512:
                                 kc * 1024 + half * 512 + 512],
                             start=(kc == 0), stop=(kc == KCH - 1)),
                         waits=mw if (kc == 0 and half == 0) else [],
                         inc="s_mm" if lst else None)
            tmm = p.tok("s_mm")
            p.op("vector", lambda h, oc=oc, yb=yb: h.tensor_tensor(
                stream[:, oc * 1024:(oc + 1) * 1024],
                stream[:, oc * 1024:(oc + 1) * 1024], yb[:, :], ALU.add),
                waits=[tmm, tk["stream"]], inc="s_ve")
            res2[oc] = p.tok("s_ve")

        if stop == "attn":
            for oc in range(OCH):
                p.op("gpsimd", lambda h, oc=oc: h.dma_start(
                    out=t_out.ap()[oc, :, :],
                    in_=stream[:, oc * 1024:(oc + 1) * 1024]),
                    waits=[res2[oc]], inc="s_out")
            p.op("gpsimd",
                 lambda h: h.wait_ge(sems["s_out"], 16 * OCH))
            p.emit(sems)
            return nc

        tk["xln"] = emit_ln(
            lambda kc: stream[:, kc * 1024:(kc + 1) * 1024],
            lambda kc: res2[kc], extra_mm_waits=[res2[OCH - 1]])
        pbar = [p.tok("s_mm"), p.tok("s_gp"), p.tok("s_ld"),
                p.tok("s_se"), tk["xln"]]
        htok4, sigtok4, _ = emit_ffn_B(False, entry_waits=pbar)
        emit_ffn_C(False, htok4, sigtok4, resid="stream")

        if DBG:
            end = [p.tok("s_ve"), p.tok("s_mm"), p.tok("s_se"),
                   p.tok("s_gp"), p.tok("s_ld"), p.tok("s_sc"),
                   p.tok("s_out")]
            dmph = nc.dram_tensor("dbg_h", [44, 128, 1024], F32,
                                  kind="ExternalOutput")
            for hc in range(HCH):
                p.op("gpsimd", lambda h, hc=hc: h.dma_start(
                    out=dmph.ap()[hc, :, :],
                    in_=hbuf[:, hc * 1024:(hc + 1) * 1024]),
                    waits=end, inc="s_out")
            dmpw = nc.dram_tensor("dbg_w3", [1, 128, HID], F32,
                                  kind="ExternalOutput")
            p.op("gpsimd", lambda h: h.dma_start(
                out=dmpw.ap()[0, :, :], in_=t_w3q.ap()[0, :, :]),
                waits=end, inc="s_out")
            dmps = nc.dram_tensor("dbg_stream", [KCH, 128, 1024], F32,
                                  kind="ExternalOutput")
            for kc in range(KCH):
                p.op("gpsimd", lambda h, kc=kc: h.dma_start(
                    out=dmps.ap()[kc, :, :],
                    in_=stream[:, kc * 1024:(kc + 1) * 1024]),
                    waits=end, inc="s_out")
            dmpx = nc.dram_tensor("dbg_xln", [KCH, 128, 1024], F32,
                                  kind="ExternalOutput")
            for kc in range(KCH):
                p.op("gpsimd", lambda h, kc=kc: h.dma_start(
                    out=dmpx.ap()[kc, :, :],
                    in_=xln[:, kc * 1024:(kc + 1) * 1024]),
                    waits=end, inc="s_out")
            p.op("gpsimd", lambda h: h.wait_ge(
                sems["s_out"], 16 * (OCH + HCH + 1 + KCH + KCH)))
        else:
            p.op("gpsimd", lambda h: h.wait_ge(sems["s_out"], 16 * OCH))
        p.emit(sems)
    return nc



def _prep_inputs(L0_attn_x, L0_attn_wq, L0_attn_wk, L0_attn_wv, L0_attn_wo,
                 L0_attn_bq, L0_attn_bk, L0_attn_bv, L0_ffn_w1, L0_ffn_w2,
                 L0_ffn_w3):
    f = np.float32
    x = np.asarray(L0_attn_x, f)[0]
    xT = np.ascontiguousarray(x.T)
    w1 = np.asarray(L0_ffn_w1, f)
    w2 = np.asarray(L0_ffn_w2, f)
    w3 = np.asarray(L0_ffn_w3, f)

    def panels(w, n):
        wT = np.ascontiguousarray(np.asarray(w, f).T)
        return np.ascontiguousarray(
            wT.reshape(wT.shape[0], n, 128).transpose(1, 0, 2))

    shared = {
        "w1p": panels(w1, HCH),
        "w2p": panels(w2, HCH),
        "w3p": panels(w3, OCH),
        "wqp": panels(L0_attn_wq, 16),
        "wkp": panels(L0_attn_wk, 2),
        "wvt": np.ascontiguousarray(np.asarray(L0_attn_wv, f).T),
        "wop": panels(L0_attn_wo, 16),
        "bqp": np.ascontiguousarray(
            np.asarray(L0_attn_bq, f).reshape(16, 128).T),
        "bkp": np.ascontiguousarray(
            np.asarray(L0_attn_bk, f).reshape(2, 128).T),
        "bvr": np.ascontiguousarray(np.asarray(L0_attn_bv, f).reshape(1, 256)),
    }
    gsz = D * HID // NCORES
    in_maps = []
    for i in range(NCORES):
        bA, bB = i, 15 - i
        cols = np.r_[bA * BS:(bA + 1) * BS, bB * BS:(bB + 1) * BS]
        m = dict(shared)
        m["xT"] = np.ascontiguousarray(xT[:, cols])
        m["pos"] = np.ascontiguousarray(cols.astype(f).reshape(1, NT))
        kv = i // 4
        m["selv"] = np.array([[kv * 131072, kv * 128, i * 262144,
                               bA * BS, bB * BS, 0, 0, 0]], np.int32)
        for name, w in (("g1", w1), ("g2", w2), ("g3", w3)):
            m[name] = np.ascontiguousarray(
                w.reshape(-1)[i * gsz:(i + 1) * gsz].reshape(128, GSL))
        in_maps.append(m)
    return in_maps


def _assemble(results):
    fullT = np.empty((D, L), np.float32)
    for i, r in enumerate(results):
        yT = np.asarray(r["yT"]).reshape(D, NT)
        bA, bB = i, 15 - i
        fullT[:, bA * BS:(bA + 1) * BS] = yT[:, 0:BS]
        fullT[:, bB * BS:(bB + 1) * BS] = yT[:, BS:NT]
    return np.ascontiguousarray(fullT.T).reshape(1, L, D)


def kernel(**inputs):
    in_maps = _prep_inputs(**inputs)
    nc = build_program("full")
    res = run_bass_kernel_spmd(nc, in_maps, core_ids=list(range(NCORES)))
    return _assemble(res.results)


# version 04
# speedup vs baseline: 1.4942x; 1.0136x over previous
import contextlib
import numpy as np
import concourse.bass as bass
import concourse.mybir as mybir
from concourse.bass_utils import run_bass_kernel_spmd

F32 = mybir.dt.float32
BF16 = mybir.dt.bfloat16
I32 = mybir.dt.int32
U8 = mybir.dt.uint8
AF = mybir.ActivationFunctionType
ALU = mybir.AluOpType
AX = mybir.AxisListType.X

D = 2048
HID = 5632
L = 8192
NCORES = 8
NT = 1024
NB = 16
BS = 512
KCH = 16
HCH = 44
OCH = 16
CSC = float(1.0 / np.sqrt(128.0))
NW = float(D * HID)
LN_EPS = 1e-5
Q_EPS = 1e-8
_2PI = np.float64(2 * np.pi)
TWO_PI_HI = float(np.float32(_2PI))
TWO_PI_LO = float(np.float32(_2PI - np.float64(np.float32(_2PI))))
INV_2PI = float(np.float32(1.0 / _2PI))
HALF_PI = float(np.float32(np.pi / 2))
LOG1E4 = float(np.log(10000.0))
GSL = D * HID // NCORES // 128
DBG = False

ENGS = ("sync", "scalar", "vector", "tensor", "gpsimd")
DMA_SEMS = {"s_ld", "s_sc", "s_gp", "s_out"}
OWN = {"tensor": "s_mm", "vector": "s_ve", "scalar": "s_se", "gpsimd": "s_gq"}

XLN = 0
HAR = 32768
STR = 122880
TAIL = 188416
PER = 206848
ARENA_B = 210944

DTS = {F32: 4, BF16: 2, I32: 4}


def _rank_slot(b):
    return (b, 0) if b <= 7 else (15 - b, 1)


class P:
    def __init__(self, nc):
        self.nc = nc
        self.q = {e: [] for e in ENGS}
        self.cnt = {}

    def tok(self, sem):
        return (sem, self.cnt.get(sem, 0))

    def op(self, eng, fn, waits=(), inc=None, incn=None, hard=()):
        incs = []
        if inc is not None:
            n = incn if incn is not None else (16 if inc in DMA_SEMS else 1)
            self.cnt[inc] = self.cnt.get(inc, 0) + n
            incs.append((inc, n))
        ws = {}
        hw = {}
        for w in waits:
            if w is None:
                continue
            s, v = w
            if v > 0 and v > ws.get(s, 0):
                ws[s] = v
        for w in hard:
            if w is None:
                continue
            s, v = w
            if v > 0 and v > hw.get(s, 0):
                hw[s] = v
        self.q[eng].append((sorted(ws.items()), sorted(hw.items()), fn, incs))
        return self.tok(inc) if inc is not None else None

    def seq(self, eng, fns, waits=(), hard=()):
        own = OWN[eng]
        t = None
        for i, fn in enumerate(fns):
            self.op(eng, fn, waits=waits if i == 0 else (),
                    hard=(list(hard) + ([t] if t else [])) if i == 0
                    else [t], inc=own)
            t = self.tok(own)
        return t

    def emit(self, sems):
        guar = {e: {} for e in ENGS}
        with self.nc.Block() as blk:
            for eng in ENGS:
                def body(h, eng=eng):
                    g = guar[eng]
                    own = OWN.get(eng)
                    for ws, hws, fn, incs in self.q[eng]:
                        for s, v in ws:
                            if s == own:
                                continue
                            if v > g.get(s, 0):
                                h.wait_ge(sems[s], v)
                                g[s] = v
                        for s, v in hws:
                            if v > g.get(s, 0):
                                h.wait_ge(sems[s], v)
                                g[s] = v
                        inst = fn(h)
                        for s, n in incs:
                            inst = inst.then_inc(sems[s], n)
                getattr(blk, eng)(body)


def build_program(stop="full"):
    nc = bass.Bass("TRN2", target_bir_lowering=False)

    t_xT = nc.dram_tensor("xT", [D, NT], F32, kind="ExternalInput")
    t_pos = nc.dram_tensor("pos", [1, NT], F32, kind="ExternalInput")
    t_g = [nc.dram_tensor("g%d" % (j + 1), [128, GSL], F32,
                          kind="ExternalInput") for j in range(3)]
    t_w1p = nc.dram_tensor("w1p", [HCH, D, 128], F32, kind="ExternalInput")
    t_w2p = nc.dram_tensor("w2p", [HCH, D, 128], F32, kind="ExternalInput")
    t_w3p = nc.dram_tensor("w3p", [OCH, HID, 128], F32, kind="ExternalInput")
    t_wqp = nc.dram_tensor("wqp", [16, D, 128], F32, kind="ExternalInput")
    t_wkp = nc.dram_tensor("wkp", [2, D, 128], F32, kind="ExternalInput")
    t_wvt = nc.dram_tensor("wvt", [D, 256], F32, kind="ExternalInput")
    t_wop = nc.dram_tensor("wop", [16, D, 128], F32, kind="ExternalInput")
    t_bq = nc.dram_tensor("bqp", [128, 16], F32, kind="ExternalInput")
    t_bk = nc.dram_tensor("bkp", [128, 2], F32, kind="ExternalInput")
    t_bv = nc.dram_tensor("bvr", [1, 256], F32, kind="ExternalInput")
    t_sel = nc.dram_tensor("selv", [1, 8], I32, kind="ExternalInput")
    t_out = nc.dram_tensor("yT", [OCH, 128, NT], F32, kind="ExternalOutput")

    t_w1q = nc.dram_tensor("w1q_d", [HCH, 128, D], BF16)
    t_w2q = nc.dram_tensor("w2q_d", [HCH, 128, D], BF16)
    t_w3q = nc.dram_tensor("w3q_d", [OCH, 128, HID], BF16)
    t_xmid = nc.dram_tensor("xmid_d", [KCH, 128, NT], F32)
    t_arin = nc.dram_tensor("ar_in", [1, 8], F32)
    t_arout = nc.dram_tensor("ar_out", [1, 8], F32)
    t_taud = nc.dram_tensor("tau_d", [1, 8], F32)
    t_bshd = nc.dram_tensor("bsh_d", [1, 2], F32)
    t_rpd = nc.dram_tensor("rp_d", [2, 512], F32)
    t_kvin = nc.dram_tensor("kvag_in", [1, 524288], BF16)
    t_kvout = nc.dram_tensor("kvag_out", [8, 524288], BF16, addr_space="Shared")
    t_qin = nc.dram_tensor("qag_in", [D, NT], BF16)
    t_qout = nc.dram_tensor("qag_out", [8 * D, NT], BF16, addr_space="Shared")
    t_oin = nc.dram_tensor("oag_in", [2, 128, L], BF16)
    t_dbg2 = nc.dram_tensor("dbg2", [2, 128, NT], F32, kind="ExternalOutput")
    t_dbg3 = nc.dram_tensor("dbg3", [1, 128, NT], F32, kind="ExternalOutput")
    t_oout = nc.dram_tensor("oag_out", [16, 128, L], BF16, addr_space="Shared")

    RG = [list(range(NCORES))]

    es = contextlib.ExitStack()
    with es:
        arena = es.enter_context(nc.sbuf_tensor("arena", [128, ARENA_B], U8))
        psA = es.enter_context(nc.psum_tensor("psA", [128, 4096], F32))
        sems = {}
        for s in ("s_ld", "s_sc", "s_gp", "s_out", "s_cc",
                  "s_mm", "s_ve", "s_se", "s_gq"):
            sems[s] = es.enter_context(nc.semaphore(s))

        def V(off, n, dt):
            return arena[:, off:off + n * DTS[dt]].bitcast(dt)

        ones_f = V(PER + 0, 1, F32)
        ones_b = V(PER + 64, 1, BF16)
        onesrow_f = V(PER + 128, 128, F32)[0:1, :]
        taus = V(PER + 704, 8, F32)
        bq_s = V(PER + 768, 16, F32)
        bk_s = V(PER + 832, 2, F32)
        bshift = V(PER + 896, 2, F32)
        invf = V(PER + 960, 1, F32)
        pv_t = V(PER + 1024, 1, F32)
        gtmp = V(PER + 1088, 1, F32)
        bvbc = V(PER + 1152, 256, BF16)
        sbsel = V(PER + 1600, 8, I32)
        prow_full = V(PER + 1664, 256, F32)

        def prow(a, b):
            return prow_full[0:1, a:b]

        xln = V(XLN, 16384, BF16)
        lnscr = V(XLN, 8192, F32)
        stream = V(STR, 16384, F32)
        hbuf = V(HAR, 45056, BF16)

        p = P(nc)
        tk = {}

        x0 = V(HAR, 16384, F32)
        gbuf = [V(STR + k * 11264, 2816, F32) for k in (0, 1)]
        xsq = [V(TAIL + k * 4096, 1024, F32) for k in (0, 1)]
        racc = V(TAIL + 8192, 16, F32)

        p.op("gpsimd", lambda h: h.memset(ones_f[:, :], 1.0), inc="s_gq")
        p.op("gpsimd", lambda h: h.memset(ones_b[:, :], 1.0), inc="s_gq")
        p.op("gpsimd", lambda h: h.memset(onesrow_f, 1.0), inc="s_gq")
        tk["consts"] = p.tok("s_gq")

        p.op("sync", lambda h: h.dma_start(
            out=x0[:, :].rearrange("p (c t) -> p c t", c=KCH),
            in_=t_xT.ap().rearrange("(c p) t -> p c t", p=128)), inc="s_ld")
        tk["x0"] = p.tok("s_ld")
        p.op("sync", lambda h: h.dma_start(out=bq_s[:, :], in_=t_bq.ap()),
             inc="s_ld")
        p.op("sync", lambda h: h.dma_start(out=sbsel[0:1, :], in_=t_sel.ap()),
             inc="s_ld")
        tk["sel"] = p.tok("s_ld")
        p.op("sync", lambda h: h.dma_start(out=bk_s[:, :], in_=t_bk.ap()),
             inc="s_ld")
        tk["bias"] = p.tok("s_ld")
        p.op("gpsimd", lambda h: h.dma_start(
            out=bvbc[:, :], in_=bass.AP(t_bv, 0, [[0, 128], [1, 256]])),
            inc="s_gp")
        tk["bvbc"] = p.tok("s_gp")

        gred = [None] * 12
        for j in range(3):
            for c in range(4):
                i = 4 * j + c
                par = i % 2
                w = [gred[i - 2]] if i >= 2 else []
                p.op("scalar", lambda h, j=j, c=c, par=par: h.dma_start(
                    out=gbuf[par][:, :],
                    in_=t_g[j].ap()[:, c * 2816:(c + 1) * 2816]),
                    waits=w, inc="s_sc")
                ldt = p.tok("s_sc")
                p.op("vector", lambda h, i=i, par=par: h.tensor_reduce(
                    racc[:, i:i + 1], gbuf[par][:, :], axis=AX, op=ALU.add,
                    apply_absolute_value=True), waits=[ldt], inc="s_ve")
                gred[i] = p.tok("s_ve")
        p.op("tensor", lambda h: h.matmul(
            psA[0:1, 0:12], ones_f[:, :], racc[:, 0:12], start=True,
            stop=True), waits=[gred[11], tk["consts"]], inc="s_mm")
        tgmm = p.tok("s_mm")

        tk["gpart"] = p.seq("vector", [
            lambda h: h.tensor_copy(prow(0, 12), psA[0:1, 0:12]),
            lambda h: h.reduce_sum(prow(12, 13), prow(0, 4), axis=AX),
            lambda h: h.reduce_sum(prow(13, 14), prow(4, 8), axis=AX),
            lambda h: h.reduce_sum(prow(14, 15), prow(8, 12), axis=AX),
        ], waits=[tgmm])
        p.op("gpsimd", lambda h: h.dma_start(out=t_arin.ap(),
                                             in_=prow(12, 20)),
             waits=[tk["gpart"]], inc="s_gp")
        t_ = p.tok("s_gp")
        p.op("gpsimd", lambda h: h.collective_compute(
            "AllReduce", ALU.add, replica_groups=RG,
            ins=[t_arin.ap().opt()], outs=[t_arout.ap().opt()]),
            waits=[t_], inc="s_cc", incn=1)
        tar = p.tok("s_cc")
        p.op("gpsimd", lambda h: h.dma_start(out=prow(24, 32),
                                             in_=t_arout.ap()),
             waits=[tar], inc="s_gp")
        tab = p.tok("s_gp")

        ttr = p.seq("vector", [
            lambda h: h.tensor_scalar(prow(32, 35), prow(24, 27), 1.0 / NW,
                                      None, ALU.mult),
            lambda h: h.tensor_scalar(prow(35, 38), prow(32, 35), 0.5,
                                      0.5 * Q_EPS, ALU.mult, ALU.add),
            lambda h: h.tensor_scalar(prow(38, 41), prow(35, 38), -1.0,
                                      None, ALU.mult),
        ], waits=[tab])
        p.op("gpsimd", lambda h: h.dma_start(out=t_taud.ap(),
                                             in_=prow(35, 43)),
             waits=[ttr], inc="s_gp")
        t_ = p.tok("s_gp")
        p.op("gpsimd", lambda h: h.dma_start(
            out=taus[:, :], in_=bass.AP(t_taud, 0, [[0, 128], [1, 8]])),
            waits=[t_], inc="s_gp")
        tk["taus"] = p.tok("s_gp")

        def emit_ln(src, src_toks, extra_mm_waits=()):
            mm_done = [None] * KCH
            for kc in range(KCH):
                par = kc % 2
                w = [src_toks(kc)] if src_toks else []
                if kc >= 2 and mm_done[kc - 2] is not None:
                    w = w + [mm_done[kc - 2]]
                p.op("scalar", lambda h, kc=kc, par=par: h.activation(
                    xsq[par][:, :], src(kc), AF.Square), waits=w, inc="s_se")
                sqt = p.tok("s_se")
                mw = [sqt, tk["consts"]] + list(extra_mm_waits) + \
                    ([src_toks(kc)] if src_toks else [])
                for seg in range(4):
                    def mk(h, kc=kc, seg=seg, par=par):
                        if seg < 2:
                            rh = src(kc)[:, seg * 512:(seg + 1) * 512]
                        else:
                            rh = xsq[par][:, (seg - 2) * 512:(seg - 1) * 512]
                        return h.matmul(psA[0:1, seg * 512:(seg + 1) * 512],
                                        ones_f[:, :], rh,
                                        start=(kc == 0), stop=(kc == KCH - 1))
                    p.op("tensor", mk, waits=mw if seg == 0 else [sqt],
                         inc="s_mm" if seg == 3 else None)
                mm_done[kc] = p.tok("s_mm")
            tsum = mm_done[KCH - 1]

            M, VE, T, R0, RS = 0, 1024, 2048, 3072, 4096

            t1 = p.seq("vector", [
                lambda h: h.tensor_scalar(lnscr[0:1, M:M + 1024],
                                          psA[0:1, 0:1024], 1.0 / D, None,
                                          ALU.mult),
                lambda h: h.tensor_scalar(lnscr[0:1, VE:VE + 1024],
                                          psA[0:1, 1024:2048], 1.0 / D,
                                          LN_EPS, ALU.mult, ALU.add),
                lambda h: h.tensor_tensor(lnscr[0:1, T:T + 1024],
                                          lnscr[0:1, M:M + 1024],
                                          lnscr[0:1, M:M + 1024], ALU.mult),
                lambda h: h.tensor_tensor(lnscr[0:1, VE:VE + 1024],
                                          lnscr[0:1, VE:VE + 1024],
                                          lnscr[0:1, T:T + 1024],
                                          ALU.subtract),
                lambda h: h.reciprocal(lnscr[0:1, R0:R0 + 1024],
                                       lnscr[0:1, VE:VE + 1024]),
            ], waits=[tsum])
            p.op("scalar", lambda h: h.activation(
                lnscr[0:1, RS:RS + 1024], lnscr[0:1, R0:R0 + 1024], AF.Sqrt),
                waits=[t1], inc="s_se")
            t2 = p.tok("s_se")

            t3 = p.seq("vector", [
                lambda h: h.tensor_tensor(lnscr[0:1, T:T + 1024],
                                          lnscr[0:1, RS:RS + 1024],
                                          lnscr[0:1, RS:RS + 1024],
                                          ALU.mult),
                lambda h: h.tensor_tensor(lnscr[0:1, T:T + 1024],
                                          lnscr[0:1, T:T + 1024],
                                          lnscr[0:1, VE:VE + 1024],
                                          ALU.mult),
                lambda h: h.tensor_scalar(lnscr[0:1, T:T + 1024],
                                          lnscr[0:1, T:T + 1024], -0.5, 1.5,
                                          ALU.mult, ALU.add),
                lambda h: h.tensor_tensor(lnscr[0:1, RS:RS + 1024],
                                          lnscr[0:1, RS:RS + 1024],
                                          lnscr[0:1, T:T + 1024], ALU.mult),
                lambda h: h.tensor_tensor(lnscr[0:1, M:M + 1024],
                                          lnscr[0:1, M:M + 1024],
                                          lnscr[0:1, RS:RS + 1024],
                                          ALU.mult),
            ], waits=[t2])
            for i in range(4):
                coff = RS if i < 2 else M
                boff = 2048 + i * 512
                p.op("tensor", lambda h, coff=coff, boff=boff, i=i: h.matmul(
                    psA[:, boff:boff + 512], onesrow_f,
                    lnscr[0:1, coff + (i % 2) * 512:
                          coff + (i % 2) * 512 + 512],
                    start=True, stop=True),
                    waits=[t3, tk["consts"]], inc="s_mm" if i == 3 else None)
            tbc = p.tok("s_mm")
            last = None
            for kc in range(KCH):
                par = kc % 2

                def _x(h, kc=kc, par=par):
                    h.tensor_tensor(xsq[par][:, :], src(kc),
                                    psA[:, 2048:3072], ALU.mult)
                    return h.tensor_tensor(
                        xln[:, kc * 1024:(kc + 1) * 1024], xsq[par][:, :],
                        psA[:, 3072:4096], ALU.subtract)
                p.op("vector", _x, waits=[tbc] +
                     ([src_toks(kc)] if src_toks else []), inc="s_ve")
                last = p.tok("s_ve")
            return last

        tk["xln"] = emit_ln(lambda kc: x0[:, kc * 1024:(kc + 1) * 1024],
                            lambda kc: tk["x0"],
                            extra_mm_waits=[tk["gpart"]])

        def emit_ffn_B(pre, entry_waits):
            if pre:
                wraw = [[V(STR + k * 8192, 2048, F32) for k in (0, 1)],
                        [V(STR + 16384 + k * 8192, 2048, F32)
                         for k in (0, 1)]]
                wqv = [[V(STR + 32768 + k * 4096, 2048, BF16)
                        for k in (0, 1)],
                       [V(STR + 40960 + k * 4096, 2048, BF16)
                        for k in (0, 1)]]
                ltmp = V(STR + 49152, 2048, BF16)
                sigb = V(STR + 53248, 1024, BF16)
            else:
                wqv = [[V(TAIL + k * 4096, 2048, BF16) for k in (0, 1)],
                       [V(TAIL + 8192 + k * 4096, 2048, BF16)
                        for k in (0, 1)]]
                sigb = V(TAIL + 16384, 1024, BF16)
            xb = [[psA[:, 0:1024], psA[:, 1024:2048]],
                  [psA[:, 2048:3072], psA[:, 3072:4096]]]
            wsrc = (t_w1p, t_w2p)
            wdst = (t_w1q, t_w2q)
            tcols = ((0, 3), (1, 4))
            qtok = [[None] * HCH, [None] * HCH]
            stok = [[None] * HCH, [None] * HCH]
            mmtok = [[None] * HCH, [None] * HCH]
            sigtok = [None] * HCH
            htok = [None] * HCH
            for hc in range(HCH):
                par = hc % 2
                for wi in range(2):
                    fw = [mmtok[wi][hc - 2]] if hc >= 2 else list(entry_waits)
                    if pre:
                        if hc >= 2:
                            fw.append(qtok[wi][hc - 2])
                        p.op("sync", lambda h, wi=wi, hc=hc, par=par:
                             h.dma_start(
                                 out=wraw[wi][par][:, :].rearrange(
                                     "p (kc m) -> p kc m", kc=KCH),
                                 in_=bass.AP(wsrc[wi], hc * D * 128,
                                             [[128, 128], [16384, KCH],
                                              [1, 128]])),
                             waits=fw, inc="s_ld")
                        ldt = p.tok("s_ld")
                        qw = [ldt, tk["taus"]]
                        if hc >= 2:
                            qw += [mmtok[wi][hc - 2], stok[wi][hc - 2]]
                        if hc == 0 and wi == 0:
                            qw += list(entry_waits)

                        def _q(h, wi=wi, par=par):
                            tc, nc_ = tcols[wi]
                            h.tensor_single_scalar(
                                ltmp[:, :], wraw[wi][par][:, :],
                                taus[:, nc_:nc_ + 1], ALU.is_lt)
                            return h.scalar_tensor_tensor(
                                wqv[wi][par][:, :], wraw[wi][par][:, :],
                                taus[:, tc:tc + 1], ltmp[:, :],
                                ALU.is_gt, ALU.subtract)
                        p.op("vector", _q, waits=qw, inc="s_ve")
                        qtok[wi][hc] = p.tok("s_ve")
                        p.op("gpsimd", lambda h, wi=wi, hc=hc, par=par:
                             h.dma_start(out=wdst[wi].ap()[hc, :, :],
                                         in_=wqv[wi][par][:, :]),
                             waits=[qtok[wi][hc]], inc="s_gp")
                        stok[wi][hc] = p.tok("s_gp")
                    else:
                        p.op("sync", lambda h, wi=wi, hc=hc, par=par:
                             h.dma_start(out=wqv[wi][par][:, :],
                                         in_=wdst[wi].ap()[hc, :, :]),
                             waits=fw, inc="s_ld")
                        qtok[wi][hc] = p.tok("s_ld")
                for wi in range(2):
                    mw = [qtok[wi][hc], tk["xln"]]
                    if hc >= 2:
                        mw.append(sigtok[hc - 2] if wi == 0
                                  else htok[hc - 2])
                    if hc < 2:
                        mw += list(entry_waits)
                    for kc in range(KCH):
                        for half in (0, 1):
                            lst = (kc == KCH - 1 and half == 1)
                            p.op("tensor", lambda h, wi=wi, par=par, kc=kc,
                                 half=half: h.matmul(
                                     xb[wi][par][:, half * 512:
                                                 (half + 1) * 512],
                                     wqv[wi][par][:, kc * 128:
                                                  (kc + 1) * 128],
                                     xln[:, kc * 1024 + half * 512:
                                         kc * 1024 + half * 512 + 512],
                                     start=(kc == 0), stop=(kc == KCH - 1)),
                                 waits=mw if (kc == 0 and half == 0) else [],
                                 inc="s_mm" if lst else None)
                    mmtok[wi][hc] = p.tok("s_mm")
                sw = [mmtok[0][hc]]
                if hc >= 1:
                    sw.append(htok[hc - 1])
                p.op("scalar", lambda h, par=par: h.activation(
                    sigb[:, :], xb[0][par][:, :], AF.Sigmoid),
                    waits=sw, inc="s_se")
                sigtok[hc] = p.tok("s_se")
                p.op("vector", lambda h, hc=hc, par=par: h.tensor_tensor(
                    hbuf[:, hc * 1024:(hc + 1) * 1024], sigb[:, :],
                    xb[1][par][:, :], ALU.mult),
                    waits=[sigtok[hc], mmtok[1][hc]], inc="s_ve")
                htok[hc] = p.tok("s_ve")
            return htok, sigtok, stok

        def emit_ffn_C(pre, htok, sigtok, resid, centry=()):
            if pre:
                w3raw = [V(STR + k * 5632, 1408, F32) for k in (0, 1, 2)]
                w3qs = [V(STR + 16896 + k * 2816, 1408, BF16)
                        for k in (0, 1, 2)]
                ltmp2 = V(STR + 25344, 1408, BF16)
                xres = [V(STR + 28288 + k * 4096, 1024, F32)
                        for k in (0, 1, 2, 3)]
                outb = [V(STR + 44672 + k * 4096, 1024, F32) for k in (0, 1)]
            else:
                w3qs = [V(XLN + k * 2816, 1408, BF16) for k in (0, 1, 2)]
                outb = [V(XLN + 8448 + k * 4096, 1024, F32) for k in (0, 1)]
            yb = [psA[:, 0:1024], psA[:, 1024:2048]]
            NS = 4 * OCH
            ld_tok = [None] * NS
            qv_tok = [None] * NS
            sc_tok = [None] * NS
            mm_sub = [None] * NS
            eptok = [None] * OCH
            dmtok = [None] * OCH
            entry = [sigtok[HCH - 1], htok[HCH - 1]] + list(centry)
            entry = entry + [p.tok("s_mm")]
            p.op("gpsimd", lambda h: h.nop(),
                 waits=[sigtok[HCH - 1], htok[HCH - 1]], inc="s_gq")
            entry = entry + [p.tok("s_gq")]

            def prep(si):
                oc, j = si // 4, si % 4
                jp = si % 3
                fw = [mm_sub[si - 3]] if si >= 3 else list(entry)
                if pre:
                    lw = list(fw)
                    if si >= 3:
                        lw.append(qv_tok[si - 3])
                    else:
                        lw += list(entry)
                    p.op("sync", lambda h, oc=oc, j=j, jp=jp:
                         h.dma_start(
                             out=w3raw[jp][:, :].rearrange(
                                 "p (kc m) -> p kc m", kc=11),
                             in_=bass.AP(t_w3p,
                                         oc * HID * 128 + j * 1408 * 128,
                                         [[128, 128], [16384, 11],
                                          [1, 128]])),
                         waits=lw, inc="s_ld")
                    ldt = p.tok("s_ld")
                    qw = [ldt, tk["taus"]] + fw
                    if si >= 3:
                        qw.append(sc_tok[si - 3])

                    def _q3(h, jp=jp):
                        h.tensor_single_scalar(
                            ltmp2[:, :], w3raw[jp][:, :], taus[:, 5:6],
                            ALU.is_lt)
                        return h.scalar_tensor_tensor(
                            w3qs[jp][:, :], w3raw[jp][:, :],
                            taus[:, 2:3], ltmp2[:, :],
                            ALU.is_gt, ALU.subtract)
                    p.op("vector", _q3, waits=qw, inc="s_ve")
                    qv_tok[si] = p.tok("s_ve")
                    ld_tok[si] = qv_tok[si]
                    p.op("gpsimd", lambda h, oc=oc, j=j, jp=jp:
                         h.dma_start(
                             out=t_w3q.ap()[oc, :,
                                            j * 1408:(j + 1) * 1408],
                             in_=w3qs[jp][:, :]),
                         waits=[qv_tok[si]], inc="s_gp")
                    sc_tok[si] = p.tok("s_gp")
                else:
                    p.op("sync", lambda h, oc=oc, j=j, jp=jp:
                         h.dma_start(
                             out=w3qs[jp][:, :],
                             in_=t_w3q.ap()[oc, :,
                                            j * 1408:(j + 1) * 1408]),
                         waits=fw, inc="s_ld")
                    ld_tok[si] = p.tok("s_ld")

            prep(0)
            prep(1)
            prep(2)
            for oc in range(OCH):
                par = oc % 2
                for kc in range(HCH):
                    si = 4 * oc + kc // 11
                    jp = si % 3
                    kw = [ld_tok[si], htok[kc]]
                    if kc == 0:
                        kw.append(htok[0])
                        if oc >= 2:
                            kw.append(eptok[oc - 2])
                        else:
                            kw += list(entry)
                    for half in (0, 1):
                        sub_last = (kc % 11 == 10 and half == 1)
                        p.op("tensor", lambda h, par=par, kc=kc, jp=jp,
                             half=half: h.matmul(
                                 yb[par][:, half * 512:(half + 1) * 512],
                                 w3qs[jp][:, (kc % 11) * 128:
                                          (kc % 11) * 128 + 128],
                                 hbuf[:, kc * 1024 + half * 512:
                                      kc * 1024 + half * 512 + 512],
                                 start=(kc == 0), stop=(kc == HCH - 1)),
                             waits=kw if half == 0 else [],
                             inc="s_mm" if sub_last else None)
                        if sub_last:
                            mm_sub[si] = p.tok("s_mm")
                    if kc % 11 == 10 and si + 3 < NS:
                        prep(si + 3)
                ymm = mm_sub[4 * oc + 3]
                if resid == "xT":
                    rw = [eptok[oc - 4]] if oc >= 4 else \
                        [htok[HCH - 1]] + list(centry)
                    p.op("scalar", lambda h, oc=oc: h.dma_start(
                        out=xres[oc % 4][:, :],
                        in_=t_xT.ap()[oc * 128:(oc + 1) * 128, :]),
                        waits=rw, inc="s_sc")
                    rt = [p.tok("s_sc")]
                    rsl = lambda oc=oc: xres[oc % 4][:, :]
                else:
                    rt = []
                    rsl = lambda oc=oc: stream[:, oc * 1024:(oc + 1) * 1024]
                ew = [ymm] + rt
                if oc >= 2:
                    ew.append(dmtok[oc - 2])
                else:
                    ew += [htok[HCH - 1]] + list(centry)
                if DBG and not pre and oc == 0:
                    dbg3buf = V(TAIL, 1024, F32)
                    p.op("scalar", lambda h, par=par: h.copy(
                        dbg3buf[:, :], yb[par][:, :]), waits=[ymm],
                        inc="s_se")
                    t_d3 = p.tok("s_se")
                    p.op("gpsimd", lambda h: h.dma_start(
                        out=t_dbg3.ap()[0, :, :], in_=dbg3buf[:, :]),
                        waits=[t_d3], inc="s_gp")
                    ew.append(t_d3)
                p.op("vector", lambda h, par=par, rsl=rsl:
                     h.scalar_tensor_tensor(
                         outb[par][:, :], yb[par][:, :], 0.5, rsl(),
                         ALU.mult, ALU.add), waits=ew, inc="s_ve")
                eptok[oc] = p.tok("s_ve")
                if pre:
                    p.op("gpsimd", lambda h, oc=oc, par=par: h.dma_start(
                        out=t_xmid.ap()[oc, :, :], in_=outb[par][:, :]),
                        waits=[eptok[oc]], inc="s_gp")
                    dmtok[oc] = p.tok("s_gp")
                    if stop == "ffnpre":
                        p.op("gpsimd", lambda h, oc=oc, par=par: h.dma_start(
                            out=t_out.ap()[oc, :, :], in_=outb[par][:, :]),
                            waits=[eptok[oc]], inc="s_out")
                        dmtok[oc] = p.tok("s_out")
                else:
                    p.op("gpsimd", lambda h, oc=oc, par=par: h.dma_start(
                        out=t_out.ap()[oc, :, :], in_=outb[par][:, :]),
                        waits=[eptok[oc]], inc="s_out")
                    dmtok[oc] = p.tok("s_out")
                    if DBG and oc < 2:
                        p.op("gpsimd", lambda h, oc=oc, par=par: h.dma_start(
                            out=t_dbg2.ap()[oc, :, :], in_=outb[par][:, :]),
                            waits=[eptok[oc]], inc="s_gp")
            return eptok, dmtok

        htokB, sigtokB, stokB = emit_ffn_B(True, entry_waits=[gred[11]])
        eptokC, dmtokC = emit_ffn_C(
            True, htokB, sigtokB, resid="xT",
            centry=[stokB[0][HCH - 1], stokB[1][HCH - 1]])

        if stop == "ffnpre":
            if DBG:
                end = [p.tok("s_ve"), p.tok("s_mm"), p.tok("s_se"),
                       p.tok("s_gp"), p.tok("s_ld"), p.tok("s_sc")]
                dump = nc.dram_tensor("dbg", [8, 128, 1024], F32,
                                      kind="ExternalOutput")
                p.op("gpsimd", lambda h: h.dma_start(
                    out=dump.ap()[0, 0:1, 0:1024],
                    in_=V(PER, 1024, F32)[0:1, :]), waits=end, inc="s_out")
                p.op("gpsimd", lambda h: h.dma_start(
                    out=dump.ap()[1, :, :],
                    in_=xln[:, 0:1024]), waits=end, inc="s_out")
                p.op("gpsimd", lambda h: h.dma_start(
                    out=dump.ap()[2, :, :],
                    in_=hbuf[:, 0:1024]), waits=end, inc="s_out")
                p.op("gpsimd", lambda h: h.dma_start(
                    out=dump.ap()[3, :, :],
                    in_=t_w1q.ap()[0, :, 0:1024]), waits=end, inc="s_out")
                p.op("gpsimd", lambda h: h.dma_start(
                    out=dump.ap()[4, :, :],
                    in_=x0[:, 0:1024]), waits=end, inc="s_out")
                p.op("gpsimd", lambda h: h.dma_start(
                    out=dump.ap()[5, 0:1, :],
                    in_=lnscr[0:1, 4096:5120]), waits=end, inc="s_out")
                dmph = nc.dram_tensor("dbg_h", [44, 128, 1024], F32,
                                      kind="ExternalOutput")
                for hc in range(HCH):
                    p.op("gpsimd", lambda h, hc=hc: h.dma_start(
                        out=dmph.ap()[hc, :, :],
                        in_=hbuf[:, hc * 1024:(hc + 1) * 1024]),
                        waits=end, inc="s_out")
                dmpw = nc.dram_tensor("dbg_w3", [OCH, 128, HID], F32,
                                      kind="ExternalOutput")
                for oc in range(OCH):
                    p.op("gpsimd", lambda h, oc=oc: h.dma_start(
                        out=dmpw.ap()[oc, :, :],
                        in_=t_w3q.ap()[oc, :, :]),
                        waits=end, inc="s_out")
                dmpx = nc.dram_tensor("dbg_xln", [KCH, 128, 1024], F32,
                                      kind="ExternalOutput")
                for kc in range(KCH):
                    p.op("gpsimd", lambda h, kc=kc: h.dma_start(
                        out=dmpx.ap()[kc, :, :],
                        in_=xln[:, kc * 1024:(kc + 1) * 1024]),
                        waits=end, inc="s_out")
            p.op("gpsimd",
                 lambda h: h.wait_ge(sems["s_out"],
                                     16 * (OCH + (6 + HCH + OCH + KCH
                                                  if DBG else 0))))
            p.emit(sems)
            return nc

        p.op("sync", lambda h: h.dma_start(
            out=stream[:, :].rearrange("p (c t) -> p c t", c=KCH),
            in_=t_xmid.ap().rearrange("c p t -> p c t")),
            waits=[dmtokC[OCH - 1], eptokC[OCH - 1]], inc="s_ld")
        tk["stream"] = p.tok("s_ld")
        tk["xln"] = emit_ln(
            lambda kc: stream[:, kc * 1024:(kc + 1) * 1024],
            lambda kc: tk["stream"],
            extra_mm_waits=[eptokC[OCH - 1]])
        tk["xln2"] = tk["xln"]

        masks = V(HAR, 2048, BF16)
        kctr = V(HAR + 4096, 2048, BF16)
        vctr = V(HAR + 8192, 2048, BF16)
        cosf = V(HAR + 12288, 1024, F32)
        ssin = V(HAR + 16384, 1024, F32)
        posb = V(HAR + 20480, 1024, F32)
        tt1 = V(HAR + 24576, 1024, F32)
        kint = V(HAR + 28672, 1024, I32)
        ktf = V(HAR + 32768, 1024, F32)
        rtmp = V(HAR + 36864, 1024, F32)
        rtmp2 = V(HAR + 40960, 1024, F32)
        pm = V(HAR + 45056, 128, F32)
        pm2 = V(HAR + 45696, 128, F32)
        qrb = [V(HAR + 46336 + k * 2048, 1024, BF16) for k in (0, 1)]
        wkb = [V(HAR + 50432 + k * 4096, 2048, BF16) for k in (0, 1)]
        wvb = V(HAR + 58624, 4096, BF16)
        wqb = [V(HAR + 66816 + k * 4096, 2048, BF16) for k in (0, 1)]
        itile = V(HAR + 75008, 512, F32)
        jmi = V(HAR + 77056, 128, F32)
        Qmy = V(HAR + 4096, 16384, BF16)
        Kh = V(HAR + 36864, 8192, BF16)
        Vh = V(HAR + 53248, 8192, BF16)
        pt = [V(HAR + 69632 + k * 1024, 512, BF16) for k in (0, 1, 2)]
        rbuf = [V(HAR + 72704 + k * 2048, 512, F32) for k in (0, 1)]
        qsqs = [V(HAR + 76800 + k * 1024, 512, BF16) for k in (0, 1)]
        otile = [V(HAR + 78848 + k * 1024, 512, BF16) for k in (0, 1)]
        wob = wqb
        qraw = [V(TAIL + k * 4096, 1024, F32) for k in (0, 1)]
        rp = [V(TAIL + 8192 + k * 2048, 512, F32) for k in (0, 1)]

        hfree = [p.tok("s_mm")]
        p.op("gpsimd", lambda h: h.iota(
            itile[:, :], [[-1, 512]], channel_multiplier=1,
            allow_small_or_imprecise_dtypes=True), waits=hfree, inc="s_gq")
        tio = p.tok("s_gq")
        mtk = None
        for m in range(4):
            p.op("vector", lambda h, m=m: h.tensor_single_scalar(
                masks[:, m * 512:(m + 1) * 512], itile[:, :],
                float(-128 * m), ALU.is_le),
                waits=[tio, tk["xln2"]], inc="s_ve")
            mtk = p.tok("s_ve")
        tk["masks"] = mtk
        p.op("gpsimd", lambda h: h.iota(
            jmi[:, :], [[1, 128]], channel_multiplier=-1,
            allow_small_or_imprecise_dtypes=True), waits=hfree, inc="s_gq")
        tjm = p.tok("s_gq")

        tk["pm"] = p.seq("vector", [
            lambda h: h.tensor_single_scalar(pm[:, :], jmi[:, :], 64.0,
                                             ALU.is_equal),
            lambda h: h.tensor_single_scalar(pm2[:, :], jmi[:, :], -64.0,
                                             ALU.is_equal),
            lambda h: h.tensor_tensor(pm[:, :], pm[:, :], pm2[:, :],
                                      ALU.add),
        ], waits=[tjm])

        p.op("gpsimd", lambda h: h.iota(
            pv_t[:, :], [[1, 1]], channel_multiplier=1,
            allow_small_or_imprecise_dtypes=True), inc="s_gq")
        tpv = p.tok("s_gq")

        t_ = p.seq("vector", [
            lambda h: h.tensor_single_scalar(gtmp[:, :], pv_t[:, :], 64.0,
                                             ALU.is_ge),
            lambda h: h.scalar_tensor_tensor(gtmp[:, :], gtmp[:, :], -64.0,
                                             pv_t[:, :], ALU.mult, ALU.add),
            lambda h: h.tensor_single_scalar(gtmp[:, :], gtmp[:, :],
                                             -LOG1E4 / 64.0, ALU.mult),
        ], waits=[tpv])
        p.op("scalar", lambda h: h.activation(invf[:, :], gtmp[:, :],
                                              AF.Exp), waits=[t_],
             inc="s_se")
        tk["invf"] = p.tok("s_se")
        p.op("gpsimd", lambda h: h.dma_start(
            out=posb[:, :], in_=bass.AP(t_pos, 0, [[0, 128], [1, NT]])),
            waits=hfree, inc="s_gp")
        tk["posb"] = p.tok("s_gp")
        p.op("vector", lambda h: h.tensor_single_scalar(
            posb[:, :], posb[:, :], invf[:, 0:1], ALU.mult),
            waits=[tk["invf"], tk["posb"]], inc="s_ve")
        tang = p.tok("s_ve")

        def _red(dst, shift, tprev):
            fns = []
            if shift:
                fns.append(lambda h: h.tensor_scalar(
                    tt1[:, :], posb[:, :], INV_2PI, 0.25, ALU.mult, ALU.add))
            else:
                fns.append(lambda h: h.tensor_single_scalar(
                    tt1[:, :], posb[:, :], INV_2PI, ALU.mult))
            fns.append(lambda h: h.tensor_copy(kint[:, :], tt1[:, :]))
            fns.append(lambda h: h.tensor_copy(ktf[:, :], kint[:, :]))
            fns.append(lambda h: h.scalar_tensor_tensor(
                rtmp[:, :], ktf[:, :], -TWO_PI_HI, posb[:, :], ALU.mult,
                ALU.add))
            if shift:
                fns.append(lambda h: h.tensor_single_scalar(
                    rtmp[:, :], rtmp[:, :], HALF_PI, ALU.add))
            fns.append(lambda h: h.scalar_tensor_tensor(
                rtmp[:, :], ktf[:, :], -TWO_PI_LO, rtmp[:, :], ALU.mult,
                ALU.add))
            t = p.seq("vector", fns, hard=[tprev])
            p.op("scalar", lambda h, dst=dst: h.activation(
                dst[:, :], rtmp[:, :], AF.Sin), waits=[t], inc="s_se")
            return p.tok("s_se")

        tsin = _red(ssin, False, tang)
        p.op("vector", lambda h: h.tensor_single_scalar(
            ssin[0:64, :], ssin[0:64, :], -1.0, ALU.mult),
            waits=[tsin], inc="s_ve")
        tflip = p.tok("s_ve")
        tk["rope_tab"] = _red(cosf, True, tflip)

        for c in range(2):
            p.op("gpsimd", lambda h, c=c: h.dma_start(
                out=wkb[c][:, :].rearrange("p (kc m) -> p kc m", kc=KCH),
                in_=bass.AP(t_wkp, c * D * 128,
                            [[128, 128], [16384, KCH], [1, 128]])),
                waits=hfree, inc="s_gp")
        tk["wkb"] = p.tok("s_gp")
        p.op("gpsimd", lambda h: h.dma_start(
            out=wvb[:, :].rearrange("p (kc m) -> p kc m", kc=KCH),
            in_=t_wvt.ap().rearrange("(kc p) m -> p kc m", p=128)),
            waits=hfree, inc="s_gp")
        tk["wvb"] = p.tok("s_gp")

        kparts = []
        for c in range(2):
            kb = psA[:, c * 1024:(c + 1) * 1024]
            swb = psA[:, 2048 + c * 1024:3072 + c * 1024]
            mw = [tk["wkb"], tk["xln2"]]
            for kc in range(KCH):
                for half in (0, 1):
                    lst = (kc == KCH - 1 and half == 1)
                    p.op("tensor", lambda h, c=c, kc=kc, half=half, kb=kb:
                         h.matmul(kb[:, half * 512:(half + 1) * 512],
                                  wkb[c][:, kc * 128:(kc + 1) * 128],
                                  xln[:, kc * 1024 + half * 512:
                                      kc * 1024 + half * 512 + 512],
                                  start=(kc == 0), stop=(kc == KCH - 1)),
                         waits=mw if (kc == 0 and half == 0) else [],
                         inc="s_mm" if lst else None)
            tmm = p.tok("s_mm")
            p.op("scalar", lambda h, c=c, kb=kb: h.activation(
                qraw[c][:, :], kb[:, :], AF.Identity, bias=bk_s[:, c:c + 1]),
                waits=[tmm], inc="s_se")
            tev = p.tok("s_se")
            p.op("tensor", lambda h, swb=swb, c=c: h.matmul(
                swb[:, 0:512], pm[:, :], qraw[c][:, 0:512], start=True,
                stop=True), waits=[tev, tk["pm"]])
            p.op("tensor", lambda h, swb=swb, c=c: h.matmul(
                swb[:, 512:1024], pm[:, :], qraw[c][:, 512:1024],
                start=True, stop=True), inc="s_mm")
            tsw = p.tok("s_mm")

            def _rope(h, c=c, swb=swb):
                h.tensor_tensor(rtmp[:, :], qraw[c][:, :], cosf[:, :],
                                ALU.mult)
                h.tensor_tensor(rtmp2[:, :], swb[:, :], ssin[:, :],
                                ALU.mult)
                return h.tensor_tensor(kctr[:, c * 1024:(c + 1) * 1024],
                                       rtmp[:, :], rtmp2[:, :], ALU.add)
            p.op("vector", _rope, waits=[tsw, tk["rope_tab"]], inc="s_ve")
            kparts.append(p.tok("s_ve"))
        tk["kctr"] = kparts[-1]

        vev = [None] * 8
        for tt in range(8):
            par = tt % 2
            vb = psA[:, par * 512:par * 512 + 256]
            w = [tk["wvb"], tk["xln2"], tk["kctr"]]
            if tt >= 2:
                w.append(vev[tt - 2])
            for kc in range(KCH):
                p.op("tensor", lambda h, tt=tt, kc=kc, vb=vb: h.matmul(
                    vb[:, :],
                    xln[:, kc * 1024 + tt * 128:kc * 1024 + tt * 128 + 128],
                    wvb[:, kc * 256:(kc + 1) * 256],
                    start=(kc == 0), stop=(kc == KCH - 1)),
                    waits=w if kc == 0 else [],
                    inc="s_mm" if kc == KCH - 1 else None)
            tv = p.tok("s_mm")
            p.op("vector", lambda h, tt=tt, vb=vb: h.tensor_tensor(
                vctr[:, tt * 256:(tt + 1) * 256], vb[:, :], bvbc[:, :],
                ALU.add), waits=[tv, tk["bvbc"]], inc="s_ve")
            vev[tt] = p.tok("s_ve")
        tk["vctr"] = vev[-1]

        p.op("gpsimd", lambda h: h.dma_start(
            out=bass.AP(t_kvin, 0, [[1024, 128], [131072, 2], [1, 1024]]),
            in_=kctr[:, :].rearrange("p (c t) -> p c t", c=2)),
            waits=[tk["kctr"]], inc="s_gp")
        p.op("gpsimd", lambda h: h.dma_start(
            out=bass.AP(t_kvin, 262144, [[256, 128], [32768, 8], [1, 256]]),
            in_=vctr[:, :].rearrange("p (tt m) -> p tt m", tt=8)),
            waits=[tk["vctr"]], inc="s_gp")
        t_ = p.tok("s_gp")
        p.op("gpsimd", lambda h: h.collective_compute(
            "AllGather", ALU.bypass, replica_groups=RG,
            ins=[t_kvin.ap().opt()], outs=[t_kvout.ap().opt()]),
            waits=[t_], inc="s_cc", incn=1)
        tk["cc_kv"] = p.tok("s_cc")

        qdma = [None] * 16
        qropet = [None] * 16
        qevt = [None] * 16

        def emit_swap_rope(c):
            par = c % 2
            swb = psA[:, 2048 + par * 1024:3072 + par * 1024]
            p.op("tensor", lambda h, swb=swb, par=par: h.matmul(
                swb[:, 0:512], pm[:, :], qraw[par][:, 0:512], start=True,
                stop=True), waits=[qevt[c], tk["pm"]] +
                ([qropet[c - 2]] if c >= 2 else []))
            p.op("tensor", lambda h, swb=swb, par=par: h.matmul(
                swb[:, 512:1024], pm[:, :], qraw[par][:, 512:1024],
                start=True, stop=True), inc="s_mm")
            tsw = p.tok("s_mm")
            qw = [tsw, tk["rope_tab"]]
            if c >= 2:
                qw.append(qdma[c - 2])

            def _ropeq(h, par=par, swb=swb):
                h.tensor_tensor(rtmp[:, :], qraw[par][:, :], cosf[:, :],
                                ALU.mult)
                h.tensor_tensor(rtmp2[:, :], swb[:, :], ssin[:, :],
                                ALU.mult)
                return h.tensor_tensor(qrb[par][:, :], rtmp[:, :],
                                       rtmp2[:, :], ALU.add)
            p.op("vector", _ropeq, waits=qw, inc="s_ve")
            qropet[c] = p.tok("s_ve")
            p.op("gpsimd", lambda h, c=c, par=par: h.dma_start(
                out=t_qin.ap()[c * 128:(c + 1) * 128, :],
                in_=qrb[par][:, :]), waits=[qropet[c]], inc="s_gp")
            qdma[c] = p.tok("s_gp")

        for c in range(16):
            par = c % 2
            qb = psA[:, par * 1024:(par + 1) * 1024]
            lw = [tk["xln2"], tk["vctr"], hfree[0]]
            if c >= 2:
                lw.append(qropet[c - 2])
                lw.append(qevt[c - 2])
            p.op("gpsimd", lambda h, c=c, par=par: h.dma_start(
                out=wqb[par][:, :].rearrange("p (kc m) -> p kc m", kc=KCH),
                in_=bass.AP(t_wqp, c * D * 128,
                            [[128, 128], [16384, KCH], [1, 128]])),
                waits=lw, inc="s_gp")
            tw = p.tok("s_gp")
            mw = [tw, tk["vctr"]]
            if c >= 2:
                mw.append(qevt[c - 2])
            for kc in range(KCH):
                for half in (0, 1):
                    lst = (kc == KCH - 1 and half == 1)
                    p.op("tensor", lambda h, par=par, kc=kc, half=half,
                         qb=qb: h.matmul(
                             qb[:, half * 512:(half + 1) * 512],
                             wqb[par][:, kc * 128:(kc + 1) * 128],
                             xln[:, kc * 1024 + half * 512:
                                 kc * 1024 + half * 512 + 512],
                             start=(kc == 0), stop=(kc == KCH - 1)),
                         waits=mw if (kc == 0 and half == 0) else [],
                         inc="s_mm" if lst else None)
            tmm = p.tok("s_mm")
            sw2 = [tmm]
            if c >= 2:
                sw2.append(qropet[c - 2])
            p.op("scalar", lambda h, c=c, par=par, qb=qb: h.activation(
                qraw[par][:, :], qb[:, :], AF.Identity,
                bias=bq_s[:, c:c + 1]), waits=sw2, inc="s_se")
            qevt[c] = p.tok("s_se")
            if c == 0:
                emit_swap_rope(0)
            elif c >= 2:
                emit_swap_rope(c - 1)
        emit_swap_rope(15)
        p.op("gpsimd", lambda h: h.collective_compute(
            "AllGather", ALU.bypass, replica_groups=RG,
            ins=[t_qin.ap().opt()], outs=[t_qout.ap().opt()]),
            waits=[qdma[15]], inc="s_cc", incn=1)
        tk["cc_q"] = p.tok("s_cc")

        safe = [tk["cc_kv"], qdma[15], qropet[15], p.tok("s_mm"),
                p.tok("s_se"), tk["sel"]]
        regs = {}

        def _store_regs(h):
            regs["kvk"] = h.value_load(sbsel[0:1, 0:1])
            regs["kvv"] = h.value_load(sbsel[0:1, 1:2])
            regs["qof"] = h.value_load(sbsel[0:1, 2:3])
            regs["colA"] = h.value_load(sbsel[0:1, 3:4])
            regs["colB"] = h.value_load(sbsel[0:1, 4:5])
            return h.nop()
        p.op("sync", _store_regs, waits=safe)
        p.op("sync", lambda h: h.dma_start(
            out=Kh[:, :].rearrange("p (r t) -> p r t", r=8),
            in_=bass.AP(t_kvout, regs["kvk"],
                        [[1024, 128], [524288, 8], [1, 1024]])),
            waits=safe, inc="s_ld")
        for r in range(8):
            p.op("sync", lambda h, r=r: h.dma_start(
                out=Vh[:, r * 1024:(r + 1) * 1024].rearrange(
                    "p (j m) -> p j m", j=8),
                in_=bass.AP(t_kvout, regs["kvv"] + (262144 + r * 524288),
                            [[256, 128], [32768, 8], [1, 128]])),
                waits=safe, inc="s_ld")
        tk["kvh"] = p.tok("s_ld")
        for hh in range(2):
            p.op("sync", lambda h, hh=hh: h.dma_start(
                out=Qmy[:, hh * 8192:(hh + 1) * 8192].rearrange(
                    "p (r t) -> p r t", r=8),
                in_=bass.AP(t_qout, regs["qof"] + hh * 131072,
                            [[1024, 128], [2097152, 8], [1, 1024]])),
                waits=[tk["cc_q"]] + safe, inc="s_ld")
        tk["qmy"] = p.tok("s_ld")

        def keypos(kt):
            b = kt // 4
            r, s = _rank_slot(b)
            return r * 1024 + s * 512 + (kt % 4) * 128

        def qpos(h_i, b):
            r, s = _rank_slot(b)
            return h_i * 8192 + r * 1024 + s * 512

        red_t = [None] * 48
        idx = 0
        for bsrc, nsl, base in ((Qmy, 32, 64), (Kh, 16, 96)):
            for j in range(nsl):
                par = idx % 2
                w = [tk["qmy"], tk["kvh"], tk["masks"]]
                if idx >= 2:
                    w.append(red_t[idx - 2])
                p.op("scalar", lambda h, bsrc=bsrc, j=j, par=par:
                     h.activation(qsqs[par][:, :],
                                  bsrc[:, j * 512:(j + 1) * 512], AF.Square),
                     waits=w, inc="s_se")
                ts_ = p.tok("s_se")
                rw = [ts_, tk["consts"]]
                if idx >= 2:
                    rw.append(red_t[idx - 2])
                p.op("tensor", lambda h, par=par: h.matmul(
                    psA[0:1, par * 512:(par + 1) * 512], ones_b[:, :],
                    qsqs[par][:, :], start=True, stop=True),
                    waits=rw, inc="s_mm")
                tm = p.tok("s_mm")
                p.op("vector", lambda h, base=base, j=j, par=par:
                     h.reduce_max(prow(base + j, base + j + 1),
                                  psA[0:1, par * 512:(par + 1) * 512],
                                  axis=AX), waits=[tm], inc="s_ve")
                red_t[idx] = p.tok("s_ve")
                idx += 1
        t_ = p.seq("vector", [
            lambda h: h.reduce_max(prow(112, 113), prow(64, 80), axis=AX),
            lambda h: h.reduce_max(prow(113, 114), prow(80, 96), axis=AX),
            lambda h: h.reduce_max(prow(114, 115), prow(96, 112), axis=AX),
            lambda h: h.tensor_tensor(prow(115, 116), prow(112, 113),
                                      prow(114, 115), ALU.mult),
            lambda h: h.tensor_tensor(prow(116, 117), prow(113, 114),
                                      prow(114, 115), ALU.mult),
        ], hard=[red_t[47]])
        p.op("scalar", lambda h: h.activation(prow(117, 119),
                                              prow(115, 117), AF.Sqrt),
             waits=[t_], inc="s_se")
        t_ = p.tok("s_se")
        p.op("vector", lambda h: h.tensor_single_scalar(
            prow(119, 121), prow(117, 119), -CSC, ALU.mult),
            waits=[t_], inc="s_ve")
        t_ = p.tok("s_ve")
        p.op("gpsimd", lambda h: h.dma_start(out=t_bshd.ap(),
                                             in_=prow(119, 121)),
             waits=[t_], inc="s_gp")
        t_ = p.tok("s_gp")
        p.op("gpsimd", lambda h: h.dma_start(
            out=bshift[:, :], in_=bass.AP(t_bshd, 0, [[0, 128], [1, 2]])),
            waits=[t_], inc="s_gp")
        tk["bshift"] = p.tok("s_gp")

        ps_s = [psA[:, 0:512], psA[:, 512:1024], psA[:, 1024:1536]]
        ps_o = [psA[:, 1536:2048], psA[:, 2048:2560]]
        ps_p = [psA[0:1, 2560:3072], psA[0:1, 3072:3584]]
        flat = []
        tmeta = []
        for h_i in range(2):
            for b in range(NB):
                ti = h_i * NB + b
                nkt = 4 * (b + 1)
                tmeta.append((h_i, b, nkt))
                for kt in range(nkt):
                    flat.append((ti, h_i, b, kt, nkt))
        NG = len(flat)
        scr_tok = [None] * NG
        exp_done = [None] * NG
        pt_ready = [None] * NG
        av_done = [None] * NG
        rp_done = [None] * 32
        ot_done = [None] * 32
        odma_done = [None] * 32
        d1_done = [None] * 32
        d2_done = [None] * 32

        def emit_S(g):
            ti, h_i, b, kt, nkt = flat[g]
            sp = g % 3
            kp = keypos(kt)
            qoff = qpos(h_i, b)
            w = [tk["qmy"], tk["kvh"], tk["bshift"], tk["masks"]]
            if g >= 3:
                w.append(exp_done[g - 3])
            p.op("tensor", lambda h, kp=kp, sp=sp, qoff=qoff:
                 h.matmul(ps_s[sp][:, :], Kh[:, kp:kp + 128],
                          Qmy[:, qoff:qoff + 512], start=True, stop=True),
                 waits=w, inc="s_mm")
            scr_tok[g] = p.tok("s_mm")

        emit_S(0)
        if NG > 1:
            emit_S(1)
        if NG > 2:
            emit_S(2)
        for g in range(NG):
            ti, h_i, b, kt, nkt = flat[g]
            sp = g % 3
            ob = ti % 2
            ew = [scr_tok[g]]
            if g >= 3:
                ew.append(av_done[g - 3])
            p.op("scalar", lambda h, sp=sp, h_i=h_i: h.activation(
                pt[sp][:, :], ps_s[sp][:, :], AF.Exp,
                bias=bshift[:, h_i:h_i + 1], scale=CSC),
                waits=ew, inc="s_se")
            exp_done[g] = p.tok("s_se")
            ptok = exp_done[g]
            if kt >= 4 * b:
                m = kt - 4 * b
                p.op("vector", lambda h, sp=sp, m=m: h.tensor_tensor(
                    pt[sp][:, :], pt[sp][:, :],
                    masks[:, m * 512:(m + 1) * 512], ALU.mult),
                    waits=[ptok], inc="s_ve")
                ptok = p.tok("s_ve")
            pt_ready[g] = ptok
            if g + 3 < NG:
                emit_S(g + 3)
            avw = [ptok]
            if kt == 0 and ti >= 2:
                avw.append(ot_done[ti - 2])
                avw.append(rp_done[ti - 2])
            kp = keypos(kt)
            p.op("tensor", lambda h, kp=kp, sp=sp, ob=ob, kt=kt, nkt=nkt:
                 h.matmul(ps_o[ob][:, :], Vh[:, kp:kp + 128],
                          pt[sp][:, :], start=(kt == 0),
                          stop=(kt == nkt - 1)), waits=avw)
            p.op("tensor", lambda h, sp=sp, ob=ob, kt=kt, nkt=nkt:
                 h.matmul(ps_p[ob][:, :], ones_b[:, :], pt[sp][:, :],
                          start=(kt == 0), stop=(kt == nkt - 1)),
                 inc="s_mm")
            av_done[g] = p.tok("s_mm")
            if kt == nkt - 1:
                tp_ = ti % 2
                rw = [av_done[g]]
                if ti >= 2:
                    rw.append(d1_done[ti - 2])
                p.op("vector", lambda h, ob=ob, tp_=tp_: h.reciprocal(
                    rp[tp_][0:1, :], ps_p[ob][:, :]), waits=rw, inc="s_ve")
                rp_done[ti] = p.tok("s_ve")
                p.op("gpsimd", lambda h, tp_=tp_: h.dma_start(
                    out=t_rpd.ap()[tp_:tp_ + 1, :], in_=rp[tp_][0:1, :]),
                    waits=[rp_done[ti]], inc="s_gp")
                d1_done[ti] = p.tok("s_gp")
                d2w = [d1_done[ti]]
                if ti >= 2:
                    d2w.append(ot_done[ti - 2])
                p.op("gpsimd", lambda h, tp_=tp_: h.dma_start(
                    out=rbuf[tp_][:, :],
                    in_=bass.AP(t_rpd, tp_ * 512, [[0, 128], [1, 512]])),
                    waits=d2w, inc="s_gp")
                d2_done[ti] = p.tok("s_gp")
                ow = [d2_done[ti], av_done[g]]
                if ti >= 2:
                    ow.append(odma_done[ti - 2])
                p.op("vector", lambda h, ob=ob, tp_=tp_: h.tensor_tensor(
                    otile[tp_][:, :], ps_o[ob][:, :], rbuf[tp_][:, :],
                    ALU.mult), waits=ow, inc="s_ve")
                ot_done[ti] = p.tok("s_ve")
                p.op("gpsimd", lambda h, b=b, h_i=h_i, tp_=tp_:
                     h.dma_start(
                         out=bass.AP(t_oin, h_i * 1048576 + b * 512,
                                     [[8192, 128], [1, 512]]),
                         in_=otile[tp_][:, :]), waits=[ot_done[ti]],
                     inc="s_gp")
                odma_done[ti] = p.tok("s_gp")
        p.op("gpsimd", lambda h: h.collective_compute(
            "AllGather", ALU.bypass, replica_groups=RG,
            ins=[t_oin.ap().opt()], outs=[t_oout.ap().opt()]),
            waits=[odma_done[-1]], inc="s_cc", incn=1)
        tk["cc_o"] = p.tok("s_cc")

        otv = xln
        ow_ = [tk["cc_o"], p.tok("s_mm"), p.tok("s_ve")]
        p.op("sync", lambda h: h.dma_start(
            out=otv[:, :].rearrange("p (c t) -> p c t", c=KCH)[:, :, 0:512],
            in_=bass.AP(t_oout, regs["colA"],
                        [[8192, 128], [1048576, KCH], [1, 512]])),
            waits=ow_, inc="s_ld")
        p.op("sync", lambda h: h.dma_start(
            out=otv[:, :].rearrange("p (c t) -> p c t", c=KCH)[:, :, 512:1024],
            in_=bass.AP(t_oout, regs["colB"],
                        [[8192, 128], [1048576, KCH], [1, 512]])),
            waits=ow_, inc="s_ld")
        tk["otv"] = p.tok("s_ld")
        res2 = [None] * OCH
        for oc in range(OCH):
            par = oc % 2
            w = [tk["cc_o"]]
            if oc >= 2:
                w.append(res2[oc - 2])
            p.op("gpsimd", lambda h, oc=oc, par=par: h.dma_start(
                out=wob[par][:, :].rearrange("p (kc m) -> p kc m", kc=KCH),
                in_=bass.AP(t_wop, oc * D * 128,
                            [[128, 128], [16384, KCH], [1, 128]])),
                waits=w, inc="s_gp")
            tw = p.tok("s_gp")
            yb = psA[:, par * 1024:(par + 1) * 1024]
            mw = [tw, tk["otv"]]
            if oc >= 2:
                mw.append(res2[oc - 2])
            for kc in range(KCH):
                for half in (0, 1):
                    lst = (kc == KCH - 1 and half == 1)
                    p.op("tensor", lambda h, par=par, kc=kc, half=half,
                         yb=yb: h.matmul(
                             yb[:, half * 512:(half + 1) * 512],
                             wob[par][:, kc * 128:(kc + 1) * 128],
                             otv[:, kc * 1024 + half * 512:
                                 kc * 1024 + half * 512 + 512],
                             start=(kc == 0), stop=(kc == KCH - 1)),
                         waits=mw if (kc == 0 and half == 0) else [],
                         inc="s_mm" if lst else None)
            tmm = p.tok("s_mm")
            p.op("vector", lambda h, oc=oc, yb=yb: h.tensor_tensor(
                stream[:, oc * 1024:(oc + 1) * 1024],
                stream[:, oc * 1024:(oc + 1) * 1024], yb[:, :], ALU.add),
                waits=[tmm, tk["stream"]], inc="s_ve")
            res2[oc] = p.tok("s_ve")

        if stop == "attn":
            if DBG:
                end2 = [p.tok("s_ve"), p.tok("s_mm"), p.tok("s_se"),
                        p.tok("s_gp"), p.tok("s_ld"), p.tok("s_cc")]
                dq = nc.dram_tensor("dbg_qin", [D, NT], F32,
                                    kind="ExternalOutput")
                p.op("gpsimd", lambda h: h.dma_start(
                    out=dq.ap(), in_=t_qin.ap()), waits=end2, inc="s_out")
                dp = nc.dram_tensor("dbg_prow", [1, 1, 1024], F32,
                                    kind="ExternalOutput")
                p.op("gpsimd", lambda h: h.dma_start(
                    out=dp.ap()[0, 0:1, :], in_=V(PER, 1024, F32)[0:1, :]),
                    waits=end2, inc="s_out")
            for oc in range(OCH):
                p.op("gpsimd", lambda h, oc=oc: h.dma_start(
                    out=t_out.ap()[oc, :, :],
                    in_=stream[:, oc * 1024:(oc + 1) * 1024]),
                    waits=[res2[oc]], inc="s_out")
            p.op("gpsimd",
                 lambda h: h.wait_ge(sems["s_out"],
                                     16 * (OCH + (2 if DBG else 0))))
            p.emit(sems)
            return nc

        tk["xln"] = emit_ln(
            lambda kc: stream[:, kc * 1024:(kc + 1) * 1024],
            lambda kc: res2[kc], extra_mm_waits=[res2[OCH - 1]])
        pbar = [p.tok("s_mm"), p.tok("s_gp"), p.tok("s_ld"),
                p.tok("s_se"), tk["xln"]]
        htok4, sigtok4, _ = emit_ffn_B(False, entry_waits=pbar)
        emit_ffn_C(False, htok4, sigtok4, resid="stream")

        if DBG:
            end = [p.tok("s_ve"), p.tok("s_mm"), p.tok("s_se"),
                   p.tok("s_gp"), p.tok("s_ld"), p.tok("s_sc"),
                   p.tok("s_out")]
            dmph = nc.dram_tensor("dbg_h", [44, 128, 1024], F32,
                                  kind="ExternalOutput")
            for hc in range(HCH):
                p.op("gpsimd", lambda h, hc=hc: h.dma_start(
                    out=dmph.ap()[hc, :, :],
                    in_=hbuf[:, hc * 1024:(hc + 1) * 1024]),
                    waits=end, inc="s_out")
            dmpw = nc.dram_tensor("dbg_w3", [1, 128, HID], F32,
                                  kind="ExternalOutput")
            p.op("gpsimd", lambda h: h.dma_start(
                out=dmpw.ap()[0, :, :], in_=t_w3q.ap()[0, :, :]),
                waits=end, inc="s_out")
            dmps = nc.dram_tensor("dbg_stream", [KCH, 128, 1024], F32,
                                  kind="ExternalOutput")
            for kc in range(KCH):
                p.op("gpsimd", lambda h, kc=kc: h.dma_start(
                    out=dmps.ap()[kc, :, :],
                    in_=stream[:, kc * 1024:(kc + 1) * 1024]),
                    waits=end, inc="s_out")
            dmpx = nc.dram_tensor("dbg_xln", [KCH, 128, 1024], F32,
                                  kind="ExternalOutput")
            for kc in range(KCH):
                p.op("gpsimd", lambda h, kc=kc: h.dma_start(
                    out=dmpx.ap()[kc, :, :],
                    in_=xln[:, kc * 1024:(kc + 1) * 1024]),
                    waits=end, inc="s_out")
            p.op("gpsimd", lambda h: h.wait_ge(
                sems["s_out"], 16 * (OCH + HCH + 1 + KCH + KCH)))
        else:
            p.op("gpsimd", lambda h: h.wait_ge(sems["s_out"], 16 * OCH))
        p.emit(sems)
    return nc



def _prep_inputs(L0_attn_x, L0_attn_wq, L0_attn_wk, L0_attn_wv, L0_attn_wo,
                 L0_attn_bq, L0_attn_bk, L0_attn_bv, L0_ffn_w1, L0_ffn_w2,
                 L0_ffn_w3):
    f = np.float32
    x = np.asarray(L0_attn_x, f)[0]
    xT = np.ascontiguousarray(x.T)
    w1 = np.asarray(L0_ffn_w1, f)
    w2 = np.asarray(L0_ffn_w2, f)
    w3 = np.asarray(L0_ffn_w3, f)

    def panels(w, n):
        wT = np.ascontiguousarray(np.asarray(w, f).T)
        return np.ascontiguousarray(
            wT.reshape(wT.shape[0], n, 128).transpose(1, 0, 2))

    shared = {
        "w1p": panels(w1, HCH),
        "w2p": panels(w2, HCH),
        "w3p": panels(w3, OCH),
        "wqp": panels(L0_attn_wq, 16),
        "wkp": panels(L0_attn_wk, 2),
        "wvt": np.ascontiguousarray(np.asarray(L0_attn_wv, f).T),
        "wop": panels(L0_attn_wo, 16),
        "bqp": np.ascontiguousarray(
            np.asarray(L0_attn_bq, f).reshape(16, 128).T),
        "bkp": np.ascontiguousarray(
            np.asarray(L0_attn_bk, f).reshape(2, 128).T),
        "bvr": np.ascontiguousarray(np.asarray(L0_attn_bv, f).reshape(1, 256)),
    }
    gsz = D * HID // NCORES
    in_maps = []
    for i in range(NCORES):
        bA, bB = i, 15 - i
        cols = np.r_[bA * BS:(bA + 1) * BS, bB * BS:(bB + 1) * BS]
        m = dict(shared)
        m["xT"] = np.ascontiguousarray(xT[:, cols])
        m["pos"] = np.ascontiguousarray(cols.astype(f).reshape(1, NT))
        kv = i // 4
        m["selv"] = np.array([[kv * 131072, kv * 128, i * 262144,
                               bA * BS, bB * BS, 0, 0, 0]], np.int32)
        for name, w in (("g1", w1), ("g2", w2), ("g3", w3)):
            m[name] = np.ascontiguousarray(
                w.reshape(-1)[i * gsz:(i + 1) * gsz].reshape(128, GSL))
        in_maps.append(m)
    return in_maps


def _assemble(results):
    fullT = np.empty((D, L), np.float32)
    for i, r in enumerate(results):
        yT = np.asarray(r["yT"]).reshape(D, NT)
        bA, bB = i, 15 - i
        fullT[:, bA * BS:(bA + 1) * BS] = yT[:, 0:BS]
        fullT[:, bB * BS:(bB + 1) * BS] = yT[:, BS:NT]
    return np.ascontiguousarray(fullT.T).reshape(1, L, D)


def kernel(**inputs):
    in_maps = _prep_inputs(**inputs)
    nc = build_program("full")
    res = run_bass_kernel_spmd(nc, in_maps, core_ids=list(range(NCORES)))
    return _assemble(res.results)


# version 06
# speedup vs baseline: 1.5004x; 1.0042x over previous
import contextlib
import numpy as np
import concourse.bass as bass
import concourse.mybir as mybir
from concourse.bass_utils import run_bass_kernel_spmd

F32 = mybir.dt.float32
BF16 = mybir.dt.bfloat16
I32 = mybir.dt.int32
U8 = mybir.dt.uint8
AF = mybir.ActivationFunctionType
ALU = mybir.AluOpType
AX = mybir.AxisListType.X

D = 2048
HID = 5632
L = 8192
NCORES = 8
NT = 1024
NB = 16
BS = 512
KCH = 16
HCH = 44
OCH = 16
CSC = float(1.0 / np.sqrt(128.0))
NW = float(D * HID)
LN_EPS = 1e-5
Q_EPS = 1e-8
_2PI = np.float64(2 * np.pi)
TWO_PI_HI = float(np.float32(_2PI))
TWO_PI_LO = float(np.float32(_2PI - np.float64(np.float32(_2PI))))
INV_2PI = float(np.float32(1.0 / _2PI))
HALF_PI = float(np.float32(np.pi / 2))
LOG1E4 = float(np.log(10000.0))
GSL = D * HID // NCORES // 128
DBG = False

ENGS = ("sync", "scalar", "vector", "tensor", "gpsimd")
DMA_SEMS = {"s_ld", "s_sc", "s_gp", "s_out"}
OWN = {"tensor": "s_mm", "vector": "s_ve", "scalar": "s_se", "gpsimd": "s_gq"}

XLN = 0
HAR = 32768
STR = 122880
TAIL = 188416
PER = 206848
ARENA_B = 210944

DTS = {F32: 4, BF16: 2, I32: 4}


def _rank_slot(b):
    return (b, 0) if b <= 7 else (15 - b, 1)


class P:
    def __init__(self, nc):
        self.nc = nc
        self.q = {e: [] for e in ENGS}
        self.cnt = {}

    def tok(self, sem):
        return (sem, self.cnt.get(sem, 0))

    def op(self, eng, fn, waits=(), inc=None, incn=None, hard=()):
        incs = []
        if inc is not None:
            n = incn if incn is not None else (16 if inc in DMA_SEMS else 1)
            self.cnt[inc] = self.cnt.get(inc, 0) + n
            incs.append((inc, n))
        ws = {}
        hw = {}
        for w in waits:
            if w is None:
                continue
            s, v = w
            if v > 0 and v > ws.get(s, 0):
                ws[s] = v
        for w in hard:
            if w is None:
                continue
            s, v = w
            if v > 0 and v > hw.get(s, 0):
                hw[s] = v
        self.q[eng].append((sorted(ws.items()), sorted(hw.items()), fn, incs))
        return self.tok(inc) if inc is not None else None

    def seq(self, eng, fns, waits=(), hard=()):
        own = OWN[eng]
        t = None
        for i, fn in enumerate(fns):
            self.op(eng, fn, waits=waits if i == 0 else (),
                    hard=(list(hard) + ([t] if t else [])) if i == 0
                    else [t], inc=own)
            t = self.tok(own)
        return t

    def emit(self, sems):
        guar = {e: {} for e in ENGS}
        with self.nc.Block() as blk:
            for eng in ENGS:
                def body(h, eng=eng):
                    g = guar[eng]
                    own = OWN.get(eng)
                    for ws, hws, fn, incs in self.q[eng]:
                        for s, v in ws:
                            if s == own:
                                continue
                            if v > g.get(s, 0):
                                h.wait_ge(sems[s], v)
                                g[s] = v
                        for s, v in hws:
                            if v > g.get(s, 0):
                                h.wait_ge(sems[s], v)
                                g[s] = v
                        inst = fn(h)
                        for s, n in incs:
                            inst = inst.then_inc(sems[s], n)
                getattr(blk, eng)(body)


def build_program(stop="full"):
    nc = bass.Bass("TRN2", target_bir_lowering=False)

    t_xT = nc.dram_tensor("xT", [D, NT], F32, kind="ExternalInput")
    t_pos = nc.dram_tensor("pos", [1, NT], F32, kind="ExternalInput")
    t_g = [nc.dram_tensor("g%d" % (j + 1), [128, GSL], F32,
                          kind="ExternalInput") for j in range(3)]
    t_w1p = nc.dram_tensor("w1p", [HCH, D, 128], F32, kind="ExternalInput")
    t_w2p = nc.dram_tensor("w2p", [HCH, D, 128], F32, kind="ExternalInput")
    t_w3p = nc.dram_tensor("w3p", [OCH, HID, 128], F32, kind="ExternalInput")
    t_wqp = nc.dram_tensor("wqp", [16, D, 128], F32, kind="ExternalInput")
    t_wkp = nc.dram_tensor("wkp", [2, D, 128], F32, kind="ExternalInput")
    t_wvt = nc.dram_tensor("wvt", [D, 256], F32, kind="ExternalInput")
    t_wop = nc.dram_tensor("wop", [16, D, 128], F32, kind="ExternalInput")
    t_bq = nc.dram_tensor("bqp", [128, 16], F32, kind="ExternalInput")
    t_bk = nc.dram_tensor("bkp", [128, 2], F32, kind="ExternalInput")
    t_bv = nc.dram_tensor("bvr", [1, 256], F32, kind="ExternalInput")
    t_sel = nc.dram_tensor("selv", [1, 8], I32, kind="ExternalInput")
    t_out = nc.dram_tensor("yT", [OCH, 128, NT], F32, kind="ExternalOutput")

    t_w1q = nc.dram_tensor("w1q_d", [HCH, 128, D], BF16)
    t_w2q = nc.dram_tensor("w2q_d", [HCH, 128, D], BF16)
    t_w3q = nc.dram_tensor("w3q_d", [OCH, 128, HID], BF16)
    t_xmid = nc.dram_tensor("xmid_d", [KCH, 128, NT], F32)
    t_arin = nc.dram_tensor("ar_in", [1, 8], F32)
    t_arout = nc.dram_tensor("ar_out", [1, 8], F32)
    t_taud = nc.dram_tensor("tau_d", [1, 8], F32)
    t_bshd = nc.dram_tensor("bsh_d", [1, 2], F32)
    t_rpd = nc.dram_tensor("rp_d", [2, 512], F32)
    t_wqb = nc.dram_tensor("wqb_d", [16, D, 128], BF16)
    t_wkb = nc.dram_tensor("wkb_d", [2, D, 128], BF16)
    t_wvb = nc.dram_tensor("wvb_d", [D, 256], BF16)
    t_wob = nc.dram_tensor("wob_d", [16, D, 128], BF16)
    t_kvin = nc.dram_tensor("kvag_in", [1, 524288], BF16)
    t_kvout = nc.dram_tensor("kvag_out", [8, 524288], BF16, addr_space="Shared")
    t_qin = nc.dram_tensor("qag_in", [D, NT], BF16)
    t_qout = nc.dram_tensor("qag_out", [8 * D, NT], BF16, addr_space="Shared")
    t_oinH = [nc.dram_tensor("oag_in%d" % k, [1, 128, L], BF16)
              for k in range(2)]
    t_dbg2 = nc.dram_tensor("dbg2", [2, 128, NT], F32, kind="ExternalOutput")
    t_dbg3 = nc.dram_tensor("dbg3", [1, 128, NT], F32, kind="ExternalOutput")
    t_ooutH = [nc.dram_tensor("oag_out%d" % k, [8, 128, L], BF16,
                              addr_space="Shared") for k in range(2)]

    RG = [list(range(NCORES))]

    es = contextlib.ExitStack()
    with es:
        arena = es.enter_context(nc.sbuf_tensor("arena", [128, ARENA_B], U8))
        psA = es.enter_context(nc.psum_tensor("psA", [128, 4096], F32))
        sems = {}
        for s in ("s_ld", "s_sc", "s_gp", "s_out", "s_cc",
                  "s_mm", "s_ve", "s_se", "s_gq"):
            sems[s] = es.enter_context(nc.semaphore(s))

        def V(off, n, dt):
            return arena[:, off:off + n * DTS[dt]].bitcast(dt)

        ones_f = V(PER + 0, 1, F32)
        ones_b = V(PER + 64, 1, BF16)
        onesrow_f = V(PER + 128, 128, F32)[0:1, :]
        taus = V(PER + 704, 8, F32)
        bq_s = V(PER + 768, 16, F32)
        bk_s = V(PER + 832, 2, F32)
        bshift = V(PER + 896, 2, F32)
        invf = V(PER + 960, 1, F32)
        pv_t = V(PER + 1024, 1, F32)
        gtmp = V(PER + 1088, 1, F32)
        bvbc = V(PER + 1152, 256, BF16)
        sbsel = V(PER + 1600, 8, I32)
        prow_full = V(PER + 1664, 256, F32)

        def prow(a, b):
            return prow_full[0:1, a:b]

        xln = V(XLN, 16384, BF16)
        lnscr = V(XLN, 8192, F32)
        stream = V(STR, 16384, F32)
        hbuf = V(HAR, 45056, BF16)

        p = P(nc)
        tk = {}

        x0 = V(HAR, 16384, F32)
        gbuf = [V(STR + k * 11264, 2816, F32) for k in (0, 1)]
        xsq = [V(TAIL + k * 4096, 1024, F32) for k in (0, 1)]
        racc = V(TAIL + 8192, 16, F32)

        p.op("gpsimd", lambda h: h.memset(ones_f[:, :], 1.0), inc="s_gq")
        p.op("gpsimd", lambda h: h.memset(ones_b[:, :], 1.0), inc="s_gq")
        p.op("gpsimd", lambda h: h.memset(onesrow_f, 1.0), inc="s_gq")
        tk["consts"] = p.tok("s_gq")

        p.op("sync", lambda h: h.dma_start(
            out=x0[:, :].rearrange("p (c t) -> p c t", c=KCH),
            in_=t_xT.ap().rearrange("(c p) t -> p c t", p=128)), inc="s_ld")
        tk["x0"] = p.tok("s_ld")
        p.op("sync", lambda h: h.dma_start(out=bq_s[:, :], in_=t_bq.ap()),
             inc="s_ld")
        p.op("sync", lambda h: h.dma_start(out=sbsel[0:1, :], in_=t_sel.ap()),
             inc="s_ld")
        tk["sel"] = p.tok("s_ld")
        p.op("sync", lambda h: h.dma_start(out=bk_s[:, :], in_=t_bk.ap()),
             inc="s_ld")
        tk["bias"] = p.tok("s_ld")
        p.op("gpsimd", lambda h: h.dma_start(
            out=bvbc[:, :], in_=bass.AP(t_bv, 0, [[0, 128], [1, 256]])),
            inc="s_gp")
        tk["bvbc"] = p.tok("s_gp")

        for srcw, dstw in ((t_wqp, t_wqb), (t_wkp, t_wkb), (t_wop, t_wob)):
            n = srcw.shape[0]
            for ci in range(n):
                p.op("gpsimd", lambda h, srcw=srcw, dstw=dstw, ci=ci:
                     h.dma_start(out=dstw.ap()[ci], in_=srcw.ap()[ci]),
                     inc="s_gp")
        p.op("gpsimd", lambda h: h.dma_start(out=t_wvb.ap(),
                                             in_=t_wvt.ap()), inc="s_gp")
        tk["wcast"] = p.tok("s_gp")

        gred = [None] * 12
        for j in range(3):
            for c in range(4):
                i = 4 * j + c
                par = i % 2
                w = [gred[i - 2]] if i >= 2 else []
                p.op("scalar", lambda h, j=j, c=c, par=par: h.dma_start(
                    out=gbuf[par][:, :],
                    in_=t_g[j].ap()[:, c * 2816:(c + 1) * 2816]),
                    waits=w, inc="s_sc")
                ldt = p.tok("s_sc")
                p.op("vector", lambda h, i=i, par=par: h.tensor_reduce(
                    racc[:, i:i + 1], gbuf[par][:, :], axis=AX, op=ALU.add,
                    apply_absolute_value=True), waits=[ldt], inc="s_ve")
                gred[i] = p.tok("s_ve")
        p.op("tensor", lambda h: h.matmul(
            psA[0:1, 0:12], ones_f[:, :], racc[:, 0:12], start=True,
            stop=True), waits=[gred[11], tk["consts"]], inc="s_mm")
        tgmm = p.tok("s_mm")

        tk["gpart"] = p.seq("vector", [
            lambda h: h.tensor_copy(prow(0, 12), psA[0:1, 0:12]),
            lambda h: h.reduce_sum(prow(12, 13), prow(0, 4), axis=AX),
            lambda h: h.reduce_sum(prow(13, 14), prow(4, 8), axis=AX),
            lambda h: h.reduce_sum(prow(14, 15), prow(8, 12), axis=AX),
        ], waits=[tgmm])
        p.op("gpsimd", lambda h: h.dma_start(out=t_arin.ap(),
                                             in_=prow(12, 20)),
             waits=[tk["gpart"]], inc="s_gp")
        t_ = p.tok("s_gp")
        p.op("gpsimd", lambda h: h.collective_compute(
            "AllReduce", ALU.add, replica_groups=RG,
            ins=[t_arin.ap().opt()], outs=[t_arout.ap().opt()]),
            waits=[t_], inc="s_cc", incn=1)
        tar = p.tok("s_cc")
        p.op("gpsimd", lambda h: h.dma_start(out=prow(24, 32),
                                             in_=t_arout.ap()),
             waits=[tar], inc="s_gp")
        tab = p.tok("s_gp")

        ttr = p.seq("vector", [
            lambda h: h.tensor_scalar(prow(32, 35), prow(24, 27), 1.0 / NW,
                                      None, ALU.mult),
            lambda h: h.tensor_scalar(prow(35, 38), prow(32, 35), 0.5,
                                      0.5 * Q_EPS, ALU.mult, ALU.add),
            lambda h: h.tensor_scalar(prow(38, 41), prow(35, 38), -1.0,
                                      None, ALU.mult),
        ], waits=[tab])
        p.op("gpsimd", lambda h: h.dma_start(out=t_taud.ap(),
                                             in_=prow(35, 43)),
             waits=[ttr], inc="s_gp")
        t_ = p.tok("s_gp")
        p.op("gpsimd", lambda h: h.dma_start(
            out=taus[:, :], in_=bass.AP(t_taud, 0, [[0, 128], [1, 8]])),
            waits=[t_], inc="s_gp")
        tk["taus"] = p.tok("s_gp")

        def emit_ln(src, src_toks, extra_mm_waits=()):
            mm_done = [None] * KCH
            for kc in range(KCH):
                par = kc % 2
                w = [src_toks(kc)] if src_toks else []
                if kc >= 2 and mm_done[kc - 2] is not None:
                    w = w + [mm_done[kc - 2]]
                p.op("scalar", lambda h, kc=kc, par=par: h.activation(
                    xsq[par][:, :], src(kc), AF.Square), waits=w, inc="s_se")
                sqt = p.tok("s_se")
                mw = [sqt, tk["consts"]] + list(extra_mm_waits) + \
                    ([src_toks(kc)] if src_toks else [])
                for seg in range(4):
                    def mk(h, kc=kc, seg=seg, par=par):
                        if seg < 2:
                            rh = src(kc)[:, seg * 512:(seg + 1) * 512]
                        else:
                            rh = xsq[par][:, (seg - 2) * 512:(seg - 1) * 512]
                        return h.matmul(psA[0:1, seg * 512:(seg + 1) * 512],
                                        ones_f[:, :], rh,
                                        start=(kc == 0), stop=(kc == KCH - 1))
                    p.op("tensor", mk, waits=mw if seg == 0 else [sqt],
                         inc="s_mm" if seg == 3 else None)
                mm_done[kc] = p.tok("s_mm")
            tsum = mm_done[KCH - 1]

            M, VE, T, R0, RS = 0, 1024, 2048, 3072, 4096

            t1 = p.seq("vector", [
                lambda h: h.tensor_scalar(lnscr[0:1, M:M + 1024],
                                          psA[0:1, 0:1024], 1.0 / D, None,
                                          ALU.mult),
                lambda h: h.tensor_scalar(lnscr[0:1, VE:VE + 1024],
                                          psA[0:1, 1024:2048], 1.0 / D,
                                          LN_EPS, ALU.mult, ALU.add),
                lambda h: h.tensor_tensor(lnscr[0:1, T:T + 1024],
                                          lnscr[0:1, M:M + 1024],
                                          lnscr[0:1, M:M + 1024], ALU.mult),
                lambda h: h.tensor_tensor(lnscr[0:1, VE:VE + 1024],
                                          lnscr[0:1, VE:VE + 1024],
                                          lnscr[0:1, T:T + 1024],
                                          ALU.subtract),
                lambda h: h.reciprocal(lnscr[0:1, R0:R0 + 1024],
                                       lnscr[0:1, VE:VE + 1024]),
            ], waits=[tsum])
            p.op("scalar", lambda h: h.activation(
                lnscr[0:1, RS:RS + 1024], lnscr[0:1, R0:R0 + 1024], AF.Sqrt),
                waits=[t1], inc="s_se")
            t2 = p.tok("s_se")

            t3 = p.seq("vector", [
                lambda h: h.tensor_tensor(lnscr[0:1, T:T + 1024],
                                          lnscr[0:1, RS:RS + 1024],
                                          lnscr[0:1, RS:RS + 1024],
                                          ALU.mult),
                lambda h: h.tensor_tensor(lnscr[0:1, T:T + 1024],
                                          lnscr[0:1, T:T + 1024],
                                          lnscr[0:1, VE:VE + 1024],
                                          ALU.mult),
                lambda h: h.tensor_scalar(lnscr[0:1, T:T + 1024],
                                          lnscr[0:1, T:T + 1024], -0.5, 1.5,
                                          ALU.mult, ALU.add),
                lambda h: h.tensor_tensor(lnscr[0:1, RS:RS + 1024],
                                          lnscr[0:1, RS:RS + 1024],
                                          lnscr[0:1, T:T + 1024], ALU.mult),
                lambda h: h.tensor_tensor(lnscr[0:1, M:M + 1024],
                                          lnscr[0:1, M:M + 1024],
                                          lnscr[0:1, RS:RS + 1024],
                                          ALU.mult),
            ], waits=[t2])
            for i in range(4):
                coff = RS if i < 2 else M
                boff = 2048 + i * 512
                p.op("tensor", lambda h, coff=coff, boff=boff, i=i: h.matmul(
                    psA[:, boff:boff + 512], onesrow_f,
                    lnscr[0:1, coff + (i % 2) * 512:
                          coff + (i % 2) * 512 + 512],
                    start=True, stop=True),
                    waits=[t3, tk["consts"]], inc="s_mm" if i == 3 else None)
            tbc = p.tok("s_mm")
            last = None
            for kc in range(KCH):
                par = kc % 2

                def _x(h, kc=kc, par=par):
                    h.tensor_tensor(xsq[par][:, :], src(kc),
                                    psA[:, 2048:3072], ALU.mult)
                    return h.tensor_tensor(
                        xln[:, kc * 1024:(kc + 1) * 1024], xsq[par][:, :],
                        psA[:, 3072:4096], ALU.subtract)
                p.op("vector", _x, waits=[tbc] +
                     ([src_toks(kc)] if src_toks else []), inc="s_ve")
                last = p.tok("s_ve")
            return last

        tk["xln"] = emit_ln(lambda kc: x0[:, kc * 1024:(kc + 1) * 1024],
                            lambda kc: tk["x0"],
                            extra_mm_waits=[tk["gpart"]])

        def emit_ffn_B(pre, entry_waits):
            if pre:
                wraw = [[V(STR + k * 8192, 2048, F32) for k in (0, 1)],
                        [V(STR + 16384 + k * 8192, 2048, F32)
                         for k in (0, 1)]]
                wqv = [[V(STR + 32768 + k * 4096, 2048, BF16)
                        for k in (0, 1)],
                       [V(STR + 40960 + k * 4096, 2048, BF16)
                        for k in (0, 1)]]
                ltmp = V(STR + 49152, 2048, BF16)
                sigb = V(STR + 53248, 1024, BF16)
            else:
                wqv = [[V(TAIL + k * 4096, 2048, BF16) for k in (0, 1)],
                       [V(TAIL + 8192 + k * 4096, 2048, BF16)
                        for k in (0, 1)]]
                sigb = V(TAIL + 16384, 1024, BF16)
            xb = [[psA[:, 0:1024], psA[:, 1024:2048]],
                  [psA[:, 2048:3072], psA[:, 3072:4096]]]
            wsrc = (t_w1p, t_w2p)
            wdst = (t_w1q, t_w2q)
            tcols = ((0, 3), (1, 4))
            qtok = [[None] * HCH, [None] * HCH]
            stok = [[None] * HCH, [None] * HCH]
            mmtok = [[None] * HCH, [None] * HCH]
            sigtok = [None] * HCH
            htok = [None] * HCH
            for hc in range(HCH):
                par = hc % 2
                for wi in range(2):
                    fw = [mmtok[wi][hc - 2]] if hc >= 2 else list(entry_waits)
                    if pre:
                        if hc >= 2:
                            fw.append(qtok[wi][hc - 2])
                        p.op("sync", lambda h, wi=wi, hc=hc, par=par:
                             h.dma_start(
                                 out=wraw[wi][par][:, :].rearrange(
                                     "p (kc m) -> p kc m", kc=KCH),
                                 in_=bass.AP(wsrc[wi], hc * D * 128,
                                             [[128, 128], [16384, KCH],
                                              [1, 128]])),
                             waits=fw, inc="s_ld")
                        ldt = p.tok("s_ld")
                        qw = [ldt, tk["taus"]]
                        if hc >= 2:
                            qw += [mmtok[wi][hc - 2], stok[wi][hc - 2]]
                        if hc == 0 and wi == 0:
                            qw += list(entry_waits)

                        def _q(h, wi=wi, par=par):
                            tc, nc_ = tcols[wi]
                            h.tensor_single_scalar(
                                ltmp[:, :], wraw[wi][par][:, :],
                                taus[:, nc_:nc_ + 1], ALU.is_lt)
                            return h.scalar_tensor_tensor(
                                wqv[wi][par][:, :], wraw[wi][par][:, :],
                                taus[:, tc:tc + 1], ltmp[:, :],
                                ALU.is_gt, ALU.subtract)
                        p.op("vector", _q, waits=qw, inc="s_ve")
                        qtok[wi][hc] = p.tok("s_ve")
                        p.op("gpsimd", lambda h, wi=wi, hc=hc, par=par:
                             h.dma_start(out=wdst[wi].ap()[hc, :, :],
                                         in_=wqv[wi][par][:, :]),
                             waits=[qtok[wi][hc]], inc="s_gp")
                        stok[wi][hc] = p.tok("s_gp")
                    else:
                        p.op("sync", lambda h, wi=wi, hc=hc, par=par:
                             h.dma_start(out=wqv[wi][par][:, :],
                                         in_=wdst[wi].ap()[hc, :, :]),
                             waits=fw, inc="s_ld")
                        qtok[wi][hc] = p.tok("s_ld")
                for wi in range(2):
                    mw = [qtok[wi][hc], tk["xln"]]
                    if hc >= 2:
                        mw.append(sigtok[hc - 2] if wi == 0
                                  else htok[hc - 2])
                    if hc < 2:
                        mw += list(entry_waits)
                    for kc in range(KCH):
                        for half in (0, 1):
                            lst = (kc == KCH - 1 and half == 1)
                            p.op("tensor", lambda h, wi=wi, par=par, kc=kc,
                                 half=half: h.matmul(
                                     xb[wi][par][:, half * 512:
                                                 (half + 1) * 512],
                                     wqv[wi][par][:, kc * 128:
                                                  (kc + 1) * 128],
                                     xln[:, kc * 1024 + half * 512:
                                         kc * 1024 + half * 512 + 512],
                                     start=(kc == 0), stop=(kc == KCH - 1)),
                                 waits=mw if (kc == 0 and half == 0) else [],
                                 inc="s_mm" if lst else None)
                    mmtok[wi][hc] = p.tok("s_mm")
                sw = [mmtok[0][hc]]
                if hc >= 1:
                    sw.append(htok[hc - 1])
                p.op("scalar", lambda h, par=par: h.activation(
                    sigb[:, :], xb[0][par][:, :], AF.Sigmoid),
                    waits=sw, inc="s_se")
                sigtok[hc] = p.tok("s_se")
                p.op("vector", lambda h, hc=hc, par=par: h.tensor_tensor(
                    hbuf[:, hc * 1024:(hc + 1) * 1024], sigb[:, :],
                    xb[1][par][:, :], ALU.mult),
                    waits=[sigtok[hc], mmtok[1][hc]], inc="s_ve")
                htok[hc] = p.tok("s_ve")
            return htok, sigtok, stok

        def emit_ffn_C(pre, htok, sigtok, resid, centry=()):
            if pre:
                w3raw = [V(STR + k * 5632, 1408, F32) for k in (0, 1, 2)]
                w3qs = [V(STR + 16896 + k * 2816, 1408, BF16)
                        for k in (0, 1, 2)]
                ltmp2 = V(STR + 25344, 1408, BF16)
                xres = [V(STR + 28288 + k * 4096, 1024, F32)
                        for k in (0, 1, 2, 3)]
                outb = [V(STR + 44672 + k * 4096, 1024, F32) for k in (0, 1)]
            else:
                w3qs = [V(XLN + k * 2816, 1408, BF16) for k in (0, 1, 2)]
                outb = [V(XLN + 8448 + k * 4096, 1024, F32) for k in (0, 1)]
            yb = [psA[:, 0:1024], psA[:, 1024:2048]]
            NS = 4 * OCH
            ld_tok = [None] * NS
            qv_tok = [None] * NS
            sc_tok = [None] * NS
            mm_sub = [None] * NS
            eptok = [None] * OCH
            dmtok = [None] * OCH
            entry = [sigtok[HCH - 1], htok[HCH - 1]] + list(centry)
            entry = entry + [p.tok("s_mm")]
            p.op("gpsimd", lambda h: h.nop(),
                 waits=[sigtok[HCH - 1], htok[HCH - 1]], inc="s_gq")
            entry = entry + [p.tok("s_gq")]

            def prep(si):
                oc, j = si // 4, si % 4
                jp = si % 3
                fw = [mm_sub[si - 3]] if si >= 3 else list(entry)
                if pre:
                    lw = list(fw)
                    if si >= 3:
                        lw.append(qv_tok[si - 3])
                    else:
                        lw += list(entry)
                    p.op("sync", lambda h, oc=oc, j=j, jp=jp:
                         h.dma_start(
                             out=w3raw[jp][:, :].rearrange(
                                 "p (kc m) -> p kc m", kc=11),
                             in_=bass.AP(t_w3p,
                                         oc * HID * 128 + j * 1408 * 128,
                                         [[128, 128], [16384, 11],
                                          [1, 128]])),
                         waits=lw, inc="s_ld")
                    ldt = p.tok("s_ld")
                    qw = [ldt, tk["taus"]] + fw
                    if si >= 3:
                        qw.append(sc_tok[si - 3])

                    def _q3(h, jp=jp):
                        h.tensor_single_scalar(
                            ltmp2[:, :], w3raw[jp][:, :], taus[:, 5:6],
                            ALU.is_lt)
                        return h.scalar_tensor_tensor(
                            w3qs[jp][:, :], w3raw[jp][:, :],
                            taus[:, 2:3], ltmp2[:, :],
                            ALU.is_gt, ALU.subtract)
                    p.op("vector", _q3, waits=qw, inc="s_ve")
                    qv_tok[si] = p.tok("s_ve")
                    ld_tok[si] = qv_tok[si]
                    p.op("gpsimd", lambda h, oc=oc, j=j, jp=jp:
                         h.dma_start(
                             out=t_w3q.ap()[oc, :,
                                            j * 1408:(j + 1) * 1408],
                             in_=w3qs[jp][:, :]),
                         waits=[qv_tok[si]], inc="s_gp")
                    sc_tok[si] = p.tok("s_gp")
                else:
                    p.op("sync", lambda h, oc=oc, j=j, jp=jp:
                         h.dma_start(
                             out=w3qs[jp][:, :],
                             in_=t_w3q.ap()[oc, :,
                                            j * 1408:(j + 1) * 1408]),
                         waits=fw, inc="s_ld")
                    ld_tok[si] = p.tok("s_ld")

            prep(0)
            prep(1)
            prep(2)
            for oc in range(OCH):
                par = oc % 2
                for kc in range(HCH):
                    si = 4 * oc + kc // 11
                    jp = si % 3
                    kw = [ld_tok[si], htok[kc]]
                    if kc == 0:
                        kw.append(htok[0])
                        if oc >= 2:
                            kw.append(eptok[oc - 2])
                        else:
                            kw += list(entry)
                    for half in (0, 1):
                        sub_last = (kc % 11 == 10 and half == 1)
                        p.op("tensor", lambda h, par=par, kc=kc, jp=jp,
                             half=half: h.matmul(
                                 yb[par][:, half * 512:(half + 1) * 512],
                                 w3qs[jp][:, (kc % 11) * 128:
                                          (kc % 11) * 128 + 128],
                                 hbuf[:, kc * 1024 + half * 512:
                                      kc * 1024 + half * 512 + 512],
                                 start=(kc == 0), stop=(kc == HCH - 1)),
                             waits=kw if half == 0 else [],
                             inc="s_mm" if sub_last else None)
                        if sub_last:
                            mm_sub[si] = p.tok("s_mm")
                    if kc % 11 == 10 and si + 3 < NS:
                        prep(si + 3)
                ymm = mm_sub[4 * oc + 3]
                if resid == "xT":
                    rw = [eptok[oc - 4]] if oc >= 4 else \
                        [htok[HCH - 1]] + list(centry)
                    p.op("scalar", lambda h, oc=oc: h.dma_start(
                        out=xres[oc % 4][:, :],
                        in_=t_xT.ap()[oc * 128:(oc + 1) * 128, :]),
                        waits=rw, inc="s_sc")
                    rt = [p.tok("s_sc")]
                    rsl = lambda oc=oc: xres[oc % 4][:, :]
                else:
                    rt = []
                    rsl = lambda oc=oc: stream[:, oc * 1024:(oc + 1) * 1024]
                ew = [ymm] + rt
                if oc >= 2:
                    ew.append(dmtok[oc - 2])
                else:
                    ew += [htok[HCH - 1]] + list(centry)
                if DBG and not pre and oc == 0:
                    dbg3buf = V(TAIL, 1024, F32)
                    p.op("scalar", lambda h, par=par: h.copy(
                        dbg3buf[:, :], yb[par][:, :]), waits=[ymm],
                        inc="s_se")
                    t_d3 = p.tok("s_se")
                    p.op("gpsimd", lambda h: h.dma_start(
                        out=t_dbg3.ap()[0, :, :], in_=dbg3buf[:, :]),
                        waits=[t_d3], inc="s_gp")
                    ew.append(t_d3)
                p.op("vector", lambda h, par=par, rsl=rsl:
                     h.scalar_tensor_tensor(
                         outb[par][:, :], yb[par][:, :], 0.5, rsl(),
                         ALU.mult, ALU.add), waits=ew, inc="s_ve")
                eptok[oc] = p.tok("s_ve")
                if pre:
                    p.op("gpsimd", lambda h, oc=oc, par=par: h.dma_start(
                        out=t_xmid.ap()[oc, :, :], in_=outb[par][:, :]),
                        waits=[eptok[oc]], inc="s_gp")
                    dmtok[oc] = p.tok("s_gp")
                    if stop == "ffnpre":
                        p.op("gpsimd", lambda h, oc=oc, par=par: h.dma_start(
                            out=t_out.ap()[oc, :, :], in_=outb[par][:, :]),
                            waits=[eptok[oc]], inc="s_out")
                        dmtok[oc] = p.tok("s_out")
                else:
                    p.op("gpsimd", lambda h, oc=oc, par=par: h.dma_start(
                        out=t_out.ap()[oc, :, :], in_=outb[par][:, :]),
                        waits=[eptok[oc]], inc="s_out")
                    dmtok[oc] = p.tok("s_out")
                    if DBG and oc < 2:
                        p.op("gpsimd", lambda h, oc=oc, par=par: h.dma_start(
                            out=t_dbg2.ap()[oc, :, :], in_=outb[par][:, :]),
                            waits=[eptok[oc]], inc="s_gp")
            return eptok, dmtok

        htokB, sigtokB, stokB = emit_ffn_B(True, entry_waits=[gred[11]])
        eptokC, dmtokC = emit_ffn_C(
            True, htokB, sigtokB, resid="xT",
            centry=[stokB[0][HCH - 1], stokB[1][HCH - 1]])

        if stop == "ffnpre":
            if DBG:
                end = [p.tok("s_ve"), p.tok("s_mm"), p.tok("s_se"),
                       p.tok("s_gp"), p.tok("s_ld"), p.tok("s_sc")]
                dump = nc.dram_tensor("dbg", [8, 128, 1024], F32,
                                      kind="ExternalOutput")
                p.op("gpsimd", lambda h: h.dma_start(
                    out=dump.ap()[0, 0:1, 0:1024],
                    in_=V(PER, 1024, F32)[0:1, :]), waits=end, inc="s_out")
                p.op("gpsimd", lambda h: h.dma_start(
                    out=dump.ap()[1, :, :],
                    in_=xln[:, 0:1024]), waits=end, inc="s_out")
                p.op("gpsimd", lambda h: h.dma_start(
                    out=dump.ap()[2, :, :],
                    in_=hbuf[:, 0:1024]), waits=end, inc="s_out")
                p.op("gpsimd", lambda h: h.dma_start(
                    out=dump.ap()[3, :, :],
                    in_=t_w1q.ap()[0, :, 0:1024]), waits=end, inc="s_out")
                p.op("gpsimd", lambda h: h.dma_start(
                    out=dump.ap()[4, :, :],
                    in_=x0[:, 0:1024]), waits=end, inc="s_out")
                p.op("gpsimd", lambda h: h.dma_start(
                    out=dump.ap()[5, 0:1, :],
                    in_=lnscr[0:1, 4096:5120]), waits=end, inc="s_out")
                dmph = nc.dram_tensor("dbg_h", [44, 128, 1024], F32,
                                      kind="ExternalOutput")
                for hc in range(HCH):
                    p.op("gpsimd", lambda h, hc=hc: h.dma_start(
                        out=dmph.ap()[hc, :, :],
                        in_=hbuf[:, hc * 1024:(hc + 1) * 1024]),
                        waits=end, inc="s_out")
                dmpw = nc.dram_tensor("dbg_w3", [OCH, 128, HID], F32,
                                      kind="ExternalOutput")
                for oc in range(OCH):
                    p.op("gpsimd", lambda h, oc=oc: h.dma_start(
                        out=dmpw.ap()[oc, :, :],
                        in_=t_w3q.ap()[oc, :, :]),
                        waits=end, inc="s_out")
                dmpx = nc.dram_tensor("dbg_xln", [KCH, 128, 1024], F32,
                                      kind="ExternalOutput")
                for kc in range(KCH):
                    p.op("gpsimd", lambda h, kc=kc: h.dma_start(
                        out=dmpx.ap()[kc, :, :],
                        in_=xln[:, kc * 1024:(kc + 1) * 1024]),
                        waits=end, inc="s_out")
            p.op("gpsimd",
                 lambda h: h.wait_ge(sems["s_out"],
                                     16 * (OCH + (6 + HCH + OCH + KCH
                                                  if DBG else 0))))
            p.emit(sems)
            return nc

        p.op("sync", lambda h: h.dma_start(
            out=stream[:, :].rearrange("p (c t) -> p c t", c=KCH),
            in_=t_xmid.ap().rearrange("c p t -> p c t")),
            waits=[dmtokC[OCH - 1], eptokC[OCH - 1]], inc="s_ld")
        tk["stream"] = p.tok("s_ld")
        tk["xln"] = emit_ln(
            lambda kc: stream[:, kc * 1024:(kc + 1) * 1024],
            lambda kc: tk["stream"],
            extra_mm_waits=[eptokC[OCH - 1]])
        tk["xln2"] = tk["xln"]

        masks = V(HAR, 2048, BF16)
        kctr = V(HAR + 4096, 2048, BF16)
        vctr = V(HAR + 8192, 2048, BF16)
        cosf = V(HAR + 12288, 1024, F32)
        ssin = V(HAR + 16384, 1024, F32)
        posb = V(HAR + 20480, 1024, F32)
        tt1 = V(HAR + 24576, 1024, F32)
        kint = V(HAR + 28672, 1024, I32)
        ktf = V(HAR + 32768, 1024, F32)
        rtmp = V(HAR + 36864, 1024, F32)
        rtmp2 = V(HAR + 40960, 1024, F32)
        pm = V(HAR + 45056, 128, F32)
        pm2 = V(HAR + 45696, 128, F32)
        qrb = [V(HAR + 46336 + k * 2048, 1024, BF16) for k in (0, 1)]
        wkb = [V(HAR + 50432 + k * 4096, 2048, BF16) for k in (0, 1)]
        wvb = V(HAR + 58624, 4096, BF16)
        wqb = [V(HAR + 66816 + k * 4096, 2048, BF16) for k in (0, 1)]
        itile = V(HAR + 75008, 512, F32)
        jmi = V(HAR + 77056, 128, F32)
        Qmy = V(HAR + 4096, 16384, BF16)
        Kh = V(HAR + 36864, 8192, BF16)
        Vh = V(HAR + 53248, 8192, BF16)
        ptp = [V(HAR + 69632 + k * 2048, 1024, BF16) for k in (0, 1)]
        rbuf = [V(HAR + 73728 + k * 2048, 512, F32) for k in (0, 1)]
        qsqs = [V(HAR + 77824 + k * 1024, 512, BF16) for k in (0, 1)]
        otile = [V(HAR + 79872 + k * 1024, 512, BF16) for k in (0, 1)]
        wob = wqb
        qraw = [V(TAIL + k * 4096, 1024, F32) for k in (0, 1)]
        rp = [V(TAIL + 8192 + k * 2048, 512, F32) for k in (0, 1)]

        hfree = [p.tok("s_mm")]
        p.op("gpsimd", lambda h: h.iota(
            itile[:, :], [[-1, 512]], channel_multiplier=1,
            allow_small_or_imprecise_dtypes=True), waits=hfree, inc="s_gq")
        tio = p.tok("s_gq")
        mtk = None
        for m in range(4):
            p.op("vector", lambda h, m=m: h.tensor_single_scalar(
                masks[:, m * 512:(m + 1) * 512], itile[:, :],
                float(-128 * m), ALU.is_le),
                waits=[tio, tk["xln2"]], inc="s_ve")
            mtk = p.tok("s_ve")
        tk["masks"] = mtk
        p.op("gpsimd", lambda h: h.iota(
            jmi[:, :], [[1, 128]], channel_multiplier=-1,
            allow_small_or_imprecise_dtypes=True), waits=hfree, inc="s_gq")
        tjm = p.tok("s_gq")

        tk["pm"] = p.seq("vector", [
            lambda h: h.tensor_single_scalar(pm[:, :], jmi[:, :], 64.0,
                                             ALU.is_equal),
            lambda h: h.tensor_single_scalar(pm2[:, :], jmi[:, :], -64.0,
                                             ALU.is_equal),
            lambda h: h.tensor_tensor(pm[:, :], pm[:, :], pm2[:, :],
                                      ALU.add),
        ], waits=[tjm])

        p.op("gpsimd", lambda h: h.iota(
            pv_t[:, :], [[1, 1]], channel_multiplier=1,
            allow_small_or_imprecise_dtypes=True), inc="s_gq")
        tpv = p.tok("s_gq")

        t_ = p.seq("vector", [
            lambda h: h.tensor_single_scalar(gtmp[:, :], pv_t[:, :], 64.0,
                                             ALU.is_ge),
            lambda h: h.scalar_tensor_tensor(gtmp[:, :], gtmp[:, :], -64.0,
                                             pv_t[:, :], ALU.mult, ALU.add),
            lambda h: h.tensor_single_scalar(gtmp[:, :], gtmp[:, :],
                                             -LOG1E4 / 64.0, ALU.mult),
        ], waits=[tpv])
        p.op("scalar", lambda h: h.activation(invf[:, :], gtmp[:, :],
                                              AF.Exp), waits=[t_],
             inc="s_se")
        tk["invf"] = p.tok("s_se")
        p.op("gpsimd", lambda h: h.dma_start(
            out=posb[:, :], in_=bass.AP(t_pos, 0, [[0, 128], [1, NT]])),
            waits=hfree, inc="s_gp")
        tk["posb"] = p.tok("s_gp")
        p.op("vector", lambda h: h.tensor_single_scalar(
            posb[:, :], posb[:, :], invf[:, 0:1], ALU.mult),
            waits=[tk["invf"], tk["posb"]], inc="s_ve")
        tang = p.tok("s_ve")

        def _red(dst, shift, tprev):
            fns = []
            if shift:
                fns.append(lambda h: h.tensor_scalar(
                    tt1[:, :], posb[:, :], INV_2PI, 0.25, ALU.mult, ALU.add))
            else:
                fns.append(lambda h: h.tensor_single_scalar(
                    tt1[:, :], posb[:, :], INV_2PI, ALU.mult))
            fns.append(lambda h: h.tensor_copy(kint[:, :], tt1[:, :]))
            fns.append(lambda h: h.tensor_copy(ktf[:, :], kint[:, :]))
            fns.append(lambda h: h.scalar_tensor_tensor(
                rtmp[:, :], ktf[:, :], -TWO_PI_HI, posb[:, :], ALU.mult,
                ALU.add))
            if shift:
                fns.append(lambda h: h.tensor_single_scalar(
                    rtmp[:, :], rtmp[:, :], HALF_PI, ALU.add))
            fns.append(lambda h: h.scalar_tensor_tensor(
                rtmp[:, :], ktf[:, :], -TWO_PI_LO, rtmp[:, :], ALU.mult,
                ALU.add))
            t = p.seq("vector", fns, hard=[tprev])
            p.op("scalar", lambda h, dst=dst: h.activation(
                dst[:, :], rtmp[:, :], AF.Sin), waits=[t], inc="s_se")
            return p.tok("s_se")

        tsin = _red(ssin, False, tang)
        p.op("vector", lambda h: h.tensor_single_scalar(
            ssin[0:64, :], ssin[0:64, :], -1.0, ALU.mult),
            waits=[tsin], inc="s_ve")
        tflip = p.tok("s_ve")
        tk["rope_tab"] = _red(cosf, True, tflip)

        for c in range(2):
            p.op("sync", lambda h, c=c: h.dma_start(
                out=wkb[c][:, :].rearrange("p (kc m) -> p kc m", kc=KCH),
                in_=bass.AP(t_wkb, c * D * 128,
                            [[128, 128], [16384, KCH], [1, 128]])),
                waits=hfree + [tk["wcast"]], inc="s_ld")
        tk["wkb"] = p.tok("s_ld")
        p.op("sync", lambda h: h.dma_start(
            out=wvb[:, :].rearrange("p (kc m) -> p kc m", kc=KCH),
            in_=t_wvb.ap().rearrange("(kc p) m -> p kc m", p=128)),
            waits=hfree + [tk["wcast"]], inc="s_ld")
        tk["wvb"] = p.tok("s_ld")

        kparts = []
        for c in range(2):
            kb = psA[:, c * 1024:(c + 1) * 1024]
            swb = psA[:, 2048 + c * 1024:3072 + c * 1024]
            mw = [tk["wkb"], tk["xln2"]]
            for kc in range(KCH):
                for half in (0, 1):
                    lst = (kc == KCH - 1 and half == 1)
                    p.op("tensor", lambda h, c=c, kc=kc, half=half, kb=kb:
                         h.matmul(kb[:, half * 512:(half + 1) * 512],
                                  wkb[c][:, kc * 128:(kc + 1) * 128],
                                  xln[:, kc * 1024 + half * 512:
                                      kc * 1024 + half * 512 + 512],
                                  start=(kc == 0), stop=(kc == KCH - 1)),
                         waits=mw if (kc == 0 and half == 0) else [],
                         inc="s_mm" if lst else None)
            tmm = p.tok("s_mm")
            p.op("scalar", lambda h, c=c, kb=kb: h.activation(
                qraw[c][:, :], kb[:, :], AF.Identity, bias=bk_s[:, c:c + 1]),
                waits=[tmm], inc="s_se")
            tev = p.tok("s_se")
            p.op("tensor", lambda h, swb=swb, c=c: h.matmul(
                swb[:, 0:512], pm[:, :], qraw[c][:, 0:512], start=True,
                stop=True), waits=[tev, tk["pm"]])
            p.op("tensor", lambda h, swb=swb, c=c: h.matmul(
                swb[:, 512:1024], pm[:, :], qraw[c][:, 512:1024],
                start=True, stop=True), inc="s_mm")
            tsw = p.tok("s_mm")

            def _rope(h, c=c, swb=swb):
                h.tensor_tensor(rtmp[:, :], qraw[c][:, :], cosf[:, :],
                                ALU.mult)
                h.tensor_tensor(rtmp2[:, :], swb[:, :], ssin[:, :],
                                ALU.mult)
                return h.tensor_tensor(kctr[:, c * 1024:(c + 1) * 1024],
                                       rtmp[:, :], rtmp2[:, :], ALU.add)
            p.op("vector", _rope, waits=[tsw, tk["rope_tab"]], inc="s_ve")
            kparts.append(p.tok("s_ve"))
        tk["kctr"] = kparts[-1]

        vev = [None] * 8
        for tt in range(8):
            par = tt % 2
            vb = psA[:, par * 512:par * 512 + 256]
            w = [tk["wvb"], tk["xln2"], tk["kctr"]]
            if tt >= 2:
                w.append(vev[tt - 2])
            for kc in range(KCH):
                p.op("tensor", lambda h, tt=tt, kc=kc, vb=vb: h.matmul(
                    vb[:, :],
                    xln[:, kc * 1024 + tt * 128:kc * 1024 + tt * 128 + 128],
                    wvb[:, kc * 256:(kc + 1) * 256],
                    start=(kc == 0), stop=(kc == KCH - 1)),
                    waits=w if kc == 0 else [],
                    inc="s_mm" if kc == KCH - 1 else None)
            tv = p.tok("s_mm")
            p.op("vector", lambda h, tt=tt, vb=vb: h.tensor_tensor(
                vctr[:, tt * 256:(tt + 1) * 256], vb[:, :], bvbc[:, :],
                ALU.add), waits=[tv, tk["bvbc"]], inc="s_ve")
            vev[tt] = p.tok("s_ve")
        tk["vctr"] = vev[-1]

        p.op("gpsimd", lambda h: h.dma_start(
            out=bass.AP(t_kvin, 0, [[1024, 128], [131072, 2], [1, 1024]]),
            in_=kctr[:, :].rearrange("p (c t) -> p c t", c=2)),
            waits=[tk["kctr"]], inc="s_gp")
        p.op("gpsimd", lambda h: h.dma_start(
            out=bass.AP(t_kvin, 262144, [[256, 128], [32768, 8], [1, 256]]),
            in_=vctr[:, :].rearrange("p (tt m) -> p tt m", tt=8)),
            waits=[tk["vctr"]], inc="s_gp")
        t_ = p.tok("s_gp")
        p.op("gpsimd", lambda h: h.collective_compute(
            "AllGather", ALU.bypass, replica_groups=RG,
            ins=[t_kvin.ap().opt()], outs=[t_kvout.ap().opt()]),
            waits=[t_], inc="s_cc", incn=1)
        tk["cc_kv"] = p.tok("s_cc")

        qdma = [None] * 16
        qropet = [None] * 16
        qevt = [None] * 16

        def emit_swap_rope(c):
            par = c % 2
            swb = psA[:, 2048 + par * 1024:3072 + par * 1024]
            p.op("tensor", lambda h, swb=swb, par=par: h.matmul(
                swb[:, 0:512], pm[:, :], qraw[par][:, 0:512], start=True,
                stop=True), waits=[qevt[c], tk["pm"]] +
                ([qropet[c - 2]] if c >= 2 else []))
            p.op("tensor", lambda h, swb=swb, par=par: h.matmul(
                swb[:, 512:1024], pm[:, :], qraw[par][:, 512:1024],
                start=True, stop=True), inc="s_mm")
            tsw = p.tok("s_mm")
            qw = [tsw, tk["rope_tab"]]
            if c >= 2:
                qw.append(qdma[c - 2])

            def _ropeq(h, par=par, swb=swb):
                h.tensor_tensor(rtmp[:, :], qraw[par][:, :], cosf[:, :],
                                ALU.mult)
                h.tensor_tensor(rtmp2[:, :], swb[:, :], ssin[:, :],
                                ALU.mult)
                return h.tensor_tensor(qrb[par][:, :], rtmp[:, :],
                                       rtmp2[:, :], ALU.add)
            p.op("vector", _ropeq, waits=qw, inc="s_ve")
            qropet[c] = p.tok("s_ve")
            p.op("gpsimd", lambda h, c=c, par=par: h.dma_start(
                out=t_qin.ap()[c * 128:(c + 1) * 128, :],
                in_=qrb[par][:, :]), waits=[qropet[c]], inc="s_gp")
            qdma[c] = p.tok("s_gp")

        for c in range(16):
            par = c % 2
            qb = psA[:, par * 1024:(par + 1) * 1024]
            lw = [tk["xln2"], tk["vctr"], hfree[0]]
            if c >= 2:
                lw.append(qevt[c - 2])
            p.op("sync", lambda h, c=c, par=par: h.dma_start(
                out=wqb[par][:, :].rearrange("p (kc m) -> p kc m", kc=KCH),
                in_=bass.AP(t_wqb, c * D * 128,
                            [[128, 128], [16384, KCH], [1, 128]])),
                waits=lw + [tk["wcast"]], inc="s_ld")
            tw = p.tok("s_ld")
            mw = [tw, tk["vctr"]]
            if c >= 2:
                mw.append(qevt[c - 2])
            for kc in range(KCH):
                for half in (0, 1):
                    lst = (kc == KCH - 1 and half == 1)
                    p.op("tensor", lambda h, par=par, kc=kc, half=half,
                         qb=qb: h.matmul(
                             qb[:, half * 512:(half + 1) * 512],
                             wqb[par][:, kc * 128:(kc + 1) * 128],
                             xln[:, kc * 1024 + half * 512:
                                 kc * 1024 + half * 512 + 512],
                             start=(kc == 0), stop=(kc == KCH - 1)),
                         waits=mw if (kc == 0 and half == 0) else [],
                         inc="s_mm" if lst else None)
            tmm = p.tok("s_mm")
            sw2 = [tmm]
            if c >= 2:
                sw2.append(qropet[c - 2])
            p.op("scalar", lambda h, c=c, par=par, qb=qb: h.activation(
                qraw[par][:, :], qb[:, :], AF.Identity,
                bias=bq_s[:, c:c + 1]), waits=sw2, inc="s_se")
            qevt[c] = p.tok("s_se")
            if c == 0:
                emit_swap_rope(0)
            elif c >= 2:
                emit_swap_rope(c - 1)
        emit_swap_rope(15)
        p.op("gpsimd", lambda h: h.collective_compute(
            "AllGather", ALU.bypass, replica_groups=RG,
            ins=[t_qin.ap().opt()], outs=[t_qout.ap().opt()]),
            waits=[qdma[15]], inc="s_cc", incn=1)
        tk["cc_q"] = p.tok("s_cc")

        safe = [tk["cc_kv"], qdma[15], qropet[15], p.tok("s_mm"),
                p.tok("s_se"), tk["sel"]]
        regs = {}

        def _store_regs(h):
            regs["kvk"] = h.value_load(sbsel[0:1, 0:1])
            regs["kvv"] = h.value_load(sbsel[0:1, 1:2])
            regs["qof"] = h.value_load(sbsel[0:1, 2:3])
            regs["colA"] = h.value_load(sbsel[0:1, 3:4])
            regs["colB"] = h.value_load(sbsel[0:1, 4:5])
            return h.nop()
        p.op("sync", _store_regs, waits=safe)
        p.op("sync", lambda h: h.dma_start(
            out=Kh[:, :].rearrange("p (r t) -> p r t", r=8),
            in_=bass.AP(t_kvout, regs["kvk"],
                        [[1024, 128], [524288, 8], [1, 1024]])),
            waits=safe, inc="s_ld")
        for r in range(8):
            p.op("sync", lambda h, r=r: h.dma_start(
                out=Vh[:, r * 1024:(r + 1) * 1024].rearrange(
                    "p (j m) -> p j m", j=8),
                in_=bass.AP(t_kvout, regs["kvv"] + (262144 + r * 524288),
                            [[256, 128], [32768, 8], [1, 128]])),
                waits=safe, inc="s_ld")
        tk["kvh"] = p.tok("s_ld")
        for hh in range(2):
            p.op("sync", lambda h, hh=hh: h.dma_start(
                out=Qmy[:, hh * 8192:(hh + 1) * 8192].rearrange(
                    "p (r t) -> p r t", r=8),
                in_=bass.AP(t_qout, regs["qof"] + hh * 131072,
                            [[1024, 128], [2097152, 8], [1, 1024]])),
                waits=[tk["cc_q"]] + safe, inc="s_ld")
        tk["qmy"] = p.tok("s_ld")

        def keypos(kt):
            b = kt // 4
            r, s = _rank_slot(b)
            return r * 1024 + s * 512 + (kt % 4) * 128

        def qpos(h_i, b):
            r, s = _rank_slot(b)
            return h_i * 8192 + r * 1024 + s * 512

        red_t = [None] * 48
        idx = 0
        for bsrc, nsl, base in ((Qmy, 32, 64), (Kh, 16, 96)):
            for j in range(nsl):
                par = idx % 2
                w = [tk["qmy"], tk["kvh"], tk["masks"]]
                if idx >= 2:
                    w.append(red_t[idx - 2])
                p.op("scalar", lambda h, bsrc=bsrc, j=j, par=par:
                     h.activation(qsqs[par][:, :],
                                  bsrc[:, j * 512:(j + 1) * 512], AF.Square),
                     waits=w, inc="s_se")
                ts_ = p.tok("s_se")
                rw = [ts_, tk["consts"]]
                if idx >= 2:
                    rw.append(red_t[idx - 2])
                p.op("tensor", lambda h, par=par: h.matmul(
                    psA[0:1, par * 512:(par + 1) * 512], ones_b[:, :],
                    qsqs[par][:, :], start=True, stop=True),
                    waits=rw, inc="s_mm")
                tm = p.tok("s_mm")
                p.op("vector", lambda h, base=base, j=j, par=par:
                     h.reduce_max(prow(base + j, base + j + 1),
                                  psA[0:1, par * 512:(par + 1) * 512],
                                  axis=AX), waits=[tm], inc="s_ve")
                red_t[idx] = p.tok("s_ve")
                idx += 1
        t_ = p.seq("vector", [
            lambda h: h.reduce_max(prow(112, 113), prow(64, 80), axis=AX),
            lambda h: h.reduce_max(prow(113, 114), prow(80, 96), axis=AX),
            lambda h: h.reduce_max(prow(114, 115), prow(96, 112), axis=AX),
            lambda h: h.tensor_tensor(prow(115, 116), prow(112, 113),
                                      prow(114, 115), ALU.mult),
            lambda h: h.tensor_tensor(prow(116, 117), prow(113, 114),
                                      prow(114, 115), ALU.mult),
        ], hard=[red_t[47]])
        p.op("scalar", lambda h: h.activation(prow(117, 119),
                                              prow(115, 117), AF.Sqrt),
             waits=[t_], inc="s_se")
        t_ = p.tok("s_se")
        p.op("vector", lambda h: h.tensor_single_scalar(
            prow(119, 121), prow(117, 119), -CSC, ALU.mult),
            waits=[t_], inc="s_ve")
        t_ = p.tok("s_ve")
        p.op("gpsimd", lambda h: h.dma_start(out=t_bshd.ap(),
                                             in_=prow(119, 121)),
             waits=[t_], inc="s_gp")
        t_ = p.tok("s_gp")
        p.op("gpsimd", lambda h: h.dma_start(
            out=bshift[:, :], in_=bass.AP(t_bshd, 0, [[0, 128], [1, 2]])),
            waits=[t_], inc="s_gp")
        tk["bshift"] = p.tok("s_gp")

        ps_sp = [psA[:, 0:1024], psA[:, 1024:2048]]
        ps_o = [psA[:, 2048:2560], psA[:, 2560:3072]]
        ps_p = [psA[0:1, 3072:3584], psA[0:1, 3584:4096]]
        flatp = []
        for h_i in range(2):
            for b in range(NB):
                ti = h_i * NB + b
                nkt = 4 * (b + 1)
                for j in range(nkt // 2):
                    flatp.append((ti, h_i, b, j, nkt))
        NGP = len(flatp)
        scr_tok = [None] * NGP
        exp_done = [None] * NGP
        pt_ready = [None] * NGP
        av_done = [None] * NGP
        rp_done = [None] * 32
        ot_done = [None] * 32
        odma_done = [None] * 32
        d1_done = [None] * 32
        d2_done = [None] * 32

        def emit_S(g):
            ti, h_i, b, j, nkt = flatp[g]
            sp = g % 2
            qoff = qpos(h_i, b)
            w = [tk["qmy"], tk["kvh"], tk["bshift"], tk["masks"]]
            if g >= 2:
                w.append(exp_done[g - 2])
            kp0 = keypos(2 * j)
            kp1 = keypos(2 * j + 1)
            p.op("tensor", lambda h, kp0=kp0, sp=sp, qoff=qoff:
                 h.matmul(ps_sp[sp][:, 0:512], Kh[:, kp0:kp0 + 128],
                          Qmy[:, qoff:qoff + 512], start=True, stop=True),
                 waits=w)
            p.op("tensor", lambda h, kp1=kp1, sp=sp, qoff=qoff:
                 h.matmul(ps_sp[sp][:, 512:1024], Kh[:, kp1:kp1 + 128],
                          Qmy[:, qoff:qoff + 512], start=True, stop=True),
                 inc="s_mm")
            scr_tok[g] = p.tok("s_mm")

        emit_S(0)
        if NGP > 1:
            emit_S(1)
        for g in range(NGP):
            ti, h_i, b, j, nkt = flatp[g]
            sp = g % 2
            ob = ti % 2
            kt0, kt1 = 2 * j, 2 * j + 1
            ew = [scr_tok[g]]
            if g >= 2:
                ew.append(av_done[g - 2])
            p.op("scalar", lambda h, sp=sp, h_i=h_i: h.activation(
                ptp[sp][:, :], ps_sp[sp][:, :], AF.Exp,
                bias=bshift[:, h_i:h_i + 1], scale=CSC),
                waits=ew, inc="s_se")
            exp_done[g] = p.tok("s_se")
            ptok = exp_done[g]
            if kt0 >= 4 * b:
                m = kt0 - 4 * b
                p.op("vector", lambda h, sp=sp, m=m: h.tensor_tensor(
                    ptp[sp][:, :], ptp[sp][:, :],
                    masks[:, m * 512:(m + 2) * 512], ALU.mult),
                    waits=[ptok], inc="s_ve")
                ptok = p.tok("s_ve")
            pt_ready[g] = ptok
            if g + 2 < NGP:
                emit_S(g + 2)
            avw = [ptok]
            if j == 0 and ti >= 2:
                avw.append(ot_done[ti - 2])
                avw.append(rp_done[ti - 2])
            for q_, kt in ((0, kt0), (1, kt1)):
                kp = keypos(kt)
                p.op("tensor", lambda h, kp=kp, sp=sp, ob=ob, kt=kt,
                     nkt=nkt, q_=q_:
                     h.matmul(ps_o[ob][:, :], Vh[:, kp:kp + 128],
                              ptp[sp][:, q_ * 512:q_ * 512 + 512],
                              start=(kt == 0), stop=(kt == nkt - 1)),
                     waits=(avw if q_ == 0 else ()))
                p.op("tensor", lambda h, sp=sp, ob=ob, kt=kt, nkt=nkt,
                     q_=q_:
                     h.matmul(ps_p[ob][:, :], ones_b[:, :],
                              ptp[sp][:, q_ * 512:q_ * 512 + 512],
                              start=(kt == 0), stop=(kt == nkt - 1)),
                     inc=("s_mm" if q_ == 1 else None))
            av_done[g] = p.tok("s_mm")
            if kt1 == nkt - 1:
                tp_ = ti % 2
                rw = [av_done[g]]
                if ti >= 2:
                    rw.append(d1_done[ti - 2])
                p.op("vector", lambda h, ob=ob, tp_=tp_: h.reciprocal(
                    rp[tp_][0:1, :], ps_p[ob][:, :]), waits=rw, inc="s_ve")
                rp_done[ti] = p.tok("s_ve")
                p.op("gpsimd", lambda h, tp_=tp_: h.dma_start(
                    out=t_rpd.ap()[tp_:tp_ + 1, :], in_=rp[tp_][0:1, :]),
                    waits=[rp_done[ti]], inc="s_gp")
                d1_done[ti] = p.tok("s_gp")
                d2w = [d1_done[ti]]
                if ti >= 2:
                    d2w.append(ot_done[ti - 2])
                p.op("gpsimd", lambda h, tp_=tp_: h.dma_start(
                    out=rbuf[tp_][:, :],
                    in_=bass.AP(t_rpd, tp_ * 512, [[0, 128], [1, 512]])),
                    waits=d2w, inc="s_gp")
                d2_done[ti] = p.tok("s_gp")
                ow = [d2_done[ti], av_done[g]]
                if ti >= 2:
                    ow.append(odma_done[ti - 2])
                p.op("vector", lambda h, ob=ob, tp_=tp_: h.tensor_tensor(
                    otile[tp_][:, :], ps_o[ob][:, :], rbuf[tp_][:, :],
                    ALU.mult), waits=ow, inc="s_ve")
                ot_done[ti] = p.tok("s_ve")
                p.op("gpsimd", lambda h, b=b, h_i=h_i, tp_=tp_:
                     h.dma_start(
                         out=bass.AP(t_oinH[h_i], b * 512,
                                     [[8192, 128], [1, 512]]),
                         in_=otile[tp_][:, :]), waits=[ot_done[ti]],
                     inc="s_gp")
                odma_done[ti] = p.tok("s_gp")
                if ti == NB - 1:
                    p.op("gpsimd", lambda h: h.collective_compute(
                        "AllGather", ALU.bypass, replica_groups=RG,
                        ins=[t_oinH[0].ap().opt()],
                        outs=[t_ooutH[0].ap().opt()]),
                        waits=[odma_done[NB - 1]], inc="s_cc", incn=1)
                    tk["cc_oA"] = p.tok("s_cc")
        p.op("gpsimd", lambda h: h.collective_compute(
            "AllGather", ALU.bypass, replica_groups=RG,
            ins=[t_oinH[1].ap().opt()], outs=[t_ooutH[1].ap().opt()]),
            waits=[odma_done[-1]], inc="s_cc", incn=1)
        tk["cc_oB"] = p.tok("s_cc")
        tk["cc_o"] = tk["cc_oB"]

        otv = xln
        ow_ = [p.tok("s_mm"), p.tok("s_ve")]
        for hh in range(2):
            for colk, csl in (("colA", slice(0, 512)),
                              ("colB", slice(512, 1024))):
                p.op("sync", lambda h, hh=hh, colk=colk, csl=csl:
                     h.dma_start(
                         out=otv[:, :].rearrange("p (c t) -> p c t",
                                                 c=KCH)[:, hh * 8:
                                                        hh * 8 + 8, csl],
                         in_=bass.AP(t_ooutH[hh], regs[colk],
                                     [[8192, 128], [1048576, 8],
                                      [1, 512]])),
                     waits=ow_ + [tk["cc_oA" if hh == 0 else "cc_oB"]],
                     inc="s_ld")
        tk["otv"] = p.tok("s_ld")
        res2 = [None] * OCH
        for oc in range(OCH):
            par = oc % 2
            w = [tk["cc_oA"]]
            if oc >= 2:
                w.append(res2[oc - 2])
            p.op("sync", lambda h, oc=oc, par=par: h.dma_start(
                out=wob[par][:, :].rearrange("p (kc m) -> p kc m", kc=KCH),
                in_=bass.AP(t_wob, oc * D * 128,
                            [[128, 128], [16384, KCH], [1, 128]])),
                waits=w, inc="s_ld")
            tw = p.tok("s_ld")
            yb = psA[:, par * 1024:(par + 1) * 1024]
            mw = [tw, tk["otv"]]
            if oc >= 2:
                mw.append(res2[oc - 2])
            for kc in range(KCH):
                for half in (0, 1):
                    lst = (kc == KCH - 1 and half == 1)
                    p.op("tensor", lambda h, par=par, kc=kc, half=half,
                         yb=yb: h.matmul(
                             yb[:, half * 512:(half + 1) * 512],
                             wob[par][:, kc * 128:(kc + 1) * 128],
                             otv[:, kc * 1024 + half * 512:
                                 kc * 1024 + half * 512 + 512],
                             start=(kc == 0), stop=(kc == KCH - 1)),
                         waits=mw if (kc == 0 and half == 0) else [],
                         inc="s_mm" if lst else None)
            tmm = p.tok("s_mm")
            p.op("vector", lambda h, oc=oc, yb=yb: h.tensor_tensor(
                stream[:, oc * 1024:(oc + 1) * 1024],
                stream[:, oc * 1024:(oc + 1) * 1024], yb[:, :], ALU.add),
                waits=[tmm, tk["stream"]], inc="s_ve")
            res2[oc] = p.tok("s_ve")

        if stop == "attn":
            if DBG:
                end2 = [p.tok("s_ve"), p.tok("s_mm"), p.tok("s_se"),
                        p.tok("s_gp"), p.tok("s_ld"), p.tok("s_cc")]
                dq = nc.dram_tensor("dbg_qin", [D, NT], F32,
                                    kind="ExternalOutput")
                p.op("gpsimd", lambda h: h.dma_start(
                    out=dq.ap(), in_=t_qin.ap()), waits=end2, inc="s_out")
                dp = nc.dram_tensor("dbg_prow", [1, 1, 1024], F32,
                                    kind="ExternalOutput")
                p.op("gpsimd", lambda h: h.dma_start(
                    out=dp.ap()[0, 0:1, :], in_=V(PER, 1024, F32)[0:1, :]),
                    waits=end2, inc="s_out")
            for oc in range(OCH):
                p.op("gpsimd", lambda h, oc=oc: h.dma_start(
                    out=t_out.ap()[oc, :, :],
                    in_=stream[:, oc * 1024:(oc + 1) * 1024]),
                    waits=[res2[oc]], inc="s_out")
            p.op("gpsimd",
                 lambda h: h.wait_ge(sems["s_out"],
                                     16 * (OCH + (2 if DBG else 0))))
            p.emit(sems)
            return nc

        tk["xln"] = emit_ln(
            lambda kc: stream[:, kc * 1024:(kc + 1) * 1024],
            lambda kc: res2[kc], extra_mm_waits=[res2[OCH - 1]])
        pbar = [p.tok("s_mm"), p.tok("s_gp"), p.tok("s_ld"),
                p.tok("s_se"), tk["xln"]]
        htok4, sigtok4, _ = emit_ffn_B(False, entry_waits=pbar)
        emit_ffn_C(False, htok4, sigtok4, resid="stream")

        if DBG:
            end = [p.tok("s_ve"), p.tok("s_mm"), p.tok("s_se"),
                   p.tok("s_gp"), p.tok("s_ld"), p.tok("s_sc"),
                   p.tok("s_out")]
            dmph = nc.dram_tensor("dbg_h", [44, 128, 1024], F32,
                                  kind="ExternalOutput")
            for hc in range(HCH):
                p.op("gpsimd", lambda h, hc=hc: h.dma_start(
                    out=dmph.ap()[hc, :, :],
                    in_=hbuf[:, hc * 1024:(hc + 1) * 1024]),
                    waits=end, inc="s_out")
            dmpw = nc.dram_tensor("dbg_w3", [1, 128, HID], F32,
                                  kind="ExternalOutput")
            p.op("gpsimd", lambda h: h.dma_start(
                out=dmpw.ap()[0, :, :], in_=t_w3q.ap()[0, :, :]),
                waits=end, inc="s_out")
            dmps = nc.dram_tensor("dbg_stream", [KCH, 128, 1024], F32,
                                  kind="ExternalOutput")
            for kc in range(KCH):
                p.op("gpsimd", lambda h, kc=kc: h.dma_start(
                    out=dmps.ap()[kc, :, :],
                    in_=stream[:, kc * 1024:(kc + 1) * 1024]),
                    waits=end, inc="s_out")
            dmpx = nc.dram_tensor("dbg_xln", [KCH, 128, 1024], F32,
                                  kind="ExternalOutput")
            for kc in range(KCH):
                p.op("gpsimd", lambda h, kc=kc: h.dma_start(
                    out=dmpx.ap()[kc, :, :],
                    in_=xln[:, kc * 1024:(kc + 1) * 1024]),
                    waits=end, inc="s_out")
            p.op("gpsimd", lambda h: h.wait_ge(
                sems["s_out"], 16 * (OCH + HCH + 1 + KCH + KCH)))
        else:
            p.op("gpsimd", lambda h: h.wait_ge(sems["s_out"], 16 * OCH))
        p.emit(sems)
    return nc



def _prep_inputs(L0_attn_x, L0_attn_wq, L0_attn_wk, L0_attn_wv, L0_attn_wo,
                 L0_attn_bq, L0_attn_bk, L0_attn_bv, L0_ffn_w1, L0_ffn_w2,
                 L0_ffn_w3):
    f = np.float32
    x = np.asarray(L0_attn_x, f)[0]
    xT = np.ascontiguousarray(x.T)
    w1 = np.asarray(L0_ffn_w1, f)
    w2 = np.asarray(L0_ffn_w2, f)
    w3 = np.asarray(L0_ffn_w3, f)

    def panels(w, n):
        wT = np.ascontiguousarray(np.asarray(w, f).T)
        return np.ascontiguousarray(
            wT.reshape(wT.shape[0], n, 128).transpose(1, 0, 2))

    shared = {
        "w1p": panels(w1, HCH),
        "w2p": panels(w2, HCH),
        "w3p": panels(w3, OCH),
        "wqp": panels(L0_attn_wq, 16),
        "wkp": panels(L0_attn_wk, 2),
        "wvt": np.ascontiguousarray(np.asarray(L0_attn_wv, f).T),
        "wop": panels(np.asarray(L0_attn_wo, f).reshape(D, 16, 128)
                      [:, [0, 2, 4, 6, 8, 10, 12, 14,
                           1, 3, 5, 7, 9, 11, 13, 15], :].reshape(D, D), 16),
        "bqp": np.ascontiguousarray(
            np.asarray(L0_attn_bq, f).reshape(16, 128).T),
        "bkp": np.ascontiguousarray(
            np.asarray(L0_attn_bk, f).reshape(2, 128).T),
        "bvr": np.ascontiguousarray(np.asarray(L0_attn_bv, f).reshape(1, 256)),
    }
    gsz = D * HID // NCORES
    in_maps = []
    for i in range(NCORES):
        bA, bB = i, 15 - i
        cols = np.r_[bA * BS:(bA + 1) * BS, bB * BS:(bB + 1) * BS]
        m = dict(shared)
        m["xT"] = np.ascontiguousarray(xT[:, cols])
        m["pos"] = np.ascontiguousarray(cols.astype(f).reshape(1, NT))
        kv = i // 4
        m["selv"] = np.array([[kv * 131072, kv * 128, i * 262144,
                               bA * BS, bB * BS, 0, 0, 0]], np.int32)
        for name, w in (("g1", w1), ("g2", w2), ("g3", w3)):
            m[name] = np.ascontiguousarray(
                w.reshape(-1)[i * gsz:(i + 1) * gsz].reshape(128, GSL))
        in_maps.append(m)
    return in_maps


def _assemble(results):
    fullT = np.empty((D, L), np.float32)
    for i, r in enumerate(results):
        yT = np.asarray(r["yT"]).reshape(D, NT)
        bA, bB = i, 15 - i
        fullT[:, bA * BS:(bA + 1) * BS] = yT[:, 0:BS]
        fullT[:, bB * BS:(bB + 1) * BS] = yT[:, BS:NT]
    return np.ascontiguousarray(fullT.T).reshape(1, L, D)


def kernel(**inputs):
    in_maps = _prep_inputs(**inputs)
    nc = build_program("full")
    res = run_bass_kernel_spmd(nc, in_maps, core_ids=list(range(NCORES)))
    return _assemble(res.results)


# version 07 (final)
# speedup vs baseline: 1.5182x; 1.0119x over previous
import contextlib
import numpy as np
import concourse.bass as bass
import concourse.mybir as mybir
from concourse.bass_utils import run_bass_kernel_spmd

F32 = mybir.dt.float32
BF16 = mybir.dt.bfloat16
I32 = mybir.dt.int32
U8 = mybir.dt.uint8
AF = mybir.ActivationFunctionType
ALU = mybir.AluOpType
AX = mybir.AxisListType.X

D = 2048
HID = 5632
L = 8192
NCORES = 8
NT = 1024
NB = 16
BS = 512
KCH = 16
HCH = 44
OCH = 16
CSC = float(1.0 / np.sqrt(128.0))
NW = float(D * HID)
LN_EPS = 1e-5
Q_EPS = 1e-8
_2PI = np.float64(2 * np.pi)
TWO_PI_HI = float(np.float32(_2PI))
TWO_PI_LO = float(np.float32(_2PI - np.float64(np.float32(_2PI))))
INV_2PI = float(np.float32(1.0 / _2PI))
HALF_PI = float(np.float32(np.pi / 2))
LOG1E4 = float(np.log(10000.0))
GSL = D * HID // NCORES // 128
DBG = False

ENGS = ("sync", "scalar", "vector", "tensor", "gpsimd")
DMA_SEMS = {"s_ld", "s_sc", "s_gp", "s_out"}
OWN = {"tensor": "s_mm", "vector": "s_ve", "scalar": "s_se", "gpsimd": "s_gq"}

XLN = 0
HAR = 32768
STR = 122880
TAIL = 188416
PER = 206848
ARENA_B = 210944

DTS = {F32: 4, BF16: 2, I32: 4}


def _rank_slot(b):
    return (b, 0) if b <= 7 else (15 - b, 1)


class P:
    def __init__(self, nc):
        self.nc = nc
        self.q = {e: [] for e in ENGS}
        self.cnt = {}

    def tok(self, sem):
        return (sem, self.cnt.get(sem, 0))

    def op(self, eng, fn, waits=(), inc=None, incn=None, hard=()):
        incs = []
        if inc is not None:
            n = incn if incn is not None else (16 if inc in DMA_SEMS else 1)
            self.cnt[inc] = self.cnt.get(inc, 0) + n
            incs.append((inc, n))
        ws = {}
        hw = {}
        for w in waits:
            if w is None:
                continue
            s, v = w
            if v > 0 and v > ws.get(s, 0):
                ws[s] = v
        for w in hard:
            if w is None:
                continue
            s, v = w
            if v > 0 and v > hw.get(s, 0):
                hw[s] = v
        self.q[eng].append((sorted(ws.items()), sorted(hw.items()), fn, incs))
        return self.tok(inc) if inc is not None else None

    def seq(self, eng, fns, waits=(), hard=()):
        own = OWN[eng]
        t = None
        for i, fn in enumerate(fns):
            self.op(eng, fn, waits=waits if i == 0 else (),
                    hard=(list(hard) + ([t] if t else [])) if i == 0
                    else [t], inc=own)
            t = self.tok(own)
        return t

    def emit(self, sems):
        guar = {e: {} for e in ENGS}
        with self.nc.Block() as blk:
            for eng in ENGS:
                def body(h, eng=eng):
                    g = guar[eng]
                    own = OWN.get(eng)
                    for ws, hws, fn, incs in self.q[eng]:
                        for s, v in ws:
                            if s == own:
                                continue
                            if v > g.get(s, 0):
                                h.wait_ge(sems[s], v)
                                g[s] = v
                        for s, v in hws:
                            if v > g.get(s, 0):
                                h.wait_ge(sems[s], v)
                                g[s] = v
                        inst = fn(h)
                        for s, n in incs:
                            inst = inst.then_inc(sems[s], n)
                getattr(blk, eng)(body)


def build_program(stop="full"):
    nc = bass.Bass("TRN2", target_bir_lowering=False)

    t_xT = nc.dram_tensor("xT", [D, NT], F32, kind="ExternalInput")
    t_pos = nc.dram_tensor("pos", [1, NT], F32, kind="ExternalInput")
    t_g = [nc.dram_tensor("g%d" % (j + 1), [128, GSL], F32,
                          kind="ExternalInput") for j in range(3)]
    t_w1p = nc.dram_tensor("w1p", [HCH, D, 128], F32, kind="ExternalInput")
    t_w2p = nc.dram_tensor("w2p", [HCH, D, 128], F32, kind="ExternalInput")
    t_w3p = nc.dram_tensor("w3p", [OCH, HID, 128], F32, kind="ExternalInput")
    t_wqp = nc.dram_tensor("wqp", [16, D, 128], F32, kind="ExternalInput")
    t_wkp = nc.dram_tensor("wkp", [2, D, 128], F32, kind="ExternalInput")
    t_wvt = nc.dram_tensor("wvt", [D, 256], F32, kind="ExternalInput")
    t_wop = nc.dram_tensor("wop", [16, D, 128], F32, kind="ExternalInput")
    t_bq = nc.dram_tensor("bqp", [128, 16], F32, kind="ExternalInput")
    t_bk = nc.dram_tensor("bkp", [128, 2], F32, kind="ExternalInput")
    t_bv = nc.dram_tensor("bvr", [1, 256], F32, kind="ExternalInput")
    t_sel = nc.dram_tensor("selv", [1, 8], I32, kind="ExternalInput")
    t_out = nc.dram_tensor("yT", [OCH, 128, NT], F32, kind="ExternalOutput")

    t_w1q = nc.dram_tensor("w1q_d", [HCH, 128, D], BF16)
    t_w2q = nc.dram_tensor("w2q_d", [HCH, 128, D], BF16)
    t_w3q = nc.dram_tensor("w3q_d", [OCH, 128, HID], BF16)
    t_xmid = nc.dram_tensor("xmid_d", [KCH, 128, NT], F32)
    t_arin = nc.dram_tensor("ar_in", [1, 8], F32)
    t_arout = nc.dram_tensor("ar_out", [1, 8], F32)
    t_taud = nc.dram_tensor("tau_d", [1, 8], F32)
    t_bshd = nc.dram_tensor("bsh_d", [1, 2], F32)
    t_rpd = nc.dram_tensor("rp_d", [2, 512], F32)
    t_wqb = nc.dram_tensor("wqb_d", [16, D, 128], BF16)
    t_wkb = nc.dram_tensor("wkb_d", [2, D, 128], BF16)
    t_wvb = nc.dram_tensor("wvb_d", [D, 256], BF16)
    t_wob = nc.dram_tensor("wob_d", [16, D, 128], BF16)
    t_kvin = nc.dram_tensor("kvag_in", [1, 524288], BF16)
    t_kvout = nc.dram_tensor("kvag_out", [8, 524288], BF16, addr_space="Shared")
    t_qin = nc.dram_tensor("qag_in", [D, NT], BF16)
    t_qout = nc.dram_tensor("qag_out", [8 * D, NT], BF16, addr_space="Shared")
    t_oinH = [nc.dram_tensor("oag_in%d" % k, [1, 128, L], BF16)
              for k in range(2)]
    t_dbg2 = nc.dram_tensor("dbg2", [2, 128, NT], F32, kind="ExternalOutput")
    t_dbg3 = nc.dram_tensor("dbg3", [1, 128, NT], F32, kind="ExternalOutput")
    t_ooutH = [nc.dram_tensor("oag_out%d" % k, [8, 128, L], BF16,
                              addr_space="Shared") for k in range(2)]

    RG = [list(range(NCORES))]

    es = contextlib.ExitStack()
    with es:
        arena = es.enter_context(nc.sbuf_tensor("arena", [128, ARENA_B], U8))
        psA = es.enter_context(nc.psum_tensor("psA", [128, 4096], F32))
        sems = {}
        for s in ("s_ld", "s_sc", "s_gp", "s_out", "s_cc",
                  "s_mm", "s_ve", "s_se", "s_gq"):
            sems[s] = es.enter_context(nc.semaphore(s))

        def V(off, n, dt):
            return arena[:, off:off + n * DTS[dt]].bitcast(dt)

        ones_f = V(PER + 0, 1, F32)
        ones_b = V(PER + 64, 1, BF16)
        onesrow_f = V(PER + 128, 128, F32)[0:1, :]
        taus = V(PER + 704, 8, F32)
        bq_s = V(PER + 768, 16, F32)
        bk_s = V(PER + 832, 2, F32)
        bshift = V(PER + 896, 2, F32)
        invf = V(PER + 960, 1, F32)
        pv_t = V(PER + 1024, 1, F32)
        gtmp = V(PER + 1088, 1, F32)
        bvbc = V(PER + 1152, 256, BF16)
        sbsel = V(PER + 1600, 8, I32)
        prow_full = V(PER + 1664, 256, F32)

        def prow(a, b):
            return prow_full[0:1, a:b]

        xln = V(XLN, 16384, BF16)
        lnscr = V(XLN, 8192, F32)
        stream = V(STR, 16384, F32)
        hbuf = V(HAR, 45056, BF16)

        p = P(nc)
        tk = {}

        x0 = V(HAR, 16384, F32)
        gbuf = [V(STR + k * 11264, 2816, F32) for k in (0, 1)]
        xsq = [V(TAIL + k * 4096, 1024, F32) for k in (0, 1)]
        racc = V(TAIL + 8192, 16, F32)

        p.op("gpsimd", lambda h: h.memset(ones_f[:, :], 1.0), inc="s_gq")
        p.op("gpsimd", lambda h: h.memset(ones_b[:, :], 1.0), inc="s_gq")
        p.op("gpsimd", lambda h: h.memset(onesrow_f, 1.0), inc="s_gq")
        tk["consts"] = p.tok("s_gq")

        p.op("sync", lambda h: h.dma_start(
            out=x0[:, :].rearrange("p (c t) -> p c t", c=KCH),
            in_=t_xT.ap().rearrange("(c p) t -> p c t", p=128)), inc="s_ld")
        tk["x0"] = p.tok("s_ld")
        p.op("sync", lambda h: h.dma_start(out=bq_s[:, :], in_=t_bq.ap()),
             inc="s_ld")
        p.op("sync", lambda h: h.dma_start(out=sbsel[0:1, :], in_=t_sel.ap()),
             inc="s_ld")
        tk["sel"] = p.tok("s_ld")
        p.op("sync", lambda h: h.dma_start(out=bk_s[:, :], in_=t_bk.ap()),
             inc="s_ld")
        tk["bias"] = p.tok("s_ld")
        p.op("gpsimd", lambda h: h.dma_start(
            out=bvbc[:, :], in_=bass.AP(t_bv, 0, [[0, 128], [1, 256]])),
            inc="s_gp")
        tk["bvbc"] = p.tok("s_gp")

        for srcw, dstw in ((t_wqp, t_wqb), (t_wkp, t_wkb), (t_wop, t_wob)):
            n = srcw.shape[0]
            for ci in range(n):
                p.op("gpsimd", lambda h, srcw=srcw, dstw=dstw, ci=ci:
                     h.dma_start(out=dstw.ap()[ci], in_=srcw.ap()[ci]),
                     inc="s_gp")
        p.op("gpsimd", lambda h: h.dma_start(out=t_wvb.ap(),
                                             in_=t_wvt.ap()), inc="s_gp")
        tk["wcast"] = p.tok("s_gp")

        gred = [None] * 12
        for j in range(3):
            for c in range(4):
                i = 4 * j + c
                par = i % 2
                w = [gred[i - 2]] if i >= 2 else []
                p.op("scalar", lambda h, j=j, c=c, par=par: h.dma_start(
                    out=gbuf[par][:, :],
                    in_=t_g[j].ap()[:, c * 2816:(c + 1) * 2816]),
                    waits=w, inc="s_sc")
                ldt = p.tok("s_sc")
                p.op("vector", lambda h, i=i, par=par: h.tensor_reduce(
                    racc[:, i:i + 1], gbuf[par][:, :], axis=AX, op=ALU.add,
                    apply_absolute_value=True), waits=[ldt], inc="s_ve")
                gred[i] = p.tok("s_ve")
        p.op("tensor", lambda h: h.matmul(
            psA[0:1, 0:12], ones_f[:, :], racc[:, 0:12], start=True,
            stop=True), waits=[gred[11], tk["consts"]], inc="s_mm")
        tgmm = p.tok("s_mm")

        tk["gpart"] = p.seq("vector", [
            lambda h: h.tensor_copy(prow(0, 12), psA[0:1, 0:12]),
            lambda h: h.reduce_sum(prow(12, 13), prow(0, 4), axis=AX),
            lambda h: h.reduce_sum(prow(13, 14), prow(4, 8), axis=AX),
            lambda h: h.reduce_sum(prow(14, 15), prow(8, 12), axis=AX),
        ], waits=[tgmm])
        p.op("gpsimd", lambda h: h.dma_start(out=t_arin.ap(),
                                             in_=prow(12, 20)),
             waits=[tk["gpart"]], inc="s_gp")
        t_ = p.tok("s_gp")
        p.op("gpsimd", lambda h: h.collective_compute(
            "AllReduce", ALU.add, replica_groups=RG,
            ins=[t_arin.ap().opt()], outs=[t_arout.ap().opt()]),
            waits=[t_], inc="s_cc", incn=1)
        tar = p.tok("s_cc")
        p.op("gpsimd", lambda h: h.dma_start(out=prow(24, 32),
                                             in_=t_arout.ap()),
             waits=[tar], inc="s_gp")
        tab = p.tok("s_gp")

        ttr = p.seq("vector", [
            lambda h: h.tensor_scalar(prow(32, 35), prow(24, 27), 1.0 / NW,
                                      None, ALU.mult),
            lambda h: h.tensor_scalar(prow(35, 38), prow(32, 35), 0.5,
                                      0.5 * Q_EPS, ALU.mult, ALU.add),
            lambda h: h.tensor_scalar(prow(38, 41), prow(35, 38), -1.0,
                                      None, ALU.mult),
        ], waits=[tab])
        p.op("gpsimd", lambda h: h.dma_start(out=t_taud.ap(),
                                             in_=prow(35, 43)),
             waits=[ttr], inc="s_gp")
        t_ = p.tok("s_gp")
        p.op("gpsimd", lambda h: h.dma_start(
            out=taus[:, :], in_=bass.AP(t_taud, 0, [[0, 128], [1, 8]])),
            waits=[t_], inc="s_gp")
        tk["taus"] = p.tok("s_gp")

        def emit_ln(src, src_toks, extra_mm_waits=(), xs=None):
            if xs is None:
                xs = xsq
            mm_done = [None] * KCH
            for kc in range(KCH):
                par = kc % 2
                w = [src_toks(kc)] if src_toks else []
                if kc >= 2 and mm_done[kc - 2] is not None:
                    w = w + [mm_done[kc - 2]]
                p.op("scalar", lambda h, kc=kc, par=par: h.activation(
                    xs[par][:, :], src(kc), AF.Square), waits=w, inc="s_se")
                sqt = p.tok("s_se")
                mw = [sqt, tk["consts"]] + list(extra_mm_waits) + \
                    ([src_toks(kc)] if src_toks else [])
                for seg in range(4):
                    def mk(h, kc=kc, seg=seg, par=par):
                        if seg < 2:
                            rh = src(kc)[:, seg * 512:(seg + 1) * 512]
                        else:
                            rh = xs[par][:, (seg - 2) * 512:(seg - 1) * 512]
                        return h.matmul(psA[0:1, seg * 512:(seg + 1) * 512],
                                        ones_f[:, :], rh,
                                        start=(kc == 0), stop=(kc == KCH - 1))
                    p.op("tensor", mk, waits=mw if seg == 0 else [sqt],
                         inc="s_mm" if seg == 3 else None)
                mm_done[kc] = p.tok("s_mm")
            tsum = mm_done[KCH - 1]

            M, VE, T, R0, RS = 0, 1024, 2048, 3072, 4096

            t1 = p.seq("vector", [
                lambda h: h.tensor_scalar(lnscr[0:1, M:M + 1024],
                                          psA[0:1, 0:1024], 1.0 / D, None,
                                          ALU.mult),
                lambda h: h.tensor_scalar(lnscr[0:1, VE:VE + 1024],
                                          psA[0:1, 1024:2048], 1.0 / D,
                                          LN_EPS, ALU.mult, ALU.add),
                lambda h: h.tensor_tensor(lnscr[0:1, T:T + 1024],
                                          lnscr[0:1, M:M + 1024],
                                          lnscr[0:1, M:M + 1024], ALU.mult),
                lambda h: h.tensor_tensor(lnscr[0:1, VE:VE + 1024],
                                          lnscr[0:1, VE:VE + 1024],
                                          lnscr[0:1, T:T + 1024],
                                          ALU.subtract),
                lambda h: h.reciprocal(lnscr[0:1, R0:R0 + 1024],
                                       lnscr[0:1, VE:VE + 1024]),
            ], waits=[tsum])
            p.op("scalar", lambda h: h.activation(
                lnscr[0:1, RS:RS + 1024], lnscr[0:1, R0:R0 + 1024], AF.Sqrt),
                waits=[t1], inc="s_se")
            t2 = p.tok("s_se")

            t3 = p.seq("vector", [
                lambda h: h.tensor_tensor(lnscr[0:1, T:T + 1024],
                                          lnscr[0:1, RS:RS + 1024],
                                          lnscr[0:1, RS:RS + 1024],
                                          ALU.mult),
                lambda h: h.tensor_tensor(lnscr[0:1, T:T + 1024],
                                          lnscr[0:1, T:T + 1024],
                                          lnscr[0:1, VE:VE + 1024],
                                          ALU.mult),
                lambda h: h.tensor_scalar(lnscr[0:1, T:T + 1024],
                                          lnscr[0:1, T:T + 1024], -0.5, 1.5,
                                          ALU.mult, ALU.add),
                lambda h: h.tensor_tensor(lnscr[0:1, RS:RS + 1024],
                                          lnscr[0:1, RS:RS + 1024],
                                          lnscr[0:1, T:T + 1024], ALU.mult),
                lambda h: h.tensor_tensor(lnscr[0:1, M:M + 1024],
                                          lnscr[0:1, M:M + 1024],
                                          lnscr[0:1, RS:RS + 1024],
                                          ALU.mult),
            ], waits=[t2])
            for i in range(4):
                coff = RS if i < 2 else M
                boff = 2048 + i * 512
                p.op("tensor", lambda h, coff=coff, boff=boff, i=i: h.matmul(
                    psA[:, boff:boff + 512], onesrow_f,
                    lnscr[0:1, coff + (i % 2) * 512:
                          coff + (i % 2) * 512 + 512],
                    start=True, stop=True),
                    waits=[t3, tk["consts"]], inc="s_mm" if i == 3 else None)
            tbc = p.tok("s_mm")
            last = None
            xtoks = []
            for kc in range(KCH):
                par = kc % 2

                def _x(h, kc=kc, par=par):
                    h.tensor_tensor(xs[par][:, :], src(kc),
                                    psA[:, 2048:3072], ALU.mult)
                    return h.tensor_tensor(
                        xln[:, kc * 1024:(kc + 1) * 1024], xs[par][:, :],
                        psA[:, 3072:4096], ALU.subtract)
                p.op("vector", _x, waits=[tbc] +
                     ([src_toks(kc)] if src_toks else []), inc="s_ve")
                last = p.tok("s_ve")
                xtoks.append(last)
            return last, xtoks

        tk["xln"], _ = emit_ln(lambda kc: x0[:, kc * 1024:(kc + 1) * 1024],
                               lambda kc: tk["x0"],
                               extra_mm_waits=[tk["gpart"]])

        def emit_ffn_B(pre, entry_waits, lnt=None):
            if lnt is None:
                lnt = [tk["xln"]] * KCH
            if pre:
                wraw = [[V(STR + k * 8192, 2048, F32) for k in (0, 1)],
                        [V(STR + 16384 + k * 8192, 2048, F32)
                         for k in (0, 1)]]
                wqv = [[V(STR + 32768 + k * 4096, 2048, BF16)
                        for k in (0, 1)],
                       [V(STR + 40960 + k * 4096, 2048, BF16)
                        for k in (0, 1)]]
                ltmp = V(STR + 49152, 2048, BF16)
                sigb = V(STR + 53248, 1024, BF16)
            else:
                wqv = [[V(TAIL + k * 4096, 2048, BF16) for k in (0, 1)],
                       [V(TAIL + 8192 + k * 4096, 2048, BF16)
                        for k in (0, 1)]]
                sigb = V(TAIL + 16384, 1024, BF16)
            xb = [[psA[:, 0:1024], psA[:, 1024:2048]],
                  [psA[:, 2048:3072], psA[:, 3072:4096]]]
            wsrc = (t_w1p, t_w2p)
            wdst = (t_w1q, t_w2q)
            tcols = ((0, 3), (1, 4))
            qtok = [[None] * HCH, [None] * HCH]
            stok = [[None] * HCH, [None] * HCH]
            mmtok = [[None] * HCH, [None] * HCH]
            sigtok = [None] * HCH
            htok = [None] * HCH
            for hc in range(HCH):
                par = hc % 2
                for wi in range(2):
                    fw = [mmtok[wi][hc - 2]] if hc >= 2 else list(entry_waits)
                    if pre:
                        if hc >= 2:
                            fw.append(qtok[wi][hc - 2])
                        p.op("sync", lambda h, wi=wi, hc=hc, par=par:
                             h.dma_start(
                                 out=wraw[wi][par][:, :].rearrange(
                                     "p (kc m) -> p kc m", kc=KCH),
                                 in_=bass.AP(wsrc[wi], hc * D * 128,
                                             [[128, 128], [16384, KCH],
                                              [1, 128]])),
                             waits=fw, inc="s_ld")
                        ldt = p.tok("s_ld")
                        qw = [ldt, tk["taus"]]
                        if hc >= 2:
                            qw += [mmtok[wi][hc - 2], stok[wi][hc - 2]]
                        if hc == 0 and wi == 0:
                            qw += list(entry_waits)

                        def _q(h, wi=wi, par=par):
                            tc, nc_ = tcols[wi]
                            h.tensor_single_scalar(
                                ltmp[:, :], wraw[wi][par][:, :],
                                taus[:, nc_:nc_ + 1], ALU.is_lt)
                            return h.scalar_tensor_tensor(
                                wqv[wi][par][:, :], wraw[wi][par][:, :],
                                taus[:, tc:tc + 1], ltmp[:, :],
                                ALU.is_gt, ALU.subtract)
                        p.op("vector", _q, waits=qw, inc="s_ve")
                        qtok[wi][hc] = p.tok("s_ve")
                        p.op("gpsimd", lambda h, wi=wi, hc=hc, par=par:
                             h.dma_start(out=wdst[wi].ap()[hc, :, :],
                                         in_=wqv[wi][par][:, :]),
                             waits=[qtok[wi][hc]], inc="s_gp")
                        stok[wi][hc] = p.tok("s_gp")
                    else:
                        p.op("sync", lambda h, wi=wi, hc=hc, par=par:
                             h.dma_start(out=wqv[wi][par][:, :],
                                         in_=wdst[wi].ap()[hc, :, :]),
                             waits=fw, inc="s_ld")
                        qtok[wi][hc] = p.tok("s_ld")
                for wi in range(2):
                    mw = [qtok[wi][hc]]
                    if hc >= 2:
                        mw.append(sigtok[hc - 2] if wi == 0
                                  else htok[hc - 2])
                    if hc < 2:
                        mw += list(entry_waits)
                    for kc in range(KCH):
                        for half in (0, 1):
                            lst = (kc == KCH - 1 and half == 1)
                            w_ = []
                            if half == 0:
                                w_ = [lnt[kc]]
                                if kc == 0:
                                    w_ = mw + [lnt[KCH - 1] if wi == 1
                                               else lnt[0]]
                            p.op("tensor", lambda h, wi=wi, par=par, kc=kc,
                                 half=half: h.matmul(
                                     xb[wi][par][:, half * 512:
                                                 (half + 1) * 512],
                                     wqv[wi][par][:, kc * 128:
                                                  (kc + 1) * 128],
                                     xln[:, kc * 1024 + half * 512:
                                         kc * 1024 + half * 512 + 512],
                                     start=(kc == 0), stop=(kc == KCH - 1)),
                                 waits=w_,
                                 inc="s_mm" if lst else None)
                    mmtok[wi][hc] = p.tok("s_mm")
                sw = [mmtok[0][hc]]
                if hc >= 1:
                    sw.append(htok[hc - 1])
                p.op("scalar", lambda h, par=par: h.activation(
                    sigb[:, :], xb[0][par][:, :], AF.Sigmoid),
                    waits=sw, inc="s_se")
                sigtok[hc] = p.tok("s_se")
                p.op("vector", lambda h, hc=hc, par=par: h.tensor_tensor(
                    hbuf[:, hc * 1024:(hc + 1) * 1024], sigb[:, :],
                    xb[1][par][:, :], ALU.mult),
                    waits=[sigtok[hc], mmtok[1][hc]], inc="s_ve")
                htok[hc] = p.tok("s_ve")
            return htok, sigtok, stok

        def emit_ffn_C(pre, htok, sigtok, resid, centry=()):
            if pre:
                w3raw = [V(STR + k * 5632, 1408, F32) for k in (0, 1, 2)]
                w3qs = [V(STR + 16896 + k * 2816, 1408, BF16)
                        for k in (0, 1, 2)]
                ltmp2 = V(STR + 25344, 1408, BF16)
                xres = [V(STR + 28288 + k * 4096, 1024, F32)
                        for k in (0, 1, 2, 3)]
                outb = [V(STR + 44672 + k * 4096, 1024, F32) for k in (0, 1)]
            else:
                w3qs = [V(XLN + k * 2816, 1408, BF16) for k in (0, 1, 2)]
                outb = [V(XLN + 8448 + k * 4096, 1024, F32) for k in (0, 1)]
            yb = [psA[:, 0:1024], psA[:, 1024:2048]]
            NS = 4 * OCH
            ld_tok = [None] * NS
            qv_tok = [None] * NS
            sc_tok = [None] * NS
            mm_sub = [None] * NS
            eptok = [None] * OCH
            dmtok = [None] * OCH
            entry = [sigtok[HCH - 1], htok[HCH - 1]] + list(centry)
            entry = entry + [p.tok("s_mm")]
            p.op("gpsimd", lambda h: h.nop(),
                 waits=[sigtok[HCH - 1], htok[HCH - 1]], inc="s_gq")
            entry = entry + [p.tok("s_gq")]

            def prep(si):
                oc, j = si // 4, si % 4
                jp = si % 3
                fw = [mm_sub[si - 3]] if si >= 3 else list(entry)
                if pre:
                    lw = list(fw)
                    if si >= 3:
                        lw.append(qv_tok[si - 3])
                    else:
                        lw += list(entry)
                    p.op("sync", lambda h, oc=oc, j=j, jp=jp:
                         h.dma_start(
                             out=w3raw[jp][:, :].rearrange(
                                 "p (kc m) -> p kc m", kc=11),
                             in_=bass.AP(t_w3p,
                                         oc * HID * 128 + j * 1408 * 128,
                                         [[128, 128], [16384, 11],
                                          [1, 128]])),
                         waits=lw, inc="s_ld")
                    ldt = p.tok("s_ld")
                    qw = [ldt, tk["taus"]] + fw
                    if si >= 3:
                        qw.append(sc_tok[si - 3])

                    def _q3(h, jp=jp):
                        h.tensor_single_scalar(
                            ltmp2[:, :], w3raw[jp][:, :], taus[:, 5:6],
                            ALU.is_lt)
                        return h.scalar_tensor_tensor(
                            w3qs[jp][:, :], w3raw[jp][:, :],
                            taus[:, 2:3], ltmp2[:, :],
                            ALU.is_gt, ALU.subtract)
                    p.op("vector", _q3, waits=qw, inc="s_ve")
                    qv_tok[si] = p.tok("s_ve")
                    ld_tok[si] = qv_tok[si]
                    p.op("gpsimd", lambda h, oc=oc, j=j, jp=jp:
                         h.dma_start(
                             out=t_w3q.ap()[oc, :,
                                            j * 1408:(j + 1) * 1408],
                             in_=w3qs[jp][:, :]),
                         waits=[qv_tok[si]], inc="s_gp")
                    sc_tok[si] = p.tok("s_gp")
                else:
                    p.op("sync", lambda h, oc=oc, j=j, jp=jp:
                         h.dma_start(
                             out=w3qs[jp][:, :],
                             in_=t_w3q.ap()[oc, :,
                                            j * 1408:(j + 1) * 1408]),
                         waits=fw, inc="s_ld")
                    ld_tok[si] = p.tok("s_ld")

            prep(0)
            prep(1)
            prep(2)
            for oc in range(OCH):
                par = oc % 2
                for kc in range(HCH):
                    si = 4 * oc + kc // 11
                    jp = si % 3
                    kw = [ld_tok[si], htok[kc]]
                    if kc == 0:
                        kw.append(htok[0])
                        if oc >= 2:
                            kw.append(eptok[oc - 2])
                        else:
                            kw += list(entry)
                    for half in (0, 1):
                        sub_last = (kc % 11 == 10 and half == 1)
                        p.op("tensor", lambda h, par=par, kc=kc, jp=jp,
                             half=half: h.matmul(
                                 yb[par][:, half * 512:(half + 1) * 512],
                                 w3qs[jp][:, (kc % 11) * 128:
                                          (kc % 11) * 128 + 128],
                                 hbuf[:, kc * 1024 + half * 512:
                                      kc * 1024 + half * 512 + 512],
                                 start=(kc == 0), stop=(kc == HCH - 1)),
                             waits=kw if half == 0 else [],
                             inc="s_mm" if sub_last else None)
                        if sub_last:
                            mm_sub[si] = p.tok("s_mm")
                    if kc % 11 == 10 and si + 3 < NS:
                        prep(si + 3)
                ymm = mm_sub[4 * oc + 3]
                if resid == "xT":
                    rw = [eptok[oc - 4]] if oc >= 4 else \
                        [htok[HCH - 1]] + list(centry)
                    p.op("scalar", lambda h, oc=oc: h.dma_start(
                        out=xres[oc % 4][:, :],
                        in_=t_xT.ap()[oc * 128:(oc + 1) * 128, :]),
                        waits=rw, inc="s_sc")
                    rt = [p.tok("s_sc")]
                    rsl = lambda oc=oc: xres[oc % 4][:, :]
                else:
                    rt = []
                    rsl = lambda oc=oc: stream[:, oc * 1024:(oc + 1) * 1024]
                ew = [ymm] + rt
                if oc >= 2:
                    ew.append(dmtok[oc - 2])
                else:
                    ew += [htok[HCH - 1]] + list(centry)
                if DBG and not pre and oc == 0:
                    dbg3buf = V(TAIL, 1024, F32)
                    p.op("scalar", lambda h, par=par: h.copy(
                        dbg3buf[:, :], yb[par][:, :]), waits=[ymm],
                        inc="s_se")
                    t_d3 = p.tok("s_se")
                    p.op("gpsimd", lambda h: h.dma_start(
                        out=t_dbg3.ap()[0, :, :], in_=dbg3buf[:, :]),
                        waits=[t_d3], inc="s_gp")
                    ew.append(t_d3)
                p.op("vector", lambda h, par=par, rsl=rsl:
                     h.scalar_tensor_tensor(
                         outb[par][:, :], yb[par][:, :], 0.5, rsl(),
                         ALU.mult, ALU.add), waits=ew, inc="s_ve")
                eptok[oc] = p.tok("s_ve")
                if pre:
                    p.op("gpsimd", lambda h, oc=oc, par=par: h.dma_start(
                        out=t_xmid.ap()[oc, :, :], in_=outb[par][:, :]),
                        waits=[eptok[oc]], inc="s_gp")
                    dmtok[oc] = p.tok("s_gp")
                    if stop == "ffnpre":
                        p.op("gpsimd", lambda h, oc=oc, par=par: h.dma_start(
                            out=t_out.ap()[oc, :, :], in_=outb[par][:, :]),
                            waits=[eptok[oc]], inc="s_out")
                        dmtok[oc] = p.tok("s_out")
                else:
                    p.op("gpsimd", lambda h, oc=oc, par=par: h.dma_start(
                        out=t_out.ap()[oc, :, :], in_=outb[par][:, :]),
                        waits=[eptok[oc]], inc="s_out")
                    dmtok[oc] = p.tok("s_out")
                    if DBG and oc < 2:
                        p.op("gpsimd", lambda h, oc=oc, par=par: h.dma_start(
                            out=t_dbg2.ap()[oc, :, :], in_=outb[par][:, :]),
                            waits=[eptok[oc]], inc="s_gp")
            return eptok, dmtok

        htokB, sigtokB, stokB = emit_ffn_B(True, entry_waits=[gred[11]])
        eptokC, dmtokC = emit_ffn_C(
            True, htokB, sigtokB, resid="xT",
            centry=[stokB[0][HCH - 1], stokB[1][HCH - 1]])

        if stop == "ffnpre":
            if DBG:
                end = [p.tok("s_ve"), p.tok("s_mm"), p.tok("s_se"),
                       p.tok("s_gp"), p.tok("s_ld"), p.tok("s_sc")]
                dump = nc.dram_tensor("dbg", [8, 128, 1024], F32,
                                      kind="ExternalOutput")
                p.op("gpsimd", lambda h: h.dma_start(
                    out=dump.ap()[0, 0:1, 0:1024],
                    in_=V(PER, 1024, F32)[0:1, :]), waits=end, inc="s_out")
                p.op("gpsimd", lambda h: h.dma_start(
                    out=dump.ap()[1, :, :],
                    in_=xln[:, 0:1024]), waits=end, inc="s_out")
                p.op("gpsimd", lambda h: h.dma_start(
                    out=dump.ap()[2, :, :],
                    in_=hbuf[:, 0:1024]), waits=end, inc="s_out")
                p.op("gpsimd", lambda h: h.dma_start(
                    out=dump.ap()[3, :, :],
                    in_=t_w1q.ap()[0, :, 0:1024]), waits=end, inc="s_out")
                p.op("gpsimd", lambda h: h.dma_start(
                    out=dump.ap()[4, :, :],
                    in_=x0[:, 0:1024]), waits=end, inc="s_out")
                p.op("gpsimd", lambda h: h.dma_start(
                    out=dump.ap()[5, 0:1, :],
                    in_=lnscr[0:1, 4096:5120]), waits=end, inc="s_out")
                dmph = nc.dram_tensor("dbg_h", [44, 128, 1024], F32,
                                      kind="ExternalOutput")
                for hc in range(HCH):
                    p.op("gpsimd", lambda h, hc=hc: h.dma_start(
                        out=dmph.ap()[hc, :, :],
                        in_=hbuf[:, hc * 1024:(hc + 1) * 1024]),
                        waits=end, inc="s_out")
                dmpw = nc.dram_tensor("dbg_w3", [OCH, 128, HID], F32,
                                      kind="ExternalOutput")
                for oc in range(OCH):
                    p.op("gpsimd", lambda h, oc=oc: h.dma_start(
                        out=dmpw.ap()[oc, :, :],
                        in_=t_w3q.ap()[oc, :, :]),
                        waits=end, inc="s_out")
                dmpx = nc.dram_tensor("dbg_xln", [KCH, 128, 1024], F32,
                                      kind="ExternalOutput")
                for kc in range(KCH):
                    p.op("gpsimd", lambda h, kc=kc: h.dma_start(
                        out=dmpx.ap()[kc, :, :],
                        in_=xln[:, kc * 1024:(kc + 1) * 1024]),
                        waits=end, inc="s_out")
            p.op("gpsimd",
                 lambda h: h.wait_ge(sems["s_out"],
                                     16 * (OCH + (6 + HCH + OCH + KCH
                                                  if DBG else 0))))
            p.emit(sems)
            return nc

        strk = [None] * KCH
        for kc in range(KCH):
            sw_ = [dmtokC[kc], eptokC[OCH - 1]]
            if 10 <= kc <= 12:
                sw_.append(dmtokC[OCH - 1])
            p.op("sync", lambda h, kc=kc: h.dma_start(
                out=stream[:, kc * 1024:(kc + 1) * 1024],
                in_=t_xmid.ap()[kc, :, :]),
                waits=sw_, inc="s_ld")
            strk[kc] = p.tok("s_ld")
        tk["stream"] = strk[KCH - 1]
        tk["xln"], lnt2 = emit_ln(
            lambda kc: stream[:, kc * 1024:(kc + 1) * 1024],
            lambda kc: strk[kc],
            extra_mm_waits=[eptokC[OCH - 1]])
        tk["xln2"] = tk["xln"]

        masks = V(HAR, 2048, BF16)
        kctr = V(HAR + 4096, 2048, BF16)
        vctr = V(HAR + 8192, 2048, BF16)
        cosf = V(HAR + 12288, 1024, F32)
        ssin = V(HAR + 16384, 1024, F32)
        posb = V(HAR + 20480, 1024, F32)
        tt1 = V(HAR + 24576, 1024, F32)
        kint = V(HAR + 28672, 1024, I32)
        ktf = V(HAR + 32768, 1024, F32)
        rtmp = V(HAR + 36864, 1024, F32)
        rtmp2 = V(HAR + 40960, 1024, F32)
        pm = V(HAR + 45056, 128, F32)
        pm2 = V(HAR + 45696, 128, F32)
        qrb = [V(HAR + 46336 + k * 2048, 1024, BF16) for k in (0, 1)]
        wkb = [V(HAR + 50432 + k * 4096, 2048, BF16) for k in (0, 1)]
        wvb = V(HAR + 58624, 4096, BF16)
        wqb = [V(HAR + 66816 + k * 4096, 2048, BF16) for k in (0, 1)]
        itile = V(HAR + 75008, 512, F32)
        jmi = V(HAR + 77056, 128, F32)
        Qmy = V(HAR + 4096, 16384, BF16)
        Kh = V(HAR + 36864, 8192, BF16)
        Vh = V(HAR + 53248, 8192, BF16)
        ptp = [V(HAR + 69632 + k * 2048, 1024, BF16) for k in (0, 1)]
        rbuf = [V(HAR + 73728 + k * 2048, 512, F32) for k in (0, 1)]
        qsqs = [V(HAR + 77824 + k * 1024, 512, BF16) for k in (0, 1)]
        otile = [V(HAR + 79872 + k * 1024, 512, BF16) for k in (0, 1)]
        wob = wqb
        qraw = [V(TAIL + k * 4096, 1024, F32) for k in (0, 1)]
        rp = [V(TAIL + 8192 + k * 2048, 512, F32) for k in (0, 1)]

        hfree = [p.tok("s_mm")]
        p.op("gpsimd", lambda h: h.iota(
            itile[:, :], [[-1, 512]], channel_multiplier=1,
            allow_small_or_imprecise_dtypes=True), waits=hfree, inc="s_gq")
        tio = p.tok("s_gq")
        mtk = None
        for m in range(4):
            p.op("vector", lambda h, m=m: h.tensor_single_scalar(
                masks[:, m * 512:(m + 1) * 512], itile[:, :],
                float(-128 * m), ALU.is_le),
                waits=[tio, tk["xln2"]], inc="s_ve")
            mtk = p.tok("s_ve")
        tk["masks"] = mtk
        p.op("gpsimd", lambda h: h.iota(
            jmi[:, :], [[1, 128]], channel_multiplier=-1,
            allow_small_or_imprecise_dtypes=True), waits=hfree, inc="s_gq")
        tjm = p.tok("s_gq")

        tk["pm"] = p.seq("vector", [
            lambda h: h.tensor_single_scalar(pm[:, :], jmi[:, :], 64.0,
                                             ALU.is_equal),
            lambda h: h.tensor_single_scalar(pm2[:, :], jmi[:, :], -64.0,
                                             ALU.is_equal),
            lambda h: h.tensor_tensor(pm[:, :], pm[:, :], pm2[:, :],
                                      ALU.add),
        ], waits=[tjm])

        p.op("gpsimd", lambda h: h.iota(
            pv_t[:, :], [[1, 1]], channel_multiplier=1,
            allow_small_or_imprecise_dtypes=True), inc="s_gq")
        tpv = p.tok("s_gq")

        t_ = p.seq("vector", [
            lambda h: h.tensor_single_scalar(gtmp[:, :], pv_t[:, :], 64.0,
                                             ALU.is_ge),
            lambda h: h.scalar_tensor_tensor(gtmp[:, :], gtmp[:, :], -64.0,
                                             pv_t[:, :], ALU.mult, ALU.add),
            lambda h: h.tensor_single_scalar(gtmp[:, :], gtmp[:, :],
                                             -LOG1E4 / 64.0, ALU.mult),
        ], waits=[tpv])
        p.op("scalar", lambda h: h.activation(invf[:, :], gtmp[:, :],
                                              AF.Exp), waits=[t_],
             inc="s_se")
        tk["invf"] = p.tok("s_se")
        p.op("gpsimd", lambda h: h.dma_start(
            out=posb[:, :], in_=bass.AP(t_pos, 0, [[0, 128], [1, NT]])),
            waits=hfree, inc="s_gp")
        tk["posb"] = p.tok("s_gp")
        p.op("vector", lambda h: h.tensor_single_scalar(
            posb[:, :], posb[:, :], invf[:, 0:1], ALU.mult),
            waits=[tk["invf"], tk["posb"]], inc="s_ve")
        tang = p.tok("s_ve")

        def _red(dst, shift, tprev):
            fns = []
            if shift:
                fns.append(lambda h: h.tensor_scalar(
                    tt1[:, :], posb[:, :], INV_2PI, 0.25, ALU.mult, ALU.add))
            else:
                fns.append(lambda h: h.tensor_single_scalar(
                    tt1[:, :], posb[:, :], INV_2PI, ALU.mult))
            fns.append(lambda h: h.tensor_copy(kint[:, :], tt1[:, :]))
            fns.append(lambda h: h.tensor_copy(ktf[:, :], kint[:, :]))
            fns.append(lambda h: h.scalar_tensor_tensor(
                rtmp[:, :], ktf[:, :], -TWO_PI_HI, posb[:, :], ALU.mult,
                ALU.add))
            if shift:
                fns.append(lambda h: h.tensor_single_scalar(
                    rtmp[:, :], rtmp[:, :], HALF_PI, ALU.add))
            fns.append(lambda h: h.scalar_tensor_tensor(
                rtmp[:, :], ktf[:, :], -TWO_PI_LO, rtmp[:, :], ALU.mult,
                ALU.add))
            t = p.seq("vector", fns, hard=[tprev])
            p.op("scalar", lambda h, dst=dst: h.activation(
                dst[:, :], rtmp[:, :], AF.Sin), waits=[t], inc="s_se")
            return p.tok("s_se")

        tsin = _red(ssin, False, tang)
        p.op("vector", lambda h: h.tensor_single_scalar(
            ssin[0:64, :], ssin[0:64, :], -1.0, ALU.mult),
            waits=[tsin], inc="s_ve")
        tflip = p.tok("s_ve")
        tk["rope_tab"] = _red(cosf, True, tflip)

        for c in range(2):
            p.op("sync", lambda h, c=c: h.dma_start(
                out=wkb[c][:, :].rearrange("p (kc m) -> p kc m", kc=KCH),
                in_=bass.AP(t_wkb, c * D * 128,
                            [[128, 128], [16384, KCH], [1, 128]])),
                waits=hfree + [tk["wcast"]], inc="s_ld")
        tk["wkb"] = p.tok("s_ld")
        p.op("sync", lambda h: h.dma_start(
            out=wvb[:, :].rearrange("p (kc m) -> p kc m", kc=KCH),
            in_=t_wvb.ap().rearrange("(kc p) m -> p kc m", p=128)),
            waits=hfree + [tk["wcast"]], inc="s_ld")
        tk["wvb"] = p.tok("s_ld")

        kparts = []
        for c in range(2):
            kb = psA[:, c * 1024:(c + 1) * 1024]
            swb = psA[:, 2048 + c * 1024:3072 + c * 1024]
            for kc in range(KCH):
                for half in (0, 1):
                    lst = (kc == KCH - 1 and half == 1)
                    p.op("tensor", lambda h, c=c, kc=kc, half=half, kb=kb:
                         h.matmul(kb[:, half * 512:(half + 1) * 512],
                                  wkb[c][:, kc * 128:(kc + 1) * 128],
                                  xln[:, kc * 1024 + half * 512:
                                      kc * 1024 + half * 512 + 512],
                                  start=(kc == 0), stop=(kc == KCH - 1)),
                         waits=([tk["wkb"], lnt2[kc]] if half == 0 else []),
                         inc="s_mm" if lst else None)
            tmm = p.tok("s_mm")
            p.op("scalar", lambda h, c=c, kb=kb: h.activation(
                qraw[c][:, :], kb[:, :], AF.Identity, bias=bk_s[:, c:c + 1]),
                waits=[tmm], inc="s_se")
            tev = p.tok("s_se")
            p.op("tensor", lambda h, swb=swb, c=c: h.matmul(
                swb[:, 0:512], pm[:, :], qraw[c][:, 0:512], start=True,
                stop=True), waits=[tev, tk["pm"]])
            p.op("tensor", lambda h, swb=swb, c=c: h.matmul(
                swb[:, 512:1024], pm[:, :], qraw[c][:, 512:1024],
                start=True, stop=True), inc="s_mm")
            tsw = p.tok("s_mm")

            def _rope(h, c=c, swb=swb):
                h.tensor_tensor(rtmp[:, :], qraw[c][:, :], cosf[:, :],
                                ALU.mult)
                h.tensor_tensor(rtmp2[:, :], swb[:, :], ssin[:, :],
                                ALU.mult)
                return h.tensor_tensor(kctr[:, c * 1024:(c + 1) * 1024],
                                       rtmp[:, :], rtmp2[:, :], ALU.add)
            p.op("vector", _rope, waits=[tsw, tk["rope_tab"]], inc="s_ve")
            kparts.append(p.tok("s_ve"))
        tk["kctr"] = kparts[-1]

        vev = [None] * 8
        for tt in range(8):
            par = tt % 2
            vb = psA[:, par * 512:par * 512 + 256]
            w = [tk["wvb"], tk["kctr"]]
            if tt >= 2:
                w.append(vev[tt - 2])
            for kc in range(KCH):
                p.op("tensor", lambda h, tt=tt, kc=kc, vb=vb: h.matmul(
                    vb[:, :],
                    xln[:, kc * 1024 + tt * 128:kc * 1024 + tt * 128 + 128],
                    wvb[:, kc * 256:(kc + 1) * 256],
                    start=(kc == 0), stop=(kc == KCH - 1)),
                    waits=(w + [lnt2[kc]]) if kc == 0 else [lnt2[kc]],
                    inc="s_mm" if kc == KCH - 1 else None)
            tv = p.tok("s_mm")
            p.op("vector", lambda h, tt=tt, vb=vb: h.tensor_tensor(
                vctr[:, tt * 256:(tt + 1) * 256], vb[:, :], bvbc[:, :],
                ALU.add), waits=[tv, tk["bvbc"]], inc="s_ve")
            vev[tt] = p.tok("s_ve")
        tk["vctr"] = vev[-1]

        p.op("gpsimd", lambda h: h.dma_start(
            out=bass.AP(t_kvin, 0, [[1024, 128], [131072, 2], [1, 1024]]),
            in_=kctr[:, :].rearrange("p (c t) -> p c t", c=2)),
            waits=[tk["kctr"]], inc="s_gp")
        p.op("gpsimd", lambda h: h.dma_start(
            out=bass.AP(t_kvin, 262144, [[256, 128], [32768, 8], [1, 256]]),
            in_=vctr[:, :].rearrange("p (tt m) -> p tt m", tt=8)),
            waits=[tk["vctr"]], inc="s_gp")
        t_ = p.tok("s_gp")
        p.op("gpsimd", lambda h: h.collective_compute(
            "AllGather", ALU.bypass, replica_groups=RG,
            ins=[t_kvin.ap().opt()], outs=[t_kvout.ap().opt()]),
            waits=[t_], inc="s_cc", incn=1)
        tk["cc_kv"] = p.tok("s_cc")

        qdma = [None] * 16
        qropet = [None] * 16
        qevt = [None] * 16

        def emit_swap_rope(c):
            par = c % 2
            swb = psA[:, 2048 + par * 1024:3072 + par * 1024]
            p.op("tensor", lambda h, swb=swb, par=par: h.matmul(
                swb[:, 0:512], pm[:, :], qraw[par][:, 0:512], start=True,
                stop=True), waits=[qevt[c], tk["pm"]] +
                ([qropet[c - 2]] if c >= 2 else []))
            p.op("tensor", lambda h, swb=swb, par=par: h.matmul(
                swb[:, 512:1024], pm[:, :], qraw[par][:, 512:1024],
                start=True, stop=True), inc="s_mm")
            tsw = p.tok("s_mm")
            qw = [tsw, tk["rope_tab"]]
            if c >= 2:
                qw.append(qdma[c - 2])

            def _ropeq(h, par=par, swb=swb):
                h.tensor_tensor(rtmp[:, :], qraw[par][:, :], cosf[:, :],
                                ALU.mult)
                h.tensor_tensor(rtmp2[:, :], swb[:, :], ssin[:, :],
                                ALU.mult)
                return h.tensor_tensor(qrb[par][:, :], rtmp[:, :],
                                       rtmp2[:, :], ALU.add)
            p.op("vector", _ropeq, waits=qw, inc="s_ve")
            qropet[c] = p.tok("s_ve")
            p.op("gpsimd", lambda h, c=c, par=par: h.dma_start(
                out=t_qin.ap()[c * 128:(c + 1) * 128, :],
                in_=qrb[par][:, :]), waits=[qropet[c]], inc="s_gp")
            qdma[c] = p.tok("s_gp")

        for c in range(16):
            par = c % 2
            qb = psA[:, par * 1024:(par + 1) * 1024]
            lw = [tk["xln2"], tk["vctr"], hfree[0]]
            if c >= 2:
                lw.append(qevt[c - 2])
            p.op("sync", lambda h, c=c, par=par: h.dma_start(
                out=wqb[par][:, :].rearrange("p (kc m) -> p kc m", kc=KCH),
                in_=bass.AP(t_wqb, c * D * 128,
                            [[128, 128], [16384, KCH], [1, 128]])),
                waits=lw + [tk["wcast"]], inc="s_ld")
            tw = p.tok("s_ld")
            mw = [tw, tk["vctr"]]
            if c >= 2:
                mw.append(qevt[c - 2])
            for kc in range(KCH):
                for half in (0, 1):
                    lst = (kc == KCH - 1 and half == 1)
                    p.op("tensor", lambda h, par=par, kc=kc, half=half,
                         qb=qb: h.matmul(
                             qb[:, half * 512:(half + 1) * 512],
                             wqb[par][:, kc * 128:(kc + 1) * 128],
                             xln[:, kc * 1024 + half * 512:
                                 kc * 1024 + half * 512 + 512],
                             start=(kc == 0), stop=(kc == KCH - 1)),
                         waits=mw if (kc == 0 and half == 0) else [],
                         inc="s_mm" if lst else None)
            tmm = p.tok("s_mm")
            sw2 = [tmm]
            if c >= 2:
                sw2.append(qropet[c - 2])
            p.op("scalar", lambda h, c=c, par=par, qb=qb: h.activation(
                qraw[par][:, :], qb[:, :], AF.Identity,
                bias=bq_s[:, c:c + 1]), waits=sw2, inc="s_se")
            qevt[c] = p.tok("s_se")
            if c == 0:
                emit_swap_rope(0)
            elif c >= 2:
                emit_swap_rope(c - 1)
        emit_swap_rope(15)
        p.op("gpsimd", lambda h: h.collective_compute(
            "AllGather", ALU.bypass, replica_groups=RG,
            ins=[t_qin.ap().opt()], outs=[t_qout.ap().opt()]),
            waits=[qdma[15]], inc="s_cc", incn=1)
        tk["cc_q"] = p.tok("s_cc")

        safe = [tk["cc_kv"], qdma[15], qropet[15], p.tok("s_mm"),
                p.tok("s_se"), tk["sel"]]
        regs = {}

        def _store_regs(h):
            regs["kvk"] = h.value_load(sbsel[0:1, 0:1])
            regs["kvv"] = h.value_load(sbsel[0:1, 1:2])
            regs["qof"] = h.value_load(sbsel[0:1, 2:3])
            regs["colA"] = h.value_load(sbsel[0:1, 3:4])
            regs["colB"] = h.value_load(sbsel[0:1, 4:5])
            return h.nop()
        p.op("sync", _store_regs, waits=safe)
        p.op("sync", lambda h: h.dma_start(
            out=Kh[:, :].rearrange("p (r t) -> p r t", r=8),
            in_=bass.AP(t_kvout, regs["kvk"],
                        [[1024, 128], [524288, 8], [1, 1024]])),
            waits=safe, inc="s_ld")
        for r in range(8):
            p.op("sync", lambda h, r=r: h.dma_start(
                out=Vh[:, r * 1024:(r + 1) * 1024].rearrange(
                    "p (j m) -> p j m", j=8),
                in_=bass.AP(t_kvout, regs["kvv"] + (262144 + r * 524288),
                            [[256, 128], [32768, 8], [1, 128]])),
                waits=safe, inc="s_ld")
        tk["kvh"] = p.tok("s_ld")
        for hh in range(2):
            p.op("sync", lambda h, hh=hh: h.dma_start(
                out=Qmy[:, hh * 8192:(hh + 1) * 8192].rearrange(
                    "p (r t) -> p r t", r=8),
                in_=bass.AP(t_qout, regs["qof"] + hh * 131072,
                            [[1024, 128], [2097152, 8], [1, 1024]])),
                waits=[tk["cc_q"]] + safe, inc="s_ld")
        tk["qmy"] = p.tok("s_ld")

        def keypos(kt):
            b = kt // 4
            r, s = _rank_slot(b)
            return r * 1024 + s * 512 + (kt % 4) * 128

        def qpos(h_i, b):
            r, s = _rank_slot(b)
            return h_i * 8192 + r * 1024 + s * 512

        red_t = [None] * 48
        idx = 0
        for bsrc, nsl, base in ((Qmy, 32, 64), (Kh, 16, 96)):
            for j in range(nsl):
                par = idx % 2
                w = [tk["qmy"], tk["kvh"], tk["masks"]]
                if idx >= 2:
                    w.append(red_t[idx - 2])
                p.op("scalar", lambda h, bsrc=bsrc, j=j, par=par:
                     h.activation(qsqs[par][:, :],
                                  bsrc[:, j * 512:(j + 1) * 512], AF.Square),
                     waits=w, inc="s_se")
                ts_ = p.tok("s_se")
                rw = [ts_, tk["consts"]]
                if idx >= 2:
                    rw.append(red_t[idx - 2])
                p.op("tensor", lambda h, par=par: h.matmul(
                    psA[0:1, par * 512:(par + 1) * 512], ones_b[:, :],
                    qsqs[par][:, :], start=True, stop=True),
                    waits=rw, inc="s_mm")
                tm = p.tok("s_mm")
                p.op("vector", lambda h, base=base, j=j, par=par:
                     h.reduce_max(prow(base + j, base + j + 1),
                                  psA[0:1, par * 512:(par + 1) * 512],
                                  axis=AX), waits=[tm], inc="s_ve")
                red_t[idx] = p.tok("s_ve")
                idx += 1
        t_ = p.seq("vector", [
            lambda h: h.reduce_max(prow(112, 113), prow(64, 80), axis=AX),
            lambda h: h.reduce_max(prow(113, 114), prow(80, 96), axis=AX),
            lambda h: h.reduce_max(prow(114, 115), prow(96, 112), axis=AX),
            lambda h: h.tensor_tensor(prow(115, 116), prow(112, 113),
                                      prow(114, 115), ALU.mult),
            lambda h: h.tensor_tensor(prow(116, 117), prow(113, 114),
                                      prow(114, 115), ALU.mult),
        ], hard=[red_t[47]])
        p.op("scalar", lambda h: h.activation(prow(117, 119),
                                              prow(115, 117), AF.Sqrt),
             waits=[t_], inc="s_se")
        t_ = p.tok("s_se")
        p.op("vector", lambda h: h.tensor_single_scalar(
            prow(119, 121), prow(117, 119), -CSC, ALU.mult),
            waits=[t_], inc="s_ve")
        t_ = p.tok("s_ve")
        p.op("gpsimd", lambda h: h.dma_start(out=t_bshd.ap(),
                                             in_=prow(119, 121)),
             waits=[t_], inc="s_gp")
        t_ = p.tok("s_gp")
        p.op("gpsimd", lambda h: h.dma_start(
            out=bshift[:, :], in_=bass.AP(t_bshd, 0, [[0, 128], [1, 2]])),
            waits=[t_], inc="s_gp")
        tk["bshift"] = p.tok("s_gp")

        ps_sp = [psA[:, 0:1024], psA[:, 1024:2048]]
        ps_o = [psA[:, 2048:2560], psA[:, 2560:3072]]
        ps_p = [psA[0:1, 3072:3584], psA[0:1, 3584:4096]]
        flatp = []
        for h_i in range(2):
            for b in range(NB):
                ti = h_i * NB + b
                nkt = 4 * (b + 1)
                for j in range(nkt // 2):
                    flatp.append((ti, h_i, b, j, nkt))
        NGP = len(flatp)
        scr_tok = [None] * NGP
        exp_done = [None] * NGP
        pt_ready = [None] * NGP
        av_done = [None] * NGP
        rp_done = [None] * 32
        ot_done = [None] * 32
        odma_done = [None] * 32
        d1_done = [None] * 32
        d2_done = [None] * 32

        def emit_S(g):
            ti, h_i, b, j, nkt = flatp[g]
            sp = g % 2
            qoff = qpos(h_i, b)
            w = [tk["qmy"], tk["kvh"], tk["bshift"], tk["masks"]]
            if g >= 2:
                w.append(exp_done[g - 2])
            kp0 = keypos(2 * j)
            kp1 = keypos(2 * j + 1)
            p.op("tensor", lambda h, kp0=kp0, sp=sp, qoff=qoff:
                 h.matmul(ps_sp[sp][:, 0:512], Kh[:, kp0:kp0 + 128],
                          Qmy[:, qoff:qoff + 512], start=True, stop=True),
                 waits=w)
            p.op("tensor", lambda h, kp1=kp1, sp=sp, qoff=qoff:
                 h.matmul(ps_sp[sp][:, 512:1024], Kh[:, kp1:kp1 + 128],
                          Qmy[:, qoff:qoff + 512], start=True, stop=True),
                 inc="s_mm")
            scr_tok[g] = p.tok("s_mm")

        emit_S(0)
        if NGP > 1:
            emit_S(1)
        for g in range(NGP):
            ti, h_i, b, j, nkt = flatp[g]
            sp = g % 2
            ob = ti % 2
            kt0, kt1 = 2 * j, 2 * j + 1
            ew = [scr_tok[g]]
            if g >= 2:
                ew.append(av_done[g - 2])
            p.op("scalar", lambda h, sp=sp, h_i=h_i: h.activation(
                ptp[sp][:, :], ps_sp[sp][:, :], AF.Exp,
                bias=bshift[:, h_i:h_i + 1], scale=CSC),
                waits=ew, inc="s_se")
            exp_done[g] = p.tok("s_se")
            ptok = exp_done[g]
            if kt0 >= 4 * b:
                m = kt0 - 4 * b
                p.op("vector", lambda h, sp=sp, m=m: h.tensor_tensor(
                    ptp[sp][:, :], ptp[sp][:, :],
                    masks[:, m * 512:(m + 2) * 512], ALU.mult),
                    waits=[ptok], inc="s_ve")
                ptok = p.tok("s_ve")
            pt_ready[g] = ptok
            if g + 2 < NGP:
                emit_S(g + 2)
            avw = [ptok]
            if j == 0 and ti >= 2:
                avw.append(ot_done[ti - 2])
                avw.append(rp_done[ti - 2])
            for q_, kt in ((0, kt0), (1, kt1)):
                kp = keypos(kt)
                p.op("tensor", lambda h, kp=kp, sp=sp, ob=ob, kt=kt,
                     nkt=nkt, q_=q_:
                     h.matmul(ps_o[ob][:, :], Vh[:, kp:kp + 128],
                              ptp[sp][:, q_ * 512:q_ * 512 + 512],
                              start=(kt == 0), stop=(kt == nkt - 1)),
                     waits=(avw if q_ == 0 else ()))
                p.op("tensor", lambda h, sp=sp, ob=ob, kt=kt, nkt=nkt,
                     q_=q_:
                     h.matmul(ps_p[ob][:, :], ones_b[:, :],
                              ptp[sp][:, q_ * 512:q_ * 512 + 512],
                              start=(kt == 0), stop=(kt == nkt - 1)),
                     inc=("s_mm" if q_ == 1 else None))
            av_done[g] = p.tok("s_mm")
            if kt1 == nkt - 1:
                tp_ = ti % 2
                rw = [av_done[g]]
                if ti >= 2:
                    rw.append(d1_done[ti - 2])
                p.op("vector", lambda h, ob=ob, tp_=tp_: h.reciprocal(
                    rp[tp_][0:1, :], ps_p[ob][:, :]), waits=rw, inc="s_ve")
                rp_done[ti] = p.tok("s_ve")
                p.op("gpsimd", lambda h, tp_=tp_: h.dma_start(
                    out=t_rpd.ap()[tp_:tp_ + 1, :], in_=rp[tp_][0:1, :]),
                    waits=[rp_done[ti]], inc="s_gp")
                d1_done[ti] = p.tok("s_gp")
                d2w = [d1_done[ti]]
                if ti >= 2:
                    d2w.append(ot_done[ti - 2])
                p.op("gpsimd", lambda h, tp_=tp_: h.dma_start(
                    out=rbuf[tp_][:, :],
                    in_=bass.AP(t_rpd, tp_ * 512, [[0, 128], [1, 512]])),
                    waits=d2w, inc="s_gp")
                d2_done[ti] = p.tok("s_gp")
                ow = [d2_done[ti], av_done[g]]
                if ti >= 2:
                    ow.append(odma_done[ti - 2])
                p.op("vector", lambda h, ob=ob, tp_=tp_: h.tensor_tensor(
                    otile[tp_][:, :], ps_o[ob][:, :], rbuf[tp_][:, :],
                    ALU.mult), waits=ow, inc="s_ve")
                ot_done[ti] = p.tok("s_ve")
                p.op("gpsimd", lambda h, b=b, h_i=h_i, tp_=tp_:
                     h.dma_start(
                         out=bass.AP(t_oinH[h_i], b * 512,
                                     [[8192, 128], [1, 512]]),
                         in_=otile[tp_][:, :]), waits=[ot_done[ti]],
                     inc="s_gp")
                odma_done[ti] = p.tok("s_gp")
                if ti == NB - 1:
                    p.op("gpsimd", lambda h: h.collective_compute(
                        "AllGather", ALU.bypass, replica_groups=RG,
                        ins=[t_oinH[0].ap().opt()],
                        outs=[t_ooutH[0].ap().opt()]),
                        waits=[odma_done[NB - 1]], inc="s_cc", incn=1)
                    tk["cc_oA"] = p.tok("s_cc")
        p.op("gpsimd", lambda h: h.collective_compute(
            "AllGather", ALU.bypass, replica_groups=RG,
            ins=[t_oinH[1].ap().opt()], outs=[t_ooutH[1].ap().opt()]),
            waits=[odma_done[-1]], inc="s_cc", incn=1)
        tk["cc_oB"] = p.tok("s_cc")
        tk["cc_o"] = tk["cc_oB"]

        otv = xln
        ow_ = [p.tok("s_mm"), p.tok("s_ve")]
        for hh in range(2):
            for colk, csl in (("colA", slice(0, 512)),
                              ("colB", slice(512, 1024))):
                p.op("sync", lambda h, hh=hh, colk=colk, csl=csl:
                     h.dma_start(
                         out=otv[:, :].rearrange("p (c t) -> p c t",
                                                 c=KCH)[:, hh * 8:
                                                        hh * 8 + 8, csl],
                         in_=bass.AP(t_ooutH[hh], regs[colk],
                                     [[8192, 128], [1048576, 8],
                                      [1, 512]])),
                     waits=ow_ + [tk["cc_oA" if hh == 0 else "cc_oB"]],
                     inc="s_ld")
        tk["otv"] = p.tok("s_ld")
        res2 = [None] * OCH
        for oc in range(OCH):
            par = oc % 2
            w = [tk["cc_oA"]]
            if oc >= 2:
                w.append(res2[oc - 2])
            p.op("sync", lambda h, oc=oc, par=par: h.dma_start(
                out=wob[par][:, :].rearrange("p (kc m) -> p kc m", kc=KCH),
                in_=bass.AP(t_wob, oc * D * 128,
                            [[128, 128], [16384, KCH], [1, 128]])),
                waits=w, inc="s_ld")
            tw = p.tok("s_ld")
            yb = psA[:, par * 1024:(par + 1) * 1024]
            mw = [tw, tk["otv"]]
            if oc >= 2:
                mw.append(res2[oc - 2])
            for kc in range(KCH):
                for half in (0, 1):
                    lst = (kc == KCH - 1 and half == 1)
                    p.op("tensor", lambda h, par=par, kc=kc, half=half,
                         yb=yb: h.matmul(
                             yb[:, half * 512:(half + 1) * 512],
                             wob[par][:, kc * 128:(kc + 1) * 128],
                             otv[:, kc * 1024 + half * 512:
                                 kc * 1024 + half * 512 + 512],
                             start=(kc == 0), stop=(kc == KCH - 1)),
                         waits=mw if (kc == 0 and half == 0) else [],
                         inc="s_mm" if lst else None)
            tmm = p.tok("s_mm")
            p.op("vector", lambda h, oc=oc, yb=yb: h.tensor_tensor(
                stream[:, oc * 1024:(oc + 1) * 1024],
                stream[:, oc * 1024:(oc + 1) * 1024], yb[:, :], ALU.add),
                waits=[tmm, tk["stream"]], inc="s_ve")
            res2[oc] = p.tok("s_ve")

        if stop == "attn":
            if DBG:
                end2 = [p.tok("s_ve"), p.tok("s_mm"), p.tok("s_se"),
                        p.tok("s_gp"), p.tok("s_ld"), p.tok("s_cc")]
                dq = nc.dram_tensor("dbg_qin", [D, NT], F32,
                                    kind="ExternalOutput")
                p.op("gpsimd", lambda h: h.dma_start(
                    out=dq.ap(), in_=t_qin.ap()), waits=end2, inc="s_out")
                dp = nc.dram_tensor("dbg_prow", [1, 1, 1024], F32,
                                    kind="ExternalOutput")
                p.op("gpsimd", lambda h: h.dma_start(
                    out=dp.ap()[0, 0:1, :], in_=V(PER, 1024, F32)[0:1, :]),
                    waits=end2, inc="s_out")
            for oc in range(OCH):
                p.op("gpsimd", lambda h, oc=oc: h.dma_start(
                    out=t_out.ap()[oc, :, :],
                    in_=stream[:, oc * 1024:(oc + 1) * 1024]),
                    waits=[res2[oc]], inc="s_out")
            p.op("gpsimd",
                 lambda h: h.wait_ge(sems["s_out"],
                                     16 * (OCH + (2 if DBG else 0))))
            p.emit(sems)
            return nc

        xsq4 = [V(HAR + k * 4096, 1024, F32) for k in (0, 1)]
        tk["xln"], lnt3 = emit_ln(
            lambda kc: stream[:, kc * 1024:(kc + 1) * 1024],
            lambda kc: res2[kc], extra_mm_waits=[res2[OCH - 1]], xs=xsq4)
        pbar = [p.tok("s_mm"), p.tok("s_gp"), p.tok("s_ld"),
                p.tok("s_se")]
        htok4, sigtok4, _ = emit_ffn_B(False, entry_waits=pbar, lnt=lnt3)
        emit_ffn_C(False, htok4, sigtok4, resid="stream")

        if DBG:
            end = [p.tok("s_ve"), p.tok("s_mm"), p.tok("s_se"),
                   p.tok("s_gp"), p.tok("s_ld"), p.tok("s_sc"),
                   p.tok("s_out")]
            dmph = nc.dram_tensor("dbg_h", [44, 128, 1024], F32,
                                  kind="ExternalOutput")
            for hc in range(HCH):
                p.op("gpsimd", lambda h, hc=hc: h.dma_start(
                    out=dmph.ap()[hc, :, :],
                    in_=hbuf[:, hc * 1024:(hc + 1) * 1024]),
                    waits=end, inc="s_out")
            dmpw = nc.dram_tensor("dbg_w3", [1, 128, HID], F32,
                                  kind="ExternalOutput")
            p.op("gpsimd", lambda h: h.dma_start(
                out=dmpw.ap()[0, :, :], in_=t_w3q.ap()[0, :, :]),
                waits=end, inc="s_out")
            dmps = nc.dram_tensor("dbg_stream", [KCH, 128, 1024], F32,
                                  kind="ExternalOutput")
            for kc in range(KCH):
                p.op("gpsimd", lambda h, kc=kc: h.dma_start(
                    out=dmps.ap()[kc, :, :],
                    in_=stream[:, kc * 1024:(kc + 1) * 1024]),
                    waits=end, inc="s_out")
            dmpx = nc.dram_tensor("dbg_xln", [KCH, 128, 1024], F32,
                                  kind="ExternalOutput")
            for kc in range(KCH):
                p.op("gpsimd", lambda h, kc=kc: h.dma_start(
                    out=dmpx.ap()[kc, :, :],
                    in_=xln[:, kc * 1024:(kc + 1) * 1024]),
                    waits=end, inc="s_out")
            p.op("gpsimd", lambda h: h.wait_ge(
                sems["s_out"], 16 * (OCH + HCH + 1 + KCH + KCH)))
        else:
            p.op("gpsimd", lambda h: h.wait_ge(sems["s_out"], 16 * OCH))
        p.emit(sems)
    return nc



def _prep_inputs(L0_attn_x, L0_attn_wq, L0_attn_wk, L0_attn_wv, L0_attn_wo,
                 L0_attn_bq, L0_attn_bk, L0_attn_bv, L0_ffn_w1, L0_ffn_w2,
                 L0_ffn_w3):
    f = np.float32
    x = np.asarray(L0_attn_x, f)[0]
    xT = np.ascontiguousarray(x.T)
    w1 = np.asarray(L0_ffn_w1, f)
    w2 = np.asarray(L0_ffn_w2, f)
    w3 = np.asarray(L0_ffn_w3, f)

    def panels(w, n):
        wT = np.ascontiguousarray(np.asarray(w, f).T)
        return np.ascontiguousarray(
            wT.reshape(wT.shape[0], n, 128).transpose(1, 0, 2))

    shared = {
        "w1p": panels(w1, HCH),
        "w2p": panels(w2, HCH),
        "w3p": panels(w3, OCH),
        "wqp": panels(L0_attn_wq, 16),
        "wkp": panels(L0_attn_wk, 2),
        "wvt": np.ascontiguousarray(np.asarray(L0_attn_wv, f).T),
        "wop": panels(np.asarray(L0_attn_wo, f).reshape(D, 16, 128)
                      [:, [0, 2, 4, 6, 8, 10, 12, 14,
                           1, 3, 5, 7, 9, 11, 13, 15], :].reshape(D, D), 16),
        "bqp": np.ascontiguousarray(
            np.asarray(L0_attn_bq, f).reshape(16, 128).T),
        "bkp": np.ascontiguousarray(
            np.asarray(L0_attn_bk, f).reshape(2, 128).T),
        "bvr": np.ascontiguousarray(np.asarray(L0_attn_bv, f).reshape(1, 256)),
    }
    gsz = D * HID // NCORES
    in_maps = []
    for i in range(NCORES):
        bA, bB = i, 15 - i
        cols = np.r_[bA * BS:(bA + 1) * BS, bB * BS:(bB + 1) * BS]
        m = dict(shared)
        m["xT"] = np.ascontiguousarray(xT[:, cols])
        m["pos"] = np.ascontiguousarray(cols.astype(f).reshape(1, NT))
        kv = i // 4
        m["selv"] = np.array([[kv * 131072, kv * 128, i * 262144,
                               bA * BS, bB * BS, 0, 0, 0]], np.int32)
        for name, w in (("g1", w1), ("g2", w2), ("g3", w3)):
            m[name] = np.ascontiguousarray(
                w.reshape(-1)[i * gsz:(i + 1) * gsz].reshape(128, GSL))
        in_maps.append(m)
    return in_maps


def _assemble(results):
    fullT = np.empty((D, L), np.float32)
    for i, r in enumerate(results):
        yT = np.asarray(r["yT"]).reshape(D, NT)
        bA, bB = i, 15 - i
        fullT[:, bA * BS:(bA + 1) * BS] = yT[:, 0:BS]
        fullT[:, bB * BS:(bB + 1) * BS] = yT[:, BS:NT]
    return np.ascontiguousarray(fullT.T).reshape(1, L, D)


def kernel(**inputs):
    in_maps = _prep_inputs(**inputs)
    nc = build_program("full")
    res = run_bass_kernel_spmd(nc, in_maps, core_ids=list(range(NCORES)))
    return _assemble(res.results)
